# version 31
# baseline (speedup 1.0000x reference)
import contextlib
import numpy as np
import concourse.bass as bass
import concourse.mybir as mybir

F32 = mybir.dt.float32
BF16 = mybir.dt.bfloat16
U8 = mybir.dt.uint8
AX = mybir.AxisListType.X
OP = mybir.AluOpType
AF = mybir.ActivationFunctionType

S, C, HD = 2048, 1024, 4096
NS, NC, NH = 16, 8, 32
W2_RES = 16
EPS = 1e-5

ENGS = ("tensor", "vector", "scalar", "gpsimd", "sync")
STRICT = True
STRICT_ENGS = {"vector", "scalar"}
ENG_SEM = {"tensor": "T", "vector": "V", "scalar": "Sc", "gpsimd": "G"}


class Plan:
    def __init__(self):
        self.ops = {e: [] for e in ENGS}
        self.cnt = {}
        self.strict_now = set()

    def emit(self, eng, fn, waits=(), inc=None, inc_by=1):
        waits = [(s, v) for (s, v) in waits if v > 0]
        if eng in ENG_SEM:
            es = ENG_SEM[eng]
            if (eng == "gpsimd" or (STRICT and eng in STRICT_ENGS)
                    or eng in self.strict_now) and self.c(es) > 0:
                waits.append((es, self.c(es)))
            if inc is None:
                inc = es
        self.ops[eng].append((fn, waits, inc, inc_by))
        if inc is not None:
            self.cnt[inc] = self.cnt.get(inc, 0) + inc_by

    def c(self, sem):
        return self.cnt.get(sem, 0)


def build_nc():
    nc = bass.Bass("TRN2", target_bir_lowering=False)
    x = nc.dram_tensor("x", [S, C], F32, kind="ExternalInput").ap()
    wq = nc.dram_tensor("wq", [C, C], F32, kind="ExternalInput").ap()
    wk = nc.dram_tensor("wk", [C, C], F32, kind="ExternalInput").ap()
    wv = nc.dram_tensor("wv", [C, C], F32, kind="ExternalInput").ap()
    gamma = nc.dram_tensor("gamma", [1, 1], F32, kind="ExternalInput").ap()
    w1 = nc.dram_tensor("w1", [C, HD], F32, kind="ExternalInput").ap()
    b1 = nc.dram_tensor("b1", [HD], F32, kind="ExternalInput").ap()
    w2 = nc.dram_tensor("w2", [HD, C], F32, kind="ExternalInput").ap()
    b2 = nc.dram_tensor("b2", [C], F32, kind="ExternalInput").ap()
    y = nc.dram_tensor("y", [S, C], F32, kind="ExternalOutput").ap()
    w1bf = nc.dram_tensor("w1bf", [C, HD], BF16, kind="Internal").ap()
    w2bf = nc.dram_tensor("w2bf", [HD, C], BF16, kind="Internal").ap()

    ctx = contextlib.ExitStack()
    sb = lambda n, s, d: ctx.enter_context(nc.sbuf_tensor(n, s, d))
    regA = sb("regA", [128, 65536], U8)
    regFG = sb("regFG", [128, 65536], U8)
    regH = sb("regH", [128, 32768], U8)
    regE = sb("regE", [128, 20480], U8)
    rA_f = regA.bitcast(F32)
    rA_b = regA.bitcast(BF16)
    rFG = regFG.bitcast(BF16)
    rH = regH.bitcast(BF16)
    rE = regE.bitcast(BF16)

    resid = lambda ti: rA_f[:, ti * 1024:(ti + 1) * 1024]
    resid2 = lambda ti, et: rA_f[:, ti * 1024 + et * 512: ti * 1024 + et * 512 + 512]

    _wtg_base = (12288, 8192)
    wtg_all2 = [rA_b[:, b:b + 4096].rearrange("p (ci d) -> p ci d", ci=8)
                for b in _wtg_base]
    WTg2 = lambda bi, ci, off, n: rA_b[:, _wtg_base[bi] + ci * 512 + off:
                                       _wtg_base[bi] + ci * 512 + off + n]
    xfm_all = rA_b[:, 16384:32768].rearrange("p (ci s) -> p ci s", ci=8)
    xfm = lambda ci, s0, n: rA_b[:, 16384 + ci * 2048 + s0: 16384 + ci * 2048 + s0 + n]
    fsl = lambda d, s0, n: rFG[:, d * 2048 + s0: d * 2048 + s0 + n]
    gsl = lambda d, s0, n: rFG[:, 16384 + d * 2048 + s0: 16384 + d * 2048 + s0 + n]
    w1sl = lambda ci, h0, n: rFG[:, ci * 4096 + h0: ci * 4096 + h0 + n]
    hsl = lambda si, c0, n: rH[:, si * 1024 + c0: si * 1024 + c0 + n]
    w2res = lambda hc, e0, n: rH[:, hc * 1024 + e0: hc * 1024 + e0 + n]
    ebT = [rE[:, 0:2048], rE[:, 2048:4096]]
    ebsm2 = [rE[:, 4096:6144], rE[:, 6144:8192]]
    x2fm2 = [rE[:, 0:2048], rE[:, 6144:8192]]
    x2f_all2 = [rE[:, 0:2048].rearrange("p (ci s) -> p ci s", ci=8),
                rE[:, 6144:8192].rearrange("p (ci s) -> p ci s", ci=8)]
    hid = [rE[:, 2048:2304], rE[:, 2304:2560]]
    w2s = [rE[:, 3072:4096], rE[:, 4096:5120], rE[:, 8192:9216], rE[:, 9216:10240]]

    ident = sb("ident", [128, 128], BF16)
    identf = sb("identf", [128, 128], F32)
    ones1 = sb("ones1", [1, 128], BF16)
    b2rowb = sb("b2rowb", [1, 1024], BF16)
    b1sb = sb("b1sb", [128, 32], F32)
    gbc = sb("gbc", [128, 1], F32)
    cdb = sb("cdb", [128, 4], F32)
    stats = sb("stats", [128, 64], F32)
    xch = sb("xch", [128, 8192], BF16)
    xc = [xch[:, k * 1024:(k + 1) * 1024] for k in range(8)]
    wbw = xch
    varsink = sb("varsink", [128, 1024], F32)

    CONSTS = [EPS, -0.5, 1.0 / 1024]
    for i, cv in enumerate(CONSTS):
        nc.const_aps.register(F32, cv, cdb[:, i:i + 1])

    NM = lambda par, st: stats[:, par * 4 + st: par * 4 + st + 1]
    NMALL = lambda par: stats[:, par * 4: par * 4 + 4]
    NEGMAX = lambda par: stats[:, 8 + par: 9 + par]
    ZP = lambda par, st: stats[:, 10 + par * 4 + st: 11 + par * 4 + st]
    ZPALL = lambda par: stats[:, 10 + par * 4: 14 + par * 4]
    ZCOL = stats[:, 18:19]
    ZICOL = stats[:, 19:20]
    ZG2 = [stats[:, 20:21], stats[:, 29:30]]
    NSUM = stats[:, 24:25]
    NEGM = stats[:, 25:26]
    VSUM = stats[:, 26:27]
    LNV = stats[:, 27:28]
    RSTD = stats[:, 28:29]
    KCOL = stats[:, 29:30]

    ps = ctx.enter_context(nc.psum_tensor("ps", [128, 4096], F32))
    ps_bf = ps.bitcast(BF16)
    bank = lambda b, off, n: ps[:, b * 512 + off: b * 512 + off + n]
    bankb = lambda b, off, n: ps_bf[:, b * 1024 + off: b * 1024 + off + n]
    bankb3 = lambda b: ps_bf[:, b * 1024:(b + 1) * 1024].rearrange(
        "p (ci s) -> p ci s", ci=8)

    sems = {}
    for name in ("T", "V", "Sc", "G", "LXC0", "LXC1", "LXC2", "LXC3",
                 "LWB0", "LWB1", "LWB2", "LWB3",
                 "LTE0", "LTE1", "LTC0", "LTC1",
                 "LW2S0", "LW2S1", "LW2S2", "LW2S3", "LXA", "LXB",
                 "LW1S0", "LW1S1", "LW1S2", "LW1S3",
                 "LW1S4", "LW1S5", "LW1S6", "LW1S7",
                 "LW2R0", "LW2R1", "LW2R2", "LW2R3", "LMISC", "LMISCG", "STY",
                 "LCW"):
        sems[name] = ctx.enter_context(nc.semaphore(name))

    P = Plan()
    pfree = {b: [] for b in range(8)}

    def v_init(v):
        v.memset(ident[:], 1.0)
        v.memset(identf[:], 1.0)
        v.memset(ones1[:], 1.0)
        ins = None
        for i, cv in enumerate(CONSTS):
            ins = v.memset(cdb[:, i:i + 1], cv)
        return ins
    P.emit("vector", v_init, inc="V")
    V_INIT = P.c("V")

    P.emit("gpsimd", lambda g: g.affine_select(
        out=ident[:], in_=ident[:], pattern=[[-1, 128]],
        compare_op=OP.is_equal, fill=0.0, base=0, channel_multiplier=1),
        waits=[("V", V_INIT)], inc="G")
    P.emit("gpsimd", lambda g: g.affine_select(
        out=identf[:], in_=identf[:], pattern=[[-1, 128]],
        compare_op=OP.is_equal, fill=0.0, base=0, channel_multiplier=1),
        inc="G")
    G_INIT = P.c("G")

    P.emit("sync", lambda e: e.dma_start(
        out=rA_f[:, 0:4 * 1024].rearrange("p (a b) -> p a b", a=4),
        in_=bass.AP(x.tensor, 0, [[1024, 128], [128 * 1024, 4], [1, 1024]])),
        inc="LXA", inc_by=16)
    P.emit("sync", lambda e: e.dma_start(
        out=varsink[0:32, 0:128], in_=bass.AP(b1.tensor, 0, [[128, 32], [1, 128]])),
        inc="LMISC", inc_by=16)
    P.emit("sync", lambda e: e.dma_start(
        out=gbc[:], in_=bass.AP(gamma.tensor, 0, [[0, 128], [1, 1]])),
        inc="LMISC", inc_by=16)
    P.emit("vector", lambda e: e.tensor_scalar_mul(gbc[:], gbc[:], 0.5),
           waits=[("LMISC", 32)], inc="V")
    P.emit("gpsimd", lambda e: e.dma_start(
        out=b2rowb[0:1, :], in_=bass.AP(b2.tensor, 0, [[1024, 1], [1, 1024]])),
        inc="LMISCG", inc_by=16)
    LMISC_ALL = P.c("LMISC")

    P.emit("tensor", lambda e: e.transpose(bank(6, 0, 32), varsink[0:32, 0:128],
                                           identf[0:32, 0:32]),
           waits=[("LMISC", 32), ("G", G_INIT)], inc="T")
    P.emit("vector", lambda e: e.tensor_copy(b1sb[:], bank(6, 0, 32)),
           waits=[("T", P.c("T"))], inc="V")
    pfree[6] = [("V", P.c("V"))]

    xT_done, xT_evac = {}, {}
    for pr in range(NS // 2):
        sl = pr % 4
        waits = [("T", xT_done[2 * pr - 7])] if pr >= 4 else []
        P.emit("gpsimd",
               (lambda pr_, sl_: lambda e: e.dma_start(
                   out=xch[:, sl_ * 2048:(sl_ + 1) * 2048].rearrange(
                       "p (a b) -> p a b", a=2),
                   in_=bass.AP(x.tensor, pr_ * 2 * 128 * 1024,
                               [[1024, 128], [128 * 1024, 2], [1, 1024]])))(pr, sl),
               waits=waits, inc=f"LXC{sl}", inc_by=16)
        for half in range(2):
            ti = 2 * pr + half
            bk = 4 + (ti % 2)
            twaits = [(f"LXC{sl}", 16 * (pr // 4 + 1)), ("G", G_INIT)] + pfree[bk]
            def tfn(e, ti=ti, bk=bk, sl=sl, half=half):
                ins = None
                for ci in range(NC):
                    ins = e.transpose(bankb(bk, ci * 128, 128),
                                      xc[sl * 2 + half][:, ci * 128:(ci + 1) * 128],
                                      ident[:])
                return ins
            P.emit("tensor", tfn, waits=twaits, inc="T")
            xT_done[ti] = P.c("T")
            P.emit("vector",
                   (lambda ti_, bk_: lambda e: e.tensor_copy(
                       xfm_all[:, :, ti_ * 128:(ti_ + 1) * 128], bankb3(bk_)))(ti, bk),
                   waits=[("T", xT_done[ti])], inc="V")
            xT_evac[ti] = P.c("V")
            pfree[bk] = [("V", xT_evac[ti])]
    XFM_ALL_V = P.c("V")

    wdram = {0: wq, 1: wk, 2: wv}
    rot = [0]

    def qkv_mm_tilegroup(lhsT_sl, rhs_sl, out_fn, N, extra_waits, evac_eng):
        bk = rot[0] % 4
        rot[0] += 1
        waits = list(extra_waits) + pfree[bk]
        def tfn(e, bk=bk, lhsT_sl=lhsT_sl, rhs_sl=rhs_sl, N=N):
            ins = None
            for ci in range(NC):
                ins = e.matmul(bank(bk, 0, N), lhsT_sl(ci), rhs_sl(ci),
                               start=(ci == 0), stop=(ci == NC - 1))
            return ins
        P.emit("tensor", tfn, waits=waits, inc="T")
        tcnt = P.c("T")
        sem = "Sc" if evac_eng == "scalar" else "V"
        def efn(e, bk=bk, out_fn=out_fn, N=N):
            out = out_fn()
            return (e.copy(out, bank(bk, 0, N)) if evac_eng == "scalar"
                    else e.tensor_copy(out, bank(bk, 0, N)))
        P.emit(evac_eng, efn, waits=[("T", tcnt)], inc=sem)
        pfree[bk] = [(sem, P.c(sem))]

    wT_done = {}
    lwb_cnt = [0, 0, 0, 0]
    wrow_idx_box = [0]
    wt_ready = {}

    def emit_w_lt(k):
        wi, grp = divmod(k, 2)
        bi = k % 2
        for rp in range(2):
            wrow_idx = wrow_idx_box[0]
            wp = wrow_idx // 2
            sl = wp % 4
            dj0 = grp * 4 + rp * 2
            if wp >= 4:
                waits = [("T", wT_done[wrow_idx - 7])]
            else:
                waits = [("T", xT_done[9 + 2 * wp])]
            P.emit("gpsimd",
                   (lambda wi_, dj_, sl_: lambda e: e.dma_start(
                       out=wbw[:, sl_ * 2048:(sl_ + 1) * 2048].rearrange(
                           "p (a b) -> p a b", a=2),
                       in_=bass.AP(wdram[wi_].tensor, dj_ * 128 * 1024,
                                   [[1024, 128], [128 * 1024, 2], [1, 1024]])))(
                       wi, dj0, sl),
                   waits=waits, inc=f"LWB{sl}", inc_by=16)
            lwb_cnt[sl] += 1
            for half in range(2):
                r = rp * 2 + half
                bk = 4 + (wrow_idx_box[0] % 4)
                twaits = [(f"LWB{sl}", 16 * lwb_cnt[sl]), ("G", G_INIT)] + pfree[bk]
                if k >= 2:
                    twaits.append(("T", wt_mm_done[k - 2]))
                def tfn(e, sl=sl, bk=bk, half=half):
                    base = sl * 2048 + half * 1024
                    ins = None
                    for ci in range(NC):
                        ins = e.transpose(
                            bankb(bk, ci * 128, 128),
                            wbw[:, base + ci * 128: base + (ci + 1) * 128],
                            ident[:])
                    return ins
                P.emit("tensor", tfn, waits=twaits, inc="T")
                wT_done[wrow_idx_box[0]] = P.c("T")
                P.emit("vector",
                       (lambda r_, bk_, bi_: lambda e: e.tensor_copy(
                           wtg_all2[bi_][:, :, r_ * 128:(r_ + 1) * 128],
                           bankb3(bk_)))(r, bk, bi),
                       waits=[("T", wT_done[wrow_idx_box[0]])], inc="V")
                pfree[bk] = [("V", P.c("V"))]
                wrow_idx_box[0] += 1
        wt_ready[k] = P.c("V")

    wt_mm_done = {}
    emit_w_lt(0)
    for k in range(6):
        wi, grp = divmod(k, 2)
        bi = k % 2
        if wi < 2:
            outsl = fsl if wi == 0 else gsl
            tiles = [(r, st) for r in range(4) for st in range(4)]
            for idx, (r, st) in enumerate(tiles):
                if idx == 8 and k + 1 < 6:
                    emit_w_lt(k + 1)
                dj = grp * 4 + r
                qkv_mm_tilegroup(
                    (lambda r_, bi_: lambda ci: WTg2(bi_, ci, r_ * 128, 128))(r, bi),
                    (lambda st_: lambda ci: xfm(ci, st_ * 512, 512))(st),
                    (lambda dj_, st_, o_=outsl: lambda: o_(dj_, st_ * 512, 512))(dj, st),
                    512, [("V", max(wt_ready[k], XFM_ALL_V))], "scalar")
        else:
            for si in range(NS):
                if si == 8 and k + 1 < 6:
                    emit_w_lt(k + 1)
                qkv_mm_tilegroup(
                    (lambda si_, bi_: lambda ci: xfm(ci, si_ * 128, 128))(si, bi),
                    (lambda bi_: lambda ci: WTg2(bi_, ci, 0, 512))(bi),
                    (lambda si_, grp_: lambda: hsl(si_, grp_ * 512, 512))(si, grp),
                    512, [("V", max(wt_ready[k], XFM_ALL_V))], "vector")
        wt_mm_done[k] = P.c("T")

    A_END_T = P.c("T")
    A_END_V = P.c("V")
    A_END_S = P.c("Sc")


    P.emit("gpsimd", lambda e: e.dma_start(
        out=rA_f[:, 4 * 1024:16 * 1024].rearrange("p (a b) -> p a b", a=12),
        in_=bass.AP(x.tensor, 4 * 128 * 1024, [[1024, 128], [128 * 1024, 12], [1, 1024]])),
        waits=[("T", A_END_T)], inc="LXB", inc_by=16)

    for i in range(4):
        P.emit("gpsimd",
               (lambda i_: lambda e: e.dma_start(
                   out=w1bf[i_ * 256:(i_ + 1) * 256, :],
                   in_=w1[i_ * 256:(i_ + 1) * 256, :]))(i),
               inc="LCW", inc_by=16)
    for i in range(8):
        P.emit("gpsimd",
               (lambda i_: lambda e: e.dma_start(
                   out=w2bf[i_ * 512:(i_ + 1) * 512, :],
                   in_=w2[i_ * 512:(i_ + 1) * 512, :]))(i),
               inc="LCW", inc_by=16)
    LCW_ALL = 192

    sc_done, exp_done, trT_done = {}, {}, {}
    ebsm_evac, ebsm_evac_h1, av_done, oevac_done = {}, {}, {}, {}
    nm_done = {}

    def emit_score_tile(ti, st, extra_waits):
        par = ti % 2
        bk = par * 4 + st
        def tfn(e, ti=ti, st=st, bk=bk):
            ins = None
            for d in range(NC):
                ins = e.matmul(bank(bk, 0, 512), gsl(d, ti * 128, 128),
                               fsl(d, st * 512, 512), start=(d == 0), stop=(d == NC - 1))
            return ins
        P.emit("tensor", tfn, waits=extra_waits, inc="T")
        sc_done[(ti, st)] = P.c("T")
        P.emit("vector",
               (lambda par_, st_, bk_: lambda e: e.reduce_max(
                   NM(par_, st_), bank(bk_, 0, 512), axis=AX, negate=True))(par, st, bk),
               waits=[("T", sc_done[(ti, st)])], inc="V")

    def emit_negmax_combine(ti):
        par = ti % 2
        P.emit("vector",
               (lambda par_: lambda e: e.tensor_reduce(
                   NEGMAX(par_), NMALL(par_), axis=AX, op=OP.min))(par), inc="V")
        nm_done[ti] = P.c("V")

    def emit_softmax(ti):
        par = ti % 2
        g0 = par * 4
        waits = [("V", nm_done[ti])]
        if ti >= 2:
            waits.append(ebsm_evac[ti - 2])
        for st in range(4):
            P.emit("scalar",
                   (lambda st_, g0_, par_: lambda e: e.activation(
                       out=ebT[par_][:, st_ * 512:(st_ + 1) * 512],
                       in_=bank(g0_ + st_, 0, 512), func=AF.Exp,
                       bias=NEGMAX(par_), scale=1.0,
                       accum_out=ZP(par_, st_)))(st, g0, par),
                   waits=(waits if st == 0 else []), inc="Sc")
        exp_done[ti] = P.c("Sc")

    lte_cnt = [0, 0]
    tail_trT = {}

    def emit_transposes(ti):
        par = ti % 2
        eb = ebsm2[par]
        if ti < NS - 2:
            waits = [("Sc", exp_done[ti])]
            if ti >= 2:
                waits.append(("T", av_done.get(ti - 2, 0)))
                waits.append((f"LTE{par}", lte_cnt[par]))
            P.emit("sync",
                   (lambda par_, eb_=eb: lambda e: e.dma_start_transpose(
                       eb_[:].rearrange("p (k c) -> p k c", k=16),
                       ebT[par_][:].rearrange("p (k c) -> p k c", k=16)))(par),
                   waits=waits, inc=f"LTE{par}", inc_by=16)
            lte_cnt[par] += 16
            ebsm_evac[ti] = (f"LTE{par}", lte_cnt[par])
        else:
            g0 = par * 4
            def tfn(e, par=par, g0=g0):
                ins = None
                for k in range(NS):
                    ins = e.transpose(
                        ps_bf[:, g0 * 1024 + k * 128: g0 * 1024 + (k + 1) * 128],
                        ebT[par][:, k * 128:(k + 1) * 128], ident[:])
                return ins
            P.emit("tensor", tfn,
                   waits=[("Sc", exp_done[ti]), (f"LTE{par}", lte_cnt[par])],
                   inc="T")
            trT = P.c("T")
            tail_trT[ti] = trT
            P.emit("vector",
                   (lambda g0_, eb_=eb: lambda e: e.tensor_copy(
                       eb_[:, 0:2048], ps_bf[:, g0_ * 1024: g0_ * 1024 + 2048]))(g0),
                   waits=[("T", trT), ("T", av_done.get(ti - 2, 0))], inc="V")
            ebsm_evac[ti] = ("V", P.c("V"))
        P.emit("vector",
               (lambda par_: lambda e: e.tensor_reduce(
                   ZCOL, ZPALL(par_), axis=AX, op=OP.add))(par),
               waits=[("Sc", exp_done[ti]), ("LMISC", LMISC_ALL)], inc="V")
        P.emit("vector", lambda e: e.reciprocal(ZICOL, ZCOL), inc="V")
        P.emit("vector",
               (lambda par_: lambda e: e.tensor_mul(ZG2[par_], ZICOL, gbc[:]))(par),
               inc="V")

    def emit_av(ti):
        g0 = (ti % 2) * 4
        eb = ebsm2[ti % 2]
        for ct in range(2):
            bk = g0 + 2 + ct
            def tfn_a(e, ct=ct, bk=bk, eb=eb):
                ins = None
                for k in range(NS):
                    ins = e.matmul(bank(bk, 0, 512), eb[:, k * 128:(k + 1) * 128],
                                   hsl(k, ct * 512, 512),
                                   start=(k == 0), stop=(k == NS - 1))
                return ins
            P.emit("tensor", tfn_a, waits=[ebsm_evac[ti]], inc="T")
        av_done[ti] = P.c("T")

    oevac_h1 = {}

    def emit_oevac(ti):
        g0 = (ti % 2) * 4
        xw = ("LXA", 16) if ti < 4 else ("LXB", 16)
        for ct in range(2):
            P.emit("vector",
                   (lambda ti_, ct_, bk_: lambda e: e.scalar_tensor_tensor(
                       out=resid2(ti_, ct_), in0=bank(bk_, 0, 512), scalar=ZG2[ti_ % 2],
                       in1=resid2(ti_, ct_), op0=OP.mult, op1=OP.add))(ti, ct, g0 + 2 + ct),
                   waits=[("T", av_done[ti]), xw], inc="V")
            if ct == 0:
                oevac_h1[ti] = P.c("V")
        oevac_done[ti] = P.c("V")

    R1COL = lambda ti: stats[:, 32 + ti:33 + ti]
    M1COL = lambda ti: stats[:, 48 + ti:49 + ti]
    BNS = stats[:, 21:22]
    BVS2 = [stats[:, 23:24], stats[:, 27:28]]
    BLL2 = [stats[:, 24:25], stats[:, 28:29]]
    K1TMP = [stats[:, 25:26], stats[:, 26:27]]

    bstats_sc = {}

    def emit_ln1_bstats(ti):
        P.emit("vector",
               (lambda t_: lambda e: e.reduce_sum(BNS, resid(t_), axis=AX,
                                                  negate=True))(ti), inc="V")
        P.emit("vector",
               (lambda t_: lambda e: e.tensor_scalar_mul(
                   M1COL(t_), BNS, 1.0 / 1024))(ti), inc="V")
        vwaits = []
        if ti >= 2:
            vwaits.append(("Sc", bstats_sc[ti - 2]))
        P.emit("vector",
               (lambda t_: lambda e: e.scalar_tensor_tensor(
                   out=varsink[:], in0=resid(t_), scalar=M1COL(t_), in1=resid(t_),
                   op0=OP.add, op1=OP.mult, accum_out=BVS2[t_ % 2]))(ti),
               waits=vwaits, inc="V")
        vv = P.c("V")
        P.emit("scalar",
               (lambda t_: lambda e: e.activation(
                   out=BLL2[t_ % 2], in_=BVS2[t_ % 2], func=AF.Ln,
                   scale=1.0 / 1024, bias=EPS))(ti),
               waits=[("V", vv)], inc="Sc")
        P.emit("scalar",
               (lambda t_: lambda e: e.activation(out=R1COL(t_), in_=BLL2[t_ % 2],
                                                  func=AF.Exp, scale=-0.5))(ti),
               inc="Sc")
        bstats_sc[ti] = P.c("Sc")

    LNS = lambda b: stats[:, b:b + 1]
    LNM = lambda b: stats[:, b + 1:b + 2]
    LNV4 = lambda b: stats[:, b + 2:b + 3]
    LNL = lambda b: stats[:, b + 3:b + 4]
    LNR = lambda b: stats[:, b + 4:b + 5]
    LNK = lambda b: stats[:, b + 5:b + 6]
    vcast_n = {}

    def emit_lnb_v1(chains):
        for b, cch in chains:
            P.emit("vector",
                   (lambda c_, b_: lambda e: e.reduce_sum(
                       LNS(b_), resid(c_), axis=AX, negate=True))(cch, b), inc="V")
        for b, cch in chains:
            P.emit("vector",
                   (lambda b_: lambda e: e.tensor_scalar_mul(
                       LNM(b_), LNS(b_), 1.0 / 1024))(b), inc="V")
        for b, cch in chains:
            P.emit("vector",
                   (lambda c_, b_: lambda e: e.scalar_tensor_tensor(
                       out=varsink[:], in0=resid(c_), scalar=LNM(b_), in1=resid(c_),
                       op0=OP.add, op1=OP.mult, accum_out=LNV4(b_)))(cch, b), inc="V")
        return P.c("V")

    def emit_lnb_sc(chains, vwait):
        for b, cch in chains:
            P.emit("scalar",
                   (lambda b_: lambda e: e.activation(
                       out=LNL(b_), in_=LNV4(b_), func=AF.Ln,
                       scale=1.0 / 1024, bias=EPS))(b),
                   waits=[("V", vwait)], inc="Sc")
        for b, cch in chains:
            P.emit("scalar",
                   (lambda b_: lambda e: e.activation(
                       out=LNR(b_), in_=LNL(b_), func=AF.Exp, scale=-0.5))(b),
                   inc="Sc")
        return P.c("Sc")

    def emit_lnb_v2(chains, scwait, cast_e8=None, store=()):
        for b, cch in chains:
            P.emit("vector",
                   (lambda b_: lambda e: e.tensor_mul(LNK(b_), LNM(b_), LNR(b_)))(b),
                   waits=[("Sc", scwait)], inc="V")
        if cast_e8 is not None:
            slot = (cast_e8 % 2) * 2
            for sj, (b, cch) in enumerate(chains[-2:]):
                P.emit("vector",
                       (lambda c_, b_, sl_=slot + sj: lambda e: e.tensor_scalar(
                           out=xc[sl_][:], in0=resid(c_), scalar1=LNR(b_),
                           scalar2=LNK(b_), op0=OP.mult, op1=OP.add))(cch, b),
                       inc="V")
            vcast_n[cast_e8] = P.c("V")
        for b, cch in chains:
            P.emit("vector",
                   (lambda c_, b_: lambda e: e.tensor_scalar(
                       out=resid(c_), in0=resid(c_), scalar1=LNR(b_), scalar2=LNK(b_),
                       op0=OP.mult, op1=OP.add))(cch, b), inc="V")
        for b, cch in store:
            vfin = P.c("V")
            P.emit("sync",
                   (lambda c_: lambda e: e.dma_start(
                       out=y[c_ * 128:(c_ + 1) * 128, :], in_=resid(c_)))(cch),
                   waits=[("V", vfin)], inc="STY", inc_by=16)

    vcast_n_early = {}

    def emit_ln1_final_c0():
        scw = bstats_sc[1]
        P.emit("vector", lambda e: e.engine_nop(), waits=[("T", A_END_T)], inc="V")
        for sj, cch in enumerate((0, 1)):
            P.emit("vector",
                   (lambda c_, sj_: lambda e: e.tensor_mul(
                       K1TMP[sj_], M1COL(c_), R1COL(c_)))(cch, sj),
                   waits=[("Sc", scw)], inc="V")
        for sj, cch in enumerate((0, 1)):
            P.emit("vector",
                   (lambda c_, sl_, sj_: lambda e: e.tensor_scalar(
                       out=xc[sl_][:], in0=resid(c_), scalar1=R1COL(c_),
                       scalar2=K1TMP[sj_], op0=OP.mult, op1=OP.add))(cch, sj, sj),
                   inc="V")
        vcast_n_early[0] = P.c("V")
        for sj, cch in enumerate((0, 1)):
            P.emit("vector",
                   (lambda c_, sj_: lambda e: e.tensor_scalar(
                       out=resid(c_), in0=resid(c_), scalar1=R1COL(c_),
                       scalar2=K1TMP[sj_], op0=OP.mult, op1=OP.add))(cch, sj),
                   inc="V")

    for ti in (0, 1):
        for st in range(4):
            bk = (ti % 2) * 4 + st
            emit_score_tile(ti, st, [("V", A_END_V), ("Sc", A_END_S)] + pfree[bk])
        emit_negmax_combine(ti)

    for tj in range(NS):
        emit_softmax(tj)
        emit_transposes(tj)
        if tj >= 1:
            emit_ln1_bstats(tj - 1)
        emit_av(tj)
        emit_oevac(tj)
        if tj + 2 < NS:
            for st in (0, 1):
                emit_score_tile(tj + 2, st, [("Sc", exp_done[tj])])
            emit_score_tile(tj + 2, 2, [("V", oevac_h1[tj])])
            emit_score_tile(tj + 2, 3, [("V", oevac_done[tj])])
            emit_negmax_combine(tj + 2)
        if tj == NS - 1:
            emit_ln1_bstats(tj)
        if tj == 4:
            emit_ln1_final_c0()
        if tj == 13:
            t_after_s15 = P.c("T")
            w1_all = rFG[:, 0:32768].rearrange("p (ei h) -> p ei h", ei=8)
            w2r3 = lambda i: rH[:, i * 4096:(i + 1) * 4096].rearrange(
                "p (a b) -> p a b", a=4)
            def w1dma(k):
                P.emit("gpsimd",
                       (lambda k_: lambda e: e.dma_start(
                           out=w1_all[:, :, k_ * 512:(k_ + 1) * 512],
                           in_=bass.AP(w1bf.tensor, k_ * 512,
                                       [[4096, 128], [128 * 4096, 8], [1, 512]])))(k),
                       waits=[("T", t_after_s15), ("LCW", 192)],
                       inc=f"LW1S{k}", inc_by=16)
            def w2rdma(i):
                P.emit("gpsimd",
                       (lambda i_: lambda e: e.dma_start(
                           out=w2r3(i_),
                           in_=bass.AP(w2bf.tensor, i_ * 4 * 128 * 1024,
                                       [[1024, 128], [128 * 1024, 4], [1, 1024]])))(i),
                       waits=[("T", P.c("T")), ("LCW", 192)],
                       inc=f"LW2R{i}", inc_by=16)
            w1dma(0)
    B_END_T = P.c("T")
    B_END_V = P.c("V")
    w1dma(1)
    w1dma(2)
    w1dma(3)
    w2rdma(0)
    w2rdma(1)
    w1dma(4)
    w2rdma(2)
    w2rdma(3)
    w1dma(5)
    w1dma(6)
    w1dma(7)

    xcT_done, x2fm_evac = {}, {}
    ffn1_done, gelu_done, ffn2_done, r2evac_done = {}, {}, {}, {}
    lw2s_cnt = [0, 0, 0, 0]
    w2s_seq = {}
    FB = (2, 3, 0)
    hid3 = [rE[:, 2048:2304], rE[:, 2304:2560], rE[:, 2560:2816]]

    ltc_cnt = [0, 0]
    x2fm_ready = {}

    def emit_c_head_t(e8):
        slot = (e8 % 2) * 2
        par = e8 % 2
        for sj in range(2):
            waits = [("V", vcast_n[e8])]
            if sj == 0:
                if e8 == 0:
                    waits.append(("T", tail_trT[14]))
                elif e8 == 1:
                    waits.append(("T", B_END_T))
                else:
                    waits.append(("T", ffn1_done[(e8 - 2) * NH + NH - 1]))
                    waits.append((f"LTC{par}", ltc_cnt[par]))
            P.emit("sync",
                   (lambda sj_, sl_, par_: lambda e: e.dma_start_transpose(
                       x2f_all2[par_][:, :, sj_ * 128:(sj_ + 1) * 128],
                       xc[sl_ + sj_][:].rearrange("p (k c) -> p k c", k=8)))(
                       sj, slot, par),
                   waits=waits, inc=f"LTC{par}", inc_by=16)
            ltc_cnt[par] += 16
        x2fm_ready[e8] = (f"LTC{par}", ltc_cnt[par])

    def w2s_prefetch(g, gate_g):
        e8, hc = divmod(g, NH)
        sl = hc % 4
        if gate_g is None:
            waits = [("T", B_END_T)]
        else:
            waits = [("T", ffn2_done[gate_g])]
        waits.append(("LCW", 192))
        lw2s_cnt[sl] += 1
        w2s_seq[g] = lw2s_cnt[sl]
        P.emit("gpsimd",
               (lambda hc_, sl_: lambda e: e.dma_start(
                   out=w2s[sl_], in_=w2bf[hc_ * 128:(hc_ + 1) * 128, :]))(hc, sl),
               waits=waits, inc=f"LW2S{sl}", inc_by=16)

    for k in range(4):
        w2s_prefetch(W2_RES + k, None)

    def emit_ffn1(g):
        e8, hc = divmod(g, NH)
        fbk = FB[g % 3]
        waits = [x2fm_ready[e8], (f"LW1S{hc // 4}", 16)]
        if g - 3 >= 0:
            waits.append(("Sc", gelu_done[g - 3]))
        else:
            waits.append(("V", oevac_done[14]))
            if g == 2:
                waits.append(ebsm_evac[14])
        def tfn(e, hc=hc, fbk=fbk, e8=e8):
            ins = None
            xb_ = x2fm2[e8 % 2]
            for ci in range(NC):
                ins = e.matmul(bank(fbk, 0, 256), w1sl(ci, hc * 128, 128),
                               xb_[:, ci * 256:(ci + 1) * 256],
                               start=(ci == 0), stop=(ci == NC - 1))
            return ins
        P.emit("tensor", tfn, waits=waits, inc="T")
        ffn1_done[g] = P.c("T")

    def emit_gelu(g):
        e8, hc = divmod(g, NH)
        fbk = FB[g % 3]
        waits = [("T", ffn1_done[g]), ("LMISC", LMISC_ALL)]
        if g - 3 >= 0:
            waits.append(("T", ffn2_done[g - 3]))
        P.emit("scalar",
               (lambda hc_, fbk_, g_: lambda e: e.activation(
                   out=hid3[g_ % 3][:, 0:256], in_=bank(fbk_, 0, 256),
                   func=AF.Gelu, bias=b1sb[:, hc_:hc_ + 1], scale=1.0))(hc, fbk, g),
               waits=waits, inc="Sc")
        gelu_done[g] = P.c("Sc")

    def emit_ffn2(g):
        e8, hc = divmod(g, NH)
        waits = [("Sc", gelu_done[g])]
        if hc < W2_RES and e8 == 0:
            waits.append((f"LW2R{hc // 4}", 16))
        if hc == 0:
            if e8 > 0:
                waits.append(("V", r2evac_done[e8 - 1]))
            else:
                waits.append(("V", B_END_V))
        if hc >= W2_RES:
            waits.append((f"LW2S{hc % 4}", 16 * w2s_seq[g]))
        def tfn2(e, hc=hc, g=g):
            ins = None
            for sj in range(2):
                for et in range(2):
                    rhs = (w2res(hc, et * 512, 512) if hc < W2_RES
                           else w2s[hc % 4][:, et * 512:(et + 1) * 512])
                    ins = e.matmul(bank(4 + sj * 2 + et, 0, 512),
                                   hid3[g % 3][:, sj * 128:(sj + 1) * 128], rhs,
                                   start=(hc == 0), stop=False)
            return ins
        P.emit("tensor", tfn2, waits=waits, inc="T")
        ffn2_done[g] = P.c("T")
        if hc >= 16:
            nxt = g + 4 if hc <= 27 else (g - hc) + NH + 16 + (hc % 4)
            if nxt < 8 * NH and nxt % NH >= 16:
                w2s_prefetch(nxt, g)

    def emit_fold_r2(e8):
        c0, c1 = 2 * e8, 2 * e8 + 1
        def foldfn(e):
            ins = None
            for sj in range(2):
                for et in range(2):
                    ins = e.matmul(bank(4 + sj * 2 + et, 0, 512), ones1[0:1, :],
                                   b2rowb[0:1, et * 512:(et + 1) * 512],
                                   start=False, stop=True)
            return ins
        P.emit("tensor", foldfn, waits=[("LMISCG", 16)], inc="T")
        fold_t = P.c("T")
        for sj, cch in enumerate((c0, c1)):
            for et in range(2):
                P.emit("vector",
                       (lambda c_, sj_, et_: lambda e: e.tensor_tensor(
                           out=resid2(c_, et_), in0=bank(4 + sj_ * 2 + et_, 0, 512),
                           in1=resid2(c_, et_), op=OP.add))(cch, sj, et),
                       waits=[("T", fold_t)], inc="V")
        r2evac_done[e8] = P.c("V")

    def emit_ln1_final_c(e8):
        slot = (e8 % 2) * 2
        scw = bstats_sc[2 * e8 + 1]
        if e8 >= 2:
            P.emit("vector", lambda e: e.engine_nop(),
                   waits=[x2fm_ready[e8 - 2]], inc="V")
        for sj, cch in enumerate((2 * e8, 2 * e8 + 1)):
            P.emit("vector",
                   (lambda c_, sj_: lambda e: e.tensor_mul(
                       K1TMP[sj_], M1COL(c_), R1COL(c_)))(cch, sj),
                   waits=[("Sc", scw)], inc="V")
        for sj, cch in enumerate((2 * e8, 2 * e8 + 1)):
            P.emit("vector",
                   (lambda c_, sl_, sj_: lambda e: e.tensor_scalar(
                       out=xc[sl_][:], in0=resid(c_), scalar1=R1COL(c_),
                       scalar2=K1TMP[sj_], op0=OP.mult, op1=OP.add))(
                       cch, slot + sj, sj),
                   inc="V")
        vcast_n[e8] = P.c("V")
        for sj, cch in enumerate((2 * e8, 2 * e8 + 1)):
            P.emit("vector",
                   (lambda c_, sj_: lambda e: e.tensor_scalar(
                       out=resid(c_), in0=resid(c_), scalar1=R1COL(c_),
                       scalar2=K1TMP[sj_], op0=OP.mult, op1=OP.add))(cch, sj),
                   inc="V")

    vcast_n[0] = vcast_n_early[0]
    emit_c_head_t(0)
    lnch, lnb_vw, lnb_sw = None, None, None
    for g in range(8 * NH):
        e8, hc = divmod(g, NH)
        emit_ffn1(g)
        if g >= 3:
            gg = g - 3
            emit_ffn2(gg)
            if gg % NH == NH - 1:
                emit_fold_r2(gg // NH)
        emit_gelu(g)
        if hc == 2 and e8 > 0:
            lnch = [(0, 2 * (e8 - 1)), (6, 2 * (e8 - 1) + 1)]
            lnb_vw = emit_lnb_v1(lnch)
        if hc == 6 and e8 + 1 < 8:
            emit_ln1_final_c(e8 + 1)
        if hc == 8 and e8 > 0:
            lnb_sw = emit_lnb_sc(lnch, lnb_vw)
        if hc == 10 and e8 > 0:
            emit_lnb_v2(lnch, lnb_sw, store=lnch)
        if hc == 12 and e8 + 1 < 8:
            emit_c_head_t(e8 + 1)
    emit_ffn2(8 * NH - 3)
    emit_ffn2(8 * NH - 2)
    emit_ffn2(8 * NH - 1)
    emit_fold_r2(7)

    ch7 = [(0, 14), (6, 15)]
    vw7 = emit_lnb_v1(ch7)
    sw7 = emit_lnb_sc(ch7, vw7)
    emit_lnb_v2(ch7, sw7, store=ch7)
    P.emit("sync", lambda e: e.wait_ge(sems["STY"], 16 * NS))

    block = ctx.enter_context(nc.Block())

    def emit_stream(eng_name):
        def run(eng):
            last = {}
            for fn, waits, inc, inc_by in P.ops[eng_name]:
                for s, v in waits:
                    if last.get(s, 0) < v:
                        eng.wait_ge(sems[s], v)
                        last[s] = v
                ins = fn(eng)
                if inc is not None:
                    if ins is None:
                        ins = eng.nop()
                    ins.then_inc(sems[inc], inc_by)
        return run

    block.tensor(emit_stream("tensor"))
    block.vector(emit_stream("vector"))
    block.scalar(emit_stream("scalar"))
    block.gpsimd(emit_stream("gpsimd"))
    block.sync(emit_stream("sync"))

    ctx.close()
    return nc


def make_in_map(inputs, core):
    return {
        "x": np.ascontiguousarray(inputs["L0_attn_x"][core]),
        "wq": np.ascontiguousarray(inputs["L0_attn_Wq"]),
        "wk": np.ascontiguousarray(inputs["L0_attn_Wk"]),
        "wv": np.ascontiguousarray(inputs["L0_attn_Wv"]),
        "gamma": np.ascontiguousarray(inputs["L0_attn_gamma"]).reshape(1, 1),
        "w1": np.ascontiguousarray(inputs["L0_ffn_w1"]),
        "b1": np.ascontiguousarray(inputs["L0_ffn_b1"]),
        "w2": np.ascontiguousarray(inputs["L0_ffn_w2"]),
        "b2": np.ascontiguousarray(inputs["L0_ffn_b2"]),
    }


def make_in_map(inputs, core):
    return {
        "x": np.ascontiguousarray(inputs["L0_attn_x"][core]),
        "wq": np.ascontiguousarray(inputs["L0_attn_Wq"]),
        "wk": np.ascontiguousarray(inputs["L0_attn_Wk"]),
        "wv": np.ascontiguousarray(inputs["L0_attn_Wv"]),
        "gamma": np.ascontiguousarray(inputs["L0_attn_gamma"]).reshape(1, 1),
        "w1": np.ascontiguousarray(inputs["L0_ffn_w1"]),
        "b1": np.ascontiguousarray(inputs["L0_ffn_b1"]),
        "w2": np.ascontiguousarray(inputs["L0_ffn_w2"]),
        "b2": np.ascontiguousarray(inputs["L0_ffn_b2"]),
    }


def make_in_map(inputs, core):
    return {
        "x": np.ascontiguousarray(inputs["L0_attn_x"][core]),
        "wq": np.ascontiguousarray(inputs["L0_attn_Wq"]),
        "wk": np.ascontiguousarray(inputs["L0_attn_Wk"]),
        "wv": np.ascontiguousarray(inputs["L0_attn_Wv"]),
        "gamma": np.ascontiguousarray(inputs["L0_attn_gamma"]).reshape(1, 1),
        "w1": np.ascontiguousarray(inputs["L0_ffn_w1"]),
        "b1": np.ascontiguousarray(inputs["L0_ffn_b1"]),
        "w2": np.ascontiguousarray(inputs["L0_ffn_w2"]),
        "b2": np.ascontiguousarray(inputs["L0_ffn_b2"]),
    }



def make_in_map(inputs, core):
    return {
        "x": np.ascontiguousarray(inputs["L0_attn_x"][core]),
        "wq": np.ascontiguousarray(inputs["L0_attn_Wq"]),
        "wk": np.ascontiguousarray(inputs["L0_attn_Wk"]),
        "wv": np.ascontiguousarray(inputs["L0_attn_Wv"]),
        "gamma": np.ascontiguousarray(inputs["L0_attn_gamma"]).reshape(1, 1),
        "w1": np.ascontiguousarray(inputs["L0_ffn_w1"]),
        "b1": np.ascontiguousarray(inputs["L0_ffn_b1"]),
        "w2": np.ascontiguousarray(inputs["L0_ffn_w2"]),
        "b2": np.ascontiguousarray(inputs["L0_ffn_b2"]),
    }



def make_in_map(inputs, core):
    return {
        "x": np.ascontiguousarray(inputs["L0_attn_x"][core]),
        "wq": np.ascontiguousarray(inputs["L0_attn_Wq"]),
        "wk": np.ascontiguousarray(inputs["L0_attn_Wk"]),
        "wv": np.ascontiguousarray(inputs["L0_attn_Wv"]),
        "gamma": np.ascontiguousarray(inputs["L0_attn_gamma"]).reshape(1, 1),
        "w1": np.ascontiguousarray(inputs["L0_ffn_w1"]),
        "b1": np.ascontiguousarray(inputs["L0_ffn_b1"]),
        "w2": np.ascontiguousarray(inputs["L0_ffn_w2"]),
        "b2": np.ascontiguousarray(inputs["L0_ffn_b2"]),
    }



def make_in_map(inputs, core):
    return {
        "x": np.ascontiguousarray(inputs["L0_attn_x"][core]),
        "wq": np.ascontiguousarray(inputs["L0_attn_Wq"]),
        "wk": np.ascontiguousarray(inputs["L0_attn_Wk"]),
        "wv": np.ascontiguousarray(inputs["L0_attn_Wv"]),
        "gamma": np.ascontiguousarray(inputs["L0_attn_gamma"]).reshape(1, 1),
        "w1": np.ascontiguousarray(inputs["L0_ffn_w1"]),
        "b1": np.ascontiguousarray(inputs["L0_ffn_b1"]),
        "w2": np.ascontiguousarray(inputs["L0_ffn_w2"]),
        "b2": np.ascontiguousarray(inputs["L0_ffn_b2"]),
    }



def make_in_map(inputs, core):
    return {
        "x": np.ascontiguousarray(inputs["L0_attn_x"][core]),
        "wq": np.ascontiguousarray(inputs["L0_attn_Wq"]),
        "wk": np.ascontiguousarray(inputs["L0_attn_Wk"]),
        "wv": np.ascontiguousarray(inputs["L0_attn_Wv"]),
        "gamma": np.ascontiguousarray(inputs["L0_attn_gamma"]).reshape(1, 1),
        "w1": np.ascontiguousarray(inputs["L0_ffn_w1"]),
        "b1": np.ascontiguousarray(inputs["L0_ffn_b1"]),
        "w2": np.ascontiguousarray(inputs["L0_ffn_w2"]),
        "b2": np.ascontiguousarray(inputs["L0_ffn_b2"]),
    }



def make_in_map(inputs, core):
    return {
        "x": np.ascontiguousarray(inputs["L0_attn_x"][core]),
        "wq": np.ascontiguousarray(inputs["L0_attn_Wq"]),
        "wk": np.ascontiguousarray(inputs["L0_attn_Wk"]),
        "wv": np.ascontiguousarray(inputs["L0_attn_Wv"]),
        "gamma": np.ascontiguousarray(inputs["L0_attn_gamma"]).reshape(1, 1),
        "w1": np.ascontiguousarray(inputs["L0_ffn_w1"]),
        "b1": np.ascontiguousarray(inputs["L0_ffn_b1"]),
        "w2": np.ascontiguousarray(inputs["L0_ffn_w2"]),
        "b2": np.ascontiguousarray(inputs["L0_ffn_b2"]),
    }



def make_in_map(inputs, core):
    return {
        "x": np.ascontiguousarray(inputs["L0_attn_x"][core]),
        "wq": np.ascontiguousarray(inputs["L0_attn_Wq"]),
        "wk": np.ascontiguousarray(inputs["L0_attn_Wk"]),
        "wv": np.ascontiguousarray(inputs["L0_attn_Wv"]),
        "gamma": np.ascontiguousarray(inputs["L0_attn_gamma"]).reshape(1, 1),
        "w1": np.ascontiguousarray(inputs["L0_ffn_w1"]),
        "b1": np.ascontiguousarray(inputs["L0_ffn_b1"]),
        "w2": np.ascontiguousarray(inputs["L0_ffn_w2"]),
        "b2": np.ascontiguousarray(inputs["L0_ffn_b2"]),
    }



def make_in_map(inputs, core):
    return {
        "x": np.ascontiguousarray(inputs["L0_attn_x"][core]),
        "wq": np.ascontiguousarray(inputs["L0_attn_Wq"]),
        "wk": np.ascontiguousarray(inputs["L0_attn_Wk"]),
        "wv": np.ascontiguousarray(inputs["L0_attn_Wv"]),
        "gamma": np.ascontiguousarray(inputs["L0_attn_gamma"]).reshape(1, 1),
        "w1": np.ascontiguousarray(inputs["L0_ffn_w1"]),
        "b1": np.ascontiguousarray(inputs["L0_ffn_b1"]),
        "w2": np.ascontiguousarray(inputs["L0_ffn_w2"]),
        "b2": np.ascontiguousarray(inputs["L0_ffn_b2"]),
    }



def make_in_map(inputs, core):
    return {
        "x": np.ascontiguousarray(inputs["L0_attn_x"][core]),
        "wq": np.ascontiguousarray(inputs["L0_attn_Wq"]),
        "wk": np.ascontiguousarray(inputs["L0_attn_Wk"]),
        "wv": np.ascontiguousarray(inputs["L0_attn_Wv"]),
        "gamma": np.ascontiguousarray(inputs["L0_attn_gamma"]).reshape(1, 1),
        "w1": np.ascontiguousarray(inputs["L0_ffn_w1"]),
        "b1": np.ascontiguousarray(inputs["L0_ffn_b1"]),
        "w2": np.ascontiguousarray(inputs["L0_ffn_w2"]),
        "b2": np.ascontiguousarray(inputs["L0_ffn_b2"]),
    }



def make_in_map(inputs, core):
    return {
        "x": np.ascontiguousarray(inputs["L0_attn_x"][core]),
        "wq": np.ascontiguousarray(inputs["L0_attn_Wq"]),
        "wk": np.ascontiguousarray(inputs["L0_attn_Wk"]),
        "wv": np.ascontiguousarray(inputs["L0_attn_Wv"]),
        "gamma": np.ascontiguousarray(inputs["L0_attn_gamma"]).reshape(1, 1),
        "w1": np.ascontiguousarray(inputs["L0_ffn_w1"]),
        "b1": np.ascontiguousarray(inputs["L0_ffn_b1"]),
        "w2": np.ascontiguousarray(inputs["L0_ffn_w2"]),
        "b2": np.ascontiguousarray(inputs["L0_ffn_b2"]),
    }



def make_in_map(inputs, core):
    return {
        "x": np.ascontiguousarray(inputs["L0_attn_x"][core]),
        "wq": np.ascontiguousarray(inputs["L0_attn_Wq"]),
        "wk": np.ascontiguousarray(inputs["L0_attn_Wk"]),
        "wv": np.ascontiguousarray(inputs["L0_attn_Wv"]),
        "gamma": np.ascontiguousarray(inputs["L0_attn_gamma"]).reshape(1, 1),
        "w1": np.ascontiguousarray(inputs["L0_ffn_w1"]),
        "b1": np.ascontiguousarray(inputs["L0_ffn_b1"]),
        "w2": np.ascontiguousarray(inputs["L0_ffn_w2"]),
        "b2": np.ascontiguousarray(inputs["L0_ffn_b2"]),
    }



def make_in_map(inputs, core):
    return {
        "x": np.ascontiguousarray(inputs["L0_attn_x"][core]),
        "wq": np.ascontiguousarray(inputs["L0_attn_Wq"]),
        "wk": np.ascontiguousarray(inputs["L0_attn_Wk"]),
        "wv": np.ascontiguousarray(inputs["L0_attn_Wv"]),
        "gamma": np.ascontiguousarray(inputs["L0_attn_gamma"]).reshape(1, 1),
        "w1": np.ascontiguousarray(inputs["L0_ffn_w1"]),
        "b1": np.ascontiguousarray(inputs["L0_ffn_b1"]),
        "w2": np.ascontiguousarray(inputs["L0_ffn_w2"]),
        "b2": np.ascontiguousarray(inputs["L0_ffn_b2"]),
    }



def make_in_map(inputs, core):
    return {
        "x": np.ascontiguousarray(inputs["L0_attn_x"][core]),
        "wq": np.ascontiguousarray(inputs["L0_attn_Wq"]),
        "wk": np.ascontiguousarray(inputs["L0_attn_Wk"]),
        "wv": np.ascontiguousarray(inputs["L0_attn_Wv"]),
        "gamma": np.ascontiguousarray(inputs["L0_attn_gamma"]).reshape(1, 1),
        "w1": np.ascontiguousarray(inputs["L0_ffn_w1"]),
        "b1": np.ascontiguousarray(inputs["L0_ffn_b1"]),
        "w2": np.ascontiguousarray(inputs["L0_ffn_w2"]),
        "b2": np.ascontiguousarray(inputs["L0_ffn_b2"]),
    }



def make_in_map(inputs, core):
    return {
        "x": np.ascontiguousarray(inputs["L0_attn_x"][core]),
        "wq": np.ascontiguousarray(inputs["L0_attn_Wq"]),
        "wk": np.ascontiguousarray(inputs["L0_attn_Wk"]),
        "wv": np.ascontiguousarray(inputs["L0_attn_Wv"]),
        "gamma": np.ascontiguousarray(inputs["L0_attn_gamma"]).reshape(1, 1),
        "w1": np.ascontiguousarray(inputs["L0_ffn_w1"]),
        "b1": np.ascontiguousarray(inputs["L0_ffn_b1"]),
        "w2": np.ascontiguousarray(inputs["L0_ffn_w2"]),
        "b2": np.ascontiguousarray(inputs["L0_ffn_b2"]),
    }



def make_in_map(inputs, core):
    return {
        "x": np.ascontiguousarray(inputs["L0_attn_x"][core]),
        "wq": np.ascontiguousarray(inputs["L0_attn_Wq"]),
        "wk": np.ascontiguousarray(inputs["L0_attn_Wk"]),
        "wv": np.ascontiguousarray(inputs["L0_attn_Wv"]),
        "gamma": np.ascontiguousarray(inputs["L0_attn_gamma"]).reshape(1, 1),
        "w1": np.ascontiguousarray(inputs["L0_ffn_w1"]),
        "b1": np.ascontiguousarray(inputs["L0_ffn_b1"]),
        "w2": np.ascontiguousarray(inputs["L0_ffn_w2"]),
        "b2": np.ascontiguousarray(inputs["L0_ffn_b2"]),
    }



def make_in_map(inputs, core):
    return {
        "x": np.ascontiguousarray(inputs["L0_attn_x"][core]),
        "wq": np.ascontiguousarray(inputs["L0_attn_Wq"]),
        "wk": np.ascontiguousarray(inputs["L0_attn_Wk"]),
        "wv": np.ascontiguousarray(inputs["L0_attn_Wv"]),
        "gamma": np.ascontiguousarray(inputs["L0_attn_gamma"]).reshape(1, 1),
        "w1": np.ascontiguousarray(inputs["L0_ffn_w1"]),
        "b1": np.ascontiguousarray(inputs["L0_ffn_b1"]),
        "w2": np.ascontiguousarray(inputs["L0_ffn_w2"]),
        "b2": np.ascontiguousarray(inputs["L0_ffn_b2"]),
    }



def make_in_map(inputs, core):
    return {
        "x": np.ascontiguousarray(inputs["L0_attn_x"][core]),
        "wq": np.ascontiguousarray(inputs["L0_attn_Wq"]),
        "wk": np.ascontiguousarray(inputs["L0_attn_Wk"]),
        "wv": np.ascontiguousarray(inputs["L0_attn_Wv"]),
        "gamma": np.ascontiguousarray(inputs["L0_attn_gamma"]).reshape(1, 1),
        "w1": np.ascontiguousarray(inputs["L0_ffn_w1"]),
        "b1": np.ascontiguousarray(inputs["L0_ffn_b1"]),
        "w2": np.ascontiguousarray(inputs["L0_ffn_w2"]),
        "b2": np.ascontiguousarray(inputs["L0_ffn_b2"]),
    }



def make_in_map(inputs, core):
    return {
        "x": np.ascontiguousarray(inputs["L0_attn_x"][core]),
        "wq": np.ascontiguousarray(inputs["L0_attn_Wq"]),
        "wk": np.ascontiguousarray(inputs["L0_attn_Wk"]),
        "wv": np.ascontiguousarray(inputs["L0_attn_Wv"]),
        "gamma": np.ascontiguousarray(inputs["L0_attn_gamma"]).reshape(1, 1),
        "w1": np.ascontiguousarray(inputs["L0_ffn_w1"]),
        "b1": np.ascontiguousarray(inputs["L0_ffn_b1"]),
        "w2": np.ascontiguousarray(inputs["L0_ffn_w2"]),
        "b2": np.ascontiguousarray(inputs["L0_ffn_b2"]),
    }



def make_in_map(inputs, core):
    return {
        "x": np.ascontiguousarray(inputs["L0_attn_x"][core]),
        "wq": np.ascontiguousarray(inputs["L0_attn_Wq"]),
        "wk": np.ascontiguousarray(inputs["L0_attn_Wk"]),
        "wv": np.ascontiguousarray(inputs["L0_attn_Wv"]),
        "gamma": np.ascontiguousarray(inputs["L0_attn_gamma"]).reshape(1, 1),
        "w1": np.ascontiguousarray(inputs["L0_ffn_w1"]),
        "b1": np.ascontiguousarray(inputs["L0_ffn_b1"]),
        "w2": np.ascontiguousarray(inputs["L0_ffn_w2"]),
        "b2": np.ascontiguousarray(inputs["L0_ffn_b2"]),
    }



def make_in_map(inputs, core):
    return {
        "x": np.ascontiguousarray(inputs["L0_attn_x"][core]),
        "wq": np.ascontiguousarray(inputs["L0_attn_Wq"]),
        "wk": np.ascontiguousarray(inputs["L0_attn_Wk"]),
        "wv": np.ascontiguousarray(inputs["L0_attn_Wv"]),
        "gamma": np.ascontiguousarray(inputs["L0_attn_gamma"]).reshape(1, 1),
        "w1": np.ascontiguousarray(inputs["L0_ffn_w1"]),
        "b1": np.ascontiguousarray(inputs["L0_ffn_b1"]),
        "w2": np.ascontiguousarray(inputs["L0_ffn_w2"]),
        "b2": np.ascontiguousarray(inputs["L0_ffn_b2"]),
    }



def make_in_map(inputs, core):
    return {
        "x": np.ascontiguousarray(inputs["L0_attn_x"][core]),
        "wq": np.ascontiguousarray(inputs["L0_attn_Wq"]),
        "wk": np.ascontiguousarray(inputs["L0_attn_Wk"]),
        "wv": np.ascontiguousarray(inputs["L0_attn_Wv"]),
        "gamma": np.ascontiguousarray(inputs["L0_attn_gamma"]).reshape(1, 1),
        "w1": np.ascontiguousarray(inputs["L0_ffn_w1"]),
        "b1": np.ascontiguousarray(inputs["L0_ffn_b1"]),
        "w2": np.ascontiguousarray(inputs["L0_ffn_w2"]),
        "b2": np.ascontiguousarray(inputs["L0_ffn_b2"]),
    }



def make_in_map(inputs, core):
    return {
        "x": np.ascontiguousarray(inputs["L0_attn_x"][core]),
        "wq": np.ascontiguousarray(inputs["L0_attn_Wq"]),
        "wk": np.ascontiguousarray(inputs["L0_attn_Wk"]),
        "wv": np.ascontiguousarray(inputs["L0_attn_Wv"]),
        "gamma": np.ascontiguousarray(inputs["L0_attn_gamma"]).reshape(1, 1),
        "w1": np.ascontiguousarray(inputs["L0_ffn_w1"]),
        "b1": np.ascontiguousarray(inputs["L0_ffn_b1"]),
        "w2": np.ascontiguousarray(inputs["L0_ffn_w2"]),
        "b2": np.ascontiguousarray(inputs["L0_ffn_b2"]),
    }



def make_in_map(inputs, core):
    return {
        "x": np.ascontiguousarray(inputs["L0_attn_x"][core]),
        "wq": np.ascontiguousarray(inputs["L0_attn_Wq"]),
        "wk": np.ascontiguousarray(inputs["L0_attn_Wk"]),
        "wv": np.ascontiguousarray(inputs["L0_attn_Wv"]),
        "gamma": np.ascontiguousarray(inputs["L0_attn_gamma"]).reshape(1, 1),
        "w1": np.ascontiguousarray(inputs["L0_ffn_w1"]),
        "b1": np.ascontiguousarray(inputs["L0_ffn_b1"]),
        "w2": np.ascontiguousarray(inputs["L0_ffn_w2"]),
        "b2": np.ascontiguousarray(inputs["L0_ffn_b2"]),
    }




def make_in_map(inputs, core):
    f32 = np.float32
    return {
        "x": np.ascontiguousarray(np.asarray(inputs["L0_attn_x"], dtype=f32)[core]),
        "wq": np.ascontiguousarray(np.asarray(inputs["L0_attn_Wq"], dtype=f32)),
        "wk": np.ascontiguousarray(np.asarray(inputs["L0_attn_Wk"], dtype=f32)),
        "wv": np.ascontiguousarray(np.asarray(inputs["L0_attn_Wv"], dtype=f32)),
        "gamma": np.ascontiguousarray(np.asarray(inputs["L0_attn_gamma"], dtype=f32)).reshape(1, 1),
        "w1": np.ascontiguousarray(np.asarray(inputs["L0_ffn_w1"], dtype=f32)),
        "b1": np.ascontiguousarray(np.asarray(inputs["L0_ffn_b1"], dtype=f32)),
        "w2": np.ascontiguousarray(np.asarray(inputs["L0_ffn_w2"], dtype=f32)),
        "b2": np.ascontiguousarray(np.asarray(inputs["L0_ffn_b2"], dtype=f32)),
    }


def kernel(**inputs):
    from concourse.bass_utils import run_bass_kernel_spmd
    nc = build_nc()
    in_maps = [make_in_map(inputs, core) for core in range(8)]
    res = run_bass_kernel_spmd(nc, in_maps, core_ids=list(range(8)))
    return np.stack([np.asarray(res.results[i]["y"], dtype=np.float32)
                     for i in range(8)], axis=0)


# version 32
# speedup vs baseline: 1.0775x; 1.0775x over previous
import contextlib
import numpy as np
import concourse.bass as bass
import concourse.mybir as mybir

F32 = mybir.dt.float32
BF16 = mybir.dt.bfloat16
U8 = mybir.dt.uint8
AX = mybir.AxisListType.X
OP = mybir.AluOpType
AF = mybir.ActivationFunctionType

S, C, HD = 2048, 1024, 4096
NS, NC, NH = 16, 8, 32
W2_RES = 16
EPS = 1e-5

ENGS = ("tensor", "vector", "scalar", "gpsimd", "sync")
STRICT = True
STRICT_ENGS = {"vector", "scalar"}
ENG_SEM = {"tensor": "T", "vector": "V", "scalar": "Sc", "gpsimd": "G"}


class Plan:
    def __init__(self):
        self.ops = {e: [] for e in ENGS}
        self.cnt = {}
        self.strict_now = set()

    def emit(self, eng, fn, waits=(), inc=None, inc_by=1):
        waits = [(s, v) for (s, v) in waits if v > 0]
        if eng in ENG_SEM:
            es = ENG_SEM[eng]
            if (eng == "gpsimd" or (STRICT and eng in STRICT_ENGS)
                    or eng in self.strict_now) and self.c(es) > 0:
                waits.append((es, self.c(es)))
            if inc is None:
                inc = es
        self.ops[eng].append((fn, waits, inc, inc_by))
        if inc is not None:
            self.cnt[inc] = self.cnt.get(inc, 0) + inc_by

    def c(self, sem):
        return self.cnt.get(sem, 0)


def build_nc():
    nc = bass.Bass("TRN2", target_bir_lowering=False)
    x = nc.dram_tensor("x", [S, C], F32, kind="ExternalInput").ap()
    wq = nc.dram_tensor("wq", [C, C], F32, kind="ExternalInput").ap()
    wk = nc.dram_tensor("wk", [C, C], F32, kind="ExternalInput").ap()
    wv = nc.dram_tensor("wv", [C, C], F32, kind="ExternalInput").ap()
    gamma = nc.dram_tensor("gamma", [1, 1], F32, kind="ExternalInput").ap()
    w1 = nc.dram_tensor("w1", [C, HD], F32, kind="ExternalInput").ap()
    b1 = nc.dram_tensor("b1", [HD], F32, kind="ExternalInput").ap()
    w2 = nc.dram_tensor("w2", [HD, C], F32, kind="ExternalInput").ap()
    b2 = nc.dram_tensor("b2", [C], F32, kind="ExternalInput").ap()
    y = nc.dram_tensor("y", [S, C], F32, kind="ExternalOutput").ap()
    w1bf = nc.dram_tensor("w1bf", [C, HD], BF16, kind="Internal").ap()
    w2bf = nc.dram_tensor("w2bf", [HD, C], BF16, kind="Internal").ap()

    ctx = contextlib.ExitStack()
    sb = lambda n, s, d: ctx.enter_context(nc.sbuf_tensor(n, s, d))
    regA = sb("regA", [128, 65536], U8)
    regFG = sb("regFG", [128, 65536], U8)
    regH = sb("regH", [128, 32768], U8)
    regE = sb("regE", [128, 20480], U8)
    rA_f = regA.bitcast(F32)
    rA_b = regA.bitcast(BF16)
    rFG = regFG.bitcast(BF16)
    rH = regH.bitcast(BF16)
    rE = regE.bitcast(BF16)

    resid = lambda ti: rA_f[:, ti * 1024:(ti + 1) * 1024]
    resid2 = lambda ti, et: rA_f[:, ti * 1024 + et * 512: ti * 1024 + et * 512 + 512]

    _wtg_base = (12288, 8192)
    wtg_all2 = [rA_b[:, b:b + 4096].rearrange("p (ci d) -> p ci d", ci=8)
                for b in _wtg_base]
    WTg2 = lambda bi, ci, off, n: rA_b[:, _wtg_base[bi] + ci * 512 + off:
                                       _wtg_base[bi] + ci * 512 + off + n]
    xfm_all = rA_b[:, 16384:32768].rearrange("p (ci s) -> p ci s", ci=8)
    xfm = lambda ci, s0, n: rA_b[:, 16384 + ci * 2048 + s0: 16384 + ci * 2048 + s0 + n]
    fsl = lambda d, s0, n: rFG[:, d * 2048 + s0: d * 2048 + s0 + n]
    gsl = lambda d, s0, n: rFG[:, 16384 + d * 2048 + s0: 16384 + d * 2048 + s0 + n]
    w1sl = lambda ci, h0, n: rFG[:, ci * 4096 + h0: ci * 4096 + h0 + n]
    hsl = lambda si, c0, n: rH[:, si * 1024 + c0: si * 1024 + c0 + n]
    w2res = lambda hc, e0, n: rH[:, hc * 1024 + e0: hc * 1024 + e0 + n]
    ebT = [rE[:, 0:2048], rE[:, 2048:4096]]
    ebsm2 = [rE[:, 4096:6144], rE[:, 6144:8192]]
    x2fm2 = [rE[:, 0:2048], rE[:, 6144:8192]]
    x2f_all2 = [rE[:, 0:2048].rearrange("p (ci s) -> p ci s", ci=8),
                rE[:, 6144:8192].rearrange("p (ci s) -> p ci s", ci=8)]
    hid = [rE[:, 2048:2304], rE[:, 2304:2560]]
    w2s = [rE[:, 3072:4096], rE[:, 4096:5120], rE[:, 8192:9216], rE[:, 9216:10240]]

    ident = sb("ident", [128, 128], BF16)
    identf = sb("identf", [128, 128], F32)
    ones1 = sb("ones1", [1, 128], BF16)
    b2rowb = sb("b2rowb", [1, 1024], BF16)
    b1sb = sb("b1sb", [128, 32], F32)
    gbc = sb("gbc", [128, 1], F32)
    cdb = sb("cdb", [128, 4], F32)
    stats = sb("stats", [128, 64], F32)
    xch = sb("xch", [128, 8192], BF16)
    xc = [xch[:, k * 1024:(k + 1) * 1024] for k in range(8)]
    wbw = xch
    varsink = sb("varsink", [128, 1024], F32)

    CONSTS = [EPS, -0.5, 1.0 / 1024]
    for i, cv in enumerate(CONSTS):
        nc.const_aps.register(F32, cv, cdb[:, i:i + 1])

    NM = lambda par, st: stats[:, par * 4 + st: par * 4 + st + 1]
    NMALL = lambda par: stats[:, par * 4: par * 4 + 4]
    NEGMAX = lambda par: stats[:, 8 + par: 9 + par]
    ZP = lambda par, st: stats[:, 10 + par * 4 + st: 11 + par * 4 + st]
    ZPALL = lambda par: stats[:, 10 + par * 4: 14 + par * 4]
    ZCOL = stats[:, 18:19]
    ZICOL = stats[:, 19:20]
    ZG2 = [stats[:, 20:21], stats[:, 29:30]]
    NSUM = stats[:, 24:25]
    NEGM = stats[:, 25:26]
    VSUM = stats[:, 26:27]
    LNV = stats[:, 27:28]
    RSTD = stats[:, 28:29]
    KCOL = stats[:, 29:30]

    ps = ctx.enter_context(nc.psum_tensor("ps", [128, 4096], F32))
    ps_bf = ps.bitcast(BF16)
    bank = lambda b, off, n: ps[:, b * 512 + off: b * 512 + off + n]
    bankb = lambda b, off, n: ps_bf[:, b * 1024 + off: b * 1024 + off + n]
    bankb3 = lambda b: ps_bf[:, b * 1024:(b + 1) * 1024].rearrange(
        "p (ci s) -> p ci s", ci=8)

    sems = {}
    for name in ("T", "V", "Sc", "G", "LXC0", "LXC1", "LXC2", "LXC3",
                 "LWB0", "LWB1", "LWB2", "LWB3",
                 "LTE0", "LTE1", "LTC0", "LTC1",
                 "LW2S0", "LW2S1", "LW2S2", "LW2S3", "LXA", "LXB",
                 "LW1S0", "LW1S1", "LW1S2", "LW1S3",
                 "LW1S4", "LW1S5", "LW1S6", "LW1S7",
                 "LW2R0", "LW2R1", "LW2R2", "LW2R3", "LMISC", "LMISCG", "STY",
                 "LCW"):
        sems[name] = ctx.enter_context(nc.semaphore(name))

    P = Plan()
    pfree = {b: [] for b in range(8)}

    def v_init(v):
        v.memset(ident[:], 1.0)
        v.memset(identf[:], 1.0)
        v.memset(ones1[:], 1.0)
        ins = None
        for i, cv in enumerate(CONSTS):
            ins = v.memset(cdb[:, i:i + 1], cv)
        return ins
    P.emit("vector", v_init, inc="V")
    V_INIT = P.c("V")

    P.emit("gpsimd", lambda g: g.affine_select(
        out=ident[:], in_=ident[:], pattern=[[-1, 128]],
        compare_op=OP.is_equal, fill=0.0, base=0, channel_multiplier=1),
        waits=[("V", V_INIT)], inc="G")
    P.emit("gpsimd", lambda g: g.affine_select(
        out=identf[:], in_=identf[:], pattern=[[-1, 128]],
        compare_op=OP.is_equal, fill=0.0, base=0, channel_multiplier=1),
        inc="G")
    G_INIT = P.c("G")

    P.emit("sync", lambda e: e.dma_start(
        out=rA_f[:, 0:4 * 1024].rearrange("p (a b) -> p a b", a=4),
        in_=bass.AP(x.tensor, 0, [[1024, 128], [128 * 1024, 4], [1, 1024]])),
        inc="LXA", inc_by=16)
    P.emit("sync", lambda e: e.dma_start(
        out=varsink[0:32, 0:128], in_=bass.AP(b1.tensor, 0, [[128, 32], [1, 128]])),
        inc="LMISC", inc_by=16)
    P.emit("sync", lambda e: e.dma_start(
        out=gbc[:], in_=bass.AP(gamma.tensor, 0, [[0, 128], [1, 1]])),
        inc="LMISC", inc_by=16)
    P.emit("vector", lambda e: e.tensor_scalar_mul(gbc[:], gbc[:], 0.5),
           waits=[("LMISC", 32)], inc="V")
    P.emit("gpsimd", lambda e: e.dma_start(
        out=b2rowb[0:1, :], in_=bass.AP(b2.tensor, 0, [[1024, 1], [1, 1024]])),
        inc="LMISCG", inc_by=16)
    LMISC_ALL = P.c("LMISC")

    P.emit("tensor", lambda e: e.transpose(bank(6, 0, 32), varsink[0:32, 0:128],
                                           identf[0:32, 0:32]),
           waits=[("LMISC", 32), ("G", G_INIT)], inc="T")
    P.emit("vector", lambda e: e.tensor_copy(b1sb[:], bank(6, 0, 32)),
           waits=[("T", P.c("T"))], inc="V")
    pfree[6] = [("V", P.c("V"))]

    xT_done, xT_evac = {}, {}
    for pr in range(NS // 2):
        sl = pr % 4
        waits = [("T", xT_done[2 * pr - 7])] if pr >= 4 else []
        P.emit("gpsimd",
               (lambda pr_, sl_: lambda e: e.dma_start(
                   out=xch[:, sl_ * 2048:(sl_ + 1) * 2048].rearrange(
                       "p (a b) -> p a b", a=2),
                   in_=bass.AP(x.tensor, pr_ * 2 * 128 * 1024,
                               [[1024, 128], [128 * 1024, 2], [1, 1024]])))(pr, sl),
               waits=waits, inc=f"LXC{sl}", inc_by=16)
        for half in range(2):
            ti = 2 * pr + half
            bk = 4 + (ti % 2)
            twaits = [(f"LXC{sl}", 16 * (pr // 4 + 1)), ("G", G_INIT)] + pfree[bk]
            def tfn(e, ti=ti, bk=bk, sl=sl, half=half):
                ins = None
                for ci in range(NC):
                    ins = e.transpose(bankb(bk, ci * 128, 128),
                                      xc[sl * 2 + half][:, ci * 128:(ci + 1) * 128],
                                      ident[:])
                return ins
            P.emit("tensor", tfn, waits=twaits, inc="T")
            xT_done[ti] = P.c("T")
            P.emit("vector",
                   (lambda ti_, bk_: lambda e: e.tensor_copy(
                       xfm_all[:, :, ti_ * 128:(ti_ + 1) * 128], bankb3(bk_)))(ti, bk),
                   waits=[("T", xT_done[ti])], inc="V")
            xT_evac[ti] = P.c("V")
            pfree[bk] = [("V", xT_evac[ti])]
    XFM_ALL_V = P.c("V")

    wdram = {0: wq, 1: wk, 2: wv}
    rot = [0]

    def qkv_mm_tilegroup(lhsT_sl, rhs_sl, out_fn, N, extra_waits, evac_eng):
        bk = rot[0] % 4
        rot[0] += 1
        waits = list(extra_waits) + pfree[bk]
        def tfn(e, bk=bk, lhsT_sl=lhsT_sl, rhs_sl=rhs_sl, N=N):
            ins = None
            for ci in range(NC):
                ins = e.matmul(bank(bk, 0, N), lhsT_sl(ci), rhs_sl(ci),
                               start=(ci == 0), stop=(ci == NC - 1))
            return ins
        P.emit("tensor", tfn, waits=waits, inc="T")
        tcnt = P.c("T")
        sem = "Sc" if evac_eng == "scalar" else "V"
        def efn(e, bk=bk, out_fn=out_fn, N=N):
            out = out_fn()
            return (e.copy(out, bank(bk, 0, N)) if evac_eng == "scalar"
                    else e.tensor_copy(out, bank(bk, 0, N)))
        P.emit(evac_eng, efn, waits=[("T", tcnt)], inc=sem)
        pfree[bk] = [(sem, P.c(sem))]

    wT_done = {}
    lwb_cnt = [0, 0, 0, 0]
    wrow_idx_box = [0]
    wt_ready = {}

    def emit_w_lt(k):
        wi, grp = divmod(k, 2)
        bi = k % 2
        for rp in range(2):
            wrow_idx = wrow_idx_box[0]
            wp = wrow_idx // 2
            sl = wp % 4
            dj0 = grp * 4 + rp * 2
            if wp >= 4:
                waits = [("T", wT_done[wrow_idx - 7])]
            else:
                waits = [("T", xT_done[9 + 2 * wp])]
            P.emit("gpsimd",
                   (lambda wi_, dj_, sl_: lambda e: e.dma_start(
                       out=wbw[:, sl_ * 2048:(sl_ + 1) * 2048].rearrange(
                           "p (a b) -> p a b", a=2),
                       in_=bass.AP(wdram[wi_].tensor, dj_ * 128 * 1024,
                                   [[1024, 128], [128 * 1024, 2], [1, 1024]])))(
                       wi, dj0, sl),
                   waits=waits, inc=f"LWB{sl}", inc_by=16)
            lwb_cnt[sl] += 1
            for half in range(2):
                r = rp * 2 + half
                bk = 4 + (wrow_idx_box[0] % 4)
                twaits = [(f"LWB{sl}", 16 * lwb_cnt[sl]), ("G", G_INIT)] + pfree[bk]
                if k >= 2:
                    twaits.append(("T", wt_mm_done[k - 2]))
                def tfn(e, sl=sl, bk=bk, half=half):
                    base = sl * 2048 + half * 1024
                    ins = None
                    for ci in range(NC):
                        ins = e.transpose(
                            bankb(bk, ci * 128, 128),
                            wbw[:, base + ci * 128: base + (ci + 1) * 128],
                            ident[:])
                    return ins
                P.emit("tensor", tfn, waits=twaits, inc="T")
                wT_done[wrow_idx_box[0]] = P.c("T")
                P.emit("vector",
                       (lambda r_, bk_, bi_: lambda e: e.tensor_copy(
                           wtg_all2[bi_][:, :, r_ * 128:(r_ + 1) * 128],
                           bankb3(bk_)))(r, bk, bi),
                       waits=[("T", wT_done[wrow_idx_box[0]])], inc="V")
                pfree[bk] = [("V", P.c("V"))]
                wrow_idx_box[0] += 1
        wt_ready[k] = P.c("V")

    wt_mm_done = {}
    emit_w_lt(0)
    for k in range(6):
        wi, grp = divmod(k, 2)
        bi = k % 2
        if wi < 2:
            outsl = fsl if wi == 0 else gsl
            tiles = [(r, st) for r in range(4) for st in range(4)]
            for idx, (r, st) in enumerate(tiles):
                if idx == 8 and k + 1 < 6:
                    emit_w_lt(k + 1)
                dj = grp * 4 + r
                qkv_mm_tilegroup(
                    (lambda r_, bi_: lambda ci: WTg2(bi_, ci, r_ * 128, 128))(r, bi),
                    (lambda st_: lambda ci: xfm(ci, st_ * 512, 512))(st),
                    (lambda dj_, st_, o_=outsl: lambda: o_(dj_, st_ * 512, 512))(dj, st),
                    512, [("V", max(wt_ready[k], XFM_ALL_V))], "scalar")
        else:
            for si in range(NS):
                if si == 8 and k + 1 < 6:
                    emit_w_lt(k + 1)
                qkv_mm_tilegroup(
                    (lambda si_, bi_: lambda ci: xfm(ci, si_ * 128, 128))(si, bi),
                    (lambda bi_: lambda ci: WTg2(bi_, ci, 0, 512))(bi),
                    (lambda si_, grp_: lambda: hsl(si_, grp_ * 512, 512))(si, grp),
                    512, [("V", max(wt_ready[k], XFM_ALL_V))], "vector")
        wt_mm_done[k] = P.c("T")

    A_END_T = P.c("T")
    A_END_V = P.c("V")
    A_END_S = P.c("Sc")

    cast_jobs = []
    for i in range(8):
        cast_jobs.append((w1bf[i * 128:(i + 1) * 128, :],
                          w1[i * 128:(i + 1) * 128, :]))
    for i in range(16):
        cast_jobs.append((w2bf[i * 256:(i + 1) * 256, :],
                          w2[i * 256:(i + 1) * 256, :]))
    LCW_ALL = len(cast_jobs) * 16
    cast_ix = [0]

    def emit_casts(n):
        for _ in range(n):
            if cast_ix[0] >= len(cast_jobs):
                return
            dst, src = cast_jobs[cast_ix[0]]
            cast_ix[0] += 1
            P.emit("gpsimd",
                   (lambda d_, s_: lambda e: e.dma_start(out=d_, in_=s_))(dst, src),
                   waits=[("T", P.c("T"))], inc="LCW", inc_by=16)

    P.emit("scalar", lambda e: e.dma_start(
        out=rA_f[:, 4 * 1024:16 * 1024].rearrange("p (a b) -> p a b", a=12),
        in_=bass.AP(x.tensor, 4 * 128 * 1024, [[1024, 128], [128 * 1024, 12], [1, 1024]])),
        waits=[("T", A_END_T)], inc="LXB", inc_by=16)

    sc_done, exp_done, trT_done = {}, {}, {}
    ebsm_evac, ebsm_evac_h1, av_done, oevac_done = {}, {}, {}, {}
    nm_done = {}

    def emit_score_tile(ti, st, extra_waits):
        par = ti % 2
        bk = par * 4 + st
        def tfn(e, ti=ti, st=st, bk=bk):
            ins = None
            for d in range(NC):
                ins = e.matmul(bank(bk, 0, 512), gsl(d, ti * 128, 128),
                               fsl(d, st * 512, 512), start=(d == 0), stop=(d == NC - 1))
            return ins
        P.emit("tensor", tfn, waits=extra_waits, inc="T")
        sc_done[(ti, st)] = P.c("T")
        P.emit("vector",
               (lambda par_, st_, bk_: lambda e: e.reduce_max(
                   NM(par_, st_), bank(bk_, 0, 512), axis=AX, negate=True))(par, st, bk),
               waits=[("T", sc_done[(ti, st)])], inc="V")

    def emit_negmax_combine(ti):
        par = ti % 2
        P.emit("vector",
               (lambda par_: lambda e: e.tensor_reduce(
                   NEGMAX(par_), NMALL(par_), axis=AX, op=OP.min))(par), inc="V")
        nm_done[ti] = P.c("V")

    def emit_softmax(ti):
        par = ti % 2
        g0 = par * 4
        waits = [("V", nm_done[ti])]
        if ti >= 2:
            waits.append(ebsm_evac[ti - 2])
        for st in range(4):
            P.emit("scalar",
                   (lambda st_, g0_, par_: lambda e: e.activation(
                       out=ebT[par_][:, st_ * 512:(st_ + 1) * 512],
                       in_=bank(g0_ + st_, 0, 512), func=AF.Exp,
                       bias=NEGMAX(par_), scale=1.0,
                       accum_out=ZP(par_, st_)))(st, g0, par),
                   waits=(waits if st == 0 else []), inc="Sc")
        exp_done[ti] = P.c("Sc")

    lte_cnt = [0, 0]
    tail_trT = {}

    def emit_transposes(ti):
        par = ti % 2
        eb = ebsm2[par]
        if ti < NS - 2:
            waits = [("Sc", exp_done[ti])]
            if ti >= 2:
                waits.append(("T", av_done.get(ti - 2, 0)))
                waits.append((f"LTE{par}", lte_cnt[par]))
            P.emit("sync",
                   (lambda par_, eb_=eb: lambda e: e.dma_start_transpose(
                       eb_[:].rearrange("p (k c) -> p k c", k=16),
                       ebT[par_][:].rearrange("p (k c) -> p k c", k=16)))(par),
                   waits=waits, inc=f"LTE{par}", inc_by=16)
            lte_cnt[par] += 16
            ebsm_evac[ti] = (f"LTE{par}", lte_cnt[par])
        else:
            g0 = par * 4
            def tfn(e, par=par, g0=g0):
                ins = None
                for k in range(NS):
                    ins = e.transpose(
                        ps_bf[:, g0 * 1024 + k * 128: g0 * 1024 + (k + 1) * 128],
                        ebT[par][:, k * 128:(k + 1) * 128], ident[:])
                return ins
            P.emit("tensor", tfn,
                   waits=[("Sc", exp_done[ti]), (f"LTE{par}", lte_cnt[par])],
                   inc="T")
            trT = P.c("T")
            tail_trT[ti] = trT
            P.emit("vector",
                   (lambda g0_, eb_=eb: lambda e: e.tensor_copy(
                       eb_[:, 0:2048], ps_bf[:, g0_ * 1024: g0_ * 1024 + 2048]))(g0),
                   waits=[("T", trT), ("T", av_done.get(ti - 2, 0))], inc="V")
            ebsm_evac[ti] = ("V", P.c("V"))
        P.emit("vector",
               (lambda par_: lambda e: e.tensor_reduce(
                   ZCOL, ZPALL(par_), axis=AX, op=OP.add))(par),
               waits=[("Sc", exp_done[ti]), ("LMISC", LMISC_ALL)], inc="V")
        P.emit("vector", lambda e: e.reciprocal(ZICOL, ZCOL), inc="V")
        P.emit("vector",
               (lambda par_: lambda e: e.tensor_mul(ZG2[par_], ZICOL, gbc[:]))(par),
               inc="V")

    def emit_av(ti):
        g0 = (ti % 2) * 4
        eb = ebsm2[ti % 2]
        for ct in range(2):
            bk = g0 + 2 + ct
            def tfn_a(e, ct=ct, bk=bk, eb=eb):
                ins = None
                for k in range(NS):
                    ins = e.matmul(bank(bk, 0, 512), eb[:, k * 128:(k + 1) * 128],
                                   hsl(k, ct * 512, 512),
                                   start=(k == 0), stop=(k == NS - 1))
                return ins
            P.emit("tensor", tfn_a, waits=[ebsm_evac[ti]], inc="T")
        av_done[ti] = P.c("T")

    oevac_h1 = {}

    def emit_oevac(ti):
        g0 = (ti % 2) * 4
        xw = ("LXA", 16) if ti < 4 else ("LXB", 16)
        for ct in range(2):
            P.emit("vector",
                   (lambda ti_, ct_, bk_: lambda e: e.scalar_tensor_tensor(
                       out=resid2(ti_, ct_), in0=bank(bk_, 0, 512), scalar=ZG2[ti_ % 2],
                       in1=resid2(ti_, ct_), op0=OP.mult, op1=OP.add))(ti, ct, g0 + 2 + ct),
                   waits=[("T", av_done[ti]), xw], inc="V")
            if ct == 0:
                oevac_h1[ti] = P.c("V")
        oevac_done[ti] = P.c("V")

    R1COL = lambda ti: stats[:, 32 + ti:33 + ti]
    M1COL = lambda ti: stats[:, 48 + ti:49 + ti]
    BNS = stats[:, 21:22]
    BVS2 = [stats[:, 23:24], stats[:, 27:28]]
    BLL2 = [stats[:, 24:25], stats[:, 28:29]]
    K1TMP = [stats[:, 25:26], stats[:, 26:27]]

    bstats_sc = {}

    def emit_ln1_bstats(ti):
        P.emit("vector",
               (lambda t_: lambda e: e.reduce_sum(BNS, resid(t_), axis=AX,
                                                  negate=True))(ti), inc="V")
        P.emit("vector",
               (lambda t_: lambda e: e.tensor_scalar_mul(
                   M1COL(t_), BNS, 1.0 / 1024))(ti), inc="V")
        vwaits = []
        if ti >= 2:
            vwaits.append(("Sc", bstats_sc[ti - 2]))
        P.emit("vector",
               (lambda t_: lambda e: e.scalar_tensor_tensor(
                   out=varsink[:], in0=resid(t_), scalar=M1COL(t_), in1=resid(t_),
                   op0=OP.add, op1=OP.mult, accum_out=BVS2[t_ % 2]))(ti),
               waits=vwaits, inc="V")
        vv = P.c("V")
        P.emit("scalar",
               (lambda t_: lambda e: e.activation(
                   out=BLL2[t_ % 2], in_=BVS2[t_ % 2], func=AF.Ln,
                   scale=1.0 / 1024, bias=EPS))(ti),
               waits=[("V", vv)], inc="Sc")
        P.emit("scalar",
               (lambda t_: lambda e: e.activation(out=R1COL(t_), in_=BLL2[t_ % 2],
                                                  func=AF.Exp, scale=-0.5))(ti),
               inc="Sc")
        bstats_sc[ti] = P.c("Sc")

    LNS = lambda b: stats[:, b:b + 1]
    LNM = lambda b: stats[:, b + 1:b + 2]
    LNV4 = lambda b: stats[:, b + 2:b + 3]
    LNL = lambda b: stats[:, b + 3:b + 4]
    LNR = lambda b: stats[:, b + 4:b + 5]
    LNK = lambda b: stats[:, b + 5:b + 6]
    vcast_n = {}

    def emit_lnb_v1(chains):
        for b, cch in chains:
            P.emit("vector",
                   (lambda c_, b_: lambda e: e.reduce_sum(
                       LNS(b_), resid(c_), axis=AX, negate=True))(cch, b), inc="V")
        for b, cch in chains:
            P.emit("vector",
                   (lambda b_: lambda e: e.tensor_scalar_mul(
                       LNM(b_), LNS(b_), 1.0 / 1024))(b), inc="V")
        for b, cch in chains:
            P.emit("vector",
                   (lambda c_, b_: lambda e: e.scalar_tensor_tensor(
                       out=varsink[:], in0=resid(c_), scalar=LNM(b_), in1=resid(c_),
                       op0=OP.add, op1=OP.mult, accum_out=LNV4(b_)))(cch, b), inc="V")
        return P.c("V")

    def emit_lnb_sc(chains, vwait):
        for b, cch in chains:
            P.emit("scalar",
                   (lambda b_: lambda e: e.activation(
                       out=LNL(b_), in_=LNV4(b_), func=AF.Ln,
                       scale=1.0 / 1024, bias=EPS))(b),
                   waits=[("V", vwait)], inc="Sc")
        for b, cch in chains:
            P.emit("scalar",
                   (lambda b_: lambda e: e.activation(
                       out=LNR(b_), in_=LNL(b_), func=AF.Exp, scale=-0.5))(b),
                   inc="Sc")
        return P.c("Sc")

    def emit_lnb_v2(chains, scwait, cast_e8=None, store=()):
        for b, cch in chains:
            P.emit("vector",
                   (lambda b_: lambda e: e.tensor_mul(LNK(b_), LNM(b_), LNR(b_)))(b),
                   waits=[("Sc", scwait)], inc="V")
        if cast_e8 is not None:
            slot = (cast_e8 % 2) * 2
            for sj, (b, cch) in enumerate(chains[-2:]):
                P.emit("vector",
                       (lambda c_, b_, sl_=slot + sj: lambda e: e.tensor_scalar(
                           out=xc[sl_][:], in0=resid(c_), scalar1=LNR(b_),
                           scalar2=LNK(b_), op0=OP.mult, op1=OP.add))(cch, b),
                       inc="V")
            vcast_n[cast_e8] = P.c("V")
        for b, cch in chains:
            P.emit("vector",
                   (lambda c_, b_: lambda e: e.tensor_scalar(
                       out=resid(c_), in0=resid(c_), scalar1=LNR(b_), scalar2=LNK(b_),
                       op0=OP.mult, op1=OP.add))(cch, b), inc="V")
        for b, cch in store:
            vfin = P.c("V")
            P.emit("sync",
                   (lambda c_: lambda e: e.dma_start(
                       out=y[c_ * 128:(c_ + 1) * 128, :], in_=resid(c_)))(cch),
                   waits=[("V", vfin)], inc="STY", inc_by=16)

    vcast_n_early = {}

    def emit_ln1_final_c0():
        scw = bstats_sc[1]
        P.emit("vector", lambda e: e.engine_nop(), waits=[("T", A_END_T)], inc="V")
        for sj, cch in enumerate((0, 1)):
            P.emit("vector",
                   (lambda c_, sj_: lambda e: e.tensor_mul(
                       K1TMP[sj_], M1COL(c_), R1COL(c_)))(cch, sj),
                   waits=[("Sc", scw)], inc="V")
        for sj, cch in enumerate((0, 1)):
            P.emit("vector",
                   (lambda c_, sl_, sj_: lambda e: e.tensor_scalar(
                       out=xc[sl_][:], in0=resid(c_), scalar1=R1COL(c_),
                       scalar2=K1TMP[sj_], op0=OP.mult, op1=OP.add))(cch, sj, sj),
                   inc="V")
        vcast_n_early[0] = P.c("V")
        for sj, cch in enumerate((0, 1)):
            P.emit("vector",
                   (lambda c_, sj_: lambda e: e.tensor_scalar(
                       out=resid(c_), in0=resid(c_), scalar1=R1COL(c_),
                       scalar2=K1TMP[sj_], op0=OP.mult, op1=OP.add))(cch, sj),
                   inc="V")

    for ti in (0, 1):
        for st in range(4):
            bk = (ti % 2) * 4 + st
            emit_score_tile(ti, st, [("V", A_END_V), ("Sc", A_END_S)] + pfree[bk])
        emit_negmax_combine(ti)

    for tj in range(NS):
        emit_softmax(tj)
        emit_transposes(tj)
        emit_casts(1)
        if tj >= 1:
            emit_ln1_bstats(tj - 1)
        emit_av(tj)
        emit_oevac(tj)
        if tj + 2 < NS:
            for st in (0, 1):
                emit_score_tile(tj + 2, st, [("Sc", exp_done[tj])])
            emit_score_tile(tj + 2, 2, [("V", oevac_h1[tj])])
            emit_score_tile(tj + 2, 3, [("V", oevac_done[tj])])
            emit_negmax_combine(tj + 2)
        emit_casts(1)
        if tj == NS - 1:
            emit_ln1_bstats(tj)
        if tj == 4:
            emit_ln1_final_c0()
        if tj == 13:
            t_after_s15 = P.c("T")
            w1_all = rFG[:, 0:32768].rearrange("p (ei h) -> p ei h", ei=8)
            w2r3 = lambda i: rH[:, i * 4096:(i + 1) * 4096].rearrange(
                "p (a b) -> p a b", a=4)
            def w1dma(k):
                P.emit("gpsimd",
                       (lambda k_: lambda e: e.dma_start(
                           out=w1_all[:, :, k_ * 512:(k_ + 1) * 512],
                           in_=bass.AP(w1bf.tensor, k_ * 512,
                                       [[4096, 128], [128 * 4096, 8], [1, 512]])))(k),
                       waits=[("T", t_after_s15), ("LCW", LCW_ALL)],
                       inc=f"LW1S{k}", inc_by=16)
            def w2rdma(i):
                P.emit("gpsimd",
                       (lambda i_: lambda e: e.dma_start(
                           out=w2r3(i_),
                           in_=bass.AP(w2bf.tensor, i_ * 4 * 128 * 1024,
                                       [[1024, 128], [128 * 1024, 4], [1, 1024]])))(i),
                       waits=[("T", P.c("T")), ("LCW", LCW_ALL)],
                       inc=f"LW2R{i}", inc_by=16)
            w1dma(0)
    B_END_T = P.c("T")
    B_END_V = P.c("V")
    w1dma(1)
    w1dma(2)
    w1dma(3)
    w2rdma(0)
    w2rdma(1)
    w1dma(4)
    w2rdma(2)
    w2rdma(3)
    w1dma(5)
    w1dma(6)
    w1dma(7)

    xcT_done, x2fm_evac = {}, {}
    ffn1_done, gelu_done, ffn2_done, r2evac_done = {}, {}, {}, {}
    lw2s_cnt = [0, 0, 0, 0]
    w2s_seq = {}
    FB = (2, 3, 0)
    hid3 = [rE[:, 2048:2304], rE[:, 2304:2560], rE[:, 2560:2816]]

    ltc_cnt = [0, 0]
    x2fm_ready = {}

    def emit_c_head_t(e8):
        slot = (e8 % 2) * 2
        par = e8 % 2
        for sj in range(2):
            waits = [("V", vcast_n[e8])]
            if sj == 0:
                if e8 == 0:
                    waits.append(("T", tail_trT[14]))
                elif e8 == 1:
                    waits.append(("T", B_END_T))
                else:
                    waits.append(("T", ffn1_done[(e8 - 2) * NH + NH - 1]))
                    waits.append((f"LTC{par}", ltc_cnt[par]))
            P.emit("sync",
                   (lambda sj_, sl_, par_: lambda e: e.dma_start_transpose(
                       x2f_all2[par_][:, :, sj_ * 128:(sj_ + 1) * 128],
                       xc[sl_ + sj_][:].rearrange("p (k c) -> p k c", k=8)))(
                       sj, slot, par),
                   waits=waits, inc=f"LTC{par}", inc_by=16)
            ltc_cnt[par] += 16
        x2fm_ready[e8] = (f"LTC{par}", ltc_cnt[par])

    def w2s_prefetch(g, gate_g):
        e8, hc = divmod(g, NH)
        sl = hc % 4
        if gate_g is None:
            waits = [("T", B_END_T)]
        else:
            waits = [("T", ffn2_done[gate_g])]
        waits.append(("LCW", LCW_ALL))
        lw2s_cnt[sl] += 1
        w2s_seq[g] = lw2s_cnt[sl]
        P.emit("gpsimd",
               (lambda hc_, sl_: lambda e: e.dma_start(
                   out=w2s[sl_], in_=w2bf[hc_ * 128:(hc_ + 1) * 128, :]))(hc, sl),
               waits=waits, inc=f"LW2S{sl}", inc_by=16)

    for k in range(4):
        w2s_prefetch(W2_RES + k, None)

    def emit_ffn1(g):
        e8, hc = divmod(g, NH)
        fbk = FB[g % 3]
        waits = [x2fm_ready[e8], (f"LW1S{hc // 4}", 16)]
        if g - 3 >= 0:
            waits.append(("Sc", gelu_done[g - 3]))
        else:
            waits.append(("V", oevac_done[14]))
            if g == 2:
                waits.append(ebsm_evac[14])
        def tfn(e, hc=hc, fbk=fbk, e8=e8):
            ins = None
            xb_ = x2fm2[e8 % 2]
            for ci in range(NC):
                ins = e.matmul(bank(fbk, 0, 256), w1sl(ci, hc * 128, 128),
                               xb_[:, ci * 256:(ci + 1) * 256],
                               start=(ci == 0), stop=(ci == NC - 1))
            return ins
        P.emit("tensor", tfn, waits=waits, inc="T")
        ffn1_done[g] = P.c("T")

    def emit_gelu(g):
        e8, hc = divmod(g, NH)
        fbk = FB[g % 3]
        waits = [("T", ffn1_done[g]), ("LMISC", LMISC_ALL)]
        if g - 3 >= 0:
            waits.append(("T", ffn2_done[g - 3]))
        P.emit("scalar",
               (lambda hc_, fbk_, g_: lambda e: e.activation(
                   out=hid3[g_ % 3][:, 0:256], in_=bank(fbk_, 0, 256),
                   func=AF.Gelu, bias=b1sb[:, hc_:hc_ + 1], scale=1.0))(hc, fbk, g),
               waits=waits, inc="Sc")
        gelu_done[g] = P.c("Sc")

    def emit_ffn2(g):
        e8, hc = divmod(g, NH)
        waits = [("Sc", gelu_done[g])]
        if hc < W2_RES and e8 == 0:
            waits.append((f"LW2R{hc // 4}", 16))
        if hc == 0:
            if e8 > 0:
                waits.append(("V", r2evac_done[e8 - 1]))
            else:
                waits.append(("V", B_END_V))
        if hc >= W2_RES:
            waits.append((f"LW2S{hc % 4}", 16 * w2s_seq[g]))
        def tfn2(e, hc=hc, g=g):
            ins = None
            for sj in range(2):
                for et in range(2):
                    rhs = (w2res(hc, et * 512, 512) if hc < W2_RES
                           else w2s[hc % 4][:, et * 512:(et + 1) * 512])
                    ins = e.matmul(bank(4 + sj * 2 + et, 0, 512),
                                   hid3[g % 3][:, sj * 128:(sj + 1) * 128], rhs,
                                   start=(hc == 0), stop=False)
            return ins
        P.emit("tensor", tfn2, waits=waits, inc="T")
        ffn2_done[g] = P.c("T")
        if hc >= 16:
            nxt = g + 4 if hc <= 27 else (g - hc) + NH + 16 + (hc % 4)
            if nxt < 8 * NH and nxt % NH >= 16:
                w2s_prefetch(nxt, g)

    def emit_fold_r2(e8):
        c0, c1 = 2 * e8, 2 * e8 + 1
        def foldfn(e):
            ins = None
            for sj in range(2):
                for et in range(2):
                    ins = e.matmul(bank(4 + sj * 2 + et, 0, 512), ones1[0:1, :],
                                   b2rowb[0:1, et * 512:(et + 1) * 512],
                                   start=False, stop=True)
            return ins
        P.emit("tensor", foldfn, waits=[("LMISCG", 16)], inc="T")
        fold_t = P.c("T")
        for sj, cch in enumerate((c0, c1)):
            for et in range(2):
                P.emit("vector",
                       (lambda c_, sj_, et_: lambda e: e.tensor_tensor(
                           out=resid2(c_, et_), in0=bank(4 + sj_ * 2 + et_, 0, 512),
                           in1=resid2(c_, et_), op=OP.add))(cch, sj, et),
                       waits=[("T", fold_t)], inc="V")
        r2evac_done[e8] = P.c("V")

    def emit_ln1_final_c(e8):
        slot = (e8 % 2) * 2
        scw = bstats_sc[2 * e8 + 1]
        if e8 >= 2:
            P.emit("vector", lambda e: e.engine_nop(),
                   waits=[x2fm_ready[e8 - 2]], inc="V")
        for sj, cch in enumerate((2 * e8, 2 * e8 + 1)):
            P.emit("vector",
                   (lambda c_, sj_: lambda e: e.tensor_mul(
                       K1TMP[sj_], M1COL(c_), R1COL(c_)))(cch, sj),
                   waits=[("Sc", scw)], inc="V")
        for sj, cch in enumerate((2 * e8, 2 * e8 + 1)):
            P.emit("vector",
                   (lambda c_, sl_, sj_: lambda e: e.tensor_scalar(
                       out=xc[sl_][:], in0=resid(c_), scalar1=R1COL(c_),
                       scalar2=K1TMP[sj_], op0=OP.mult, op1=OP.add))(
                       cch, slot + sj, sj),
                   inc="V")
        vcast_n[e8] = P.c("V")
        for sj, cch in enumerate((2 * e8, 2 * e8 + 1)):
            P.emit("vector",
                   (lambda c_, sj_: lambda e: e.tensor_scalar(
                       out=resid(c_), in0=resid(c_), scalar1=R1COL(c_),
                       scalar2=K1TMP[sj_], op0=OP.mult, op1=OP.add))(cch, sj),
                   inc="V")

    vcast_n[0] = vcast_n_early[0]
    emit_c_head_t(0)
    lnch, lnb_vw, lnb_sw = None, None, None
    for g in range(8 * NH):
        e8, hc = divmod(g, NH)
        emit_ffn1(g)
        if g >= 3:
            gg = g - 3
            emit_ffn2(gg)
            if gg % NH == NH - 1:
                emit_fold_r2(gg // NH)
        emit_gelu(g)
        if hc == 2 and e8 > 0:
            lnch = [(0, 2 * (e8 - 1)), (6, 2 * (e8 - 1) + 1)]
            lnb_vw = emit_lnb_v1(lnch)
        if hc == 6 and e8 + 1 < 8:
            emit_ln1_final_c(e8 + 1)
        if hc == 8 and e8 > 0:
            lnb_sw = emit_lnb_sc(lnch, lnb_vw)
        if hc == 10 and e8 > 0:
            emit_lnb_v2(lnch, lnb_sw, store=lnch)
        if hc == 12 and e8 + 1 < 8:
            emit_c_head_t(e8 + 1)
    emit_ffn2(8 * NH - 3)
    emit_ffn2(8 * NH - 2)
    emit_ffn2(8 * NH - 1)
    emit_fold_r2(7)

    ch7 = [(0, 14), (6, 15)]
    vw7 = emit_lnb_v1(ch7)
    sw7 = emit_lnb_sc(ch7, vw7)
    emit_lnb_v2(ch7, sw7, store=ch7)
    P.emit("sync", lambda e: e.wait_ge(sems["STY"], 16 * NS))

    block = ctx.enter_context(nc.Block())

    def emit_stream(eng_name):
        def run(eng):
            last = {}
            for fn, waits, inc, inc_by in P.ops[eng_name]:
                for s, v in waits:
                    if last.get(s, 0) < v:
                        eng.wait_ge(sems[s], v)
                        last[s] = v
                ins = fn(eng)
                if inc is not None:
                    if ins is None:
                        ins = eng.nop()
                    ins.then_inc(sems[inc], inc_by)
        return run

    block.tensor(emit_stream("tensor"))
    block.vector(emit_stream("vector"))
    block.scalar(emit_stream("scalar"))
    block.gpsimd(emit_stream("gpsimd"))
    block.sync(emit_stream("sync"))

    ctx.close()
    return nc


def make_in_map(inputs, core):
    return {
        "x": np.ascontiguousarray(inputs["L0_attn_x"][core]),
        "wq": np.ascontiguousarray(inputs["L0_attn_Wq"]),
        "wk": np.ascontiguousarray(inputs["L0_attn_Wk"]),
        "wv": np.ascontiguousarray(inputs["L0_attn_Wv"]),
        "gamma": np.ascontiguousarray(inputs["L0_attn_gamma"]).reshape(1, 1),
        "w1": np.ascontiguousarray(inputs["L0_ffn_w1"]),
        "b1": np.ascontiguousarray(inputs["L0_ffn_b1"]),
        "w2": np.ascontiguousarray(inputs["L0_ffn_w2"]),
        "b2": np.ascontiguousarray(inputs["L0_ffn_b2"]),
    }


def make_in_map(inputs, core):
    return {
        "x": np.ascontiguousarray(inputs["L0_attn_x"][core]),
        "wq": np.ascontiguousarray(inputs["L0_attn_Wq"]),
        "wk": np.ascontiguousarray(inputs["L0_attn_Wk"]),
        "wv": np.ascontiguousarray(inputs["L0_attn_Wv"]),
        "gamma": np.ascontiguousarray(inputs["L0_attn_gamma"]).reshape(1, 1),
        "w1": np.ascontiguousarray(inputs["L0_ffn_w1"]),
        "b1": np.ascontiguousarray(inputs["L0_ffn_b1"]),
        "w2": np.ascontiguousarray(inputs["L0_ffn_w2"]),
        "b2": np.ascontiguousarray(inputs["L0_ffn_b2"]),
    }


def make_in_map(inputs, core):
    return {
        "x": np.ascontiguousarray(inputs["L0_attn_x"][core]),
        "wq": np.ascontiguousarray(inputs["L0_attn_Wq"]),
        "wk": np.ascontiguousarray(inputs["L0_attn_Wk"]),
        "wv": np.ascontiguousarray(inputs["L0_attn_Wv"]),
        "gamma": np.ascontiguousarray(inputs["L0_attn_gamma"]).reshape(1, 1),
        "w1": np.ascontiguousarray(inputs["L0_ffn_w1"]),
        "b1": np.ascontiguousarray(inputs["L0_ffn_b1"]),
        "w2": np.ascontiguousarray(inputs["L0_ffn_w2"]),
        "b2": np.ascontiguousarray(inputs["L0_ffn_b2"]),
    }


def make_in_map(inputs, core):
    return {
        "x": np.ascontiguousarray(inputs["L0_attn_x"][core]),
        "wq": np.ascontiguousarray(inputs["L0_attn_Wq"]),
        "wk": np.ascontiguousarray(inputs["L0_attn_Wk"]),
        "wv": np.ascontiguousarray(inputs["L0_attn_Wv"]),
        "gamma": np.ascontiguousarray(inputs["L0_attn_gamma"]).reshape(1, 1),
        "w1": np.ascontiguousarray(inputs["L0_ffn_w1"]),
        "b1": np.ascontiguousarray(inputs["L0_ffn_b1"]),
        "w2": np.ascontiguousarray(inputs["L0_ffn_w2"]),
        "b2": np.ascontiguousarray(inputs["L0_ffn_b2"]),
    }



def make_in_map(inputs, core):
    return {
        "x": np.ascontiguousarray(inputs["L0_attn_x"][core]),
        "wq": np.ascontiguousarray(inputs["L0_attn_Wq"]),
        "wk": np.ascontiguousarray(inputs["L0_attn_Wk"]),
        "wv": np.ascontiguousarray(inputs["L0_attn_Wv"]),
        "gamma": np.ascontiguousarray(inputs["L0_attn_gamma"]).reshape(1, 1),
        "w1": np.ascontiguousarray(inputs["L0_ffn_w1"]),
        "b1": np.ascontiguousarray(inputs["L0_ffn_b1"]),
        "w2": np.ascontiguousarray(inputs["L0_ffn_w2"]),
        "b2": np.ascontiguousarray(inputs["L0_ffn_b2"]),
    }



def make_in_map(inputs, core):
    return {
        "x": np.ascontiguousarray(inputs["L0_attn_x"][core]),
        "wq": np.ascontiguousarray(inputs["L0_attn_Wq"]),
        "wk": np.ascontiguousarray(inputs["L0_attn_Wk"]),
        "wv": np.ascontiguousarray(inputs["L0_attn_Wv"]),
        "gamma": np.ascontiguousarray(inputs["L0_attn_gamma"]).reshape(1, 1),
        "w1": np.ascontiguousarray(inputs["L0_ffn_w1"]),
        "b1": np.ascontiguousarray(inputs["L0_ffn_b1"]),
        "w2": np.ascontiguousarray(inputs["L0_ffn_w2"]),
        "b2": np.ascontiguousarray(inputs["L0_ffn_b2"]),
    }



def make_in_map(inputs, core):
    return {
        "x": np.ascontiguousarray(inputs["L0_attn_x"][core]),
        "wq": np.ascontiguousarray(inputs["L0_attn_Wq"]),
        "wk": np.ascontiguousarray(inputs["L0_attn_Wk"]),
        "wv": np.ascontiguousarray(inputs["L0_attn_Wv"]),
        "gamma": np.ascontiguousarray(inputs["L0_attn_gamma"]).reshape(1, 1),
        "w1": np.ascontiguousarray(inputs["L0_ffn_w1"]),
        "b1": np.ascontiguousarray(inputs["L0_ffn_b1"]),
        "w2": np.ascontiguousarray(inputs["L0_ffn_w2"]),
        "b2": np.ascontiguousarray(inputs["L0_ffn_b2"]),
    }



def make_in_map(inputs, core):
    return {
        "x": np.ascontiguousarray(inputs["L0_attn_x"][core]),
        "wq": np.ascontiguousarray(inputs["L0_attn_Wq"]),
        "wk": np.ascontiguousarray(inputs["L0_attn_Wk"]),
        "wv": np.ascontiguousarray(inputs["L0_attn_Wv"]),
        "gamma": np.ascontiguousarray(inputs["L0_attn_gamma"]).reshape(1, 1),
        "w1": np.ascontiguousarray(inputs["L0_ffn_w1"]),
        "b1": np.ascontiguousarray(inputs["L0_ffn_b1"]),
        "w2": np.ascontiguousarray(inputs["L0_ffn_w2"]),
        "b2": np.ascontiguousarray(inputs["L0_ffn_b2"]),
    }



def make_in_map(inputs, core):
    return {
        "x": np.ascontiguousarray(inputs["L0_attn_x"][core]),
        "wq": np.ascontiguousarray(inputs["L0_attn_Wq"]),
        "wk": np.ascontiguousarray(inputs["L0_attn_Wk"]),
        "wv": np.ascontiguousarray(inputs["L0_attn_Wv"]),
        "gamma": np.ascontiguousarray(inputs["L0_attn_gamma"]).reshape(1, 1),
        "w1": np.ascontiguousarray(inputs["L0_ffn_w1"]),
        "b1": np.ascontiguousarray(inputs["L0_ffn_b1"]),
        "w2": np.ascontiguousarray(inputs["L0_ffn_w2"]),
        "b2": np.ascontiguousarray(inputs["L0_ffn_b2"]),
    }



def make_in_map(inputs, core):
    return {
        "x": np.ascontiguousarray(inputs["L0_attn_x"][core]),
        "wq": np.ascontiguousarray(inputs["L0_attn_Wq"]),
        "wk": np.ascontiguousarray(inputs["L0_attn_Wk"]),
        "wv": np.ascontiguousarray(inputs["L0_attn_Wv"]),
        "gamma": np.ascontiguousarray(inputs["L0_attn_gamma"]).reshape(1, 1),
        "w1": np.ascontiguousarray(inputs["L0_ffn_w1"]),
        "b1": np.ascontiguousarray(inputs["L0_ffn_b1"]),
        "w2": np.ascontiguousarray(inputs["L0_ffn_w2"]),
        "b2": np.ascontiguousarray(inputs["L0_ffn_b2"]),
    }



def make_in_map(inputs, core):
    return {
        "x": np.ascontiguousarray(inputs["L0_attn_x"][core]),
        "wq": np.ascontiguousarray(inputs["L0_attn_Wq"]),
        "wk": np.ascontiguousarray(inputs["L0_attn_Wk"]),
        "wv": np.ascontiguousarray(inputs["L0_attn_Wv"]),
        "gamma": np.ascontiguousarray(inputs["L0_attn_gamma"]).reshape(1, 1),
        "w1": np.ascontiguousarray(inputs["L0_ffn_w1"]),
        "b1": np.ascontiguousarray(inputs["L0_ffn_b1"]),
        "w2": np.ascontiguousarray(inputs["L0_ffn_w2"]),
        "b2": np.ascontiguousarray(inputs["L0_ffn_b2"]),
    }



def make_in_map(inputs, core):
    return {
        "x": np.ascontiguousarray(inputs["L0_attn_x"][core]),
        "wq": np.ascontiguousarray(inputs["L0_attn_Wq"]),
        "wk": np.ascontiguousarray(inputs["L0_attn_Wk"]),
        "wv": np.ascontiguousarray(inputs["L0_attn_Wv"]),
        "gamma": np.ascontiguousarray(inputs["L0_attn_gamma"]).reshape(1, 1),
        "w1": np.ascontiguousarray(inputs["L0_ffn_w1"]),
        "b1": np.ascontiguousarray(inputs["L0_ffn_b1"]),
        "w2": np.ascontiguousarray(inputs["L0_ffn_w2"]),
        "b2": np.ascontiguousarray(inputs["L0_ffn_b2"]),
    }



def make_in_map(inputs, core):
    return {
        "x": np.ascontiguousarray(inputs["L0_attn_x"][core]),
        "wq": np.ascontiguousarray(inputs["L0_attn_Wq"]),
        "wk": np.ascontiguousarray(inputs["L0_attn_Wk"]),
        "wv": np.ascontiguousarray(inputs["L0_attn_Wv"]),
        "gamma": np.ascontiguousarray(inputs["L0_attn_gamma"]).reshape(1, 1),
        "w1": np.ascontiguousarray(inputs["L0_ffn_w1"]),
        "b1": np.ascontiguousarray(inputs["L0_ffn_b1"]),
        "w2": np.ascontiguousarray(inputs["L0_ffn_w2"]),
        "b2": np.ascontiguousarray(inputs["L0_ffn_b2"]),
    }



def make_in_map(inputs, core):
    return {
        "x": np.ascontiguousarray(inputs["L0_attn_x"][core]),
        "wq": np.ascontiguousarray(inputs["L0_attn_Wq"]),
        "wk": np.ascontiguousarray(inputs["L0_attn_Wk"]),
        "wv": np.ascontiguousarray(inputs["L0_attn_Wv"]),
        "gamma": np.ascontiguousarray(inputs["L0_attn_gamma"]).reshape(1, 1),
        "w1": np.ascontiguousarray(inputs["L0_ffn_w1"]),
        "b1": np.ascontiguousarray(inputs["L0_ffn_b1"]),
        "w2": np.ascontiguousarray(inputs["L0_ffn_w2"]),
        "b2": np.ascontiguousarray(inputs["L0_ffn_b2"]),
    }



def make_in_map(inputs, core):
    return {
        "x": np.ascontiguousarray(inputs["L0_attn_x"][core]),
        "wq": np.ascontiguousarray(inputs["L0_attn_Wq"]),
        "wk": np.ascontiguousarray(inputs["L0_attn_Wk"]),
        "wv": np.ascontiguousarray(inputs["L0_attn_Wv"]),
        "gamma": np.ascontiguousarray(inputs["L0_attn_gamma"]).reshape(1, 1),
        "w1": np.ascontiguousarray(inputs["L0_ffn_w1"]),
        "b1": np.ascontiguousarray(inputs["L0_ffn_b1"]),
        "w2": np.ascontiguousarray(inputs["L0_ffn_w2"]),
        "b2": np.ascontiguousarray(inputs["L0_ffn_b2"]),
    }



def make_in_map(inputs, core):
    return {
        "x": np.ascontiguousarray(inputs["L0_attn_x"][core]),
        "wq": np.ascontiguousarray(inputs["L0_attn_Wq"]),
        "wk": np.ascontiguousarray(inputs["L0_attn_Wk"]),
        "wv": np.ascontiguousarray(inputs["L0_attn_Wv"]),
        "gamma": np.ascontiguousarray(inputs["L0_attn_gamma"]).reshape(1, 1),
        "w1": np.ascontiguousarray(inputs["L0_ffn_w1"]),
        "b1": np.ascontiguousarray(inputs["L0_ffn_b1"]),
        "w2": np.ascontiguousarray(inputs["L0_ffn_w2"]),
        "b2": np.ascontiguousarray(inputs["L0_ffn_b2"]),
    }



def make_in_map(inputs, core):
    return {
        "x": np.ascontiguousarray(inputs["L0_attn_x"][core]),
        "wq": np.ascontiguousarray(inputs["L0_attn_Wq"]),
        "wk": np.ascontiguousarray(inputs["L0_attn_Wk"]),
        "wv": np.ascontiguousarray(inputs["L0_attn_Wv"]),
        "gamma": np.ascontiguousarray(inputs["L0_attn_gamma"]).reshape(1, 1),
        "w1": np.ascontiguousarray(inputs["L0_ffn_w1"]),
        "b1": np.ascontiguousarray(inputs["L0_ffn_b1"]),
        "w2": np.ascontiguousarray(inputs["L0_ffn_w2"]),
        "b2": np.ascontiguousarray(inputs["L0_ffn_b2"]),
    }



def make_in_map(inputs, core):
    return {
        "x": np.ascontiguousarray(inputs["L0_attn_x"][core]),
        "wq": np.ascontiguousarray(inputs["L0_attn_Wq"]),
        "wk": np.ascontiguousarray(inputs["L0_attn_Wk"]),
        "wv": np.ascontiguousarray(inputs["L0_attn_Wv"]),
        "gamma": np.ascontiguousarray(inputs["L0_attn_gamma"]).reshape(1, 1),
        "w1": np.ascontiguousarray(inputs["L0_ffn_w1"]),
        "b1": np.ascontiguousarray(inputs["L0_ffn_b1"]),
        "w2": np.ascontiguousarray(inputs["L0_ffn_w2"]),
        "b2": np.ascontiguousarray(inputs["L0_ffn_b2"]),
    }



def make_in_map(inputs, core):
    return {
        "x": np.ascontiguousarray(inputs["L0_attn_x"][core]),
        "wq": np.ascontiguousarray(inputs["L0_attn_Wq"]),
        "wk": np.ascontiguousarray(inputs["L0_attn_Wk"]),
        "wv": np.ascontiguousarray(inputs["L0_attn_Wv"]),
        "gamma": np.ascontiguousarray(inputs["L0_attn_gamma"]).reshape(1, 1),
        "w1": np.ascontiguousarray(inputs["L0_ffn_w1"]),
        "b1": np.ascontiguousarray(inputs["L0_ffn_b1"]),
        "w2": np.ascontiguousarray(inputs["L0_ffn_w2"]),
        "b2": np.ascontiguousarray(inputs["L0_ffn_b2"]),
    }



def make_in_map(inputs, core):
    return {
        "x": np.ascontiguousarray(inputs["L0_attn_x"][core]),
        "wq": np.ascontiguousarray(inputs["L0_attn_Wq"]),
        "wk": np.ascontiguousarray(inputs["L0_attn_Wk"]),
        "wv": np.ascontiguousarray(inputs["L0_attn_Wv"]),
        "gamma": np.ascontiguousarray(inputs["L0_attn_gamma"]).reshape(1, 1),
        "w1": np.ascontiguousarray(inputs["L0_ffn_w1"]),
        "b1": np.ascontiguousarray(inputs["L0_ffn_b1"]),
        "w2": np.ascontiguousarray(inputs["L0_ffn_w2"]),
        "b2": np.ascontiguousarray(inputs["L0_ffn_b2"]),
    }



def make_in_map(inputs, core):
    return {
        "x": np.ascontiguousarray(inputs["L0_attn_x"][core]),
        "wq": np.ascontiguousarray(inputs["L0_attn_Wq"]),
        "wk": np.ascontiguousarray(inputs["L0_attn_Wk"]),
        "wv": np.ascontiguousarray(inputs["L0_attn_Wv"]),
        "gamma": np.ascontiguousarray(inputs["L0_attn_gamma"]).reshape(1, 1),
        "w1": np.ascontiguousarray(inputs["L0_ffn_w1"]),
        "b1": np.ascontiguousarray(inputs["L0_ffn_b1"]),
        "w2": np.ascontiguousarray(inputs["L0_ffn_w2"]),
        "b2": np.ascontiguousarray(inputs["L0_ffn_b2"]),
    }



def make_in_map(inputs, core):
    return {
        "x": np.ascontiguousarray(inputs["L0_attn_x"][core]),
        "wq": np.ascontiguousarray(inputs["L0_attn_Wq"]),
        "wk": np.ascontiguousarray(inputs["L0_attn_Wk"]),
        "wv": np.ascontiguousarray(inputs["L0_attn_Wv"]),
        "gamma": np.ascontiguousarray(inputs["L0_attn_gamma"]).reshape(1, 1),
        "w1": np.ascontiguousarray(inputs["L0_ffn_w1"]),
        "b1": np.ascontiguousarray(inputs["L0_ffn_b1"]),
        "w2": np.ascontiguousarray(inputs["L0_ffn_w2"]),
        "b2": np.ascontiguousarray(inputs["L0_ffn_b2"]),
    }



def make_in_map(inputs, core):
    return {
        "x": np.ascontiguousarray(inputs["L0_attn_x"][core]),
        "wq": np.ascontiguousarray(inputs["L0_attn_Wq"]),
        "wk": np.ascontiguousarray(inputs["L0_attn_Wk"]),
        "wv": np.ascontiguousarray(inputs["L0_attn_Wv"]),
        "gamma": np.ascontiguousarray(inputs["L0_attn_gamma"]).reshape(1, 1),
        "w1": np.ascontiguousarray(inputs["L0_ffn_w1"]),
        "b1": np.ascontiguousarray(inputs["L0_ffn_b1"]),
        "w2": np.ascontiguousarray(inputs["L0_ffn_w2"]),
        "b2": np.ascontiguousarray(inputs["L0_ffn_b2"]),
    }



def make_in_map(inputs, core):
    return {
        "x": np.ascontiguousarray(inputs["L0_attn_x"][core]),
        "wq": np.ascontiguousarray(inputs["L0_attn_Wq"]),
        "wk": np.ascontiguousarray(inputs["L0_attn_Wk"]),
        "wv": np.ascontiguousarray(inputs["L0_attn_Wv"]),
        "gamma": np.ascontiguousarray(inputs["L0_attn_gamma"]).reshape(1, 1),
        "w1": np.ascontiguousarray(inputs["L0_ffn_w1"]),
        "b1": np.ascontiguousarray(inputs["L0_ffn_b1"]),
        "w2": np.ascontiguousarray(inputs["L0_ffn_w2"]),
        "b2": np.ascontiguousarray(inputs["L0_ffn_b2"]),
    }



def make_in_map(inputs, core):
    return {
        "x": np.ascontiguousarray(inputs["L0_attn_x"][core]),
        "wq": np.ascontiguousarray(inputs["L0_attn_Wq"]),
        "wk": np.ascontiguousarray(inputs["L0_attn_Wk"]),
        "wv": np.ascontiguousarray(inputs["L0_attn_Wv"]),
        "gamma": np.ascontiguousarray(inputs["L0_attn_gamma"]).reshape(1, 1),
        "w1": np.ascontiguousarray(inputs["L0_ffn_w1"]),
        "b1": np.ascontiguousarray(inputs["L0_ffn_b1"]),
        "w2": np.ascontiguousarray(inputs["L0_ffn_w2"]),
        "b2": np.ascontiguousarray(inputs["L0_ffn_b2"]),
    }



def make_in_map(inputs, core):
    return {
        "x": np.ascontiguousarray(inputs["L0_attn_x"][core]),
        "wq": np.ascontiguousarray(inputs["L0_attn_Wq"]),
        "wk": np.ascontiguousarray(inputs["L0_attn_Wk"]),
        "wv": np.ascontiguousarray(inputs["L0_attn_Wv"]),
        "gamma": np.ascontiguousarray(inputs["L0_attn_gamma"]).reshape(1, 1),
        "w1": np.ascontiguousarray(inputs["L0_ffn_w1"]),
        "b1": np.ascontiguousarray(inputs["L0_ffn_b1"]),
        "w2": np.ascontiguousarray(inputs["L0_ffn_w2"]),
        "b2": np.ascontiguousarray(inputs["L0_ffn_b2"]),
    }




def make_in_map(inputs, core):
    f32 = np.float32
    return {
        "x": np.ascontiguousarray(np.asarray(inputs["L0_attn_x"], dtype=f32)[core]),
        "wq": np.ascontiguousarray(np.asarray(inputs["L0_attn_Wq"], dtype=f32)),
        "wk": np.ascontiguousarray(np.asarray(inputs["L0_attn_Wk"], dtype=f32)),
        "wv": np.ascontiguousarray(np.asarray(inputs["L0_attn_Wv"], dtype=f32)),
        "gamma": np.ascontiguousarray(np.asarray(inputs["L0_attn_gamma"], dtype=f32)).reshape(1, 1),
        "w1": np.ascontiguousarray(np.asarray(inputs["L0_ffn_w1"], dtype=f32)),
        "b1": np.ascontiguousarray(np.asarray(inputs["L0_ffn_b1"], dtype=f32)),
        "w2": np.ascontiguousarray(np.asarray(inputs["L0_ffn_w2"], dtype=f32)),
        "b2": np.ascontiguousarray(np.asarray(inputs["L0_ffn_b2"], dtype=f32)),
    }


def kernel(**inputs):
    from concourse.bass_utils import run_bass_kernel_spmd
    nc = build_nc()
    in_maps = [make_in_map(inputs, core) for core in range(8)]
    res = run_bass_kernel_spmd(nc, in_maps, core_ids=list(range(8)))
    return np.stack([np.asarray(res.results[i]["y"], dtype=np.float32)
                     for i in range(8)], axis=0)


# version 33
# speedup vs baseline: 1.1096x; 1.0298x over previous
import contextlib
import numpy as np
import concourse.bass as bass
import concourse.mybir as mybir

F32 = mybir.dt.float32
BF16 = mybir.dt.bfloat16
U8 = mybir.dt.uint8
AX = mybir.AxisListType.X
OP = mybir.AluOpType
AF = mybir.ActivationFunctionType

S, C, HD = 2048, 1024, 4096
NS, NC, NH = 16, 8, 32
W2_RES = 16
EPS = 1e-5

ENGS = ("tensor", "vector", "scalar", "gpsimd", "sync")
STRICT = True
STRICT_ENGS = {"vector", "scalar"}
ENG_SEM = {"tensor": "T", "vector": "V", "scalar": "Sc", "gpsimd": "G"}


class Plan:
    def __init__(self):
        self.ops = {e: [] for e in ENGS}
        self.cnt = {}
        self.strict_now = set()

    def emit(self, eng, fn, waits=(), inc=None, inc_by=1):
        waits = [(s, v) for (s, v) in waits if v > 0]
        if eng in ENG_SEM:
            es = ENG_SEM[eng]
            if (eng == "gpsimd" or (STRICT and eng in STRICT_ENGS)
                    or eng in self.strict_now) and self.c(es) > 0:
                waits.append((es, self.c(es)))
            if inc is None:
                inc = es
        self.ops[eng].append((fn, waits, inc, inc_by))
        if inc is not None:
            self.cnt[inc] = self.cnt.get(inc, 0) + inc_by

    def c(self, sem):
        return self.cnt.get(sem, 0)


def build_nc():
    nc = bass.Bass("TRN2", target_bir_lowering=False)
    x = nc.dram_tensor("x", [S, C], F32, kind="ExternalInput").ap()
    wq = nc.dram_tensor("wq", [C, C], F32, kind="ExternalInput").ap()
    wk = nc.dram_tensor("wk", [C, C], F32, kind="ExternalInput").ap()
    wv = nc.dram_tensor("wv", [C, C], F32, kind="ExternalInput").ap()
    gamma = nc.dram_tensor("gamma", [1, 1], F32, kind="ExternalInput").ap()
    w1 = nc.dram_tensor("w1", [C, HD], F32, kind="ExternalInput").ap()
    b1 = nc.dram_tensor("b1", [HD], F32, kind="ExternalInput").ap()
    w2 = nc.dram_tensor("w2", [HD, C], F32, kind="ExternalInput").ap()
    b2 = nc.dram_tensor("b2", [C], F32, kind="ExternalInput").ap()
    y = nc.dram_tensor("y", [S, C], F32, kind="ExternalOutput").ap()
    w1bf = nc.dram_tensor("w1bf", [C, HD], BF16, kind="Internal").ap()
    w2bf = nc.dram_tensor("w2bf", [HD, C], BF16, kind="Internal").ap()

    ctx = contextlib.ExitStack()
    sb = lambda n, s, d: ctx.enter_context(nc.sbuf_tensor(n, s, d))
    regA = sb("regA", [128, 65536], U8)
    regFG = sb("regFG", [128, 65536], U8)
    regH = sb("regH", [128, 32768], U8)
    regE = sb("regE", [128, 20480], U8)
    rA_f = regA.bitcast(F32)
    rA_b = regA.bitcast(BF16)
    rFG = regFG.bitcast(BF16)
    rH = regH.bitcast(BF16)
    rE = regE.bitcast(BF16)

    resid = lambda ti: rA_f[:, ti * 1024:(ti + 1) * 1024]
    resid2 = lambda ti, et: rA_f[:, ti * 1024 + et * 512: ti * 1024 + et * 512 + 512]

    _wtg_base = (12288, 8192)
    wtg_all2 = [rA_b[:, b:b + 4096].rearrange("p (ci d) -> p ci d", ci=8)
                for b in _wtg_base]
    WTg2 = lambda bi, ci, off, n: rA_b[:, _wtg_base[bi] + ci * 512 + off:
                                       _wtg_base[bi] + ci * 512 + off + n]
    xfm_all = rA_b[:, 16384:32768].rearrange("p (ci s) -> p ci s", ci=8)
    xfm = lambda ci, s0, n: rA_b[:, 16384 + ci * 2048 + s0: 16384 + ci * 2048 + s0 + n]
    fsl = lambda d, s0, n: rFG[:, d * 2048 + s0: d * 2048 + s0 + n]
    gsl = lambda d, s0, n: rFG[:, 16384 + d * 2048 + s0: 16384 + d * 2048 + s0 + n]
    w1sl = lambda ci, h0, n: rFG[:, ci * 4096 + h0: ci * 4096 + h0 + n]
    hsl = lambda si, c0, n: rH[:, si * 1024 + c0: si * 1024 + c0 + n]
    w2res = lambda hc, e0, n: rH[:, hc * 1024 + e0: hc * 1024 + e0 + n]
    ebT = [rE[:, 0:2048], rE[:, 2048:4096]]
    ebsm2 = [rE[:, 4096:6144], rE[:, 6144:8192]]
    x2fm2 = [rE[:, 0:2048], rE[:, 6144:8192]]
    x2f_all2 = [rE[:, 0:2048].rearrange("p (ci s) -> p ci s", ci=8),
                rE[:, 6144:8192].rearrange("p (ci s) -> p ci s", ci=8)]
    hid = [rE[:, 2048:2304], rE[:, 2304:2560]]
    w2s = [rE[:, 3072:4096], rE[:, 4096:5120], rE[:, 8192:9216], rE[:, 9216:10240]]

    ident = sb("ident", [128, 128], BF16)
    identf = sb("identf", [128, 128], F32)
    ones1 = sb("ones1", [1, 128], BF16)
    b2rowb = sb("b2rowb", [1, 1024], BF16)
    b1sb = sb("b1sb", [128, 32], F32)
    gbc = sb("gbc", [128, 1], F32)
    cdb = sb("cdb", [128, 4], F32)
    stats = sb("stats", [128, 64], F32)
    xch = sb("xch", [128, 8192], BF16)
    xc = [xch[:, k * 1024:(k + 1) * 1024] for k in range(8)]
    wbw = xch
    varsink = sb("varsink", [128, 1024], F32)

    CONSTS = [EPS, -0.5, 1.0 / 1024]
    for i, cv in enumerate(CONSTS):
        nc.const_aps.register(F32, cv, cdb[:, i:i + 1])

    NM = lambda par, st: stats[:, par * 4 + st: par * 4 + st + 1]
    NMALL = lambda par: stats[:, par * 4: par * 4 + 4]
    NEGMAX = lambda par: stats[:, 8 + par: 9 + par]
    ZP = lambda par, st: stats[:, 10 + par * 4 + st: 11 + par * 4 + st]
    ZPALL = lambda par: stats[:, 10 + par * 4: 14 + par * 4]
    ZCOL = stats[:, 18:19]
    ZICOL = stats[:, 19:20]
    ZG2 = [stats[:, 20:21], stats[:, 29:30]]
    NSUM = stats[:, 24:25]
    NEGM = stats[:, 25:26]
    VSUM = stats[:, 26:27]
    LNV = stats[:, 27:28]
    RSTD = stats[:, 28:29]
    KCOL = stats[:, 29:30]

    ps = ctx.enter_context(nc.psum_tensor("ps", [128, 4096], F32))
    ps_bf = ps.bitcast(BF16)
    bank = lambda b, off, n: ps[:, b * 512 + off: b * 512 + off + n]
    bankb = lambda b, off, n: ps_bf[:, b * 1024 + off: b * 1024 + off + n]
    bankb3 = lambda b: ps_bf[:, b * 1024:(b + 1) * 1024].rearrange(
        "p (ci s) -> p ci s", ci=8)

    sems = {}
    for name in ("T", "V", "Sc", "G", "LXC0", "LXC1", "LXC2", "LXC3",
                 "LWB0", "LWB1", "LWB2", "LWB3",
                 "LTE0", "LTE1", "LTC0", "LTC1",
                 "LW2S0", "LW2S1", "LW2S2", "LW2S3", "LXA", "LXB",
                 "LW1S0", "LW1S1", "LW1S2", "LW1S3",
                 "LW1S4", "LW1S5", "LW1S6", "LW1S7",
                 "LW2R0", "LW2R1", "LW2R2", "LW2R3", "LMISC", "LMISCG", "STY",
                 "LCW", "LXB2"):
        sems[name] = ctx.enter_context(nc.semaphore(name))

    P = Plan()
    pfree = {b: [] for b in range(8)}

    def v_init(v):
        v.memset(ident[:], 1.0)
        v.memset(identf[:], 1.0)
        v.memset(ones1[:], 1.0)
        ins = None
        for i, cv in enumerate(CONSTS):
            ins = v.memset(cdb[:, i:i + 1], cv)
        return ins
    P.emit("vector", v_init, inc="V")
    V_INIT = P.c("V")

    P.emit("gpsimd", lambda g: g.affine_select(
        out=ident[:], in_=ident[:], pattern=[[-1, 128]],
        compare_op=OP.is_equal, fill=0.0, base=0, channel_multiplier=1),
        waits=[("V", V_INIT)], inc="G")
    P.emit("gpsimd", lambda g: g.affine_select(
        out=identf[:], in_=identf[:], pattern=[[-1, 128]],
        compare_op=OP.is_equal, fill=0.0, base=0, channel_multiplier=1),
        inc="G")
    G_INIT = P.c("G")

    P.emit("sync", lambda e: e.dma_start(
        out=rA_f[:, 0:4 * 1024].rearrange("p (a b) -> p a b", a=4),
        in_=bass.AP(x.tensor, 0, [[1024, 128], [128 * 1024, 4], [1, 1024]])),
        inc="LXA", inc_by=16)
    P.emit("sync", lambda e: e.dma_start(
        out=varsink[0:32, 0:128], in_=bass.AP(b1.tensor, 0, [[128, 32], [1, 128]])),
        inc="LMISC", inc_by=16)
    P.emit("sync", lambda e: e.dma_start(
        out=gbc[:], in_=bass.AP(gamma.tensor, 0, [[0, 128], [1, 1]])),
        inc="LMISC", inc_by=16)
    P.emit("vector", lambda e: e.tensor_scalar_mul(gbc[:], gbc[:], 0.5),
           waits=[("LMISC", 32)], inc="V")
    P.emit("gpsimd", lambda e: e.dma_start(
        out=b2rowb[0:1, :], in_=bass.AP(b2.tensor, 0, [[1024, 1], [1, 1024]])),
        inc="LMISCG", inc_by=16)
    LMISC_ALL = P.c("LMISC")

    P.emit("tensor", lambda e: e.transpose(bank(6, 0, 32), varsink[0:32, 0:128],
                                           identf[0:32, 0:32]),
           waits=[("LMISC", 32), ("G", G_INIT)], inc="T")
    P.emit("vector", lambda e: e.tensor_copy(b1sb[:], bank(6, 0, 32)),
           waits=[("T", P.c("T"))], inc="V")
    pfree[6] = [("V", P.c("V"))]

    xT_done, xT_evac = {}, {}
    for pr in range(NS // 2):
        sl = pr % 4
        waits = [("T", xT_done[2 * pr - 7])] if pr >= 4 else []
        P.emit("gpsimd",
               (lambda pr_, sl_: lambda e: e.dma_start(
                   out=xch[:, sl_ * 2048:(sl_ + 1) * 2048].rearrange(
                       "p (a b) -> p a b", a=2),
                   in_=bass.AP(x.tensor, pr_ * 2 * 128 * 1024,
                               [[1024, 128], [128 * 1024, 2], [1, 1024]])))(pr, sl),
               waits=waits, inc=f"LXC{sl}", inc_by=16)
        for half in range(2):
            ti = 2 * pr + half
            bk = 4 + (ti % 2)
            twaits = [(f"LXC{sl}", 16 * (pr // 4 + 1)), ("G", G_INIT)] + pfree[bk]
            def tfn(e, ti=ti, bk=bk, sl=sl, half=half):
                ins = None
                for ci in range(NC):
                    ins = e.transpose(bankb(bk, ci * 128, 128),
                                      xc[sl * 2 + half][:, ci * 128:(ci + 1) * 128],
                                      ident[:])
                return ins
            P.emit("tensor", tfn, waits=twaits, inc="T")
            xT_done[ti] = P.c("T")
            P.emit("vector",
                   (lambda ti_, bk_: lambda e: e.tensor_copy(
                       xfm_all[:, :, ti_ * 128:(ti_ + 1) * 128], bankb3(bk_)))(ti, bk),
                   waits=[("T", xT_done[ti])], inc="V")
            xT_evac[ti] = P.c("V")
            pfree[bk] = [("V", xT_evac[ti])]
    XFM_ALL_V = P.c("V")

    wdram = {0: wq, 1: wk, 2: wv}
    rot = [0]

    def qkv_mm_tilegroup(lhsT_sl, rhs_sl, out_fn, N, extra_waits, evac_eng):
        bk = rot[0] % 4
        rot[0] += 1
        waits = list(extra_waits) + pfree[bk]
        def tfn(e, bk=bk, lhsT_sl=lhsT_sl, rhs_sl=rhs_sl, N=N):
            ins = None
            for ci in range(NC):
                ins = e.matmul(bank(bk, 0, N), lhsT_sl(ci), rhs_sl(ci),
                               start=(ci == 0), stop=(ci == NC - 1))
            return ins
        P.emit("tensor", tfn, waits=waits, inc="T")
        tcnt = P.c("T")
        sem = "Sc" if evac_eng == "scalar" else "V"
        def efn(e, bk=bk, out_fn=out_fn, N=N):
            out = out_fn()
            return (e.copy(out, bank(bk, 0, N)) if evac_eng == "scalar"
                    else e.tensor_copy(out, bank(bk, 0, N)))
        P.emit(evac_eng, efn, waits=[("T", tcnt)], inc=sem)
        pfree[bk] = [(sem, P.c(sem))]

    wT_done = {}
    lwb_cnt = [0, 0, 0, 0]
    wrow_idx_box = [0]
    wt_ready = {}

    def emit_w_lt(k):
        wi, grp = divmod(k, 2)
        bi = k % 2
        for rp in range(2):
            wrow_idx = wrow_idx_box[0]
            wp = wrow_idx // 2
            sl = wp % 4
            dj0 = grp * 4 + rp * 2
            if wp >= 4:
                waits = [("T", wT_done[wrow_idx - 7])]
            else:
                waits = [("T", xT_done[9 + 2 * wp])]
            P.emit("gpsimd",
                   (lambda wi_, dj_, sl_: lambda e: e.dma_start(
                       out=wbw[:, sl_ * 2048:(sl_ + 1) * 2048].rearrange(
                           "p (a b) -> p a b", a=2),
                       in_=bass.AP(wdram[wi_].tensor, dj_ * 128 * 1024,
                                   [[1024, 128], [128 * 1024, 2], [1, 1024]])))(
                       wi, dj0, sl),
                   waits=waits, inc=f"LWB{sl}", inc_by=16)
            lwb_cnt[sl] += 1
            for half in range(2):
                r = rp * 2 + half
                bk = 4 + (wrow_idx_box[0] % 4)
                twaits = [(f"LWB{sl}", 16 * lwb_cnt[sl]), ("G", G_INIT)] + pfree[bk]
                if k >= 2:
                    twaits.append(("T", wt_mm_done[k - 2]))
                def tfn(e, sl=sl, bk=bk, half=half):
                    base = sl * 2048 + half * 1024
                    ins = None
                    for ci in range(NC):
                        ins = e.transpose(
                            bankb(bk, ci * 128, 128),
                            wbw[:, base + ci * 128: base + (ci + 1) * 128],
                            ident[:])
                    return ins
                P.emit("tensor", tfn, waits=twaits, inc="T")
                wT_done[wrow_idx_box[0]] = P.c("T")
                P.emit("vector",
                       (lambda r_, bk_, bi_: lambda e: e.tensor_copy(
                           wtg_all2[bi_][:, :, r_ * 128:(r_ + 1) * 128],
                           bankb3(bk_)))(r, bk, bi),
                       waits=[("T", wT_done[wrow_idx_box[0]])], inc="V")
                pfree[bk] = [("V", P.c("V"))]
                wrow_idx_box[0] += 1
        wt_ready[k] = P.c("V")

    wt_mm_done = {}
    emit_w_lt(0)
    for k in range(6):
        wi, grp = divmod(k, 2)
        bi = k % 2
        if wi < 2:
            outsl = fsl if wi == 0 else gsl
            tiles = [(r, st) for r in range(4) for st in range(4)]
            for idx, (r, st) in enumerate(tiles):
                if idx == 8 and k + 1 < 6:
                    emit_w_lt(k + 1)
                dj = grp * 4 + r
                qkv_mm_tilegroup(
                    (lambda r_, bi_: lambda ci: WTg2(bi_, ci, r_ * 128, 128))(r, bi),
                    (lambda st_: lambda ci: xfm(ci, st_ * 512, 512))(st),
                    (lambda dj_, st_, o_=outsl: lambda: o_(dj_, st_ * 512, 512))(dj, st),
                    512, [("V", max(wt_ready[k], XFM_ALL_V))], "scalar")
        else:
            for si in range(NS):
                if si == 8 and k + 1 < 6:
                    emit_w_lt(k + 1)
                qkv_mm_tilegroup(
                    (lambda si_, bi_: lambda ci: xfm(ci, si_ * 128, 128))(si, bi),
                    (lambda bi_: lambda ci: WTg2(bi_, ci, 0, 512))(bi),
                    (lambda si_, grp_: lambda: hsl(si_, grp_ * 512, 512))(si, grp),
                    512, [("V", max(wt_ready[k], XFM_ALL_V))], "vector")
        wt_mm_done[k] = P.c("T")

    A_END_T = P.c("T")
    A_END_V = P.c("V")
    A_END_S = P.c("Sc")

    for i in range(8):
        P.emit("gpsimd",
               (lambda i_: lambda e: e.dma_start(
                   out=w1bf[i_ * 128:(i_ + 1) * 128, :],
                   in_=w1[i_ * 128:(i_ + 1) * 128, :]))(i),
               inc="LCW", inc_by=16)
    for i in range(16):
        P.emit("gpsimd",
               (lambda i_: lambda e: e.dma_start(
                   out=w2bf[i_ * 256:(i_ + 1) * 256, :],
                   in_=w2[i_ * 256:(i_ + 1) * 256, :]))(i),
               inc="LCW", inc_by=16)
    LCW_ALL = 384

    P.emit("scalar", lambda e: e.dma_start(
        out=rA_f[:, 4 * 1024:10 * 1024].rearrange("p (a b) -> p a b", a=6),
        in_=bass.AP(x.tensor, 4 * 128 * 1024, [[1024, 128], [128 * 1024, 6], [1, 1024]])),
        waits=[("T", A_END_T)], inc="LXB", inc_by=16)

    sc_done, exp_done, trT_done = {}, {}, {}
    ebsm_evac, ebsm_evac_h1, av_done, oevac_done = {}, {}, {}, {}
    nm_done = {}

    def emit_score_tile(ti, st, extra_waits):
        par = ti % 2
        bk = par * 4 + st
        def tfn(e, ti=ti, st=st, bk=bk):
            ins = None
            for d in range(NC):
                ins = e.matmul(bank(bk, 0, 512), gsl(d, ti * 128, 128),
                               fsl(d, st * 512, 512), start=(d == 0), stop=(d == NC - 1))
            return ins
        P.emit("tensor", tfn, waits=extra_waits, inc="T")
        sc_done[(ti, st)] = P.c("T")
        P.emit("vector",
               (lambda par_, st_, bk_: lambda e: e.reduce_max(
                   NM(par_, st_), bank(bk_, 0, 512), axis=AX, negate=True))(par, st, bk),
               waits=[("T", sc_done[(ti, st)])], inc="V")

    def emit_negmax_combine(ti):
        par = ti % 2
        P.emit("vector",
               (lambda par_: lambda e: e.tensor_reduce(
                   NEGMAX(par_), NMALL(par_), axis=AX, op=OP.min))(par), inc="V")
        nm_done[ti] = P.c("V")

    def emit_softmax(ti):
        par = ti % 2
        g0 = par * 4
        waits = [("V", nm_done[ti])]
        if ti >= 2:
            waits.append(ebsm_evac[ti - 2])
        for st in range(4):
            P.emit("scalar",
                   (lambda st_, g0_, par_: lambda e: e.activation(
                       out=ebT[par_][:, st_ * 512:(st_ + 1) * 512],
                       in_=bank(g0_ + st_, 0, 512), func=AF.Exp,
                       bias=NEGMAX(par_), scale=1.0,
                       accum_out=ZP(par_, st_)))(st, g0, par),
                   waits=(waits if st == 0 else []), inc="Sc")
        exp_done[ti] = P.c("Sc")

    lte_cnt = [0, 0]
    tail_trT = {}

    def emit_transposes(ti):
        par = ti % 2
        eb = ebsm2[par]
        if 2 <= ti < NS - 2:
            waits = [("Sc", exp_done[ti])]
            if ti >= 2:
                waits.append(("T", av_done.get(ti - 2, 0)))
                waits.append((f"LTE{par}", lte_cnt[par]))
            P.emit("sync",
                   (lambda par_, eb_=eb: lambda e: e.dma_start_transpose(
                       eb_[:].rearrange("p (k c) -> p k c", k=16),
                       ebT[par_][:].rearrange("p (k c) -> p k c", k=16)))(par),
                   waits=waits, inc=f"LTE{par}", inc_by=16)
            lte_cnt[par] += 16
            ebsm_evac[ti] = (f"LTE{par}", lte_cnt[par])
        else:
            g0 = par * 4
            def tfn(e, par=par, g0=g0):
                ins = None
                for k in range(NS):
                    ins = e.transpose(
                        ps_bf[:, g0 * 1024 + k * 128: g0 * 1024 + (k + 1) * 128],
                        ebT[par][:, k * 128:(k + 1) * 128], ident[:])
                return ins
            P.emit("tensor", tfn,
                   waits=[("Sc", exp_done[ti]), (f"LTE{par}", lte_cnt[par])],
                   inc="T")
            trT = P.c("T")
            tail_trT[ti] = trT
            P.emit("vector",
                   (lambda g0_, eb_=eb: lambda e: e.tensor_copy(
                       eb_[:, 0:2048], ps_bf[:, g0_ * 1024: g0_ * 1024 + 2048]))(g0),
                   waits=[("T", trT), ("T", av_done.get(ti - 2, 0))], inc="V")
            ebsm_evac[ti] = ("V", P.c("V"))
        P.emit("vector",
               (lambda par_: lambda e: e.tensor_reduce(
                   ZCOL, ZPALL(par_), axis=AX, op=OP.add))(par),
               waits=[("Sc", exp_done[ti]), ("LMISC", LMISC_ALL)], inc="V")
        P.emit("vector", lambda e: e.reciprocal(ZICOL, ZCOL), inc="V")
        P.emit("vector",
               (lambda par_: lambda e: e.tensor_mul(ZG2[par_], ZICOL, gbc[:]))(par),
               inc="V")

    def emit_av(ti):
        g0 = (ti % 2) * 4
        eb = ebsm2[ti % 2]
        for ct in range(2):
            bk = g0 + 2 + ct
            def tfn_a(e, ct=ct, bk=bk, eb=eb):
                ins = None
                for k in range(NS):
                    ins = e.matmul(bank(bk, 0, 512), eb[:, k * 128:(k + 1) * 128],
                                   hsl(k, ct * 512, 512),
                                   start=(k == 0), stop=(k == NS - 1))
                return ins
            P.emit("tensor", tfn_a, waits=[ebsm_evac[ti]], inc="T")
        av_done[ti] = P.c("T")

    oevac_h1 = {}

    def emit_oevac(ti):
        g0 = (ti % 2) * 4
        xw = (("LXA", 16) if ti < 4 else
              ("LXB", 16) if ti < 10 else ("LXB2", 16))
        for ct in range(2):
            P.emit("vector",
                   (lambda ti_, ct_, bk_: lambda e: e.scalar_tensor_tensor(
                       out=resid2(ti_, ct_), in0=bank(bk_, 0, 512), scalar=ZG2[ti_ % 2],
                       in1=resid2(ti_, ct_), op0=OP.mult, op1=OP.add))(ti, ct, g0 + 2 + ct),
                   waits=[("T", av_done[ti]), xw], inc="V")
            if ct == 0:
                oevac_h1[ti] = P.c("V")
        oevac_done[ti] = P.c("V")

    R1COL = lambda ti: stats[:, 32 + ti:33 + ti]
    M1COL = lambda ti: stats[:, 48 + ti:49 + ti]
    BNS = stats[:, 21:22]
    BVS2 = [stats[:, 23:24], stats[:, 27:28]]
    BLL2 = [stats[:, 24:25], stats[:, 28:29]]
    K1TMP = [stats[:, 25:26], stats[:, 26:27]]

    bstats_sc = {}

    def emit_ln1_bstats(ti):
        P.emit("vector",
               (lambda t_: lambda e: e.reduce_sum(BNS, resid(t_), axis=AX,
                                                  negate=True))(ti), inc="V")
        P.emit("vector",
               (lambda t_: lambda e: e.tensor_scalar_mul(
                   M1COL(t_), BNS, 1.0 / 1024))(ti), inc="V")
        vwaits = []
        if ti >= 2:
            vwaits.append(("Sc", bstats_sc[ti - 2]))
        P.emit("vector",
               (lambda t_: lambda e: e.scalar_tensor_tensor(
                   out=varsink[:], in0=resid(t_), scalar=M1COL(t_), in1=resid(t_),
                   op0=OP.add, op1=OP.mult, accum_out=BVS2[t_ % 2]))(ti),
               waits=vwaits, inc="V")
        vv = P.c("V")
        P.emit("scalar",
               (lambda t_: lambda e: e.activation(
                   out=BLL2[t_ % 2], in_=BVS2[t_ % 2], func=AF.Ln,
                   scale=1.0 / 1024, bias=EPS))(ti),
               waits=[("V", vv)], inc="Sc")
        P.emit("scalar",
               (lambda t_: lambda e: e.activation(out=R1COL(t_), in_=BLL2[t_ % 2],
                                                  func=AF.Exp, scale=-0.5))(ti),
               inc="Sc")
        bstats_sc[ti] = P.c("Sc")

    LNS = lambda b: stats[:, b:b + 1]
    LNM = lambda b: stats[:, b + 1:b + 2]
    LNV4 = lambda b: stats[:, b + 2:b + 3]
    LNL = lambda b: stats[:, b + 3:b + 4]
    LNR = lambda b: stats[:, b + 4:b + 5]
    LNK = lambda b: stats[:, b + 5:b + 6]
    vcast_n = {}

    def emit_lnb_v1(chains):
        for b, cch in chains:
            P.emit("vector",
                   (lambda c_, b_: lambda e: e.reduce_sum(
                       LNS(b_), resid(c_), axis=AX, negate=True))(cch, b), inc="V")
        for b, cch in chains:
            P.emit("vector",
                   (lambda b_: lambda e: e.tensor_scalar_mul(
                       LNM(b_), LNS(b_), 1.0 / 1024))(b), inc="V")
        for b, cch in chains:
            P.emit("vector",
                   (lambda c_, b_: lambda e: e.scalar_tensor_tensor(
                       out=varsink[:], in0=resid(c_), scalar=LNM(b_), in1=resid(c_),
                       op0=OP.add, op1=OP.mult, accum_out=LNV4(b_)))(cch, b), inc="V")
        return P.c("V")

    def emit_lnb_sc(chains, vwait):
        for b, cch in chains:
            P.emit("scalar",
                   (lambda b_: lambda e: e.activation(
                       out=LNL(b_), in_=LNV4(b_), func=AF.Ln,
                       scale=1.0 / 1024, bias=EPS))(b),
                   waits=[("V", vwait)], inc="Sc")
        for b, cch in chains:
            P.emit("scalar",
                   (lambda b_: lambda e: e.activation(
                       out=LNR(b_), in_=LNL(b_), func=AF.Exp, scale=-0.5))(b),
                   inc="Sc")
        return P.c("Sc")

    def emit_lnb_v2(chains, scwait, cast_e8=None, store=()):
        for b, cch in chains:
            P.emit("vector",
                   (lambda b_: lambda e: e.tensor_mul(LNK(b_), LNM(b_), LNR(b_)))(b),
                   waits=[("Sc", scwait)], inc="V")
        if cast_e8 is not None:
            slot = (cast_e8 % 2) * 2
            for sj, (b, cch) in enumerate(chains[-2:]):
                P.emit("vector",
                       (lambda c_, b_, sl_=slot + sj: lambda e: e.tensor_scalar(
                           out=xc[sl_][:], in0=resid(c_), scalar1=LNR(b_),
                           scalar2=LNK(b_), op0=OP.mult, op1=OP.add))(cch, b),
                       inc="V")
            vcast_n[cast_e8] = P.c("V")
        for b, cch in chains:
            P.emit("vector",
                   (lambda c_, b_: lambda e: e.tensor_scalar(
                       out=resid(c_), in0=resid(c_), scalar1=LNR(b_), scalar2=LNK(b_),
                       op0=OP.mult, op1=OP.add))(cch, b), inc="V")
        for b, cch in store:
            vfin = P.c("V")
            P.emit("sync",
                   (lambda c_: lambda e: e.dma_start(
                       out=y[c_ * 128:(c_ + 1) * 128, :], in_=resid(c_)))(cch),
                   waits=[("V", vfin)], inc="STY", inc_by=16)

    vcast_n_early = {}

    def emit_ln1_final_c0():
        scw = bstats_sc[1]
        P.emit("vector", lambda e: e.engine_nop(), waits=[("T", A_END_T)], inc="V")
        for sj, cch in enumerate((0, 1)):
            P.emit("vector",
                   (lambda c_, sj_: lambda e: e.tensor_mul(
                       K1TMP[sj_], M1COL(c_), R1COL(c_)))(cch, sj),
                   waits=[("Sc", scw)], inc="V")
        for sj, cch in enumerate((0, 1)):
            P.emit("vector",
                   (lambda c_, sl_, sj_: lambda e: e.tensor_scalar(
                       out=xc[sl_][:], in0=resid(c_), scalar1=R1COL(c_),
                       scalar2=K1TMP[sj_], op0=OP.mult, op1=OP.add))(cch, sj, sj),
                   inc="V")
        vcast_n_early[0] = P.c("V")
        for sj, cch in enumerate((0, 1)):
            P.emit("vector",
                   (lambda c_, sj_: lambda e: e.tensor_scalar(
                       out=resid(c_), in0=resid(c_), scalar1=R1COL(c_),
                       scalar2=K1TMP[sj_], op0=OP.mult, op1=OP.add))(cch, sj),
                   inc="V")

    for ti in (0, 1):
        for st in range(4):
            bk = (ti % 2) * 4 + st
            emit_score_tile(ti, st, [("V", A_END_V), ("Sc", A_END_S)] + pfree[bk])
        emit_negmax_combine(ti)

    for tj in range(NS):
        emit_softmax(tj)
        emit_transposes(tj)
        if tj >= 1:
            emit_ln1_bstats(tj - 1)
        emit_av(tj)
        emit_oevac(tj)
        if tj == 4:
            P.emit("gpsimd", lambda e: e.dma_start(
                out=rA_f[:, 10 * 1024:16 * 1024].rearrange("p (a b) -> p a b", a=6),
                in_=bass.AP(x.tensor, 10 * 128 * 1024,
                            [[1024, 128], [128 * 1024, 6], [1, 1024]])),
                waits=[("T", P.c("T"))], inc="LXB2", inc_by=16)
        if tj + 2 < NS:
            for st in (0, 1):
                ws = [("Sc", exp_done[tj])]
                if tj < 2:
                    ws.append(ebsm_evac[tj])
                emit_score_tile(tj + 2, st, ws)
            emit_score_tile(tj + 2, 2, [("V", oevac_h1[tj])])
            emit_score_tile(tj + 2, 3, [("V", oevac_done[tj])])
            emit_negmax_combine(tj + 2)
        if tj == NS - 1:
            emit_ln1_bstats(tj)
        if tj == 4:
            emit_ln1_final_c0()
        if tj == 13:
            t_after_s15 = P.c("T")
            w1_all = rFG[:, 0:32768].rearrange("p (ei h) -> p ei h", ei=8)
            w2r3 = lambda i: rH[:, i * 4096:(i + 1) * 4096].rearrange(
                "p (a b) -> p a b", a=4)
            def w1dma(k):
                P.emit("gpsimd",
                       (lambda k_: lambda e: e.dma_start(
                           out=w1_all[:, :, k_ * 512:(k_ + 1) * 512],
                           in_=bass.AP(w1bf.tensor, k_ * 512,
                                       [[4096, 128], [128 * 4096, 8], [1, 512]])))(k),
                       waits=[("T", t_after_s15), ("LCW", LCW_ALL)],
                       inc=f"LW1S{k}", inc_by=16)
            def w2rdma(i):
                P.emit("gpsimd",
                       (lambda i_: lambda e: e.dma_start(
                           out=w2r3(i_),
                           in_=bass.AP(w2bf.tensor, i_ * 4 * 128 * 1024,
                                       [[1024, 128], [128 * 1024, 4], [1, 1024]])))(i),
                       waits=[("T", P.c("T")), ("LCW", LCW_ALL)],
                       inc=f"LW2R{i}", inc_by=16)
            w1dma(0)
    B_END_T = P.c("T")
    B_END_V = P.c("V")
    w1dma(1)
    w1dma(2)
    w1dma(3)
    w2rdma(0)
    w2rdma(1)
    w1dma(4)
    w2rdma(2)
    w2rdma(3)
    w1dma(5)
    w1dma(6)
    w1dma(7)

    xcT_done, x2fm_evac = {}, {}
    ffn1_done, gelu_done, ffn2_done, r2evac_done = {}, {}, {}, {}
    lw2s_cnt = [0, 0, 0, 0]
    w2s_seq = {}
    FB = (2, 3, 0)
    hid3 = [rE[:, 2048:2304], rE[:, 2304:2560], rE[:, 2560:2816]]

    ltc_cnt = [0, 0]
    x2fm_ready = {}

    def emit_c_head_t(e8):
        slot = (e8 % 2) * 2
        par = e8 % 2
        for sj in range(2):
            waits = [("V", vcast_n[e8])]
            if sj == 0:
                if e8 == 0:
                    waits.append(("T", tail_trT[14]))
                elif e8 == 1:
                    waits.append(("T", B_END_T))
                else:
                    waits.append(("T", ffn1_done[(e8 - 2) * NH + NH - 1]))
                    waits.append((f"LTC{par}", ltc_cnt[par]))
            P.emit("sync",
                   (lambda sj_, sl_, par_: lambda e: e.dma_start_transpose(
                       x2f_all2[par_][:, :, sj_ * 128:(sj_ + 1) * 128],
                       xc[sl_ + sj_][:].rearrange("p (k c) -> p k c", k=8)))(
                       sj, slot, par),
                   waits=waits, inc=f"LTC{par}", inc_by=16)
            ltc_cnt[par] += 16
        x2fm_ready[e8] = (f"LTC{par}", ltc_cnt[par])

    def w2s_prefetch(g, gate_g):
        e8, hc = divmod(g, NH)
        sl = hc % 4
        if gate_g is None:
            waits = [("T", B_END_T)]
        else:
            waits = [("T", ffn2_done[gate_g])]
        waits.append(("LCW", LCW_ALL))
        lw2s_cnt[sl] += 1
        w2s_seq[g] = lw2s_cnt[sl]
        P.emit("gpsimd",
               (lambda hc_, sl_: lambda e: e.dma_start(
                   out=w2s[sl_], in_=w2bf[hc_ * 128:(hc_ + 1) * 128, :]))(hc, sl),
               waits=waits, inc=f"LW2S{sl}", inc_by=16)

    for k in range(4):
        w2s_prefetch(W2_RES + k, None)

    def emit_ffn1(g):
        e8, hc = divmod(g, NH)
        fbk = FB[g % 3]
        waits = [x2fm_ready[e8], (f"LW1S{hc // 4}", 16)]
        if g - 3 >= 0:
            waits.append(("Sc", gelu_done[g - 3]))
        else:
            waits.append(("V", oevac_done[14]))
            if g == 2:
                waits.append(ebsm_evac[14])
        def tfn(e, hc=hc, fbk=fbk, e8=e8):
            ins = None
            xb_ = x2fm2[e8 % 2]
            for ci in range(NC):
                ins = e.matmul(bank(fbk, 0, 256), w1sl(ci, hc * 128, 128),
                               xb_[:, ci * 256:(ci + 1) * 256],
                               start=(ci == 0), stop=(ci == NC - 1))
            return ins
        P.emit("tensor", tfn, waits=waits, inc="T")
        ffn1_done[g] = P.c("T")

    def emit_gelu(g):
        e8, hc = divmod(g, NH)
        fbk = FB[g % 3]
        waits = [("T", ffn1_done[g]), ("LMISC", LMISC_ALL)]
        if g - 3 >= 0:
            waits.append(("T", ffn2_done[g - 3]))
        P.emit("scalar",
               (lambda hc_, fbk_, g_: lambda e: e.activation(
                   out=hid3[g_ % 3][:, 0:256], in_=bank(fbk_, 0, 256),
                   func=AF.Gelu, bias=b1sb[:, hc_:hc_ + 1], scale=1.0))(hc, fbk, g),
               waits=waits, inc="Sc")
        gelu_done[g] = P.c("Sc")

    def emit_ffn2(g):
        e8, hc = divmod(g, NH)
        waits = [("Sc", gelu_done[g])]
        if hc < W2_RES and e8 == 0:
            waits.append((f"LW2R{hc // 4}", 16))
        if hc == 0:
            if e8 > 0:
                waits.append(("V", r2evac_done[e8 - 1]))
            else:
                waits.append(("V", B_END_V))
        if hc >= W2_RES:
            waits.append((f"LW2S{hc % 4}", 16 * w2s_seq[g]))
        def tfn2(e, hc=hc, g=g):
            ins = None
            for sj in range(2):
                for et in range(2):
                    rhs = (w2res(hc, et * 512, 512) if hc < W2_RES
                           else w2s[hc % 4][:, et * 512:(et + 1) * 512])
                    ins = e.matmul(bank(4 + sj * 2 + et, 0, 512),
                                   hid3[g % 3][:, sj * 128:(sj + 1) * 128], rhs,
                                   start=(hc == 0), stop=False)
            return ins
        P.emit("tensor", tfn2, waits=waits, inc="T")
        ffn2_done[g] = P.c("T")
        if hc >= 16:
            nxt = g + 4 if hc <= 27 else (g - hc) + NH + 16 + (hc % 4)
            if nxt < 8 * NH and nxt % NH >= 16:
                w2s_prefetch(nxt, g)

    def emit_fold_r2(e8):
        c0, c1 = 2 * e8, 2 * e8 + 1
        def foldfn(e):
            ins = None
            for sj in range(2):
                for et in range(2):
                    ins = e.matmul(bank(4 + sj * 2 + et, 0, 512), ones1[0:1, :],
                                   b2rowb[0:1, et * 512:(et + 1) * 512],
                                   start=False, stop=True)
            return ins
        P.emit("tensor", foldfn, waits=[("LMISCG", 16)], inc="T")
        fold_t = P.c("T")
        for sj, cch in enumerate((c0, c1)):
            for et in range(2):
                P.emit("vector",
                       (lambda c_, sj_, et_: lambda e: e.tensor_tensor(
                           out=resid2(c_, et_), in0=bank(4 + sj_ * 2 + et_, 0, 512),
                           in1=resid2(c_, et_), op=OP.add))(cch, sj, et),
                       waits=[("T", fold_t)], inc="V")
        r2evac_done[e8] = P.c("V")

    def emit_ln1_final_c(e8):
        slot = (e8 % 2) * 2
        scw = bstats_sc[2 * e8 + 1]
        if e8 >= 2:
            P.emit("vector", lambda e: e.engine_nop(),
                   waits=[x2fm_ready[e8 - 2]], inc="V")
        for sj, cch in enumerate((2 * e8, 2 * e8 + 1)):
            P.emit("vector",
                   (lambda c_, sj_: lambda e: e.tensor_mul(
                       K1TMP[sj_], M1COL(c_), R1COL(c_)))(cch, sj),
                   waits=[("Sc", scw)], inc="V")
        for sj, cch in enumerate((2 * e8, 2 * e8 + 1)):
            P.emit("vector",
                   (lambda c_, sl_, sj_: lambda e: e.tensor_scalar(
                       out=xc[sl_][:], in0=resid(c_), scalar1=R1COL(c_),
                       scalar2=K1TMP[sj_], op0=OP.mult, op1=OP.add))(
                       cch, slot + sj, sj),
                   inc="V")
        vcast_n[e8] = P.c("V")
        for sj, cch in enumerate((2 * e8, 2 * e8 + 1)):
            P.emit("vector",
                   (lambda c_, sj_: lambda e: e.tensor_scalar(
                       out=resid(c_), in0=resid(c_), scalar1=R1COL(c_),
                       scalar2=K1TMP[sj_], op0=OP.mult, op1=OP.add))(cch, sj),
                   inc="V")

    vcast_n[0] = vcast_n_early[0]
    emit_c_head_t(0)
    lnch, lnb_vw, lnb_sw = None, None, None
    for g in range(8 * NH):
        e8, hc = divmod(g, NH)
        emit_ffn1(g)
        if g >= 3:
            gg = g - 3
            emit_ffn2(gg)
            if gg % NH == NH - 1:
                emit_fold_r2(gg // NH)
        emit_gelu(g)
        if hc == 2 and e8 > 0:
            lnch = [(0, 2 * (e8 - 1)), (6, 2 * (e8 - 1) + 1)]
            lnb_vw = emit_lnb_v1(lnch)
        if hc == 6 and e8 + 1 < 8:
            emit_ln1_final_c(e8 + 1)
        if hc == 8 and e8 > 0:
            lnb_sw = emit_lnb_sc(lnch, lnb_vw)
        if hc == 10 and e8 > 0:
            emit_lnb_v2(lnch, lnb_sw, store=lnch)
        if hc == 12 and e8 + 1 < 8:
            emit_c_head_t(e8 + 1)
    emit_ffn2(8 * NH - 3)
    emit_ffn2(8 * NH - 2)
    emit_ffn2(8 * NH - 1)
    emit_fold_r2(7)

    ch7 = [(0, 14), (6, 15)]
    vw7 = emit_lnb_v1(ch7)
    sw7 = emit_lnb_sc(ch7, vw7)
    emit_lnb_v2(ch7, sw7, store=ch7)
    P.emit("sync", lambda e: e.wait_ge(sems["STY"], 16 * NS))

    block = ctx.enter_context(nc.Block())

    def emit_stream(eng_name):
        def run(eng):
            last = {}
            for fn, waits, inc, inc_by in P.ops[eng_name]:
                for s, v in waits:
                    if last.get(s, 0) < v:
                        eng.wait_ge(sems[s], v)
                        last[s] = v
                ins = fn(eng)
                if inc is not None:
                    if ins is None:
                        ins = eng.nop()
                    ins.then_inc(sems[inc], inc_by)
        return run

    block.tensor(emit_stream("tensor"))
    block.vector(emit_stream("vector"))
    block.scalar(emit_stream("scalar"))
    block.gpsimd(emit_stream("gpsimd"))
    block.sync(emit_stream("sync"))

    ctx.close()
    return nc


def make_in_map(inputs, core):
    return {
        "x": np.ascontiguousarray(inputs["L0_attn_x"][core]),
        "wq": np.ascontiguousarray(inputs["L0_attn_Wq"]),
        "wk": np.ascontiguousarray(inputs["L0_attn_Wk"]),
        "wv": np.ascontiguousarray(inputs["L0_attn_Wv"]),
        "gamma": np.ascontiguousarray(inputs["L0_attn_gamma"]).reshape(1, 1),
        "w1": np.ascontiguousarray(inputs["L0_ffn_w1"]),
        "b1": np.ascontiguousarray(inputs["L0_ffn_b1"]),
        "w2": np.ascontiguousarray(inputs["L0_ffn_w2"]),
        "b2": np.ascontiguousarray(inputs["L0_ffn_b2"]),
    }


def make_in_map(inputs, core):
    return {
        "x": np.ascontiguousarray(inputs["L0_attn_x"][core]),
        "wq": np.ascontiguousarray(inputs["L0_attn_Wq"]),
        "wk": np.ascontiguousarray(inputs["L0_attn_Wk"]),
        "wv": np.ascontiguousarray(inputs["L0_attn_Wv"]),
        "gamma": np.ascontiguousarray(inputs["L0_attn_gamma"]).reshape(1, 1),
        "w1": np.ascontiguousarray(inputs["L0_ffn_w1"]),
        "b1": np.ascontiguousarray(inputs["L0_ffn_b1"]),
        "w2": np.ascontiguousarray(inputs["L0_ffn_w2"]),
        "b2": np.ascontiguousarray(inputs["L0_ffn_b2"]),
    }


def make_in_map(inputs, core):
    return {
        "x": np.ascontiguousarray(inputs["L0_attn_x"][core]),
        "wq": np.ascontiguousarray(inputs["L0_attn_Wq"]),
        "wk": np.ascontiguousarray(inputs["L0_attn_Wk"]),
        "wv": np.ascontiguousarray(inputs["L0_attn_Wv"]),
        "gamma": np.ascontiguousarray(inputs["L0_attn_gamma"]).reshape(1, 1),
        "w1": np.ascontiguousarray(inputs["L0_ffn_w1"]),
        "b1": np.ascontiguousarray(inputs["L0_ffn_b1"]),
        "w2": np.ascontiguousarray(inputs["L0_ffn_w2"]),
        "b2": np.ascontiguousarray(inputs["L0_ffn_b2"]),
    }


def make_in_map(inputs, core):
    return {
        "x": np.ascontiguousarray(inputs["L0_attn_x"][core]),
        "wq": np.ascontiguousarray(inputs["L0_attn_Wq"]),
        "wk": np.ascontiguousarray(inputs["L0_attn_Wk"]),
        "wv": np.ascontiguousarray(inputs["L0_attn_Wv"]),
        "gamma": np.ascontiguousarray(inputs["L0_attn_gamma"]).reshape(1, 1),
        "w1": np.ascontiguousarray(inputs["L0_ffn_w1"]),
        "b1": np.ascontiguousarray(inputs["L0_ffn_b1"]),
        "w2": np.ascontiguousarray(inputs["L0_ffn_w2"]),
        "b2": np.ascontiguousarray(inputs["L0_ffn_b2"]),
    }


def make_in_map(inputs, core):
    return {
        "x": np.ascontiguousarray(inputs["L0_attn_x"][core]),
        "wq": np.ascontiguousarray(inputs["L0_attn_Wq"]),
        "wk": np.ascontiguousarray(inputs["L0_attn_Wk"]),
        "wv": np.ascontiguousarray(inputs["L0_attn_Wv"]),
        "gamma": np.ascontiguousarray(inputs["L0_attn_gamma"]).reshape(1, 1),
        "w1": np.ascontiguousarray(inputs["L0_ffn_w1"]),
        "b1": np.ascontiguousarray(inputs["L0_ffn_b1"]),
        "w2": np.ascontiguousarray(inputs["L0_ffn_w2"]),
        "b2": np.ascontiguousarray(inputs["L0_ffn_b2"]),
    }



def make_in_map(inputs, core):
    return {
        "x": np.ascontiguousarray(inputs["L0_attn_x"][core]),
        "wq": np.ascontiguousarray(inputs["L0_attn_Wq"]),
        "wk": np.ascontiguousarray(inputs["L0_attn_Wk"]),
        "wv": np.ascontiguousarray(inputs["L0_attn_Wv"]),
        "gamma": np.ascontiguousarray(inputs["L0_attn_gamma"]).reshape(1, 1),
        "w1": np.ascontiguousarray(inputs["L0_ffn_w1"]),
        "b1": np.ascontiguousarray(inputs["L0_ffn_b1"]),
        "w2": np.ascontiguousarray(inputs["L0_ffn_w2"]),
        "b2": np.ascontiguousarray(inputs["L0_ffn_b2"]),
    }



def make_in_map(inputs, core):
    return {
        "x": np.ascontiguousarray(inputs["L0_attn_x"][core]),
        "wq": np.ascontiguousarray(inputs["L0_attn_Wq"]),
        "wk": np.ascontiguousarray(inputs["L0_attn_Wk"]),
        "wv": np.ascontiguousarray(inputs["L0_attn_Wv"]),
        "gamma": np.ascontiguousarray(inputs["L0_attn_gamma"]).reshape(1, 1),
        "w1": np.ascontiguousarray(inputs["L0_ffn_w1"]),
        "b1": np.ascontiguousarray(inputs["L0_ffn_b1"]),
        "w2": np.ascontiguousarray(inputs["L0_ffn_w2"]),
        "b2": np.ascontiguousarray(inputs["L0_ffn_b2"]),
    }



def make_in_map(inputs, core):
    return {
        "x": np.ascontiguousarray(inputs["L0_attn_x"][core]),
        "wq": np.ascontiguousarray(inputs["L0_attn_Wq"]),
        "wk": np.ascontiguousarray(inputs["L0_attn_Wk"]),
        "wv": np.ascontiguousarray(inputs["L0_attn_Wv"]),
        "gamma": np.ascontiguousarray(inputs["L0_attn_gamma"]).reshape(1, 1),
        "w1": np.ascontiguousarray(inputs["L0_ffn_w1"]),
        "b1": np.ascontiguousarray(inputs["L0_ffn_b1"]),
        "w2": np.ascontiguousarray(inputs["L0_ffn_w2"]),
        "b2": np.ascontiguousarray(inputs["L0_ffn_b2"]),
    }



def make_in_map(inputs, core):
    return {
        "x": np.ascontiguousarray(inputs["L0_attn_x"][core]),
        "wq": np.ascontiguousarray(inputs["L0_attn_Wq"]),
        "wk": np.ascontiguousarray(inputs["L0_attn_Wk"]),
        "wv": np.ascontiguousarray(inputs["L0_attn_Wv"]),
        "gamma": np.ascontiguousarray(inputs["L0_attn_gamma"]).reshape(1, 1),
        "w1": np.ascontiguousarray(inputs["L0_ffn_w1"]),
        "b1": np.ascontiguousarray(inputs["L0_ffn_b1"]),
        "w2": np.ascontiguousarray(inputs["L0_ffn_w2"]),
        "b2": np.ascontiguousarray(inputs["L0_ffn_b2"]),
    }



def make_in_map(inputs, core):
    return {
        "x": np.ascontiguousarray(inputs["L0_attn_x"][core]),
        "wq": np.ascontiguousarray(inputs["L0_attn_Wq"]),
        "wk": np.ascontiguousarray(inputs["L0_attn_Wk"]),
        "wv": np.ascontiguousarray(inputs["L0_attn_Wv"]),
        "gamma": np.ascontiguousarray(inputs["L0_attn_gamma"]).reshape(1, 1),
        "w1": np.ascontiguousarray(inputs["L0_ffn_w1"]),
        "b1": np.ascontiguousarray(inputs["L0_ffn_b1"]),
        "w2": np.ascontiguousarray(inputs["L0_ffn_w2"]),
        "b2": np.ascontiguousarray(inputs["L0_ffn_b2"]),
    }



def make_in_map(inputs, core):
    return {
        "x": np.ascontiguousarray(inputs["L0_attn_x"][core]),
        "wq": np.ascontiguousarray(inputs["L0_attn_Wq"]),
        "wk": np.ascontiguousarray(inputs["L0_attn_Wk"]),
        "wv": np.ascontiguousarray(inputs["L0_attn_Wv"]),
        "gamma": np.ascontiguousarray(inputs["L0_attn_gamma"]).reshape(1, 1),
        "w1": np.ascontiguousarray(inputs["L0_ffn_w1"]),
        "b1": np.ascontiguousarray(inputs["L0_ffn_b1"]),
        "w2": np.ascontiguousarray(inputs["L0_ffn_w2"]),
        "b2": np.ascontiguousarray(inputs["L0_ffn_b2"]),
    }



def make_in_map(inputs, core):
    return {
        "x": np.ascontiguousarray(inputs["L0_attn_x"][core]),
        "wq": np.ascontiguousarray(inputs["L0_attn_Wq"]),
        "wk": np.ascontiguousarray(inputs["L0_attn_Wk"]),
        "wv": np.ascontiguousarray(inputs["L0_attn_Wv"]),
        "gamma": np.ascontiguousarray(inputs["L0_attn_gamma"]).reshape(1, 1),
        "w1": np.ascontiguousarray(inputs["L0_ffn_w1"]),
        "b1": np.ascontiguousarray(inputs["L0_ffn_b1"]),
        "w2": np.ascontiguousarray(inputs["L0_ffn_w2"]),
        "b2": np.ascontiguousarray(inputs["L0_ffn_b2"]),
    }



def make_in_map(inputs, core):
    return {
        "x": np.ascontiguousarray(inputs["L0_attn_x"][core]),
        "wq": np.ascontiguousarray(inputs["L0_attn_Wq"]),
        "wk": np.ascontiguousarray(inputs["L0_attn_Wk"]),
        "wv": np.ascontiguousarray(inputs["L0_attn_Wv"]),
        "gamma": np.ascontiguousarray(inputs["L0_attn_gamma"]).reshape(1, 1),
        "w1": np.ascontiguousarray(inputs["L0_ffn_w1"]),
        "b1": np.ascontiguousarray(inputs["L0_ffn_b1"]),
        "w2": np.ascontiguousarray(inputs["L0_ffn_w2"]),
        "b2": np.ascontiguousarray(inputs["L0_ffn_b2"]),
    }



def make_in_map(inputs, core):
    return {
        "x": np.ascontiguousarray(inputs["L0_attn_x"][core]),
        "wq": np.ascontiguousarray(inputs["L0_attn_Wq"]),
        "wk": np.ascontiguousarray(inputs["L0_attn_Wk"]),
        "wv": np.ascontiguousarray(inputs["L0_attn_Wv"]),
        "gamma": np.ascontiguousarray(inputs["L0_attn_gamma"]).reshape(1, 1),
        "w1": np.ascontiguousarray(inputs["L0_ffn_w1"]),
        "b1": np.ascontiguousarray(inputs["L0_ffn_b1"]),
        "w2": np.ascontiguousarray(inputs["L0_ffn_w2"]),
        "b2": np.ascontiguousarray(inputs["L0_ffn_b2"]),
    }



def make_in_map(inputs, core):
    return {
        "x": np.ascontiguousarray(inputs["L0_attn_x"][core]),
        "wq": np.ascontiguousarray(inputs["L0_attn_Wq"]),
        "wk": np.ascontiguousarray(inputs["L0_attn_Wk"]),
        "wv": np.ascontiguousarray(inputs["L0_attn_Wv"]),
        "gamma": np.ascontiguousarray(inputs["L0_attn_gamma"]).reshape(1, 1),
        "w1": np.ascontiguousarray(inputs["L0_ffn_w1"]),
        "b1": np.ascontiguousarray(inputs["L0_ffn_b1"]),
        "w2": np.ascontiguousarray(inputs["L0_ffn_w2"]),
        "b2": np.ascontiguousarray(inputs["L0_ffn_b2"]),
    }



def make_in_map(inputs, core):
    return {
        "x": np.ascontiguousarray(inputs["L0_attn_x"][core]),
        "wq": np.ascontiguousarray(inputs["L0_attn_Wq"]),
        "wk": np.ascontiguousarray(inputs["L0_attn_Wk"]),
        "wv": np.ascontiguousarray(inputs["L0_attn_Wv"]),
        "gamma": np.ascontiguousarray(inputs["L0_attn_gamma"]).reshape(1, 1),
        "w1": np.ascontiguousarray(inputs["L0_ffn_w1"]),
        "b1": np.ascontiguousarray(inputs["L0_ffn_b1"]),
        "w2": np.ascontiguousarray(inputs["L0_ffn_w2"]),
        "b2": np.ascontiguousarray(inputs["L0_ffn_b2"]),
    }



def make_in_map(inputs, core):
    return {
        "x": np.ascontiguousarray(inputs["L0_attn_x"][core]),
        "wq": np.ascontiguousarray(inputs["L0_attn_Wq"]),
        "wk": np.ascontiguousarray(inputs["L0_attn_Wk"]),
        "wv": np.ascontiguousarray(inputs["L0_attn_Wv"]),
        "gamma": np.ascontiguousarray(inputs["L0_attn_gamma"]).reshape(1, 1),
        "w1": np.ascontiguousarray(inputs["L0_ffn_w1"]),
        "b1": np.ascontiguousarray(inputs["L0_ffn_b1"]),
        "w2": np.ascontiguousarray(inputs["L0_ffn_w2"]),
        "b2": np.ascontiguousarray(inputs["L0_ffn_b2"]),
    }



def make_in_map(inputs, core):
    return {
        "x": np.ascontiguousarray(inputs["L0_attn_x"][core]),
        "wq": np.ascontiguousarray(inputs["L0_attn_Wq"]),
        "wk": np.ascontiguousarray(inputs["L0_attn_Wk"]),
        "wv": np.ascontiguousarray(inputs["L0_attn_Wv"]),
        "gamma": np.ascontiguousarray(inputs["L0_attn_gamma"]).reshape(1, 1),
        "w1": np.ascontiguousarray(inputs["L0_ffn_w1"]),
        "b1": np.ascontiguousarray(inputs["L0_ffn_b1"]),
        "w2": np.ascontiguousarray(inputs["L0_ffn_w2"]),
        "b2": np.ascontiguousarray(inputs["L0_ffn_b2"]),
    }



def make_in_map(inputs, core):
    return {
        "x": np.ascontiguousarray(inputs["L0_attn_x"][core]),
        "wq": np.ascontiguousarray(inputs["L0_attn_Wq"]),
        "wk": np.ascontiguousarray(inputs["L0_attn_Wk"]),
        "wv": np.ascontiguousarray(inputs["L0_attn_Wv"]),
        "gamma": np.ascontiguousarray(inputs["L0_attn_gamma"]).reshape(1, 1),
        "w1": np.ascontiguousarray(inputs["L0_ffn_w1"]),
        "b1": np.ascontiguousarray(inputs["L0_ffn_b1"]),
        "w2": np.ascontiguousarray(inputs["L0_ffn_w2"]),
        "b2": np.ascontiguousarray(inputs["L0_ffn_b2"]),
    }



def make_in_map(inputs, core):
    return {
        "x": np.ascontiguousarray(inputs["L0_attn_x"][core]),
        "wq": np.ascontiguousarray(inputs["L0_attn_Wq"]),
        "wk": np.ascontiguousarray(inputs["L0_attn_Wk"]),
        "wv": np.ascontiguousarray(inputs["L0_attn_Wv"]),
        "gamma": np.ascontiguousarray(inputs["L0_attn_gamma"]).reshape(1, 1),
        "w1": np.ascontiguousarray(inputs["L0_ffn_w1"]),
        "b1": np.ascontiguousarray(inputs["L0_ffn_b1"]),
        "w2": np.ascontiguousarray(inputs["L0_ffn_w2"]),
        "b2": np.ascontiguousarray(inputs["L0_ffn_b2"]),
    }



def make_in_map(inputs, core):
    return {
        "x": np.ascontiguousarray(inputs["L0_attn_x"][core]),
        "wq": np.ascontiguousarray(inputs["L0_attn_Wq"]),
        "wk": np.ascontiguousarray(inputs["L0_attn_Wk"]),
        "wv": np.ascontiguousarray(inputs["L0_attn_Wv"]),
        "gamma": np.ascontiguousarray(inputs["L0_attn_gamma"]).reshape(1, 1),
        "w1": np.ascontiguousarray(inputs["L0_ffn_w1"]),
        "b1": np.ascontiguousarray(inputs["L0_ffn_b1"]),
        "w2": np.ascontiguousarray(inputs["L0_ffn_w2"]),
        "b2": np.ascontiguousarray(inputs["L0_ffn_b2"]),
    }



def make_in_map(inputs, core):
    return {
        "x": np.ascontiguousarray(inputs["L0_attn_x"][core]),
        "wq": np.ascontiguousarray(inputs["L0_attn_Wq"]),
        "wk": np.ascontiguousarray(inputs["L0_attn_Wk"]),
        "wv": np.ascontiguousarray(inputs["L0_attn_Wv"]),
        "gamma": np.ascontiguousarray(inputs["L0_attn_gamma"]).reshape(1, 1),
        "w1": np.ascontiguousarray(inputs["L0_ffn_w1"]),
        "b1": np.ascontiguousarray(inputs["L0_ffn_b1"]),
        "w2": np.ascontiguousarray(inputs["L0_ffn_w2"]),
        "b2": np.ascontiguousarray(inputs["L0_ffn_b2"]),
    }



def make_in_map(inputs, core):
    return {
        "x": np.ascontiguousarray(inputs["L0_attn_x"][core]),
        "wq": np.ascontiguousarray(inputs["L0_attn_Wq"]),
        "wk": np.ascontiguousarray(inputs["L0_attn_Wk"]),
        "wv": np.ascontiguousarray(inputs["L0_attn_Wv"]),
        "gamma": np.ascontiguousarray(inputs["L0_attn_gamma"]).reshape(1, 1),
        "w1": np.ascontiguousarray(inputs["L0_ffn_w1"]),
        "b1": np.ascontiguousarray(inputs["L0_ffn_b1"]),
        "w2": np.ascontiguousarray(inputs["L0_ffn_w2"]),
        "b2": np.ascontiguousarray(inputs["L0_ffn_b2"]),
    }



def make_in_map(inputs, core):
    return {
        "x": np.ascontiguousarray(inputs["L0_attn_x"][core]),
        "wq": np.ascontiguousarray(inputs["L0_attn_Wq"]),
        "wk": np.ascontiguousarray(inputs["L0_attn_Wk"]),
        "wv": np.ascontiguousarray(inputs["L0_attn_Wv"]),
        "gamma": np.ascontiguousarray(inputs["L0_attn_gamma"]).reshape(1, 1),
        "w1": np.ascontiguousarray(inputs["L0_ffn_w1"]),
        "b1": np.ascontiguousarray(inputs["L0_ffn_b1"]),
        "w2": np.ascontiguousarray(inputs["L0_ffn_w2"]),
        "b2": np.ascontiguousarray(inputs["L0_ffn_b2"]),
    }



def make_in_map(inputs, core):
    return {
        "x": np.ascontiguousarray(inputs["L0_attn_x"][core]),
        "wq": np.ascontiguousarray(inputs["L0_attn_Wq"]),
        "wk": np.ascontiguousarray(inputs["L0_attn_Wk"]),
        "wv": np.ascontiguousarray(inputs["L0_attn_Wv"]),
        "gamma": np.ascontiguousarray(inputs["L0_attn_gamma"]).reshape(1, 1),
        "w1": np.ascontiguousarray(inputs["L0_ffn_w1"]),
        "b1": np.ascontiguousarray(inputs["L0_ffn_b1"]),
        "w2": np.ascontiguousarray(inputs["L0_ffn_w2"]),
        "b2": np.ascontiguousarray(inputs["L0_ffn_b2"]),
    }



def make_in_map(inputs, core):
    return {
        "x": np.ascontiguousarray(inputs["L0_attn_x"][core]),
        "wq": np.ascontiguousarray(inputs["L0_attn_Wq"]),
        "wk": np.ascontiguousarray(inputs["L0_attn_Wk"]),
        "wv": np.ascontiguousarray(inputs["L0_attn_Wv"]),
        "gamma": np.ascontiguousarray(inputs["L0_attn_gamma"]).reshape(1, 1),
        "w1": np.ascontiguousarray(inputs["L0_ffn_w1"]),
        "b1": np.ascontiguousarray(inputs["L0_ffn_b1"]),
        "w2": np.ascontiguousarray(inputs["L0_ffn_w2"]),
        "b2": np.ascontiguousarray(inputs["L0_ffn_b2"]),
    }



def make_in_map(inputs, core):
    return {
        "x": np.ascontiguousarray(inputs["L0_attn_x"][core]),
        "wq": np.ascontiguousarray(inputs["L0_attn_Wq"]),
        "wk": np.ascontiguousarray(inputs["L0_attn_Wk"]),
        "wv": np.ascontiguousarray(inputs["L0_attn_Wv"]),
        "gamma": np.ascontiguousarray(inputs["L0_attn_gamma"]).reshape(1, 1),
        "w1": np.ascontiguousarray(inputs["L0_ffn_w1"]),
        "b1": np.ascontiguousarray(inputs["L0_ffn_b1"]),
        "w2": np.ascontiguousarray(inputs["L0_ffn_w2"]),
        "b2": np.ascontiguousarray(inputs["L0_ffn_b2"]),
    }




def make_in_map(inputs, core):
    f32 = np.float32
    return {
        "x": np.ascontiguousarray(np.asarray(inputs["L0_attn_x"], dtype=f32)[core]),
        "wq": np.ascontiguousarray(np.asarray(inputs["L0_attn_Wq"], dtype=f32)),
        "wk": np.ascontiguousarray(np.asarray(inputs["L0_attn_Wk"], dtype=f32)),
        "wv": np.ascontiguousarray(np.asarray(inputs["L0_attn_Wv"], dtype=f32)),
        "gamma": np.ascontiguousarray(np.asarray(inputs["L0_attn_gamma"], dtype=f32)).reshape(1, 1),
        "w1": np.ascontiguousarray(np.asarray(inputs["L0_ffn_w1"], dtype=f32)),
        "b1": np.ascontiguousarray(np.asarray(inputs["L0_ffn_b1"], dtype=f32)),
        "w2": np.ascontiguousarray(np.asarray(inputs["L0_ffn_w2"], dtype=f32)),
        "b2": np.ascontiguousarray(np.asarray(inputs["L0_ffn_b2"], dtype=f32)),
    }


def kernel(**inputs):
    from concourse.bass_utils import run_bass_kernel_spmd
    nc = build_nc()
    in_maps = [make_in_map(inputs, core) for core in range(8)]
    res = run_bass_kernel_spmd(nc, in_maps, core_ids=list(range(8)))
    return np.stack([np.asarray(res.results[i]["y"], dtype=np.float32)
                     for i in range(8)], axis=0)


# version 34
# speedup vs baseline: 1.1206x; 1.0099x over previous
import contextlib
import numpy as np
import concourse.bass as bass
import concourse.mybir as mybir

F32 = mybir.dt.float32
BF16 = mybir.dt.bfloat16
U8 = mybir.dt.uint8
AX = mybir.AxisListType.X
OP = mybir.AluOpType
AF = mybir.ActivationFunctionType

S, C, HD = 2048, 1024, 4096
NS, NC, NH = 16, 8, 32
W2_RES = 16
EPS = 1e-5

ENGS = ("tensor", "vector", "scalar", "gpsimd", "sync")
STRICT = True
STRICT_ENGS = {"vector", "scalar"}
ENG_SEM = {"tensor": "T", "vector": "V", "scalar": "Sc", "gpsimd": "G"}


class Plan:
    def __init__(self):
        self.ops = {e: [] for e in ENGS}
        self.cnt = {}
        self.strict_now = set()

    def emit(self, eng, fn, waits=(), inc=None, inc_by=1):
        waits = [(s, v) for (s, v) in waits if v > 0]
        if eng in ENG_SEM:
            es = ENG_SEM[eng]
            if (eng == "gpsimd" or (STRICT and eng in STRICT_ENGS)
                    or eng in self.strict_now) and self.c(es) > 0:
                waits.append((es, self.c(es)))
            if inc is None:
                inc = es
        self.ops[eng].append((fn, waits, inc, inc_by))
        if inc is not None:
            self.cnt[inc] = self.cnt.get(inc, 0) + inc_by

    def c(self, sem):
        return self.cnt.get(sem, 0)


def build_nc():
    nc = bass.Bass("TRN2", target_bir_lowering=False)
    x = nc.dram_tensor("x", [S, C], F32, kind="ExternalInput").ap()
    wq = nc.dram_tensor("wq", [C, C], F32, kind="ExternalInput").ap()
    wk = nc.dram_tensor("wk", [C, C], F32, kind="ExternalInput").ap()
    wv = nc.dram_tensor("wv", [C, C], F32, kind="ExternalInput").ap()
    gamma = nc.dram_tensor("gamma", [1, 1], F32, kind="ExternalInput").ap()
    w1 = nc.dram_tensor("w1", [C, HD], F32, kind="ExternalInput").ap()
    b1 = nc.dram_tensor("b1", [HD], F32, kind="ExternalInput").ap()
    w2 = nc.dram_tensor("w2", [HD, C], F32, kind="ExternalInput").ap()
    b2 = nc.dram_tensor("b2", [C], F32, kind="ExternalInput").ap()
    y = nc.dram_tensor("y", [S, C], F32, kind="ExternalOutput").ap()
    w1bf = nc.dram_tensor("w1bf", [C, HD], BF16, kind="Internal").ap()
    w2bf = nc.dram_tensor("w2bf", [HD, C], BF16, kind="Internal").ap()

    ctx = contextlib.ExitStack()
    sb = lambda n, s, d: ctx.enter_context(nc.sbuf_tensor(n, s, d))
    regA = sb("regA", [128, 65536], U8)
    regFG = sb("regFG", [128, 65536], U8)
    regH = sb("regH", [128, 32768], U8)
    regE = sb("regE", [128, 20480], U8)
    rA_f = regA.bitcast(F32)
    rA_b = regA.bitcast(BF16)
    rFG = regFG.bitcast(BF16)
    rH = regH.bitcast(BF16)
    rE = regE.bitcast(BF16)

    resid = lambda ti: rA_f[:, ti * 1024:(ti + 1) * 1024]
    resid2 = lambda ti, et: rA_f[:, ti * 1024 + et * 512: ti * 1024 + et * 512 + 512]

    _wtg_base = (12288, 8192)
    wtg_all2 = [rA_b[:, b:b + 4096].rearrange("p (ci d) -> p ci d", ci=8)
                for b in _wtg_base]
    WTg2 = lambda bi, ci, off, n: rA_b[:, _wtg_base[bi] + ci * 512 + off:
                                       _wtg_base[bi] + ci * 512 + off + n]
    xfm_all = rA_b[:, 16384:32768].rearrange("p (ci s) -> p ci s", ci=8)
    xfm = lambda ci, s0, n: rA_b[:, 16384 + ci * 2048 + s0: 16384 + ci * 2048 + s0 + n]
    fsl = lambda d, s0, n: rFG[:, d * 2048 + s0: d * 2048 + s0 + n]
    gsl = lambda d, s0, n: rFG[:, 16384 + d * 2048 + s0: 16384 + d * 2048 + s0 + n]
    w1sl = lambda ci, h0, n: rFG[:, ci * 4096 + h0: ci * 4096 + h0 + n]
    hsl = lambda si, c0, n: rH[:, si * 1024 + c0: si * 1024 + c0 + n]
    w2res = lambda hc, e0, n: rH[:, hc * 1024 + e0: hc * 1024 + e0 + n]
    ebT = [rE[:, 0:2048], rE[:, 2048:4096]]
    ebsm2 = [rE[:, 4096:6144], rE[:, 6144:8192]]
    x2fm2 = [rE[:, 0:2048], rE[:, 6144:8192]]
    x2f_all2 = [rE[:, 0:2048].rearrange("p (ci s) -> p ci s", ci=8),
                rE[:, 6144:8192].rearrange("p (ci s) -> p ci s", ci=8)]
    hid = [rE[:, 2048:2304], rE[:, 2304:2560]]
    w2s = [rE[:, 3072:4096], rE[:, 4096:5120], rE[:, 8192:9216], rE[:, 9216:10240]]

    ident = sb("ident", [128, 128], BF16)
    identf = sb("identf", [128, 128], F32)
    ones1 = sb("ones1", [1, 128], BF16)
    b2rowb = sb("b2rowb", [1, 1024], BF16)
    b1sb = sb("b1sb", [128, 32], F32)
    gbc = sb("gbc", [128, 1], F32)
    cdb = sb("cdb", [128, 4], F32)
    stats = sb("stats", [128, 64], F32)
    xch = sb("xch", [128, 8192], BF16)
    xc = [xch[:, k * 1024:(k + 1) * 1024] for k in range(8)]
    wbw = xch
    varsink = sb("varsink", [128, 1024], F32)

    CONSTS = [EPS, -0.5, 1.0 / 1024]
    for i, cv in enumerate(CONSTS):
        nc.const_aps.register(F32, cv, cdb[:, i:i + 1])

    NM = lambda par, st: stats[:, par * 4 + st: par * 4 + st + 1]
    NMALL = lambda par: stats[:, par * 4: par * 4 + 4]
    NEGMAX = lambda par: stats[:, 8 + par: 9 + par]
    ZP = lambda par, st: stats[:, 10 + par * 4 + st: 11 + par * 4 + st]
    ZPALL = lambda par: stats[:, 10 + par * 4: 14 + par * 4]
    ZCOL = stats[:, 18:19]
    ZICOL = stats[:, 19:20]
    ZG2 = [stats[:, 20:21], stats[:, 29:30]]
    NSUM = stats[:, 24:25]
    NEGM = stats[:, 25:26]
    VSUM = stats[:, 26:27]
    LNV = stats[:, 27:28]
    RSTD = stats[:, 28:29]
    KCOL = stats[:, 29:30]

    ps = ctx.enter_context(nc.psum_tensor("ps", [128, 4096], F32))
    ps_bf = ps.bitcast(BF16)
    bank = lambda b, off, n: ps[:, b * 512 + off: b * 512 + off + n]
    bankb = lambda b, off, n: ps_bf[:, b * 1024 + off: b * 1024 + off + n]
    bankb3 = lambda b: ps_bf[:, b * 1024:(b + 1) * 1024].rearrange(
        "p (ci s) -> p ci s", ci=8)

    sems = {}
    for name in ("T", "V", "Sc", "G", "LXC0", "LXC1", "LXC2", "LXC3",
                 "LWB0", "LWB1", "LWB2", "LWB3",
                 "LTE0", "LTE1", "LTC0", "LTC1",
                 "LW2S0", "LW2S1", "LW2S2", "LW2S3", "LXA", "LXB",
                 "LW1S0", "LW1S1", "LW1S2", "LW1S3",
                 "LW1S4", "LW1S5", "LW1S6", "LW1S7",
                 "LW2R0", "LW2R1", "LW2R2", "LW2R3", "LMISC", "LMISCG", "STY",
                 "LCW", "LXB2"):
        sems[name] = ctx.enter_context(nc.semaphore(name))

    P = Plan()
    pfree = {b: [] for b in range(8)}

    def v_init(v):
        v.memset(ident[:], 1.0)
        v.memset(identf[:], 1.0)
        v.memset(ones1[:], 1.0)
        ins = None
        for i, cv in enumerate(CONSTS):
            ins = v.memset(cdb[:, i:i + 1], cv)
        return ins
    P.emit("vector", v_init, inc="V")
    V_INIT = P.c("V")

    P.emit("gpsimd", lambda g: g.affine_select(
        out=ident[:], in_=ident[:], pattern=[[-1, 128]],
        compare_op=OP.is_equal, fill=0.0, base=0, channel_multiplier=1),
        waits=[("V", V_INIT)], inc="G")
    P.emit("gpsimd", lambda g: g.affine_select(
        out=identf[:], in_=identf[:], pattern=[[-1, 128]],
        compare_op=OP.is_equal, fill=0.0, base=0, channel_multiplier=1),
        inc="G")
    G_INIT = P.c("G")

    P.emit("sync", lambda e: e.dma_start(
        out=rA_f[:, 0:4 * 1024].rearrange("p (a b) -> p a b", a=4),
        in_=bass.AP(x.tensor, 0, [[1024, 128], [128 * 1024, 4], [1, 1024]])),
        inc="LXA", inc_by=16)
    P.emit("sync", lambda e: e.dma_start(
        out=varsink[0:32, 0:128], in_=bass.AP(b1.tensor, 0, [[128, 32], [1, 128]])),
        inc="LMISC", inc_by=16)
    P.emit("sync", lambda e: e.dma_start(
        out=gbc[:], in_=bass.AP(gamma.tensor, 0, [[0, 128], [1, 1]])),
        inc="LMISC", inc_by=16)
    P.emit("vector", lambda e: e.tensor_scalar_mul(gbc[:], gbc[:], 0.5),
           waits=[("LMISC", 32)], inc="V")
    P.emit("gpsimd", lambda e: e.dma_start(
        out=b2rowb[0:1, :], in_=bass.AP(b2.tensor, 0, [[1024, 1], [1, 1024]])),
        inc="LMISCG", inc_by=16)
    LMISC_ALL = P.c("LMISC")

    P.emit("tensor", lambda e: e.transpose(bank(6, 0, 32), varsink[0:32, 0:128],
                                           identf[0:32, 0:32]),
           waits=[("LMISC", 32), ("G", G_INIT)], inc="T")
    P.emit("vector", lambda e: e.tensor_copy(b1sb[:], bank(6, 0, 32)),
           waits=[("T", P.c("T"))], inc="V")
    pfree[6] = [("V", P.c("V"))]

    xT_done, xT_evac = {}, {}
    for pr in range(NS // 2):
        sl = pr % 4
        waits = [("T", xT_done[2 * pr - 7])] if pr >= 4 else []
        P.emit("gpsimd",
               (lambda pr_, sl_: lambda e: e.dma_start(
                   out=xch[:, sl_ * 2048:(sl_ + 1) * 2048].rearrange(
                       "p (a b) -> p a b", a=2),
                   in_=bass.AP(x.tensor, pr_ * 2 * 128 * 1024,
                               [[1024, 128], [128 * 1024, 2], [1, 1024]])))(pr, sl),
               waits=waits, inc=f"LXC{sl}", inc_by=16)
        for half in range(2):
            ti = 2 * pr + half
            bk = 4 + (ti % 2)
            twaits = [(f"LXC{sl}", 16 * (pr // 4 + 1)), ("G", G_INIT)] + pfree[bk]
            def tfn(e, ti=ti, bk=bk, sl=sl, half=half):
                ins = None
                for ci in range(NC):
                    ins = e.transpose(bankb(bk, ci * 128, 128),
                                      xc[sl * 2 + half][:, ci * 128:(ci + 1) * 128],
                                      ident[:])
                return ins
            P.emit("tensor", tfn, waits=twaits, inc="T")
            xT_done[ti] = P.c("T")
            P.emit("vector",
                   (lambda ti_, bk_: lambda e: e.tensor_copy(
                       xfm_all[:, :, ti_ * 128:(ti_ + 1) * 128], bankb3(bk_)))(ti, bk),
                   waits=[("T", xT_done[ti])], inc="V")
            xT_evac[ti] = P.c("V")
            pfree[bk] = [("V", xT_evac[ti])]
    XFM_ALL_V = P.c("V")

    wdram = {0: wq, 1: wk, 2: wv}
    rot = [0]

    def qkv_mm_tilegroup(lhsT_sl, rhs_sl, out_fn, N, extra_waits, evac_eng):
        bk = rot[0] % 4
        rot[0] += 1
        waits = list(extra_waits) + pfree[bk]
        def tfn(e, bk=bk, lhsT_sl=lhsT_sl, rhs_sl=rhs_sl, N=N):
            ins = None
            for ci in range(NC):
                ins = e.matmul(bank(bk, 0, N), lhsT_sl(ci), rhs_sl(ci),
                               start=(ci == 0), stop=(ci == NC - 1))
            return ins
        P.emit("tensor", tfn, waits=waits, inc="T")
        tcnt = P.c("T")
        sem = "Sc" if evac_eng == "scalar" else "V"
        def efn(e, bk=bk, out_fn=out_fn, N=N):
            out = out_fn()
            return (e.copy(out, bank(bk, 0, N)) if evac_eng == "scalar"
                    else e.tensor_copy(out, bank(bk, 0, N)))
        P.emit(evac_eng, efn, waits=[("T", tcnt)], inc=sem)
        pfree[bk] = [(sem, P.c(sem))]

    wT_done = {}
    lwb_cnt = [0, 0, 0, 0]
    wrow_idx_box = [0]
    wt_ready = {}

    def emit_w_lt(k):
        wi, grp = divmod(k, 2)
        bi = k % 2
        for rp in range(2):
            wrow_idx = wrow_idx_box[0]
            wp = wrow_idx // 2
            sl = wp % 4
            dj0 = grp * 4 + rp * 2
            if wp >= 4:
                waits = [("T", wT_done[wrow_idx - 7])]
            else:
                waits = [("T", xT_done[9 + 2 * wp])]
            P.emit("gpsimd",
                   (lambda wi_, dj_, sl_: lambda e: e.dma_start(
                       out=wbw[:, sl_ * 2048:(sl_ + 1) * 2048].rearrange(
                           "p (a b) -> p a b", a=2),
                       in_=bass.AP(wdram[wi_].tensor, dj_ * 128 * 1024,
                                   [[1024, 128], [128 * 1024, 2], [1, 1024]])))(
                       wi, dj0, sl),
                   waits=waits, inc=f"LWB{sl}", inc_by=16)
            lwb_cnt[sl] += 1
            for half in range(2):
                r = rp * 2 + half
                bk = 4 + (wrow_idx_box[0] % 4)
                twaits = [(f"LWB{sl}", 16 * lwb_cnt[sl]), ("G", G_INIT)] + pfree[bk]
                if k >= 2:
                    twaits.append(("T", wt_mm_done[k - 2]))
                def tfn(e, sl=sl, bk=bk, half=half):
                    base = sl * 2048 + half * 1024
                    ins = None
                    for ci in range(NC):
                        ins = e.transpose(
                            bankb(bk, ci * 128, 128),
                            wbw[:, base + ci * 128: base + (ci + 1) * 128],
                            ident[:])
                    return ins
                P.emit("tensor", tfn, waits=twaits, inc="T")
                wT_done[wrow_idx_box[0]] = P.c("T")
                P.emit("vector",
                       (lambda r_, bk_, bi_: lambda e: e.tensor_copy(
                           wtg_all2[bi_][:, :, r_ * 128:(r_ + 1) * 128],
                           bankb3(bk_)))(r, bk, bi),
                       waits=[("T", wT_done[wrow_idx_box[0]])], inc="V")
                pfree[bk] = [("V", P.c("V"))]
                wrow_idx_box[0] += 1
        wt_ready[k] = P.c("V")

    wt_mm_done = {}
    emit_w_lt(0)
    for k in range(6):
        wi, grp = divmod(k, 2)
        bi = k % 2
        if wi < 2:
            outsl = fsl if wi == 0 else gsl
            tiles = [(r, st) for r in range(4) for st in range(4)]
            for idx, (r, st) in enumerate(tiles):
                if idx == 8 and k + 1 < 6:
                    emit_w_lt(k + 1)
                dj = grp * 4 + r
                qkv_mm_tilegroup(
                    (lambda r_, bi_: lambda ci: WTg2(bi_, ci, r_ * 128, 128))(r, bi),
                    (lambda st_: lambda ci: xfm(ci, st_ * 512, 512))(st),
                    (lambda dj_, st_, o_=outsl: lambda: o_(dj_, st_ * 512, 512))(dj, st),
                    512, [("V", max(wt_ready[k], XFM_ALL_V))], "scalar")
        else:
            for si in range(NS):
                if si == 8 and k + 1 < 6:
                    emit_w_lt(k + 1)
                qkv_mm_tilegroup(
                    (lambda si_, bi_: lambda ci: xfm(ci, si_ * 128, 128))(si, bi),
                    (lambda bi_: lambda ci: WTg2(bi_, ci, 0, 512))(bi),
                    (lambda si_, grp_: lambda: hsl(si_, grp_ * 512, 512))(si, grp),
                    512, [("V", max(wt_ready[k], XFM_ALL_V))], "vector")
        wt_mm_done[k] = P.c("T")

    A_END_T = P.c("T")
    A_END_V = P.c("V")
    A_END_S = P.c("Sc")

    for i in range(8):
        P.emit("gpsimd",
               (lambda i_: lambda e: e.dma_start(
                   out=w1bf[i_ * 128:(i_ + 1) * 128, :],
                   in_=w1[i_ * 128:(i_ + 1) * 128, :]))(i),
               inc="LCW", inc_by=16)
    for i in range(16):
        P.emit("gpsimd",
               (lambda i_: lambda e: e.dma_start(
                   out=w2bf[i_ * 256:(i_ + 1) * 256, :],
                   in_=w2[i_ * 256:(i_ + 1) * 256, :]))(i),
               inc="LCW", inc_by=16)
    LCW_ALL = 384

    P.emit("scalar", lambda e: e.dma_start(
        out=rA_f[:, 4 * 1024:16 * 1024].rearrange("p (a b) -> p a b", a=12),
        in_=bass.AP(x.tensor, 4 * 128 * 1024, [[1024, 128], [128 * 1024, 12], [1, 1024]])),
        waits=[("T", A_END_T)], inc="LXB", inc_by=16)

    sc_done, exp_done, trT_done = {}, {}, {}
    ebsm_evac, ebsm_evac_h1, av_done, oevac_done = {}, {}, {}, {}
    nm_done = {}

    def emit_score_tile(ti, st, extra_waits):
        par = ti % 2
        bk = par * 4 + st
        def tfn(e, ti=ti, st=st, bk=bk):
            ins = None
            for d in range(NC):
                ins = e.matmul(bank(bk, 0, 512), gsl(d, ti * 128, 128),
                               fsl(d, st * 512, 512), start=(d == 0), stop=(d == NC - 1))
            return ins
        P.emit("tensor", tfn, waits=extra_waits, inc="T")
        sc_done[(ti, st)] = P.c("T")
        P.emit("vector",
               (lambda par_, st_, bk_: lambda e: e.reduce_max(
                   NM(par_, st_), bank(bk_, 0, 512), axis=AX, negate=True))(par, st, bk),
               waits=[("T", sc_done[(ti, st)])], inc="V")

    def emit_negmax_combine(ti):
        par = ti % 2
        P.emit("vector",
               (lambda par_: lambda e: e.tensor_reduce(
                   NEGMAX(par_), NMALL(par_), axis=AX, op=OP.min))(par), inc="V")
        nm_done[ti] = P.c("V")

    def emit_softmax(ti):
        par = ti % 2
        g0 = par * 4
        waits = [("V", nm_done[ti])]
        if ti >= 2:
            waits.append(ebsm_evac[ti - 2])
        for st in range(4):
            P.emit("scalar",
                   (lambda st_, g0_, par_: lambda e: e.activation(
                       out=ebT[par_][:, st_ * 512:(st_ + 1) * 512],
                       in_=bank(g0_ + st_, 0, 512), func=AF.Exp,
                       bias=NEGMAX(par_), scale=1.0,
                       accum_out=ZP(par_, st_)))(st, g0, par),
                   waits=(waits if st == 0 else []), inc="Sc")
        exp_done[ti] = P.c("Sc")

    lte_cnt = [0, 0]
    tail_trT = {}

    def emit_transposes(ti):
        par = ti % 2
        eb = ebsm2[par]
        if 2 <= ti < NS - 2:
            waits = [("Sc", exp_done[ti])]
            if ti >= 2:
                waits.append(("T", av_done.get(ti - 2, 0)))
                waits.append((f"LTE{par}", lte_cnt[par]))
            P.emit("sync",
                   (lambda par_, eb_=eb: lambda e: e.dma_start_transpose(
                       eb_[:].rearrange("p (k c) -> p k c", k=16),
                       ebT[par_][:].rearrange("p (k c) -> p k c", k=16)))(par),
                   waits=waits, inc=f"LTE{par}", inc_by=16)
            lte_cnt[par] += 16
            ebsm_evac[ti] = (f"LTE{par}", lte_cnt[par])
        else:
            g0 = par * 4
            def tfn(e, par=par, g0=g0):
                ins = None
                for k in range(NS):
                    ins = e.transpose(
                        ps_bf[:, g0 * 1024 + k * 128: g0 * 1024 + (k + 1) * 128],
                        ebT[par][:, k * 128:(k + 1) * 128], ident[:])
                return ins
            P.emit("tensor", tfn,
                   waits=[("Sc", exp_done[ti]), (f"LTE{par}", lte_cnt[par])],
                   inc="T")
            trT = P.c("T")
            tail_trT[ti] = trT
            P.emit("vector",
                   (lambda g0_, eb_=eb: lambda e: e.tensor_copy(
                       eb_[:, 0:2048], ps_bf[:, g0_ * 1024: g0_ * 1024 + 2048]))(g0),
                   waits=[("T", trT), ("T", av_done.get(ti - 2, 0))], inc="V")
            ebsm_evac[ti] = ("V", P.c("V"))
        P.emit("vector",
               (lambda par_: lambda e: e.tensor_reduce(
                   ZCOL, ZPALL(par_), axis=AX, op=OP.add))(par),
               waits=[("Sc", exp_done[ti]), ("LMISC", LMISC_ALL)], inc="V")
        P.emit("vector", lambda e: e.reciprocal(ZICOL, ZCOL), inc="V")
        P.emit("vector",
               (lambda par_: lambda e: e.tensor_mul(ZG2[par_], ZICOL, gbc[:]))(par),
               inc="V")

    def emit_av(ti):
        g0 = (ti % 2) * 4
        eb = ebsm2[ti % 2]
        for ct in range(2):
            bk = g0 + 2 + ct
            def tfn_a(e, ct=ct, bk=bk, eb=eb):
                ins = None
                for k in range(NS):
                    ins = e.matmul(bank(bk, 0, 512), eb[:, k * 128:(k + 1) * 128],
                                   hsl(k, ct * 512, 512),
                                   start=(k == 0), stop=(k == NS - 1))
                return ins
            P.emit("tensor", tfn_a, waits=[ebsm_evac[ti]], inc="T")
        av_done[ti] = P.c("T")

    oevac_h1 = {}

    def emit_oevac(ti):
        g0 = (ti % 2) * 4
        xw = ("LXA", 16) if ti < 4 else ("LXB", 16)
        for ct in range(2):
            P.emit("vector",
                   (lambda ti_, ct_, bk_: lambda e: e.scalar_tensor_tensor(
                       out=resid2(ti_, ct_), in0=bank(bk_, 0, 512), scalar=ZG2[ti_ % 2],
                       in1=resid2(ti_, ct_), op0=OP.mult, op1=OP.add))(ti, ct, g0 + 2 + ct),
                   waits=[("T", av_done[ti]), xw], inc="V")
            if ct == 0:
                oevac_h1[ti] = P.c("V")
        oevac_done[ti] = P.c("V")

    R1COL = lambda ti: stats[:, 32 + ti:33 + ti]
    M1COL = lambda ti: stats[:, 48 + ti:49 + ti]
    BNS = stats[:, 21:22]
    BVS2 = [stats[:, 23:24], stats[:, 27:28]]
    BLL2 = [stats[:, 24:25], stats[:, 28:29]]
    K1TMP = [stats[:, 25:26], stats[:, 26:27]]

    bstats_sc = {}

    def emit_ln1_bstats(ti):
        P.emit("vector",
               (lambda t_: lambda e: e.reduce_sum(BNS, resid(t_), axis=AX,
                                                  negate=True))(ti), inc="V")
        P.emit("vector",
               (lambda t_: lambda e: e.tensor_scalar_mul(
                   M1COL(t_), BNS, 1.0 / 1024))(ti), inc="V")
        vwaits = []
        if ti >= 2:
            vwaits.append(("Sc", bstats_sc[ti - 2]))
        P.emit("vector",
               (lambda t_: lambda e: e.scalar_tensor_tensor(
                   out=varsink[:], in0=resid(t_), scalar=M1COL(t_), in1=resid(t_),
                   op0=OP.add, op1=OP.mult, accum_out=BVS2[t_ % 2]))(ti),
               waits=vwaits, inc="V")
        vv = P.c("V")
        P.emit("scalar",
               (lambda t_: lambda e: e.activation(
                   out=BLL2[t_ % 2], in_=BVS2[t_ % 2], func=AF.Ln,
                   scale=1.0 / 1024, bias=EPS))(ti),
               waits=[("V", vv)], inc="Sc")
        P.emit("scalar",
               (lambda t_: lambda e: e.activation(out=R1COL(t_), in_=BLL2[t_ % 2],
                                                  func=AF.Exp, scale=-0.5))(ti),
               inc="Sc")
        bstats_sc[ti] = P.c("Sc")

    LNS = lambda b: stats[:, b:b + 1]
    LNM = lambda b: stats[:, b + 1:b + 2]
    LNV4 = lambda b: stats[:, b + 2:b + 3]
    LNL = lambda b: stats[:, b + 3:b + 4]
    LNR = lambda b: stats[:, b + 4:b + 5]
    LNK = lambda b: stats[:, b + 5:b + 6]
    vcast_n = {}

    def emit_lnb_v1(chains):
        for b, cch in chains:
            P.emit("vector",
                   (lambda c_, b_: lambda e: e.reduce_sum(
                       LNS(b_), resid(c_), axis=AX, negate=True))(cch, b), inc="V")
        for b, cch in chains:
            P.emit("vector",
                   (lambda b_: lambda e: e.tensor_scalar_mul(
                       LNM(b_), LNS(b_), 1.0 / 1024))(b), inc="V")
        for b, cch in chains:
            P.emit("vector",
                   (lambda c_, b_: lambda e: e.scalar_tensor_tensor(
                       out=varsink[:], in0=resid(c_), scalar=LNM(b_), in1=resid(c_),
                       op0=OP.add, op1=OP.mult, accum_out=LNV4(b_)))(cch, b), inc="V")
        return P.c("V")

    def emit_lnb_sc(chains, vwait):
        for b, cch in chains:
            P.emit("scalar",
                   (lambda b_: lambda e: e.activation(
                       out=LNL(b_), in_=LNV4(b_), func=AF.Ln,
                       scale=1.0 / 1024, bias=EPS))(b),
                   waits=[("V", vwait)], inc="Sc")
        for b, cch in chains:
            P.emit("scalar",
                   (lambda b_: lambda e: e.activation(
                       out=LNR(b_), in_=LNL(b_), func=AF.Exp, scale=-0.5))(b),
                   inc="Sc")
        return P.c("Sc")

    def emit_lnb_v2(chains, scwait, cast_e8=None, store=()):
        for b, cch in chains:
            P.emit("vector",
                   (lambda b_: lambda e: e.tensor_mul(LNK(b_), LNM(b_), LNR(b_)))(b),
                   waits=[("Sc", scwait)], inc="V")
        if cast_e8 is not None:
            slot = (cast_e8 % 2) * 2
            for sj, (b, cch) in enumerate(chains[-2:]):
                P.emit("vector",
                       (lambda c_, b_, sl_=slot + sj: lambda e: e.tensor_scalar(
                           out=xc[sl_][:], in0=resid(c_), scalar1=LNR(b_),
                           scalar2=LNK(b_), op0=OP.mult, op1=OP.add))(cch, b),
                       inc="V")
            vcast_n[cast_e8] = P.c("V")
        for b, cch in chains:
            P.emit("vector",
                   (lambda c_, b_: lambda e: e.tensor_scalar(
                       out=resid(c_), in0=resid(c_), scalar1=LNR(b_), scalar2=LNK(b_),
                       op0=OP.mult, op1=OP.add))(cch, b), inc="V")
        for b, cch in store:
            vfin = P.c("V")
            P.emit("sync",
                   (lambda c_: lambda e: e.dma_start(
                       out=y[c_ * 128:(c_ + 1) * 128, :], in_=resid(c_)))(cch),
                   waits=[("V", vfin)], inc="STY", inc_by=16)

    vcast_n_early = {}

    def emit_ln1_final_c0():
        scw = bstats_sc[1]
        P.emit("vector", lambda e: e.engine_nop(), waits=[("T", A_END_T)], inc="V")
        for sj, cch in enumerate((0, 1)):
            P.emit("vector",
                   (lambda c_, sj_: lambda e: e.tensor_mul(
                       K1TMP[sj_], M1COL(c_), R1COL(c_)))(cch, sj),
                   waits=[("Sc", scw)], inc="V")
        for sj, cch in enumerate((0, 1)):
            P.emit("vector",
                   (lambda c_, sl_, sj_: lambda e: e.tensor_scalar(
                       out=xc[sl_][:], in0=resid(c_), scalar1=R1COL(c_),
                       scalar2=K1TMP[sj_], op0=OP.mult, op1=OP.add))(cch, sj, sj),
                   inc="V")
        vcast_n_early[0] = P.c("V")
        for sj, cch in enumerate((0, 1)):
            P.emit("vector",
                   (lambda c_, sj_: lambda e: e.tensor_scalar(
                       out=resid(c_), in0=resid(c_), scalar1=R1COL(c_),
                       scalar2=K1TMP[sj_], op0=OP.mult, op1=OP.add))(cch, sj),
                   inc="V")

    for ti in (0, 1):
        for st in range(4):
            bk = (ti % 2) * 4 + st
            emit_score_tile(ti, st, [("V", A_END_V), ("Sc", A_END_S)] + pfree[bk])
        emit_negmax_combine(ti)

    for tj in range(NS):
        emit_softmax(tj)
        emit_transposes(tj)
        if tj >= 1:
            emit_ln1_bstats(tj - 1)
        emit_av(tj)
        emit_oevac(tj)
        if tj + 2 < NS:
            for st in (0, 1):
                ws = [("Sc", exp_done[tj])]
                if tj < 2:
                    ws.append(ebsm_evac[tj])
                emit_score_tile(tj + 2, st, ws)
            emit_score_tile(tj + 2, 2, [("V", oevac_h1[tj])])
            emit_score_tile(tj + 2, 3, [("V", oevac_done[tj])])
            emit_negmax_combine(tj + 2)
        if tj == NS - 1:
            emit_ln1_bstats(tj)
        if tj == 4:
            emit_ln1_final_c0()
        if tj == 13:
            t_after_s15 = P.c("T")
            w1_all = rFG[:, 0:32768].rearrange("p (ei h) -> p ei h", ei=8)
            w2r3 = lambda i: rH[:, i * 4096:(i + 1) * 4096].rearrange(
                "p (a b) -> p a b", a=4)
            def w1dma(k):
                P.emit("gpsimd",
                       (lambda k_: lambda e: e.dma_start(
                           out=w1_all[:, :, k_ * 512:(k_ + 1) * 512],
                           in_=bass.AP(w1bf.tensor, k_ * 512,
                                       [[4096, 128], [128 * 4096, 8], [1, 512]])))(k),
                       waits=[("T", t_after_s15), ("LCW", LCW_ALL)],
                       inc=f"LW1S{k}", inc_by=16)
            def w2rdma(i):
                P.emit("gpsimd",
                       (lambda i_: lambda e: e.dma_start(
                           out=w2r3(i_),
                           in_=bass.AP(w2bf.tensor, i_ * 4 * 128 * 1024,
                                       [[1024, 128], [128 * 1024, 4], [1, 1024]])))(i),
                       waits=[("T", P.c("T")), ("LCW", LCW_ALL)],
                       inc=f"LW2R{i}", inc_by=16)
            w1dma(0)
    B_END_T = P.c("T")
    B_END_V = P.c("V")
    w1dma(1)
    w1dma(2)
    w1dma(3)
    w2rdma(0)
    w2rdma(1)
    w1dma(4)
    w2rdma(2)
    w2rdma(3)
    w1dma(5)
    w1dma(6)
    w1dma(7)

    xcT_done, x2fm_evac = {}, {}
    ffn1_done, gelu_done, ffn2_done, r2evac_done = {}, {}, {}, {}
    lw2s_cnt = [0, 0, 0, 0]
    w2s_seq = {}
    FB = (2, 3, 0)
    hid3 = [rE[:, 2048:2304], rE[:, 2304:2560], rE[:, 2560:2816]]

    ltc_cnt = [0, 0]
    x2fm_ready = {}

    def emit_c_head_t(e8):
        slot = (e8 % 2) * 2
        par = e8 % 2
        for sj in range(2):
            waits = [("V", vcast_n[e8])]
            if sj == 0:
                if e8 == 0:
                    waits.append(("T", tail_trT[14]))
                elif e8 == 1:
                    waits.append(("T", B_END_T))
                else:
                    waits.append(("T", ffn1_done[(e8 - 2) * NH + NH - 1]))
                    waits.append((f"LTC{par}", ltc_cnt[par]))
            P.emit("sync",
                   (lambda sj_, sl_, par_: lambda e: e.dma_start_transpose(
                       x2f_all2[par_][:, :, sj_ * 128:(sj_ + 1) * 128],
                       xc[sl_ + sj_][:].rearrange("p (k c) -> p k c", k=8)))(
                       sj, slot, par),
                   waits=waits, inc=f"LTC{par}", inc_by=16)
            ltc_cnt[par] += 16
        x2fm_ready[e8] = (f"LTC{par}", ltc_cnt[par])

    def w2s_prefetch(g, gate_g):
        e8, hc = divmod(g, NH)
        sl = hc % 4
        if gate_g is None:
            waits = [("T", B_END_T)]
        else:
            waits = [("T", ffn2_done[gate_g])]
        waits.append(("LCW", LCW_ALL))
        lw2s_cnt[sl] += 1
        w2s_seq[g] = lw2s_cnt[sl]
        P.emit("gpsimd",
               (lambda hc_, sl_: lambda e: e.dma_start(
                   out=w2s[sl_], in_=w2bf[hc_ * 128:(hc_ + 1) * 128, :]))(hc, sl),
               waits=waits, inc=f"LW2S{sl}", inc_by=16)

    for k in range(4):
        w2s_prefetch(W2_RES + k, None)

    def emit_ffn1(g):
        e8, hc = divmod(g, NH)
        fbk = FB[g % 3]
        waits = [x2fm_ready[e8], (f"LW1S{hc // 4}", 16)]
        if g - 3 >= 0:
            waits.append(("Sc", gelu_done[g - 3]))
        else:
            waits.append(("V", oevac_done[14]))
            if g == 2:
                waits.append(ebsm_evac[14])
        def tfn(e, hc=hc, fbk=fbk, e8=e8):
            ins = None
            xb_ = x2fm2[e8 % 2]
            for ci in range(NC):
                ins = e.matmul(bank(fbk, 0, 256), w1sl(ci, hc * 128, 128),
                               xb_[:, ci * 256:(ci + 1) * 256],
                               start=(ci == 0), stop=(ci == NC - 1))
            return ins
        P.emit("tensor", tfn, waits=waits, inc="T")
        ffn1_done[g] = P.c("T")

    def emit_gelu(g):
        e8, hc = divmod(g, NH)
        fbk = FB[g % 3]
        waits = [("T", ffn1_done[g]), ("LMISC", LMISC_ALL)]
        if g - 3 >= 0:
            waits.append(("T", ffn2_done[g - 3]))
        P.emit("scalar",
               (lambda hc_, fbk_, g_: lambda e: e.activation(
                   out=hid3[g_ % 3][:, 0:256], in_=bank(fbk_, 0, 256),
                   func=AF.Gelu, bias=b1sb[:, hc_:hc_ + 1], scale=1.0))(hc, fbk, g),
               waits=waits, inc="Sc")
        gelu_done[g] = P.c("Sc")

    def emit_ffn2(g):
        e8, hc = divmod(g, NH)
        waits = [("Sc", gelu_done[g])]
        if hc < W2_RES and e8 == 0:
            waits.append((f"LW2R{hc // 4}", 16))
        if hc == 0:
            if e8 > 0:
                waits.append(("V", r2evac_done[e8 - 1]))
            else:
                waits.append(("V", B_END_V))
        if hc >= W2_RES:
            waits.append((f"LW2S{hc % 4}", 16 * w2s_seq[g]))
        def tfn2(e, hc=hc, g=g):
            ins = None
            for sj in range(2):
                for et in range(2):
                    rhs = (w2res(hc, et * 512, 512) if hc < W2_RES
                           else w2s[hc % 4][:, et * 512:(et + 1) * 512])
                    ins = e.matmul(bank(4 + sj * 2 + et, 0, 512),
                                   hid3[g % 3][:, sj * 128:(sj + 1) * 128], rhs,
                                   start=(hc == 0), stop=False)
            return ins
        P.emit("tensor", tfn2, waits=waits, inc="T")
        ffn2_done[g] = P.c("T")
        if hc >= 16:
            nxt = g + 4 if hc <= 27 else (g - hc) + NH + 16 + (hc % 4)
            if nxt < 8 * NH and nxt % NH >= 16:
                w2s_prefetch(nxt, g)

    def emit_fold_r2(e8):
        c0, c1 = 2 * e8, 2 * e8 + 1
        def foldfn(e):
            ins = None
            for sj in range(2):
                for et in range(2):
                    ins = e.matmul(bank(4 + sj * 2 + et, 0, 512), ones1[0:1, :],
                                   b2rowb[0:1, et * 512:(et + 1) * 512],
                                   start=False, stop=True)
            return ins
        P.emit("tensor", foldfn, waits=[("LMISCG", 16)], inc="T")
        fold_t = P.c("T")
        for sj, cch in enumerate((c0, c1)):
            for et in range(2):
                P.emit("vector",
                       (lambda c_, sj_, et_: lambda e: e.tensor_tensor(
                           out=resid2(c_, et_), in0=bank(4 + sj_ * 2 + et_, 0, 512),
                           in1=resid2(c_, et_), op=OP.add))(cch, sj, et),
                       waits=[("T", fold_t)], inc="V")
        r2evac_done[e8] = P.c("V")

    def emit_ln1_final_c(e8):
        slot = (e8 % 2) * 2
        scw = bstats_sc[2 * e8 + 1]
        if e8 >= 2:
            P.emit("vector", lambda e: e.engine_nop(),
                   waits=[x2fm_ready[e8 - 2]], inc="V")
        for sj, cch in enumerate((2 * e8, 2 * e8 + 1)):
            P.emit("vector",
                   (lambda c_, sj_: lambda e: e.tensor_mul(
                       K1TMP[sj_], M1COL(c_), R1COL(c_)))(cch, sj),
                   waits=[("Sc", scw)], inc="V")
        for sj, cch in enumerate((2 * e8, 2 * e8 + 1)):
            P.emit("vector",
                   (lambda c_, sl_, sj_: lambda e: e.tensor_scalar(
                       out=xc[sl_][:], in0=resid(c_), scalar1=R1COL(c_),
                       scalar2=K1TMP[sj_], op0=OP.mult, op1=OP.add))(
                       cch, slot + sj, sj),
                   inc="V")
        vcast_n[e8] = P.c("V")
        for sj, cch in enumerate((2 * e8, 2 * e8 + 1)):
            P.emit("vector",
                   (lambda c_, sj_: lambda e: e.tensor_scalar(
                       out=resid(c_), in0=resid(c_), scalar1=R1COL(c_),
                       scalar2=K1TMP[sj_], op0=OP.mult, op1=OP.add))(cch, sj),
                   inc="V")

    vcast_n[0] = vcast_n_early[0]
    emit_c_head_t(0)
    lnch, lnb_vw, lnb_sw = None, None, None
    for g in range(8 * NH):
        e8, hc = divmod(g, NH)
        emit_ffn1(g)
        if g >= 3:
            gg = g - 3
            emit_ffn2(gg)
            if gg % NH == NH - 1:
                emit_fold_r2(gg // NH)
        emit_gelu(g)
        if hc == 2 and e8 > 0:
            lnch = [(0, 2 * (e8 - 1)), (6, 2 * (e8 - 1) + 1)]
            lnb_vw = emit_lnb_v1(lnch)
        if hc == 6 and e8 + 1 < 8:
            emit_ln1_final_c(e8 + 1)
        if hc == 8 and e8 > 0:
            lnb_sw = emit_lnb_sc(lnch, lnb_vw)
        if hc == 10 and e8 > 0:
            emit_lnb_v2(lnch, lnb_sw, store=lnch)
        if hc == 12 and e8 + 1 < 8:
            emit_c_head_t(e8 + 1)
    emit_ffn2(8 * NH - 3)
    emit_ffn2(8 * NH - 2)
    emit_ffn2(8 * NH - 1)
    emit_fold_r2(7)

    ch7 = [(0, 14), (6, 15)]
    vw7 = emit_lnb_v1(ch7)
    sw7 = emit_lnb_sc(ch7, vw7)
    emit_lnb_v2(ch7, sw7, store=ch7)
    P.emit("sync", lambda e: e.wait_ge(sems["STY"], 16 * NS))

    block = ctx.enter_context(nc.Block())

    def emit_stream(eng_name):
        def run(eng):
            last = {}
            for fn, waits, inc, inc_by in P.ops[eng_name]:
                for s, v in waits:
                    if last.get(s, 0) < v:
                        eng.wait_ge(sems[s], v)
                        last[s] = v
                ins = fn(eng)
                if inc is not None:
                    if ins is None:
                        ins = eng.nop()
                    ins.then_inc(sems[inc], inc_by)
        return run

    block.tensor(emit_stream("tensor"))
    block.vector(emit_stream("vector"))
    block.scalar(emit_stream("scalar"))
    block.gpsimd(emit_stream("gpsimd"))
    block.sync(emit_stream("sync"))

    ctx.close()
    return nc


def make_in_map(inputs, core):
    return {
        "x": np.ascontiguousarray(inputs["L0_attn_x"][core]),
        "wq": np.ascontiguousarray(inputs["L0_attn_Wq"]),
        "wk": np.ascontiguousarray(inputs["L0_attn_Wk"]),
        "wv": np.ascontiguousarray(inputs["L0_attn_Wv"]),
        "gamma": np.ascontiguousarray(inputs["L0_attn_gamma"]).reshape(1, 1),
        "w1": np.ascontiguousarray(inputs["L0_ffn_w1"]),
        "b1": np.ascontiguousarray(inputs["L0_ffn_b1"]),
        "w2": np.ascontiguousarray(inputs["L0_ffn_w2"]),
        "b2": np.ascontiguousarray(inputs["L0_ffn_b2"]),
    }


def make_in_map(inputs, core):
    return {
        "x": np.ascontiguousarray(inputs["L0_attn_x"][core]),
        "wq": np.ascontiguousarray(inputs["L0_attn_Wq"]),
        "wk": np.ascontiguousarray(inputs["L0_attn_Wk"]),
        "wv": np.ascontiguousarray(inputs["L0_attn_Wv"]),
        "gamma": np.ascontiguousarray(inputs["L0_attn_gamma"]).reshape(1, 1),
        "w1": np.ascontiguousarray(inputs["L0_ffn_w1"]),
        "b1": np.ascontiguousarray(inputs["L0_ffn_b1"]),
        "w2": np.ascontiguousarray(inputs["L0_ffn_w2"]),
        "b2": np.ascontiguousarray(inputs["L0_ffn_b2"]),
    }


def make_in_map(inputs, core):
    return {
        "x": np.ascontiguousarray(inputs["L0_attn_x"][core]),
        "wq": np.ascontiguousarray(inputs["L0_attn_Wq"]),
        "wk": np.ascontiguousarray(inputs["L0_attn_Wk"]),
        "wv": np.ascontiguousarray(inputs["L0_attn_Wv"]),
        "gamma": np.ascontiguousarray(inputs["L0_attn_gamma"]).reshape(1, 1),
        "w1": np.ascontiguousarray(inputs["L0_ffn_w1"]),
        "b1": np.ascontiguousarray(inputs["L0_ffn_b1"]),
        "w2": np.ascontiguousarray(inputs["L0_ffn_w2"]),
        "b2": np.ascontiguousarray(inputs["L0_ffn_b2"]),
    }


def make_in_map(inputs, core):
    return {
        "x": np.ascontiguousarray(inputs["L0_attn_x"][core]),
        "wq": np.ascontiguousarray(inputs["L0_attn_Wq"]),
        "wk": np.ascontiguousarray(inputs["L0_attn_Wk"]),
        "wv": np.ascontiguousarray(inputs["L0_attn_Wv"]),
        "gamma": np.ascontiguousarray(inputs["L0_attn_gamma"]).reshape(1, 1),
        "w1": np.ascontiguousarray(inputs["L0_ffn_w1"]),
        "b1": np.ascontiguousarray(inputs["L0_ffn_b1"]),
        "w2": np.ascontiguousarray(inputs["L0_ffn_w2"]),
        "b2": np.ascontiguousarray(inputs["L0_ffn_b2"]),
    }


def make_in_map(inputs, core):
    return {
        "x": np.ascontiguousarray(inputs["L0_attn_x"][core]),
        "wq": np.ascontiguousarray(inputs["L0_attn_Wq"]),
        "wk": np.ascontiguousarray(inputs["L0_attn_Wk"]),
        "wv": np.ascontiguousarray(inputs["L0_attn_Wv"]),
        "gamma": np.ascontiguousarray(inputs["L0_attn_gamma"]).reshape(1, 1),
        "w1": np.ascontiguousarray(inputs["L0_ffn_w1"]),
        "b1": np.ascontiguousarray(inputs["L0_ffn_b1"]),
        "w2": np.ascontiguousarray(inputs["L0_ffn_w2"]),
        "b2": np.ascontiguousarray(inputs["L0_ffn_b2"]),
    }


def make_in_map(inputs, core):
    return {
        "x": np.ascontiguousarray(inputs["L0_attn_x"][core]),
        "wq": np.ascontiguousarray(inputs["L0_attn_Wq"]),
        "wk": np.ascontiguousarray(inputs["L0_attn_Wk"]),
        "wv": np.ascontiguousarray(inputs["L0_attn_Wv"]),
        "gamma": np.ascontiguousarray(inputs["L0_attn_gamma"]).reshape(1, 1),
        "w1": np.ascontiguousarray(inputs["L0_ffn_w1"]),
        "b1": np.ascontiguousarray(inputs["L0_ffn_b1"]),
        "w2": np.ascontiguousarray(inputs["L0_ffn_w2"]),
        "b2": np.ascontiguousarray(inputs["L0_ffn_b2"]),
    }



def make_in_map(inputs, core):
    return {
        "x": np.ascontiguousarray(inputs["L0_attn_x"][core]),
        "wq": np.ascontiguousarray(inputs["L0_attn_Wq"]),
        "wk": np.ascontiguousarray(inputs["L0_attn_Wk"]),
        "wv": np.ascontiguousarray(inputs["L0_attn_Wv"]),
        "gamma": np.ascontiguousarray(inputs["L0_attn_gamma"]).reshape(1, 1),
        "w1": np.ascontiguousarray(inputs["L0_ffn_w1"]),
        "b1": np.ascontiguousarray(inputs["L0_ffn_b1"]),
        "w2": np.ascontiguousarray(inputs["L0_ffn_w2"]),
        "b2": np.ascontiguousarray(inputs["L0_ffn_b2"]),
    }



def make_in_map(inputs, core):
    return {
        "x": np.ascontiguousarray(inputs["L0_attn_x"][core]),
        "wq": np.ascontiguousarray(inputs["L0_attn_Wq"]),
        "wk": np.ascontiguousarray(inputs["L0_attn_Wk"]),
        "wv": np.ascontiguousarray(inputs["L0_attn_Wv"]),
        "gamma": np.ascontiguousarray(inputs["L0_attn_gamma"]).reshape(1, 1),
        "w1": np.ascontiguousarray(inputs["L0_ffn_w1"]),
        "b1": np.ascontiguousarray(inputs["L0_ffn_b1"]),
        "w2": np.ascontiguousarray(inputs["L0_ffn_w2"]),
        "b2": np.ascontiguousarray(inputs["L0_ffn_b2"]),
    }



def make_in_map(inputs, core):
    return {
        "x": np.ascontiguousarray(inputs["L0_attn_x"][core]),
        "wq": np.ascontiguousarray(inputs["L0_attn_Wq"]),
        "wk": np.ascontiguousarray(inputs["L0_attn_Wk"]),
        "wv": np.ascontiguousarray(inputs["L0_attn_Wv"]),
        "gamma": np.ascontiguousarray(inputs["L0_attn_gamma"]).reshape(1, 1),
        "w1": np.ascontiguousarray(inputs["L0_ffn_w1"]),
        "b1": np.ascontiguousarray(inputs["L0_ffn_b1"]),
        "w2": np.ascontiguousarray(inputs["L0_ffn_w2"]),
        "b2": np.ascontiguousarray(inputs["L0_ffn_b2"]),
    }



def make_in_map(inputs, core):
    return {
        "x": np.ascontiguousarray(inputs["L0_attn_x"][core]),
        "wq": np.ascontiguousarray(inputs["L0_attn_Wq"]),
        "wk": np.ascontiguousarray(inputs["L0_attn_Wk"]),
        "wv": np.ascontiguousarray(inputs["L0_attn_Wv"]),
        "gamma": np.ascontiguousarray(inputs["L0_attn_gamma"]).reshape(1, 1),
        "w1": np.ascontiguousarray(inputs["L0_ffn_w1"]),
        "b1": np.ascontiguousarray(inputs["L0_ffn_b1"]),
        "w2": np.ascontiguousarray(inputs["L0_ffn_w2"]),
        "b2": np.ascontiguousarray(inputs["L0_ffn_b2"]),
    }



def make_in_map(inputs, core):
    return {
        "x": np.ascontiguousarray(inputs["L0_attn_x"][core]),
        "wq": np.ascontiguousarray(inputs["L0_attn_Wq"]),
        "wk": np.ascontiguousarray(inputs["L0_attn_Wk"]),
        "wv": np.ascontiguousarray(inputs["L0_attn_Wv"]),
        "gamma": np.ascontiguousarray(inputs["L0_attn_gamma"]).reshape(1, 1),
        "w1": np.ascontiguousarray(inputs["L0_ffn_w1"]),
        "b1": np.ascontiguousarray(inputs["L0_ffn_b1"]),
        "w2": np.ascontiguousarray(inputs["L0_ffn_w2"]),
        "b2": np.ascontiguousarray(inputs["L0_ffn_b2"]),
    }



def make_in_map(inputs, core):
    return {
        "x": np.ascontiguousarray(inputs["L0_attn_x"][core]),
        "wq": np.ascontiguousarray(inputs["L0_attn_Wq"]),
        "wk": np.ascontiguousarray(inputs["L0_attn_Wk"]),
        "wv": np.ascontiguousarray(inputs["L0_attn_Wv"]),
        "gamma": np.ascontiguousarray(inputs["L0_attn_gamma"]).reshape(1, 1),
        "w1": np.ascontiguousarray(inputs["L0_ffn_w1"]),
        "b1": np.ascontiguousarray(inputs["L0_ffn_b1"]),
        "w2": np.ascontiguousarray(inputs["L0_ffn_w2"]),
        "b2": np.ascontiguousarray(inputs["L0_ffn_b2"]),
    }



def make_in_map(inputs, core):
    return {
        "x": np.ascontiguousarray(inputs["L0_attn_x"][core]),
        "wq": np.ascontiguousarray(inputs["L0_attn_Wq"]),
        "wk": np.ascontiguousarray(inputs["L0_attn_Wk"]),
        "wv": np.ascontiguousarray(inputs["L0_attn_Wv"]),
        "gamma": np.ascontiguousarray(inputs["L0_attn_gamma"]).reshape(1, 1),
        "w1": np.ascontiguousarray(inputs["L0_ffn_w1"]),
        "b1": np.ascontiguousarray(inputs["L0_ffn_b1"]),
        "w2": np.ascontiguousarray(inputs["L0_ffn_w2"]),
        "b2": np.ascontiguousarray(inputs["L0_ffn_b2"]),
    }



def make_in_map(inputs, core):
    return {
        "x": np.ascontiguousarray(inputs["L0_attn_x"][core]),
        "wq": np.ascontiguousarray(inputs["L0_attn_Wq"]),
        "wk": np.ascontiguousarray(inputs["L0_attn_Wk"]),
        "wv": np.ascontiguousarray(inputs["L0_attn_Wv"]),
        "gamma": np.ascontiguousarray(inputs["L0_attn_gamma"]).reshape(1, 1),
        "w1": np.ascontiguousarray(inputs["L0_ffn_w1"]),
        "b1": np.ascontiguousarray(inputs["L0_ffn_b1"]),
        "w2": np.ascontiguousarray(inputs["L0_ffn_w2"]),
        "b2": np.ascontiguousarray(inputs["L0_ffn_b2"]),
    }



def make_in_map(inputs, core):
    return {
        "x": np.ascontiguousarray(inputs["L0_attn_x"][core]),
        "wq": np.ascontiguousarray(inputs["L0_attn_Wq"]),
        "wk": np.ascontiguousarray(inputs["L0_attn_Wk"]),
        "wv": np.ascontiguousarray(inputs["L0_attn_Wv"]),
        "gamma": np.ascontiguousarray(inputs["L0_attn_gamma"]).reshape(1, 1),
        "w1": np.ascontiguousarray(inputs["L0_ffn_w1"]),
        "b1": np.ascontiguousarray(inputs["L0_ffn_b1"]),
        "w2": np.ascontiguousarray(inputs["L0_ffn_w2"]),
        "b2": np.ascontiguousarray(inputs["L0_ffn_b2"]),
    }



def make_in_map(inputs, core):
    return {
        "x": np.ascontiguousarray(inputs["L0_attn_x"][core]),
        "wq": np.ascontiguousarray(inputs["L0_attn_Wq"]),
        "wk": np.ascontiguousarray(inputs["L0_attn_Wk"]),
        "wv": np.ascontiguousarray(inputs["L0_attn_Wv"]),
        "gamma": np.ascontiguousarray(inputs["L0_attn_gamma"]).reshape(1, 1),
        "w1": np.ascontiguousarray(inputs["L0_ffn_w1"]),
        "b1": np.ascontiguousarray(inputs["L0_ffn_b1"]),
        "w2": np.ascontiguousarray(inputs["L0_ffn_w2"]),
        "b2": np.ascontiguousarray(inputs["L0_ffn_b2"]),
    }



def make_in_map(inputs, core):
    return {
        "x": np.ascontiguousarray(inputs["L0_attn_x"][core]),
        "wq": np.ascontiguousarray(inputs["L0_attn_Wq"]),
        "wk": np.ascontiguousarray(inputs["L0_attn_Wk"]),
        "wv": np.ascontiguousarray(inputs["L0_attn_Wv"]),
        "gamma": np.ascontiguousarray(inputs["L0_attn_gamma"]).reshape(1, 1),
        "w1": np.ascontiguousarray(inputs["L0_ffn_w1"]),
        "b1": np.ascontiguousarray(inputs["L0_ffn_b1"]),
        "w2": np.ascontiguousarray(inputs["L0_ffn_w2"]),
        "b2": np.ascontiguousarray(inputs["L0_ffn_b2"]),
    }



def make_in_map(inputs, core):
    return {
        "x": np.ascontiguousarray(inputs["L0_attn_x"][core]),
        "wq": np.ascontiguousarray(inputs["L0_attn_Wq"]),
        "wk": np.ascontiguousarray(inputs["L0_attn_Wk"]),
        "wv": np.ascontiguousarray(inputs["L0_attn_Wv"]),
        "gamma": np.ascontiguousarray(inputs["L0_attn_gamma"]).reshape(1, 1),
        "w1": np.ascontiguousarray(inputs["L0_ffn_w1"]),
        "b1": np.ascontiguousarray(inputs["L0_ffn_b1"]),
        "w2": np.ascontiguousarray(inputs["L0_ffn_w2"]),
        "b2": np.ascontiguousarray(inputs["L0_ffn_b2"]),
    }



def make_in_map(inputs, core):
    return {
        "x": np.ascontiguousarray(inputs["L0_attn_x"][core]),
        "wq": np.ascontiguousarray(inputs["L0_attn_Wq"]),
        "wk": np.ascontiguousarray(inputs["L0_attn_Wk"]),
        "wv": np.ascontiguousarray(inputs["L0_attn_Wv"]),
        "gamma": np.ascontiguousarray(inputs["L0_attn_gamma"]).reshape(1, 1),
        "w1": np.ascontiguousarray(inputs["L0_ffn_w1"]),
        "b1": np.ascontiguousarray(inputs["L0_ffn_b1"]),
        "w2": np.ascontiguousarray(inputs["L0_ffn_w2"]),
        "b2": np.ascontiguousarray(inputs["L0_ffn_b2"]),
    }



def make_in_map(inputs, core):
    return {
        "x": np.ascontiguousarray(inputs["L0_attn_x"][core]),
        "wq": np.ascontiguousarray(inputs["L0_attn_Wq"]),
        "wk": np.ascontiguousarray(inputs["L0_attn_Wk"]),
        "wv": np.ascontiguousarray(inputs["L0_attn_Wv"]),
        "gamma": np.ascontiguousarray(inputs["L0_attn_gamma"]).reshape(1, 1),
        "w1": np.ascontiguousarray(inputs["L0_ffn_w1"]),
        "b1": np.ascontiguousarray(inputs["L0_ffn_b1"]),
        "w2": np.ascontiguousarray(inputs["L0_ffn_w2"]),
        "b2": np.ascontiguousarray(inputs["L0_ffn_b2"]),
    }



def make_in_map(inputs, core):
    return {
        "x": np.ascontiguousarray(inputs["L0_attn_x"][core]),
        "wq": np.ascontiguousarray(inputs["L0_attn_Wq"]),
        "wk": np.ascontiguousarray(inputs["L0_attn_Wk"]),
        "wv": np.ascontiguousarray(inputs["L0_attn_Wv"]),
        "gamma": np.ascontiguousarray(inputs["L0_attn_gamma"]).reshape(1, 1),
        "w1": np.ascontiguousarray(inputs["L0_ffn_w1"]),
        "b1": np.ascontiguousarray(inputs["L0_ffn_b1"]),
        "w2": np.ascontiguousarray(inputs["L0_ffn_w2"]),
        "b2": np.ascontiguousarray(inputs["L0_ffn_b2"]),
    }



def make_in_map(inputs, core):
    return {
        "x": np.ascontiguousarray(inputs["L0_attn_x"][core]),
        "wq": np.ascontiguousarray(inputs["L0_attn_Wq"]),
        "wk": np.ascontiguousarray(inputs["L0_attn_Wk"]),
        "wv": np.ascontiguousarray(inputs["L0_attn_Wv"]),
        "gamma": np.ascontiguousarray(inputs["L0_attn_gamma"]).reshape(1, 1),
        "w1": np.ascontiguousarray(inputs["L0_ffn_w1"]),
        "b1": np.ascontiguousarray(inputs["L0_ffn_b1"]),
        "w2": np.ascontiguousarray(inputs["L0_ffn_w2"]),
        "b2": np.ascontiguousarray(inputs["L0_ffn_b2"]),
    }



def make_in_map(inputs, core):
    return {
        "x": np.ascontiguousarray(inputs["L0_attn_x"][core]),
        "wq": np.ascontiguousarray(inputs["L0_attn_Wq"]),
        "wk": np.ascontiguousarray(inputs["L0_attn_Wk"]),
        "wv": np.ascontiguousarray(inputs["L0_attn_Wv"]),
        "gamma": np.ascontiguousarray(inputs["L0_attn_gamma"]).reshape(1, 1),
        "w1": np.ascontiguousarray(inputs["L0_ffn_w1"]),
        "b1": np.ascontiguousarray(inputs["L0_ffn_b1"]),
        "w2": np.ascontiguousarray(inputs["L0_ffn_w2"]),
        "b2": np.ascontiguousarray(inputs["L0_ffn_b2"]),
    }



def make_in_map(inputs, core):
    return {
        "x": np.ascontiguousarray(inputs["L0_attn_x"][core]),
        "wq": np.ascontiguousarray(inputs["L0_attn_Wq"]),
        "wk": np.ascontiguousarray(inputs["L0_attn_Wk"]),
        "wv": np.ascontiguousarray(inputs["L0_attn_Wv"]),
        "gamma": np.ascontiguousarray(inputs["L0_attn_gamma"]).reshape(1, 1),
        "w1": np.ascontiguousarray(inputs["L0_ffn_w1"]),
        "b1": np.ascontiguousarray(inputs["L0_ffn_b1"]),
        "w2": np.ascontiguousarray(inputs["L0_ffn_w2"]),
        "b2": np.ascontiguousarray(inputs["L0_ffn_b2"]),
    }



def make_in_map(inputs, core):
    return {
        "x": np.ascontiguousarray(inputs["L0_attn_x"][core]),
        "wq": np.ascontiguousarray(inputs["L0_attn_Wq"]),
        "wk": np.ascontiguousarray(inputs["L0_attn_Wk"]),
        "wv": np.ascontiguousarray(inputs["L0_attn_Wv"]),
        "gamma": np.ascontiguousarray(inputs["L0_attn_gamma"]).reshape(1, 1),
        "w1": np.ascontiguousarray(inputs["L0_ffn_w1"]),
        "b1": np.ascontiguousarray(inputs["L0_ffn_b1"]),
        "w2": np.ascontiguousarray(inputs["L0_ffn_w2"]),
        "b2": np.ascontiguousarray(inputs["L0_ffn_b2"]),
    }



def make_in_map(inputs, core):
    return {
        "x": np.ascontiguousarray(inputs["L0_attn_x"][core]),
        "wq": np.ascontiguousarray(inputs["L0_attn_Wq"]),
        "wk": np.ascontiguousarray(inputs["L0_attn_Wk"]),
        "wv": np.ascontiguousarray(inputs["L0_attn_Wv"]),
        "gamma": np.ascontiguousarray(inputs["L0_attn_gamma"]).reshape(1, 1),
        "w1": np.ascontiguousarray(inputs["L0_ffn_w1"]),
        "b1": np.ascontiguousarray(inputs["L0_ffn_b1"]),
        "w2": np.ascontiguousarray(inputs["L0_ffn_w2"]),
        "b2": np.ascontiguousarray(inputs["L0_ffn_b2"]),
    }



def make_in_map(inputs, core):
    return {
        "x": np.ascontiguousarray(inputs["L0_attn_x"][core]),
        "wq": np.ascontiguousarray(inputs["L0_attn_Wq"]),
        "wk": np.ascontiguousarray(inputs["L0_attn_Wk"]),
        "wv": np.ascontiguousarray(inputs["L0_attn_Wv"]),
        "gamma": np.ascontiguousarray(inputs["L0_attn_gamma"]).reshape(1, 1),
        "w1": np.ascontiguousarray(inputs["L0_ffn_w1"]),
        "b1": np.ascontiguousarray(inputs["L0_ffn_b1"]),
        "w2": np.ascontiguousarray(inputs["L0_ffn_w2"]),
        "b2": np.ascontiguousarray(inputs["L0_ffn_b2"]),
    }



def make_in_map(inputs, core):
    return {
        "x": np.ascontiguousarray(inputs["L0_attn_x"][core]),
        "wq": np.ascontiguousarray(inputs["L0_attn_Wq"]),
        "wk": np.ascontiguousarray(inputs["L0_attn_Wk"]),
        "wv": np.ascontiguousarray(inputs["L0_attn_Wv"]),
        "gamma": np.ascontiguousarray(inputs["L0_attn_gamma"]).reshape(1, 1),
        "w1": np.ascontiguousarray(inputs["L0_ffn_w1"]),
        "b1": np.ascontiguousarray(inputs["L0_ffn_b1"]),
        "w2": np.ascontiguousarray(inputs["L0_ffn_w2"]),
        "b2": np.ascontiguousarray(inputs["L0_ffn_b2"]),
    }




def make_in_map(inputs, core):
    f32 = np.float32
    return {
        "x": np.ascontiguousarray(np.asarray(inputs["L0_attn_x"], dtype=f32)[core]),
        "wq": np.ascontiguousarray(np.asarray(inputs["L0_attn_Wq"], dtype=f32)),
        "wk": np.ascontiguousarray(np.asarray(inputs["L0_attn_Wk"], dtype=f32)),
        "wv": np.ascontiguousarray(np.asarray(inputs["L0_attn_Wv"], dtype=f32)),
        "gamma": np.ascontiguousarray(np.asarray(inputs["L0_attn_gamma"], dtype=f32)).reshape(1, 1),
        "w1": np.ascontiguousarray(np.asarray(inputs["L0_ffn_w1"], dtype=f32)),
        "b1": np.ascontiguousarray(np.asarray(inputs["L0_ffn_b1"], dtype=f32)),
        "w2": np.ascontiguousarray(np.asarray(inputs["L0_ffn_w2"], dtype=f32)),
        "b2": np.ascontiguousarray(np.asarray(inputs["L0_ffn_b2"], dtype=f32)),
    }


def kernel(**inputs):
    from concourse.bass_utils import run_bass_kernel_spmd
    nc = build_nc()
    in_maps = [make_in_map(inputs, core) for core in range(8)]
    res = run_bass_kernel_spmd(nc, in_maps, core_ids=list(range(8)))
    return np.stack([np.asarray(res.results[i]["y"], dtype=np.float32)
                     for i in range(8)], axis=0)


# version 35
# speedup vs baseline: 1.1373x; 1.0149x over previous
import contextlib
import functools
import numpy as np
import concourse.bass as bass
import concourse.mybir as mybir

F32 = mybir.dt.float32
BF16 = mybir.dt.bfloat16
U8 = mybir.dt.uint8
AX = mybir.AxisListType.X
OP = mybir.AluOpType
AF = mybir.ActivationFunctionType

S, C, HD = 2048, 1024, 4096
NS, NC, NH = 16, 8, 32
W2_RES = 16
EPS = 1e-5

ENGS = ("tensor", "vector", "scalar", "gpsimd", "sync")
STRICT = True
STRICT_ENGS = {"vector", "scalar"}
ENG_SEM = {"tensor": "T", "vector": "V", "scalar": "Sc", "gpsimd": "G"}


class Plan:
    def __init__(self):
        self.ops = {e: [] for e in ENGS}
        self.cnt = {}
        self.strict_now = set()

    def emit(self, eng, fn, waits=(), inc=None, inc_by=1):
        waits = [(s, v) for (s, v) in waits if v > 0]
        if eng in ENG_SEM:
            es = ENG_SEM[eng]
            if (eng == "gpsimd" or (STRICT and eng in STRICT_ENGS)
                    or eng in self.strict_now) and self.c(es) > 0:
                waits.append((es, self.c(es)))
            if inc is None:
                inc = es
        self.ops[eng].append((fn, waits, inc, inc_by))
        if inc is not None:
            self.cnt[inc] = self.cnt.get(inc, 0) + inc_by

    def c(self, sem):
        return self.cnt.get(sem, 0)


def build_nc():
    nc = bass.Bass("TRN2", target_bir_lowering=False)
    x = nc.dram_tensor("x", [S, C], F32, kind="ExternalInput").ap()
    wq = nc.dram_tensor("wq", [C, C], F32, kind="ExternalInput").ap()
    wk = nc.dram_tensor("wk", [C, C], F32, kind="ExternalInput").ap()
    wv = nc.dram_tensor("wv", [C, C], F32, kind="ExternalInput").ap()
    gamma = nc.dram_tensor("gamma", [1, 1], F32, kind="ExternalInput").ap()
    w1 = nc.dram_tensor("w1", [C, HD], F32, kind="ExternalInput").ap()
    b1 = nc.dram_tensor("b1", [HD], F32, kind="ExternalInput").ap()
    w2 = nc.dram_tensor("w2", [HD, C], F32, kind="ExternalInput").ap()
    b2 = nc.dram_tensor("b2", [C], F32, kind="ExternalInput").ap()
    y = nc.dram_tensor("y", [S, C], F32, kind="ExternalOutput").ap()
    w1bf = nc.dram_tensor("w1bf", [C, HD], BF16, kind="Internal").ap()
    w2bf = nc.dram_tensor("w2bf", [HD, C], BF16, kind="Internal").ap()

    ctx = contextlib.ExitStack()
    sb = lambda n, s, d: ctx.enter_context(nc.sbuf_tensor(n, s, d))
    regA = sb("regA", [128, 65536], U8)
    regFG = sb("regFG", [128, 65536], U8)
    regH = sb("regH", [128, 32768], U8)
    regE = sb("regE", [128, 20480], U8)
    rA_f = regA.bitcast(F32)
    rA_b = regA.bitcast(BF16)
    rFG = regFG.bitcast(BF16)
    rH = regH.bitcast(BF16)
    rE = regE.bitcast(BF16)

    resid = lambda ti: rA_f[:, ti * 1024:(ti + 1) * 1024]
    resid2 = lambda ti, et: rA_f[:, ti * 1024 + et * 512: ti * 1024 + et * 512 + 512]

    _wtg_base = (12288, 8192)
    wtg_all2 = [rA_b[:, b:b + 4096].rearrange("p (ci d) -> p ci d", ci=8)
                for b in _wtg_base]
    WTg2 = lambda bi, ci, off, n: rA_b[:, _wtg_base[bi] + ci * 512 + off:
                                       _wtg_base[bi] + ci * 512 + off + n]
    xfm_all = rA_b[:, 16384:32768].rearrange("p (ci s) -> p ci s", ci=8)
    xfm = lambda ci, s0, n: rA_b[:, 16384 + ci * 2048 + s0: 16384 + ci * 2048 + s0 + n]
    fsl = lambda d, s0, n: rFG[:, d * 2048 + s0: d * 2048 + s0 + n]
    gsl = lambda d, s0, n: rFG[:, 16384 + d * 2048 + s0: 16384 + d * 2048 + s0 + n]
    w1sl = lambda ci, h0, n: rFG[:, ci * 4096 + h0: ci * 4096 + h0 + n]
    hsl = lambda si, c0, n: rH[:, si * 1024 + c0: si * 1024 + c0 + n]
    w2res = lambda hc, e0, n: rH[:, hc * 1024 + e0: hc * 1024 + e0 + n]
    ebT = [rE[:, 0:2048], rE[:, 2048:4096]]
    ebsm2 = [rE[:, 4096:6144], rE[:, 6144:8192]]
    x2fm2 = [rE[:, 0:2048], rE[:, 6144:8192]]
    x2f_all2 = [rE[:, 0:2048].rearrange("p (ci s) -> p ci s", ci=8),
                rE[:, 6144:8192].rearrange("p (ci s) -> p ci s", ci=8)]
    hid = [rE[:, 2048:2304], rE[:, 2304:2560]]
    w2s = [rE[:, 3072:4096], rE[:, 4096:5120], rE[:, 8192:9216], rE[:, 9216:10240]]

    ident = sb("ident", [128, 128], BF16)
    identf = sb("identf", [128, 128], F32)
    ones1 = sb("ones1", [1, 128], BF16)
    b2rowb = sb("b2rowb", [1, 1024], BF16)
    b1sb = sb("b1sb", [128, 32], F32)
    gbc = sb("gbc", [128, 1], F32)
    cdb = sb("cdb", [128, 4], F32)
    stats = sb("stats", [128, 64], F32)
    xch = sb("xch", [128, 8192], BF16)
    xc = [xch[:, k * 1024:(k + 1) * 1024] for k in range(8)]
    wbw = xch
    varsink = sb("varsink", [128, 1024], F32)

    CONSTS = [EPS, -0.5, 1.0 / 1024]
    for i, cv in enumerate(CONSTS):
        nc.const_aps.register(F32, cv, cdb[:, i:i + 1])

    NM = lambda par, st: stats[:, par * 4 + st: par * 4 + st + 1]
    NMALL = lambda par: stats[:, par * 4: par * 4 + 4]
    NEGMAX = lambda par: stats[:, 8 + par: 9 + par]
    ZP = lambda par, st: stats[:, 10 + par * 4 + st: 11 + par * 4 + st]
    ZPALL = lambda par: stats[:, 10 + par * 4: 14 + par * 4]
    ZCOL = stats[:, 18:19]
    ZICOL = stats[:, 19:20]
    ZG2 = [stats[:, 20:21], stats[:, 29:30]]
    NSUM = stats[:, 24:25]
    NEGM = stats[:, 25:26]
    VSUM = stats[:, 26:27]
    LNV = stats[:, 27:28]
    RSTD = stats[:, 28:29]
    KCOL = stats[:, 29:30]

    ps = ctx.enter_context(nc.psum_tensor("ps", [128, 4096], F32))
    ps_bf = ps.bitcast(BF16)
    bank = lambda b, off, n: ps[:, b * 512 + off: b * 512 + off + n]
    bankb = lambda b, off, n: ps_bf[:, b * 1024 + off: b * 1024 + off + n]
    bankb3 = lambda b: ps_bf[:, b * 1024:(b + 1) * 1024].rearrange(
        "p (ci s) -> p ci s", ci=8)

    sems = {}
    for name in ("T", "V", "Sc", "G", "LXC0", "LXC1", "LXC2", "LXC3",
                 "LWB0", "LWB1", "LWB2", "LWB3",
                 "LTE0", "LTE1", "LTC0", "LTC1",
                 "LW2S0", "LW2S1", "LW2S2", "LW2S3", "LXA", "LXB",
                 "LW1S0", "LW1S1", "LW1S2", "LW1S3",
                 "LW1S4", "LW1S5", "LW1S6", "LW1S7",
                 "LW2R0", "LW2R1", "LW2R2", "LW2R3", "LMISC", "LMISCG", "STY",
                 "LCW", "LXB2"):
        sems[name] = ctx.enter_context(nc.semaphore(name))

    P = Plan()
    pfree = {b: [] for b in range(8)}

    def v_init(v):
        v.memset(ident[:], 1.0)
        v.memset(identf[:], 1.0)
        v.memset(ones1[:], 1.0)
        ins = None
        for i, cv in enumerate(CONSTS):
            ins = v.memset(cdb[:, i:i + 1], cv)
        return ins
    P.emit("vector", v_init, inc="V")
    V_INIT = P.c("V")

    P.emit("gpsimd", lambda g: g.affine_select(
        out=ident[:], in_=ident[:], pattern=[[-1, 128]],
        compare_op=OP.is_equal, fill=0.0, base=0, channel_multiplier=1),
        waits=[("V", V_INIT)], inc="G")
    P.emit("gpsimd", lambda g: g.affine_select(
        out=identf[:], in_=identf[:], pattern=[[-1, 128]],
        compare_op=OP.is_equal, fill=0.0, base=0, channel_multiplier=1),
        inc="G")
    G_INIT = P.c("G")

    P.emit("sync", lambda e: e.dma_start(
        out=rA_f[:, 0:4 * 1024].rearrange("p (a b) -> p a b", a=4),
        in_=bass.AP(x.tensor, 0, [[1024, 128], [128 * 1024, 4], [1, 1024]])),
        inc="LXA", inc_by=16)
    P.emit("sync", lambda e: e.dma_start(
        out=varsink[0:32, 0:128], in_=bass.AP(b1.tensor, 0, [[128, 32], [1, 128]])),
        inc="LMISC", inc_by=16)
    P.emit("sync", lambda e: e.dma_start(
        out=gbc[:], in_=bass.AP(gamma.tensor, 0, [[0, 128], [1, 1]])),
        inc="LMISC", inc_by=16)
    P.emit("vector", lambda e: e.tensor_scalar_mul(gbc[:], gbc[:], 0.5),
           waits=[("LMISC", 32)], inc="V")
    P.emit("gpsimd", lambda e: e.dma_start(
        out=b2rowb[0:1, :], in_=bass.AP(b2.tensor, 0, [[1024, 1], [1, 1024]])),
        inc="LMISCG", inc_by=16)
    LMISC_ALL = P.c("LMISC")

    P.emit("tensor", lambda e: e.transpose(bank(6, 0, 32), varsink[0:32, 0:128],
                                           identf[0:32, 0:32]),
           waits=[("LMISC", 32), ("G", G_INIT)], inc="T")
    P.emit("vector", lambda e: e.tensor_copy(b1sb[:], bank(6, 0, 32)),
           waits=[("T", P.c("T"))], inc="V")
    pfree[6] = [("V", P.c("V"))]

    xT_done, xT_evac = {}, {}
    for pr in range(NS // 2):
        sl = pr % 4
        waits = [("T", xT_done[2 * pr - 7])] if pr >= 4 else []
        P.emit("gpsimd",
               (lambda pr_, sl_: lambda e: e.dma_start(
                   out=xch[:, sl_ * 2048:(sl_ + 1) * 2048].rearrange(
                       "p (a b) -> p a b", a=2),
                   in_=bass.AP(x.tensor, pr_ * 2 * 128 * 1024,
                               [[1024, 128], [128 * 1024, 2], [1, 1024]])))(pr, sl),
               waits=waits, inc=f"LXC{sl}", inc_by=16)
        for half in range(2):
            ti = 2 * pr + half
            bk = 4 + (ti % 2)
            twaits = [(f"LXC{sl}", 16 * (pr // 4 + 1)), ("G", G_INIT)] + pfree[bk]
            def tfn(e, ti=ti, bk=bk, sl=sl, half=half):
                ins = None
                for ci in range(NC):
                    ins = e.transpose(bankb(bk, ci * 128, 128),
                                      xc[sl * 2 + half][:, ci * 128:(ci + 1) * 128],
                                      ident[:])
                return ins
            P.emit("tensor", tfn, waits=twaits, inc="T")
            xT_done[ti] = P.c("T")
            P.emit("vector",
                   (lambda ti_, bk_: lambda e: e.tensor_copy(
                       xfm_all[:, :, ti_ * 128:(ti_ + 1) * 128], bankb3(bk_)))(ti, bk),
                   waits=[("T", xT_done[ti])], inc="V")
            xT_evac[ti] = P.c("V")
            pfree[bk] = [("V", xT_evac[ti])]
    XFM_ALL_V = P.c("V")

    wdram = {0: wq, 1: wk, 2: wv}
    rot = [0]

    def qkv_mm_tilegroup(lhsT_sl, rhs_sl, out_fn, N, extra_waits, evac_eng):
        bk = rot[0] % 4
        rot[0] += 1
        waits = list(extra_waits) + pfree[bk]
        def tfn(e, bk=bk, lhsT_sl=lhsT_sl, rhs_sl=rhs_sl, N=N):
            ins = None
            for ci in range(NC):
                ins = e.matmul(bank(bk, 0, N), lhsT_sl(ci), rhs_sl(ci),
                               start=(ci == 0), stop=(ci == NC - 1))
            return ins
        P.emit("tensor", tfn, waits=waits, inc="T")
        tcnt = P.c("T")
        sem = "Sc" if evac_eng == "scalar" else "V"
        def efn(e, bk=bk, out_fn=out_fn, N=N):
            out = out_fn()
            return (e.copy(out, bank(bk, 0, N)) if evac_eng == "scalar"
                    else e.tensor_copy(out, bank(bk, 0, N)))
        P.emit(evac_eng, efn, waits=[("T", tcnt)], inc=sem)
        pfree[bk] = [(sem, P.c(sem))]

    wT_done = {}
    lwb_cnt = [0, 0, 0, 0]
    wrow_idx_box = [0]
    wt_ready = {}

    def emit_w_lt(k):
        wi, grp = divmod(k, 2)
        bi = k % 2
        for rp in range(2):
            wrow_idx = wrow_idx_box[0]
            wp = wrow_idx // 2
            sl = wp % 4
            dj0 = grp * 4 + rp * 2
            if wp >= 4:
                waits = [("T", wT_done[wrow_idx - 7])]
            else:
                waits = [("T", xT_done[9 + 2 * wp])]
            P.emit("gpsimd",
                   (lambda wi_, dj_, sl_: lambda e: e.dma_start(
                       out=wbw[:, sl_ * 2048:(sl_ + 1) * 2048].rearrange(
                           "p (a b) -> p a b", a=2),
                       in_=bass.AP(wdram[wi_].tensor, dj_ * 128 * 1024,
                                   [[1024, 128], [128 * 1024, 2], [1, 1024]])))(
                       wi, dj0, sl),
                   waits=waits, inc=f"LWB{sl}", inc_by=16)
            lwb_cnt[sl] += 1
            for half in range(2):
                r = rp * 2 + half
                bk = 4 + (wrow_idx_box[0] % 4)
                twaits = [(f"LWB{sl}", 16 * lwb_cnt[sl]), ("G", G_INIT)] + pfree[bk]
                if k >= 2:
                    twaits.append(("T", wt_mm_done[k - 2]))
                def tfn(e, sl=sl, bk=bk, half=half):
                    base = sl * 2048 + half * 1024
                    ins = None
                    for ci in range(NC):
                        ins = e.transpose(
                            bankb(bk, ci * 128, 128),
                            wbw[:, base + ci * 128: base + (ci + 1) * 128],
                            ident[:])
                    return ins
                P.emit("tensor", tfn, waits=twaits, inc="T")
                wT_done[wrow_idx_box[0]] = P.c("T")
                P.emit("vector",
                       (lambda r_, bk_, bi_: lambda e: e.tensor_copy(
                           wtg_all2[bi_][:, :, r_ * 128:(r_ + 1) * 128],
                           bankb3(bk_)))(r, bk, bi),
                       waits=[("T", wT_done[wrow_idx_box[0]])], inc="V")
                pfree[bk] = [("V", P.c("V"))]
                wrow_idx_box[0] += 1
        wt_ready[k] = P.c("V")

    wt_mm_done = {}
    emit_w_lt(0)
    for k in range(6):
        wi, grp = divmod(k, 2)
        bi = k % 2
        if wi < 2:
            outsl = fsl if wi == 0 else gsl
            tiles = [(r, st) for r in range(4) for st in range(4)]
            for idx, (r, st) in enumerate(tiles):
                if idx == 8 and k + 1 < 6:
                    emit_w_lt(k + 1)
                dj = grp * 4 + r
                qkv_mm_tilegroup(
                    (lambda r_, bi_: lambda ci: WTg2(bi_, ci, r_ * 128, 128))(r, bi),
                    (lambda st_: lambda ci: xfm(ci, st_ * 512, 512))(st),
                    (lambda dj_, st_, o_=outsl: lambda: o_(dj_, st_ * 512, 512))(dj, st),
                    512, [("V", max(wt_ready[k], XFM_ALL_V))], "scalar")
        else:
            for si in range(NS):
                if si == 8 and k + 1 < 6:
                    emit_w_lt(k + 1)
                qkv_mm_tilegroup(
                    (lambda si_, bi_: lambda ci: xfm(ci, si_ * 128, 128))(si, bi),
                    (lambda bi_: lambda ci: WTg2(bi_, ci, 0, 512))(bi),
                    (lambda si_, grp_: lambda: hsl(si_, grp_ * 512, 512))(si, grp),
                    512, [("V", max(wt_ready[k], XFM_ALL_V))], "vector")
        wt_mm_done[k] = P.c("T")

    A_END_T = P.c("T")
    A_END_V = P.c("V")
    A_END_S = P.c("Sc")

    for i in range(8):
        P.emit("gpsimd",
               (lambda i_: lambda e: e.dma_start(
                   out=w1bf[i_ * 128:(i_ + 1) * 128, :],
                   in_=w1[i_ * 128:(i_ + 1) * 128, :]))(i),
               inc="LCW", inc_by=16)
    for i in range(16):
        P.emit("gpsimd",
               (lambda i_: lambda e: e.dma_start(
                   out=w2bf[i_ * 256:(i_ + 1) * 256, :],
                   in_=w2[i_ * 256:(i_ + 1) * 256, :]))(i),
               inc="LCW", inc_by=16)
    LCW_ALL = 384

    P.emit("scalar", lambda e: e.dma_start(
        out=rA_f[:, 4 * 1024:16 * 1024].rearrange("p (a b) -> p a b", a=12),
        in_=bass.AP(x.tensor, 4 * 128 * 1024, [[1024, 128], [128 * 1024, 12], [1, 1024]])),
        waits=[("T", A_END_T)], inc="LXB", inc_by=16)

    sc_done, exp_done, trT_done = {}, {}, {}
    ebsm_evac, ebsm_evac_h1, av_done, oevac_done = {}, {}, {}, {}
    nm_done = {}

    def emit_score_tile(ti, st, extra_waits):
        par = ti % 2
        bk = par * 4 + st
        def tfn(e, ti=ti, st=st, bk=bk):
            ins = None
            for d in range(NC):
                ins = e.matmul(bank(bk, 0, 512), gsl(d, ti * 128, 128),
                               fsl(d, st * 512, 512), start=(d == 0), stop=(d == NC - 1))
            return ins
        P.emit("tensor", tfn, waits=extra_waits, inc="T")
        sc_done[(ti, st)] = P.c("T")
        P.emit("vector",
               (lambda par_, st_, bk_: lambda e: e.reduce_max(
                   NM(par_, st_), bank(bk_, 0, 512), axis=AX, negate=True))(par, st, bk),
               waits=[("T", sc_done[(ti, st)])], inc="V")

    def emit_negmax_combine(ti):
        par = ti % 2
        P.emit("vector",
               (lambda par_: lambda e: e.tensor_reduce(
                   NEGMAX(par_), NMALL(par_), axis=AX, op=OP.min))(par), inc="V")
        nm_done[ti] = P.c("V")

    def emit_softmax(ti):
        par = ti % 2
        g0 = par * 4
        waits = [("V", nm_done[ti])]
        if ti >= 2:
            waits.append(ebsm_evac[ti - 2])
        for st in range(4):
            P.emit("scalar",
                   (lambda st_, g0_, par_: lambda e: e.activation(
                       out=ebT[par_][:, st_ * 512:(st_ + 1) * 512],
                       in_=bank(g0_ + st_, 0, 512), func=AF.Exp,
                       bias=NEGMAX(par_), scale=1.0,
                       accum_out=ZP(par_, st_)))(st, g0, par),
                   waits=(waits if st == 0 else []), inc="Sc")
        exp_done[ti] = P.c("Sc")

    lte_cnt = [0, 0]
    tail_trT = {}

    def emit_transposes(ti):
        par = ti % 2
        eb = ebsm2[par]
        if 2 <= ti < NS - 2:
            waits = [("Sc", exp_done[ti])]
            if ti >= 2:
                waits.append(("T", av_done.get(ti - 2, 0)))
                waits.append((f"LTE{par}", lte_cnt[par]))
            P.emit("sync",
                   (lambda par_, eb_=eb: lambda e: e.dma_start_transpose(
                       eb_[:].rearrange("p (k c) -> p k c", k=16),
                       ebT[par_][:].rearrange("p (k c) -> p k c", k=16)))(par),
                   waits=waits, inc=f"LTE{par}", inc_by=16)
            lte_cnt[par] += 16
            ebsm_evac[ti] = (f"LTE{par}", lte_cnt[par])
        else:
            g0 = par * 4
            def tfn(e, par=par, g0=g0):
                ins = None
                for k in range(NS):
                    ins = e.transpose(
                        ps_bf[:, g0 * 1024 + k * 128: g0 * 1024 + (k + 1) * 128],
                        ebT[par][:, k * 128:(k + 1) * 128], ident[:])
                return ins
            P.emit("tensor", tfn,
                   waits=[("Sc", exp_done[ti]), (f"LTE{par}", lte_cnt[par])],
                   inc="T")
            trT = P.c("T")
            tail_trT[ti] = trT
            P.emit("vector",
                   (lambda g0_, eb_=eb: lambda e: e.tensor_copy(
                       eb_[:, 0:2048], ps_bf[:, g0_ * 1024: g0_ * 1024 + 2048]))(g0),
                   waits=[("T", trT), ("T", av_done.get(ti - 2, 0))], inc="V")
            ebsm_evac[ti] = ("V", P.c("V"))
        P.emit("vector",
               (lambda par_: lambda e: e.tensor_reduce(
                   ZCOL, ZPALL(par_), axis=AX, op=OP.add))(par),
               waits=[("Sc", exp_done[ti]), ("LMISC", LMISC_ALL)], inc="V")
        P.emit("vector", lambda e: e.reciprocal(ZICOL, ZCOL), inc="V")
        P.emit("vector",
               (lambda par_: lambda e: e.tensor_mul(ZG2[par_], ZICOL, gbc[:]))(par),
               inc="V")

    def emit_av(ti):
        g0 = (ti % 2) * 4
        eb = ebsm2[ti % 2]
        for ct in range(2):
            bk = g0 + 2 + ct
            def tfn_a(e, ct=ct, bk=bk, eb=eb):
                ins = None
                for k in range(NS):
                    ins = e.matmul(bank(bk, 0, 512), eb[:, k * 128:(k + 1) * 128],
                                   hsl(k, ct * 512, 512),
                                   start=(k == 0), stop=(k == NS - 1))
                return ins
            P.emit("tensor", tfn_a, waits=[ebsm_evac[ti]], inc="T")
        av_done[ti] = P.c("T")

    oevac_h1 = {}

    def emit_oevac(ti):
        g0 = (ti % 2) * 4
        xw = ("LXA", 16) if ti < 4 else ("LXB", 16)
        for ct in range(2):
            P.emit("vector",
                   (lambda ti_, ct_, bk_: lambda e: e.scalar_tensor_tensor(
                       out=resid2(ti_, ct_), in0=bank(bk_, 0, 512), scalar=ZG2[ti_ % 2],
                       in1=resid2(ti_, ct_), op0=OP.mult, op1=OP.add))(ti, ct, g0 + 2 + ct),
                   waits=[("T", av_done[ti]), xw], inc="V")
            if ct == 0:
                oevac_h1[ti] = P.c("V")
        oevac_done[ti] = P.c("V")

    R1COL = lambda ti: stats[:, 32 + ti:33 + ti]
    M1COL = lambda ti: stats[:, 48 + ti:49 + ti]
    BNS = stats[:, 21:22]
    BVS2 = [stats[:, 23:24], stats[:, 27:28]]
    BLL2 = [stats[:, 24:25], stats[:, 28:29]]
    K1TMP = [stats[:, 25:26], stats[:, 26:27]]

    bstats_sc = {}

    def emit_ln1_bstats(ti):
        P.emit("vector",
               (lambda t_: lambda e: e.reduce_sum(BNS, resid(t_), axis=AX,
                                                  negate=True))(ti), inc="V")
        P.emit("vector",
               (lambda t_: lambda e: e.tensor_scalar_mul(
                   M1COL(t_), BNS, 1.0 / 1024))(ti), inc="V")
        vwaits = []
        if ti >= 2:
            vwaits.append(("Sc", bstats_sc[ti - 2]))
        P.emit("vector",
               (lambda t_: lambda e: e.scalar_tensor_tensor(
                   out=varsink[:], in0=resid(t_), scalar=M1COL(t_), in1=resid(t_),
                   op0=OP.add, op1=OP.mult, accum_out=BVS2[t_ % 2]))(ti),
               waits=vwaits, inc="V")
        vv = P.c("V")
        P.emit("scalar",
               (lambda t_: lambda e: e.activation(
                   out=BLL2[t_ % 2], in_=BVS2[t_ % 2], func=AF.Ln,
                   scale=1.0 / 1024, bias=EPS))(ti),
               waits=[("V", vv)], inc="Sc")
        P.emit("scalar",
               (lambda t_: lambda e: e.activation(out=R1COL(t_), in_=BLL2[t_ % 2],
                                                  func=AF.Exp, scale=-0.5))(ti),
               inc="Sc")
        bstats_sc[ti] = P.c("Sc")

    LNS = lambda b: stats[:, b:b + 1]
    LNM = lambda b: stats[:, b + 1:b + 2]
    LNV4 = lambda b: stats[:, b + 2:b + 3]
    LNL = lambda b: stats[:, b + 3:b + 4]
    LNR = lambda b: stats[:, b + 4:b + 5]
    LNK = lambda b: stats[:, b + 5:b + 6]
    vcast_n = {}

    def emit_lnb_v1(chains):
        for b, cch in chains:
            P.emit("vector",
                   (lambda c_, b_: lambda e: e.reduce_sum(
                       LNS(b_), resid(c_), axis=AX, negate=True))(cch, b), inc="V")
        for b, cch in chains:
            P.emit("vector",
                   (lambda b_: lambda e: e.tensor_scalar_mul(
                       LNM(b_), LNS(b_), 1.0 / 1024))(b), inc="V")
        for b, cch in chains:
            P.emit("vector",
                   (lambda c_, b_: lambda e: e.scalar_tensor_tensor(
                       out=varsink[:], in0=resid(c_), scalar=LNM(b_), in1=resid(c_),
                       op0=OP.add, op1=OP.mult, accum_out=LNV4(b_)))(cch, b), inc="V")
        return P.c("V")

    def emit_lnb_sc(chains, vwait):
        for b, cch in chains:
            P.emit("scalar",
                   (lambda b_: lambda e: e.activation(
                       out=LNL(b_), in_=LNV4(b_), func=AF.Ln,
                       scale=1.0 / 1024, bias=EPS))(b),
                   waits=[("V", vwait)], inc="Sc")
        for b, cch in chains:
            P.emit("scalar",
                   (lambda b_: lambda e: e.activation(
                       out=LNR(b_), in_=LNL(b_), func=AF.Exp, scale=-0.5))(b),
                   inc="Sc")
        return P.c("Sc")

    def emit_lnb_v2(chains, scwait, cast_e8=None, store=()):
        for b, cch in chains:
            P.emit("vector",
                   (lambda b_: lambda e: e.tensor_mul(LNK(b_), LNM(b_), LNR(b_)))(b),
                   waits=[("Sc", scwait)], inc="V")
        if cast_e8 is not None:
            slot = (cast_e8 % 2) * 2
            for sj, (b, cch) in enumerate(chains[-2:]):
                P.emit("vector",
                       (lambda c_, b_, sl_=slot + sj: lambda e: e.tensor_scalar(
                           out=xc[sl_][:], in0=resid(c_), scalar1=LNR(b_),
                           scalar2=LNK(b_), op0=OP.mult, op1=OP.add))(cch, b),
                       inc="V")
            vcast_n[cast_e8] = P.c("V")
        for b, cch in chains:
            P.emit("vector",
                   (lambda c_, b_: lambda e: e.tensor_scalar(
                       out=resid(c_), in0=resid(c_), scalar1=LNR(b_), scalar2=LNK(b_),
                       op0=OP.mult, op1=OP.add))(cch, b), inc="V")
        for b, cch in store:
            vfin = P.c("V")
            P.emit("sync",
                   (lambda c_: lambda e: e.dma_start(
                       out=y[c_ * 128:(c_ + 1) * 128, :], in_=resid(c_)))(cch),
                   waits=[("V", vfin)], inc="STY", inc_by=16)

    vcast_n_early = {}

    def emit_ln1_final_c0():
        scw = bstats_sc[1]
        P.emit("vector", lambda e: e.engine_nop(), waits=[("T", A_END_T)], inc="V")
        for sj, cch in enumerate((0, 1)):
            P.emit("vector",
                   (lambda c_, sj_: lambda e: e.tensor_mul(
                       K1TMP[sj_], M1COL(c_), R1COL(c_)))(cch, sj),
                   waits=[("Sc", scw)], inc="V")
        for sj, cch in enumerate((0, 1)):
            P.emit("vector",
                   (lambda c_, sl_, sj_: lambda e: e.tensor_scalar(
                       out=xc[sl_][:], in0=resid(c_), scalar1=R1COL(c_),
                       scalar2=K1TMP[sj_], op0=OP.mult, op1=OP.add))(cch, sj, sj),
                   inc="V")
        vcast_n_early[0] = P.c("V")
        for sj, cch in enumerate((0, 1)):
            P.emit("vector",
                   (lambda c_, sj_: lambda e: e.tensor_scalar(
                       out=resid(c_), in0=resid(c_), scalar1=R1COL(c_),
                       scalar2=K1TMP[sj_], op0=OP.mult, op1=OP.add))(cch, sj),
                   inc="V")

    for ti in (0, 1):
        for st in range(4):
            bk = (ti % 2) * 4 + st
            emit_score_tile(ti, st, [("V", A_END_V), ("Sc", A_END_S)] + pfree[bk])
        emit_negmax_combine(ti)

    for tj in range(NS):
        emit_softmax(tj)
        emit_transposes(tj)
        if tj >= 1:
            emit_ln1_bstats(tj - 1)
        emit_av(tj)
        emit_oevac(tj)
        if tj + 2 < NS:
            for st in (0, 1):
                ws = [("Sc", exp_done[tj])]
                if tj < 2:
                    ws.append(ebsm_evac[tj])
                emit_score_tile(tj + 2, st, ws)
            emit_score_tile(tj + 2, 2, [("V", oevac_h1[tj])])
            emit_score_tile(tj + 2, 3, [("V", oevac_done[tj])])
            emit_negmax_combine(tj + 2)
        if tj == NS - 1:
            emit_ln1_bstats(tj)
        if tj == 4:
            emit_ln1_final_c0()
        if tj == 13:
            t_after_s15 = P.c("T")
            w1_all = rFG[:, 0:32768].rearrange("p (ei h) -> p ei h", ei=8)
            w2r3 = lambda i: rH[:, i * 4096:(i + 1) * 4096].rearrange(
                "p (a b) -> p a b", a=4)
            def w1dma(k):
                P.emit("gpsimd",
                       (lambda k_: lambda e: e.dma_start(
                           out=w1_all[:, :, k_ * 512:(k_ + 1) * 512],
                           in_=bass.AP(w1bf.tensor, k_ * 512,
                                       [[4096, 128], [128 * 4096, 8], [1, 512]])))(k),
                       waits=[("T", t_after_s15), ("LCW", LCW_ALL)],
                       inc=f"LW1S{k}", inc_by=16)
            def w2rdma(i):
                P.emit("gpsimd",
                       (lambda i_: lambda e: e.dma_start(
                           out=w2r3(i_),
                           in_=bass.AP(w2bf.tensor, i_ * 4 * 128 * 1024,
                                       [[1024, 128], [128 * 1024, 4], [1, 1024]])))(i),
                       waits=[("T", P.c("T")), ("LCW", LCW_ALL)],
                       inc=f"LW2R{i}", inc_by=16)
            w1dma(0)
    B_END_T = P.c("T")
    B_END_V = P.c("V")
    w1dma(1)
    w1dma(2)
    w1dma(3)
    w2rdma(0)
    w2rdma(1)
    w1dma(4)
    w2rdma(2)
    w2rdma(3)
    w1dma(5)
    w1dma(6)
    w1dma(7)

    xcT_done, x2fm_evac = {}, {}
    ffn1_done, gelu_done, ffn2_done, r2evac_done = {}, {}, {}, {}
    r2evac_h1 = {}
    lw2s_cnt = [0, 0, 0, 0]
    w2s_seq = {}
    FB = (2, 3, 0)
    hid3 = [rE[:, 2048:2304], rE[:, 2304:2560], rE[:, 2560:2816]]

    ltc_cnt = [0, 0]
    x2fm_ready = {}

    def emit_c_head_t(e8):
        slot = (e8 % 2) * 2
        par = e8 % 2
        for sj in range(2):
            waits = [("V", vcast_n[e8])]
            if sj == 0:
                if e8 == 0:
                    waits.append(("T", tail_trT[14]))
                elif e8 == 1:
                    waits.append(("T", B_END_T))
                else:
                    waits.append(("T", ffn1_done[(e8 - 2) * NH + NH - 1]))
                    waits.append((f"LTC{par}", ltc_cnt[par]))
            P.emit("sync",
                   (lambda sj_, sl_, par_: lambda e: e.dma_start_transpose(
                       x2f_all2[par_][:, :, sj_ * 128:(sj_ + 1) * 128],
                       xc[sl_ + sj_][:].rearrange("p (k c) -> p k c", k=8)))(
                       sj, slot, par),
                   waits=waits, inc=f"LTC{par}", inc_by=16)
            ltc_cnt[par] += 16
        x2fm_ready[e8] = (f"LTC{par}", ltc_cnt[par])

    def w2s_prefetch(g, gate_g):
        e8, hc = divmod(g, NH)
        sl = hc % 4
        if gate_g is None:
            waits = [("T", B_END_T)]
        else:
            waits = [("T", ffn2_done[gate_g])]
        waits.append(("LCW", LCW_ALL))
        lw2s_cnt[sl] += 1
        w2s_seq[g] = lw2s_cnt[sl]
        P.emit("gpsimd",
               (lambda hc_, sl_: lambda e: e.dma_start(
                   out=w2s[sl_], in_=w2bf[hc_ * 128:(hc_ + 1) * 128, :]))(hc, sl),
               waits=waits, inc=f"LW2S{sl}", inc_by=16)

    for k in range(4):
        w2s_prefetch(W2_RES + k, None)

    def emit_ffn1(g):
        e8, hc = divmod(g, NH)
        fbk = FB[g % 3]
        waits = [x2fm_ready[e8], (f"LW1S{hc // 4}", 16)]
        if g - 3 >= 0:
            waits.append(("Sc", gelu_done[g - 3]))
        else:
            waits.append(("V", oevac_done[14]))
            if g == 2:
                waits.append(ebsm_evac[14])
        def tfn(e, hc=hc, fbk=fbk, e8=e8):
            ins = None
            xb_ = x2fm2[e8 % 2]
            for ci in range(NC):
                ins = e.matmul(bank(fbk, 0, 256), w1sl(ci, hc * 128, 128),
                               xb_[:, ci * 256:(ci + 1) * 256],
                               start=(ci == 0), stop=(ci == NC - 1))
            return ins
        P.emit("tensor", tfn, waits=waits, inc="T")
        ffn1_done[g] = P.c("T")

    def emit_gelu(g):
        e8, hc = divmod(g, NH)
        fbk = FB[g % 3]
        waits = [("T", ffn1_done[g]), ("LMISC", LMISC_ALL)]
        if g - 3 >= 0:
            waits.append(("T", ffn2_done[g - 3]))
        P.emit("scalar",
               (lambda hc_, fbk_, g_: lambda e: e.activation(
                   out=hid3[g_ % 3][:, 0:256], in_=bank(fbk_, 0, 256),
                   func=AF.Gelu, bias=b1sb[:, hc_:hc_ + 1], scale=1.0))(hc, fbk, g),
               waits=waits, inc="Sc")
        gelu_done[g] = P.c("Sc")

    def emit_ffn2(g):
        e8, hc = divmod(g, NH)
        waits = [("Sc", gelu_done[g])]
        if hc < W2_RES and e8 == 0:
            waits.append((f"LW2R{hc // 4}", 16))
        if hc >= W2_RES:
            waits.append((f"LW2S{hc % 4}", 16 * w2s_seq[g]))
        def tfn2(e, hc=hc, g=g, sjs=(0, 1)):
            ins = None
            for sj in sjs:
                for et in range(2):
                    rhs = (w2res(hc, et * 512, 512) if hc < W2_RES
                           else w2s[hc % 4][:, et * 512:(et + 1) * 512])
                    ins = e.matmul(bank(4 + sj * 2 + et, 0, 512),
                                   hid3[g % 3][:, sj * 128:(sj + 1) * 128], rhs,
                                   start=(hc == 0), stop=False)
            return ins
        if hc == 0:
            if e8 > 0:
                w0, w1_ = [("V", r2evac_h1[e8 - 1])], [("V", r2evac_done[e8 - 1])]
            else:
                w0 = w1_ = [("V", B_END_V)]
            P.emit("tensor", functools.partial(tfn2, sjs=(0,)),
                   waits=waits + w0, inc="T")
            P.emit("tensor", functools.partial(tfn2, sjs=(1,)),
                   waits=w1_, inc="T")
        else:
            P.emit("tensor", tfn2, waits=waits, inc="T")
        ffn2_done[g] = P.c("T")
        if hc >= 16:
            nxt = g + 4 if hc <= 27 else (g - hc) + NH + 16 + (hc % 4)
            if nxt < 8 * NH and nxt % NH >= 16:
                w2s_prefetch(nxt, g)

    def emit_fold_r2(e8):
        c0, c1 = 2 * e8, 2 * e8 + 1
        def foldfn(e, sj):
            ins = None
            for et in range(2):
                ins = e.matmul(bank(4 + sj * 2 + et, 0, 512), ones1[0:1, :],
                               b2rowb[0:1, et * 512:(et + 1) * 512],
                               start=False, stop=True)
            return ins
        for sj, cch in enumerate((c0, c1)):
            P.emit("tensor", (lambda sj_: lambda e: foldfn(e, sj_))(sj),
                   waits=([("LMISCG", 16)] if sj == 0 else []), inc="T")
            fold_t = P.c("T")
            P.emit("vector",
                   (lambda c_, sj_: lambda e: e.tensor_tensor(
                       out=resid(c_), in0=ps[:, (4 + sj_ * 2) * 512:
                                              (6 + sj_ * 2) * 512],
                       in1=resid(c_), op=OP.add))(cch, sj),
                   waits=[("T", fold_t)], inc="V")
            if sj == 0:
                r2evac_h1[e8] = P.c("V")
        r2evac_done[e8] = P.c("V")

    def emit_ln1_final_c(e8):
        slot = (e8 % 2) * 2
        scw = bstats_sc[2 * e8 + 1]
        if e8 >= 2:
            P.emit("vector", lambda e: e.engine_nop(),
                   waits=[x2fm_ready[e8 - 2]], inc="V")
        for sj, cch in enumerate((2 * e8, 2 * e8 + 1)):
            P.emit("vector",
                   (lambda c_, sj_: lambda e: e.tensor_mul(
                       K1TMP[sj_], M1COL(c_), R1COL(c_)))(cch, sj),
                   waits=[("Sc", scw)], inc="V")
        for sj, cch in enumerate((2 * e8, 2 * e8 + 1)):
            P.emit("vector",
                   (lambda c_, sl_, sj_: lambda e: e.tensor_scalar(
                       out=xc[sl_][:], in0=resid(c_), scalar1=R1COL(c_),
                       scalar2=K1TMP[sj_], op0=OP.mult, op1=OP.add))(
                       cch, slot + sj, sj),
                   inc="V")
        vcast_n[e8] = P.c("V")
        for sj, cch in enumerate((2 * e8, 2 * e8 + 1)):
            P.emit("vector",
                   (lambda c_, sj_: lambda e: e.tensor_scalar(
                       out=resid(c_), in0=resid(c_), scalar1=R1COL(c_),
                       scalar2=K1TMP[sj_], op0=OP.mult, op1=OP.add))(cch, sj),
                   inc="V")

    vcast_n[0] = vcast_n_early[0]
    emit_c_head_t(0)
    lnch, lnb_vw, lnb_sw = None, None, None
    for g in range(8 * NH):
        e8, hc = divmod(g, NH)
        emit_ffn1(g)
        if g >= 3:
            gg = g - 3
            emit_ffn2(gg)
            if gg % NH == NH - 1:
                emit_fold_r2(gg // NH)
        emit_gelu(g)
        if hc == 2 and e8 > 0:
            lnch = [(0, 2 * (e8 - 1)), (6, 2 * (e8 - 1) + 1)]
            lnb_vw = emit_lnb_v1(lnch)
        if hc == 6 and e8 + 1 < 8:
            emit_ln1_final_c(e8 + 1)
        if hc == 8 and e8 > 0:
            lnb_sw = emit_lnb_sc(lnch, lnb_vw)
        if hc == 10 and e8 > 0:
            emit_lnb_v2(lnch, lnb_sw, store=lnch)
        if hc == 12 and e8 + 1 < 8:
            emit_c_head_t(e8 + 1)
    emit_ffn2(8 * NH - 3)
    emit_ffn2(8 * NH - 2)
    emit_ffn2(8 * NH - 1)
    emit_fold_r2(7)

    ch7 = [(0, 14), (6, 15)]
    vw7 = emit_lnb_v1(ch7)
    sw7 = emit_lnb_sc(ch7, vw7)
    emit_lnb_v2(ch7, sw7, store=ch7)
    P.emit("sync", lambda e: e.wait_ge(sems["STY"], 16 * NS))

    block = ctx.enter_context(nc.Block())

    def emit_stream(eng_name):
        def run(eng):
            last = {}
            for fn, waits, inc, inc_by in P.ops[eng_name]:
                for s, v in waits:
                    if last.get(s, 0) < v:
                        eng.wait_ge(sems[s], v)
                        last[s] = v
                ins = fn(eng)
                if inc is not None:
                    if ins is None:
                        ins = eng.nop()
                    ins.then_inc(sems[inc], inc_by)
        return run

    block.tensor(emit_stream("tensor"))
    block.vector(emit_stream("vector"))
    block.scalar(emit_stream("scalar"))
    block.gpsimd(emit_stream("gpsimd"))
    block.sync(emit_stream("sync"))

    ctx.close()
    return nc


def make_in_map(inputs, core):
    return {
        "x": np.ascontiguousarray(inputs["L0_attn_x"][core]),
        "wq": np.ascontiguousarray(inputs["L0_attn_Wq"]),
        "wk": np.ascontiguousarray(inputs["L0_attn_Wk"]),
        "wv": np.ascontiguousarray(inputs["L0_attn_Wv"]),
        "gamma": np.ascontiguousarray(inputs["L0_attn_gamma"]).reshape(1, 1),
        "w1": np.ascontiguousarray(inputs["L0_ffn_w1"]),
        "b1": np.ascontiguousarray(inputs["L0_ffn_b1"]),
        "w2": np.ascontiguousarray(inputs["L0_ffn_w2"]),
        "b2": np.ascontiguousarray(inputs["L0_ffn_b2"]),
    }


def make_in_map(inputs, core):
    return {
        "x": np.ascontiguousarray(inputs["L0_attn_x"][core]),
        "wq": np.ascontiguousarray(inputs["L0_attn_Wq"]),
        "wk": np.ascontiguousarray(inputs["L0_attn_Wk"]),
        "wv": np.ascontiguousarray(inputs["L0_attn_Wv"]),
        "gamma": np.ascontiguousarray(inputs["L0_attn_gamma"]).reshape(1, 1),
        "w1": np.ascontiguousarray(inputs["L0_ffn_w1"]),
        "b1": np.ascontiguousarray(inputs["L0_ffn_b1"]),
        "w2": np.ascontiguousarray(inputs["L0_ffn_w2"]),
        "b2": np.ascontiguousarray(inputs["L0_ffn_b2"]),
    }


def make_in_map(inputs, core):
    return {
        "x": np.ascontiguousarray(inputs["L0_attn_x"][core]),
        "wq": np.ascontiguousarray(inputs["L0_attn_Wq"]),
        "wk": np.ascontiguousarray(inputs["L0_attn_Wk"]),
        "wv": np.ascontiguousarray(inputs["L0_attn_Wv"]),
        "gamma": np.ascontiguousarray(inputs["L0_attn_gamma"]).reshape(1, 1),
        "w1": np.ascontiguousarray(inputs["L0_ffn_w1"]),
        "b1": np.ascontiguousarray(inputs["L0_ffn_b1"]),
        "w2": np.ascontiguousarray(inputs["L0_ffn_w2"]),
        "b2": np.ascontiguousarray(inputs["L0_ffn_b2"]),
    }


def make_in_map(inputs, core):
    return {
        "x": np.ascontiguousarray(inputs["L0_attn_x"][core]),
        "wq": np.ascontiguousarray(inputs["L0_attn_Wq"]),
        "wk": np.ascontiguousarray(inputs["L0_attn_Wk"]),
        "wv": np.ascontiguousarray(inputs["L0_attn_Wv"]),
        "gamma": np.ascontiguousarray(inputs["L0_attn_gamma"]).reshape(1, 1),
        "w1": np.ascontiguousarray(inputs["L0_ffn_w1"]),
        "b1": np.ascontiguousarray(inputs["L0_ffn_b1"]),
        "w2": np.ascontiguousarray(inputs["L0_ffn_w2"]),
        "b2": np.ascontiguousarray(inputs["L0_ffn_b2"]),
    }


def make_in_map(inputs, core):
    return {
        "x": np.ascontiguousarray(inputs["L0_attn_x"][core]),
        "wq": np.ascontiguousarray(inputs["L0_attn_Wq"]),
        "wk": np.ascontiguousarray(inputs["L0_attn_Wk"]),
        "wv": np.ascontiguousarray(inputs["L0_attn_Wv"]),
        "gamma": np.ascontiguousarray(inputs["L0_attn_gamma"]).reshape(1, 1),
        "w1": np.ascontiguousarray(inputs["L0_ffn_w1"]),
        "b1": np.ascontiguousarray(inputs["L0_ffn_b1"]),
        "w2": np.ascontiguousarray(inputs["L0_ffn_w2"]),
        "b2": np.ascontiguousarray(inputs["L0_ffn_b2"]),
    }


def make_in_map(inputs, core):
    return {
        "x": np.ascontiguousarray(inputs["L0_attn_x"][core]),
        "wq": np.ascontiguousarray(inputs["L0_attn_Wq"]),
        "wk": np.ascontiguousarray(inputs["L0_attn_Wk"]),
        "wv": np.ascontiguousarray(inputs["L0_attn_Wv"]),
        "gamma": np.ascontiguousarray(inputs["L0_attn_gamma"]).reshape(1, 1),
        "w1": np.ascontiguousarray(inputs["L0_ffn_w1"]),
        "b1": np.ascontiguousarray(inputs["L0_ffn_b1"]),
        "w2": np.ascontiguousarray(inputs["L0_ffn_w2"]),
        "b2": np.ascontiguousarray(inputs["L0_ffn_b2"]),
    }


def make_in_map(inputs, core):
    return {
        "x": np.ascontiguousarray(inputs["L0_attn_x"][core]),
        "wq": np.ascontiguousarray(inputs["L0_attn_Wq"]),
        "wk": np.ascontiguousarray(inputs["L0_attn_Wk"]),
        "wv": np.ascontiguousarray(inputs["L0_attn_Wv"]),
        "gamma": np.ascontiguousarray(inputs["L0_attn_gamma"]).reshape(1, 1),
        "w1": np.ascontiguousarray(inputs["L0_ffn_w1"]),
        "b1": np.ascontiguousarray(inputs["L0_ffn_b1"]),
        "w2": np.ascontiguousarray(inputs["L0_ffn_w2"]),
        "b2": np.ascontiguousarray(inputs["L0_ffn_b2"]),
    }



def make_in_map(inputs, core):
    return {
        "x": np.ascontiguousarray(inputs["L0_attn_x"][core]),
        "wq": np.ascontiguousarray(inputs["L0_attn_Wq"]),
        "wk": np.ascontiguousarray(inputs["L0_attn_Wk"]),
        "wv": np.ascontiguousarray(inputs["L0_attn_Wv"]),
        "gamma": np.ascontiguousarray(inputs["L0_attn_gamma"]).reshape(1, 1),
        "w1": np.ascontiguousarray(inputs["L0_ffn_w1"]),
        "b1": np.ascontiguousarray(inputs["L0_ffn_b1"]),
        "w2": np.ascontiguousarray(inputs["L0_ffn_w2"]),
        "b2": np.ascontiguousarray(inputs["L0_ffn_b2"]),
    }



def make_in_map(inputs, core):
    return {
        "x": np.ascontiguousarray(inputs["L0_attn_x"][core]),
        "wq": np.ascontiguousarray(inputs["L0_attn_Wq"]),
        "wk": np.ascontiguousarray(inputs["L0_attn_Wk"]),
        "wv": np.ascontiguousarray(inputs["L0_attn_Wv"]),
        "gamma": np.ascontiguousarray(inputs["L0_attn_gamma"]).reshape(1, 1),
        "w1": np.ascontiguousarray(inputs["L0_ffn_w1"]),
        "b1": np.ascontiguousarray(inputs["L0_ffn_b1"]),
        "w2": np.ascontiguousarray(inputs["L0_ffn_w2"]),
        "b2": np.ascontiguousarray(inputs["L0_ffn_b2"]),
    }



def make_in_map(inputs, core):
    return {
        "x": np.ascontiguousarray(inputs["L0_attn_x"][core]),
        "wq": np.ascontiguousarray(inputs["L0_attn_Wq"]),
        "wk": np.ascontiguousarray(inputs["L0_attn_Wk"]),
        "wv": np.ascontiguousarray(inputs["L0_attn_Wv"]),
        "gamma": np.ascontiguousarray(inputs["L0_attn_gamma"]).reshape(1, 1),
        "w1": np.ascontiguousarray(inputs["L0_ffn_w1"]),
        "b1": np.ascontiguousarray(inputs["L0_ffn_b1"]),
        "w2": np.ascontiguousarray(inputs["L0_ffn_w2"]),
        "b2": np.ascontiguousarray(inputs["L0_ffn_b2"]),
    }



def make_in_map(inputs, core):
    return {
        "x": np.ascontiguousarray(inputs["L0_attn_x"][core]),
        "wq": np.ascontiguousarray(inputs["L0_attn_Wq"]),
        "wk": np.ascontiguousarray(inputs["L0_attn_Wk"]),
        "wv": np.ascontiguousarray(inputs["L0_attn_Wv"]),
        "gamma": np.ascontiguousarray(inputs["L0_attn_gamma"]).reshape(1, 1),
        "w1": np.ascontiguousarray(inputs["L0_ffn_w1"]),
        "b1": np.ascontiguousarray(inputs["L0_ffn_b1"]),
        "w2": np.ascontiguousarray(inputs["L0_ffn_w2"]),
        "b2": np.ascontiguousarray(inputs["L0_ffn_b2"]),
    }



def make_in_map(inputs, core):
    return {
        "x": np.ascontiguousarray(inputs["L0_attn_x"][core]),
        "wq": np.ascontiguousarray(inputs["L0_attn_Wq"]),
        "wk": np.ascontiguousarray(inputs["L0_attn_Wk"]),
        "wv": np.ascontiguousarray(inputs["L0_attn_Wv"]),
        "gamma": np.ascontiguousarray(inputs["L0_attn_gamma"]).reshape(1, 1),
        "w1": np.ascontiguousarray(inputs["L0_ffn_w1"]),
        "b1": np.ascontiguousarray(inputs["L0_ffn_b1"]),
        "w2": np.ascontiguousarray(inputs["L0_ffn_w2"]),
        "b2": np.ascontiguousarray(inputs["L0_ffn_b2"]),
    }



def make_in_map(inputs, core):
    return {
        "x": np.ascontiguousarray(inputs["L0_attn_x"][core]),
        "wq": np.ascontiguousarray(inputs["L0_attn_Wq"]),
        "wk": np.ascontiguousarray(inputs["L0_attn_Wk"]),
        "wv": np.ascontiguousarray(inputs["L0_attn_Wv"]),
        "gamma": np.ascontiguousarray(inputs["L0_attn_gamma"]).reshape(1, 1),
        "w1": np.ascontiguousarray(inputs["L0_ffn_w1"]),
        "b1": np.ascontiguousarray(inputs["L0_ffn_b1"]),
        "w2": np.ascontiguousarray(inputs["L0_ffn_w2"]),
        "b2": np.ascontiguousarray(inputs["L0_ffn_b2"]),
    }



def make_in_map(inputs, core):
    return {
        "x": np.ascontiguousarray(inputs["L0_attn_x"][core]),
        "wq": np.ascontiguousarray(inputs["L0_attn_Wq"]),
        "wk": np.ascontiguousarray(inputs["L0_attn_Wk"]),
        "wv": np.ascontiguousarray(inputs["L0_attn_Wv"]),
        "gamma": np.ascontiguousarray(inputs["L0_attn_gamma"]).reshape(1, 1),
        "w1": np.ascontiguousarray(inputs["L0_ffn_w1"]),
        "b1": np.ascontiguousarray(inputs["L0_ffn_b1"]),
        "w2": np.ascontiguousarray(inputs["L0_ffn_w2"]),
        "b2": np.ascontiguousarray(inputs["L0_ffn_b2"]),
    }



def make_in_map(inputs, core):
    return {
        "x": np.ascontiguousarray(inputs["L0_attn_x"][core]),
        "wq": np.ascontiguousarray(inputs["L0_attn_Wq"]),
        "wk": np.ascontiguousarray(inputs["L0_attn_Wk"]),
        "wv": np.ascontiguousarray(inputs["L0_attn_Wv"]),
        "gamma": np.ascontiguousarray(inputs["L0_attn_gamma"]).reshape(1, 1),
        "w1": np.ascontiguousarray(inputs["L0_ffn_w1"]),
        "b1": np.ascontiguousarray(inputs["L0_ffn_b1"]),
        "w2": np.ascontiguousarray(inputs["L0_ffn_w2"]),
        "b2": np.ascontiguousarray(inputs["L0_ffn_b2"]),
    }



def make_in_map(inputs, core):
    return {
        "x": np.ascontiguousarray(inputs["L0_attn_x"][core]),
        "wq": np.ascontiguousarray(inputs["L0_attn_Wq"]),
        "wk": np.ascontiguousarray(inputs["L0_attn_Wk"]),
        "wv": np.ascontiguousarray(inputs["L0_attn_Wv"]),
        "gamma": np.ascontiguousarray(inputs["L0_attn_gamma"]).reshape(1, 1),
        "w1": np.ascontiguousarray(inputs["L0_ffn_w1"]),
        "b1": np.ascontiguousarray(inputs["L0_ffn_b1"]),
        "w2": np.ascontiguousarray(inputs["L0_ffn_w2"]),
        "b2": np.ascontiguousarray(inputs["L0_ffn_b2"]),
    }



def make_in_map(inputs, core):
    return {
        "x": np.ascontiguousarray(inputs["L0_attn_x"][core]),
        "wq": np.ascontiguousarray(inputs["L0_attn_Wq"]),
        "wk": np.ascontiguousarray(inputs["L0_attn_Wk"]),
        "wv": np.ascontiguousarray(inputs["L0_attn_Wv"]),
        "gamma": np.ascontiguousarray(inputs["L0_attn_gamma"]).reshape(1, 1),
        "w1": np.ascontiguousarray(inputs["L0_ffn_w1"]),
        "b1": np.ascontiguousarray(inputs["L0_ffn_b1"]),
        "w2": np.ascontiguousarray(inputs["L0_ffn_w2"]),
        "b2": np.ascontiguousarray(inputs["L0_ffn_b2"]),
    }



def make_in_map(inputs, core):
    return {
        "x": np.ascontiguousarray(inputs["L0_attn_x"][core]),
        "wq": np.ascontiguousarray(inputs["L0_attn_Wq"]),
        "wk": np.ascontiguousarray(inputs["L0_attn_Wk"]),
        "wv": np.ascontiguousarray(inputs["L0_attn_Wv"]),
        "gamma": np.ascontiguousarray(inputs["L0_attn_gamma"]).reshape(1, 1),
        "w1": np.ascontiguousarray(inputs["L0_ffn_w1"]),
        "b1": np.ascontiguousarray(inputs["L0_ffn_b1"]),
        "w2": np.ascontiguousarray(inputs["L0_ffn_w2"]),
        "b2": np.ascontiguousarray(inputs["L0_ffn_b2"]),
    }



def make_in_map(inputs, core):
    return {
        "x": np.ascontiguousarray(inputs["L0_attn_x"][core]),
        "wq": np.ascontiguousarray(inputs["L0_attn_Wq"]),
        "wk": np.ascontiguousarray(inputs["L0_attn_Wk"]),
        "wv": np.ascontiguousarray(inputs["L0_attn_Wv"]),
        "gamma": np.ascontiguousarray(inputs["L0_attn_gamma"]).reshape(1, 1),
        "w1": np.ascontiguousarray(inputs["L0_ffn_w1"]),
        "b1": np.ascontiguousarray(inputs["L0_ffn_b1"]),
        "w2": np.ascontiguousarray(inputs["L0_ffn_w2"]),
        "b2": np.ascontiguousarray(inputs["L0_ffn_b2"]),
    }



def make_in_map(inputs, core):
    return {
        "x": np.ascontiguousarray(inputs["L0_attn_x"][core]),
        "wq": np.ascontiguousarray(inputs["L0_attn_Wq"]),
        "wk": np.ascontiguousarray(inputs["L0_attn_Wk"]),
        "wv": np.ascontiguousarray(inputs["L0_attn_Wv"]),
        "gamma": np.ascontiguousarray(inputs["L0_attn_gamma"]).reshape(1, 1),
        "w1": np.ascontiguousarray(inputs["L0_ffn_w1"]),
        "b1": np.ascontiguousarray(inputs["L0_ffn_b1"]),
        "w2": np.ascontiguousarray(inputs["L0_ffn_w2"]),
        "b2": np.ascontiguousarray(inputs["L0_ffn_b2"]),
    }



def make_in_map(inputs, core):
    return {
        "x": np.ascontiguousarray(inputs["L0_attn_x"][core]),
        "wq": np.ascontiguousarray(inputs["L0_attn_Wq"]),
        "wk": np.ascontiguousarray(inputs["L0_attn_Wk"]),
        "wv": np.ascontiguousarray(inputs["L0_attn_Wv"]),
        "gamma": np.ascontiguousarray(inputs["L0_attn_gamma"]).reshape(1, 1),
        "w1": np.ascontiguousarray(inputs["L0_ffn_w1"]),
        "b1": np.ascontiguousarray(inputs["L0_ffn_b1"]),
        "w2": np.ascontiguousarray(inputs["L0_ffn_w2"]),
        "b2": np.ascontiguousarray(inputs["L0_ffn_b2"]),
    }



def make_in_map(inputs, core):
    return {
        "x": np.ascontiguousarray(inputs["L0_attn_x"][core]),
        "wq": np.ascontiguousarray(inputs["L0_attn_Wq"]),
        "wk": np.ascontiguousarray(inputs["L0_attn_Wk"]),
        "wv": np.ascontiguousarray(inputs["L0_attn_Wv"]),
        "gamma": np.ascontiguousarray(inputs["L0_attn_gamma"]).reshape(1, 1),
        "w1": np.ascontiguousarray(inputs["L0_ffn_w1"]),
        "b1": np.ascontiguousarray(inputs["L0_ffn_b1"]),
        "w2": np.ascontiguousarray(inputs["L0_ffn_w2"]),
        "b2": np.ascontiguousarray(inputs["L0_ffn_b2"]),
    }



def make_in_map(inputs, core):
    return {
        "x": np.ascontiguousarray(inputs["L0_attn_x"][core]),
        "wq": np.ascontiguousarray(inputs["L0_attn_Wq"]),
        "wk": np.ascontiguousarray(inputs["L0_attn_Wk"]),
        "wv": np.ascontiguousarray(inputs["L0_attn_Wv"]),
        "gamma": np.ascontiguousarray(inputs["L0_attn_gamma"]).reshape(1, 1),
        "w1": np.ascontiguousarray(inputs["L0_ffn_w1"]),
        "b1": np.ascontiguousarray(inputs["L0_ffn_b1"]),
        "w2": np.ascontiguousarray(inputs["L0_ffn_w2"]),
        "b2": np.ascontiguousarray(inputs["L0_ffn_b2"]),
    }



def make_in_map(inputs, core):
    return {
        "x": np.ascontiguousarray(inputs["L0_attn_x"][core]),
        "wq": np.ascontiguousarray(inputs["L0_attn_Wq"]),
        "wk": np.ascontiguousarray(inputs["L0_attn_Wk"]),
        "wv": np.ascontiguousarray(inputs["L0_attn_Wv"]),
        "gamma": np.ascontiguousarray(inputs["L0_attn_gamma"]).reshape(1, 1),
        "w1": np.ascontiguousarray(inputs["L0_ffn_w1"]),
        "b1": np.ascontiguousarray(inputs["L0_ffn_b1"]),
        "w2": np.ascontiguousarray(inputs["L0_ffn_w2"]),
        "b2": np.ascontiguousarray(inputs["L0_ffn_b2"]),
    }



def make_in_map(inputs, core):
    return {
        "x": np.ascontiguousarray(inputs["L0_attn_x"][core]),
        "wq": np.ascontiguousarray(inputs["L0_attn_Wq"]),
        "wk": np.ascontiguousarray(inputs["L0_attn_Wk"]),
        "wv": np.ascontiguousarray(inputs["L0_attn_Wv"]),
        "gamma": np.ascontiguousarray(inputs["L0_attn_gamma"]).reshape(1, 1),
        "w1": np.ascontiguousarray(inputs["L0_ffn_w1"]),
        "b1": np.ascontiguousarray(inputs["L0_ffn_b1"]),
        "w2": np.ascontiguousarray(inputs["L0_ffn_w2"]),
        "b2": np.ascontiguousarray(inputs["L0_ffn_b2"]),
    }



def make_in_map(inputs, core):
    return {
        "x": np.ascontiguousarray(inputs["L0_attn_x"][core]),
        "wq": np.ascontiguousarray(inputs["L0_attn_Wq"]),
        "wk": np.ascontiguousarray(inputs["L0_attn_Wk"]),
        "wv": np.ascontiguousarray(inputs["L0_attn_Wv"]),
        "gamma": np.ascontiguousarray(inputs["L0_attn_gamma"]).reshape(1, 1),
        "w1": np.ascontiguousarray(inputs["L0_ffn_w1"]),
        "b1": np.ascontiguousarray(inputs["L0_ffn_b1"]),
        "w2": np.ascontiguousarray(inputs["L0_ffn_w2"]),
        "b2": np.ascontiguousarray(inputs["L0_ffn_b2"]),
    }



def make_in_map(inputs, core):
    return {
        "x": np.ascontiguousarray(inputs["L0_attn_x"][core]),
        "wq": np.ascontiguousarray(inputs["L0_attn_Wq"]),
        "wk": np.ascontiguousarray(inputs["L0_attn_Wk"]),
        "wv": np.ascontiguousarray(inputs["L0_attn_Wv"]),
        "gamma": np.ascontiguousarray(inputs["L0_attn_gamma"]).reshape(1, 1),
        "w1": np.ascontiguousarray(inputs["L0_ffn_w1"]),
        "b1": np.ascontiguousarray(inputs["L0_ffn_b1"]),
        "w2": np.ascontiguousarray(inputs["L0_ffn_w2"]),
        "b2": np.ascontiguousarray(inputs["L0_ffn_b2"]),
    }



def make_in_map(inputs, core):
    return {
        "x": np.ascontiguousarray(inputs["L0_attn_x"][core]),
        "wq": np.ascontiguousarray(inputs["L0_attn_Wq"]),
        "wk": np.ascontiguousarray(inputs["L0_attn_Wk"]),
        "wv": np.ascontiguousarray(inputs["L0_attn_Wv"]),
        "gamma": np.ascontiguousarray(inputs["L0_attn_gamma"]).reshape(1, 1),
        "w1": np.ascontiguousarray(inputs["L0_ffn_w1"]),
        "b1": np.ascontiguousarray(inputs["L0_ffn_b1"]),
        "w2": np.ascontiguousarray(inputs["L0_ffn_w2"]),
        "b2": np.ascontiguousarray(inputs["L0_ffn_b2"]),
    }



def make_in_map(inputs, core):
    return {
        "x": np.ascontiguousarray(inputs["L0_attn_x"][core]),
        "wq": np.ascontiguousarray(inputs["L0_attn_Wq"]),
        "wk": np.ascontiguousarray(inputs["L0_attn_Wk"]),
        "wv": np.ascontiguousarray(inputs["L0_attn_Wv"]),
        "gamma": np.ascontiguousarray(inputs["L0_attn_gamma"]).reshape(1, 1),
        "w1": np.ascontiguousarray(inputs["L0_ffn_w1"]),
        "b1": np.ascontiguousarray(inputs["L0_ffn_b1"]),
        "w2": np.ascontiguousarray(inputs["L0_ffn_w2"]),
        "b2": np.ascontiguousarray(inputs["L0_ffn_b2"]),
    }




def make_in_map(inputs, core):
    f32 = np.float32
    return {
        "x": np.ascontiguousarray(np.asarray(inputs["L0_attn_x"], dtype=f32)[core]),
        "wq": np.ascontiguousarray(np.asarray(inputs["L0_attn_Wq"], dtype=f32)),
        "wk": np.ascontiguousarray(np.asarray(inputs["L0_attn_Wk"], dtype=f32)),
        "wv": np.ascontiguousarray(np.asarray(inputs["L0_attn_Wv"], dtype=f32)),
        "gamma": np.ascontiguousarray(np.asarray(inputs["L0_attn_gamma"], dtype=f32)).reshape(1, 1),
        "w1": np.ascontiguousarray(np.asarray(inputs["L0_ffn_w1"], dtype=f32)),
        "b1": np.ascontiguousarray(np.asarray(inputs["L0_ffn_b1"], dtype=f32)),
        "w2": np.ascontiguousarray(np.asarray(inputs["L0_ffn_w2"], dtype=f32)),
        "b2": np.ascontiguousarray(np.asarray(inputs["L0_ffn_b2"], dtype=f32)),
    }


def kernel(**inputs):
    from concourse.bass_utils import run_bass_kernel_spmd
    nc = build_nc()
    in_maps = [make_in_map(inputs, core) for core in range(8)]
    res = run_bass_kernel_spmd(nc, in_maps, core_ids=list(range(8)))
    return np.stack([np.asarray(res.results[i]["y"], dtype=np.float32)
                     for i in range(8)], axis=0)


# version 36
# speedup vs baseline: 1.1374x; 1.0001x over previous
import contextlib
import functools
import numpy as np
import concourse.bass as bass
import concourse.mybir as mybir

F32 = mybir.dt.float32
BF16 = mybir.dt.bfloat16
U8 = mybir.dt.uint8
AX = mybir.AxisListType.X
OP = mybir.AluOpType
AF = mybir.ActivationFunctionType

S, C, HD = 2048, 1024, 4096
NS, NC, NH = 16, 8, 32
W2_RES = 16
EPS = 1e-5

ENGS = ("tensor", "vector", "scalar", "gpsimd", "sync")
STRICT = True
STRICT_ENGS = {"vector", "scalar"}
ENG_SEM = {"tensor": "T", "vector": "V", "scalar": "Sc", "gpsimd": "G"}


class Plan:
    def __init__(self):
        self.ops = {e: [] for e in ENGS}
        self.cnt = {}
        self.strict_now = set()

    def emit(self, eng, fn, waits=(), inc=None, inc_by=1):
        waits = [(s, v) for (s, v) in waits if v > 0]
        if eng in ENG_SEM:
            es = ENG_SEM[eng]
            if (eng == "gpsimd" or (STRICT and eng in STRICT_ENGS)
                    or eng in self.strict_now) and self.c(es) > 0:
                waits.append((es, self.c(es)))
            if inc is None:
                inc = es
        self.ops[eng].append((fn, waits, inc, inc_by))
        if inc is not None:
            self.cnt[inc] = self.cnt.get(inc, 0) + inc_by

    def c(self, sem):
        return self.cnt.get(sem, 0)


def build_nc():
    nc = bass.Bass("TRN2", target_bir_lowering=False)
    x = nc.dram_tensor("x", [S, C], F32, kind="ExternalInput").ap()
    wq = nc.dram_tensor("wq", [C, C], F32, kind="ExternalInput").ap()
    wk = nc.dram_tensor("wk", [C, C], F32, kind="ExternalInput").ap()
    wv = nc.dram_tensor("wv", [C, C], F32, kind="ExternalInput").ap()
    gamma = nc.dram_tensor("gamma", [1, 1], F32, kind="ExternalInput").ap()
    w1 = nc.dram_tensor("w1", [C, HD], F32, kind="ExternalInput").ap()
    b1 = nc.dram_tensor("b1", [HD], F32, kind="ExternalInput").ap()
    w2 = nc.dram_tensor("w2", [HD, C], F32, kind="ExternalInput").ap()
    b2 = nc.dram_tensor("b2", [C], F32, kind="ExternalInput").ap()
    y = nc.dram_tensor("y", [S, C], F32, kind="ExternalOutput").ap()
    w1bf = nc.dram_tensor("w1bf", [C, HD], BF16, kind="Internal").ap()
    w2bf = nc.dram_tensor("w2bf", [HD, C], BF16, kind="Internal").ap()

    ctx = contextlib.ExitStack()
    sb = lambda n, s, d: ctx.enter_context(nc.sbuf_tensor(n, s, d))
    regA = sb("regA", [128, 65536], U8)
    regFG = sb("regFG", [128, 65536], U8)
    regH = sb("regH", [128, 32768], U8)
    regE = sb("regE", [128, 20480], U8)
    rA_f = regA.bitcast(F32)
    rA_b = regA.bitcast(BF16)
    rFG = regFG.bitcast(BF16)
    rH = regH.bitcast(BF16)
    rE = regE.bitcast(BF16)

    resid = lambda ti: rA_f[:, ti * 1024:(ti + 1) * 1024]
    resid2 = lambda ti, et: rA_f[:, ti * 1024 + et * 512: ti * 1024 + et * 512 + 512]

    _wtg_base = (12288, 8192)
    wtg_all2 = [rA_b[:, b:b + 4096].rearrange("p (ci d) -> p ci d", ci=8)
                for b in _wtg_base]
    WTg2 = lambda bi, ci, off, n: rA_b[:, _wtg_base[bi] + ci * 512 + off:
                                       _wtg_base[bi] + ci * 512 + off + n]
    xfm_all = rA_b[:, 16384:32768].rearrange("p (ci s) -> p ci s", ci=8)
    xfm = lambda ci, s0, n: rA_b[:, 16384 + ci * 2048 + s0: 16384 + ci * 2048 + s0 + n]
    fsl = lambda d, s0, n: rFG[:, d * 2048 + s0: d * 2048 + s0 + n]
    gsl = lambda d, s0, n: rFG[:, 16384 + d * 2048 + s0: 16384 + d * 2048 + s0 + n]
    w1sl = lambda ci, h0, n: rFG[:, ci * 4096 + h0: ci * 4096 + h0 + n]
    hsl = lambda si, c0, n: rH[:, si * 1024 + c0: si * 1024 + c0 + n]
    w2res = lambda hc, e0, n: rH[:, hc * 1024 + e0: hc * 1024 + e0 + n]
    ebT = [rE[:, 0:2048], rE[:, 2048:4096]]
    ebsm2 = [rE[:, 4096:6144], rE[:, 6144:8192]]
    x2fm2 = [rE[:, 0:2048], rE[:, 6144:8192]]
    x2f_all2 = [rE[:, 0:2048].rearrange("p (ci s) -> p ci s", ci=8),
                rE[:, 6144:8192].rearrange("p (ci s) -> p ci s", ci=8)]
    hid = [rE[:, 2048:2304], rE[:, 2304:2560]]
    w2s = [rE[:, 3072:4096], rE[:, 4096:5120], rE[:, 8192:9216], rE[:, 9216:10240]]

    ident = sb("ident", [128, 128], BF16)
    identf = sb("identf", [128, 128], F32)
    ones1 = sb("ones1", [1, 128], BF16)
    b2rowb = sb("b2rowb", [1, 1024], BF16)
    b1sb = sb("b1sb", [128, 32], F32)
    gbc = sb("gbc", [128, 1], F32)
    cdb = sb("cdb", [128, 4], F32)
    stats = sb("stats", [128, 64], F32)
    xch = sb("xch", [128, 8192], BF16)
    xc = [xch[:, k * 1024:(k + 1) * 1024] for k in range(8)]
    wbw = xch
    varsink = sb("varsink", [128, 1024], F32)

    CONSTS = [EPS, -0.5, 1.0 / 1024]
    for i, cv in enumerate(CONSTS):
        nc.const_aps.register(F32, cv, cdb[:, i:i + 1])

    NM = lambda par, st: stats[:, par * 4 + st: par * 4 + st + 1]
    NMALL = lambda par: stats[:, par * 4: par * 4 + 4]
    NEGMAX = lambda par: stats[:, 8 + par: 9 + par]
    ZP = lambda par, st: stats[:, 10 + par * 4 + st: 11 + par * 4 + st]
    ZPALL = lambda par: stats[:, 10 + par * 4: 14 + par * 4]
    ZCOL = stats[:, 18:19]
    ZICOL = stats[:, 19:20]
    ZG2 = [stats[:, 20:21], stats[:, 29:30]]
    NSUM = stats[:, 24:25]
    NEGM = stats[:, 25:26]
    VSUM = stats[:, 26:27]
    LNV = stats[:, 27:28]
    RSTD = stats[:, 28:29]
    KCOL = stats[:, 29:30]

    ps = ctx.enter_context(nc.psum_tensor("ps", [128, 4096], F32))
    ps_bf = ps.bitcast(BF16)
    bank = lambda b, off, n: ps[:, b * 512 + off: b * 512 + off + n]
    bankb = lambda b, off, n: ps_bf[:, b * 1024 + off: b * 1024 + off + n]
    bankb3 = lambda b: ps_bf[:, b * 1024:(b + 1) * 1024].rearrange(
        "p (ci s) -> p ci s", ci=8)

    sems = {}
    for name in ("T", "V", "Sc", "G", "LXC0", "LXC1", "LXC2", "LXC3",
                 "LWB0", "LWB1", "LWB2", "LWB3",
                 "LTE0", "LTE1", "LTC0", "LTC1",
                 "LW2S0", "LW2S1", "LW2S2", "LW2S3", "LXA", "LXB",
                 "LW1S0", "LW1S1", "LW1S2", "LW1S3",
                 "LW1S4", "LW1S5", "LW1S6", "LW1S7",
                 "LW2R0", "LW2R1", "LW2R2", "LW2R3", "LMISC", "LMISCG", "STY",
                 "LCW", "LXB2"):
        sems[name] = ctx.enter_context(nc.semaphore(name))

    P = Plan()
    pfree = {b: [] for b in range(8)}

    def v_init(v):
        v.memset(ident[:], 1.0)
        v.memset(identf[:], 1.0)
        v.memset(ones1[:], 1.0)
        ins = None
        for i, cv in enumerate(CONSTS):
            ins = v.memset(cdb[:, i:i + 1], cv)
        return ins
    P.emit("vector", v_init, inc="V")
    V_INIT = P.c("V")

    P.emit("gpsimd", lambda g: g.affine_select(
        out=ident[:], in_=ident[:], pattern=[[-1, 128]],
        compare_op=OP.is_equal, fill=0.0, base=0, channel_multiplier=1),
        waits=[("V", V_INIT)], inc="G")
    P.emit("gpsimd", lambda g: g.affine_select(
        out=identf[:], in_=identf[:], pattern=[[-1, 128]],
        compare_op=OP.is_equal, fill=0.0, base=0, channel_multiplier=1),
        inc="G")
    G_INIT = P.c("G")

    P.emit("sync", lambda e: e.dma_start(
        out=rA_f[:, 0:4 * 1024].rearrange("p (a b) -> p a b", a=4),
        in_=bass.AP(x.tensor, 0, [[1024, 128], [128 * 1024, 4], [1, 1024]])),
        inc="LXA", inc_by=16)
    P.emit("sync", lambda e: e.dma_start(
        out=varsink[0:32, 0:128], in_=bass.AP(b1.tensor, 0, [[128, 32], [1, 128]])),
        inc="LMISC", inc_by=16)
    P.emit("sync", lambda e: e.dma_start(
        out=gbc[:], in_=bass.AP(gamma.tensor, 0, [[0, 128], [1, 1]])),
        inc="LMISC", inc_by=16)
    P.emit("vector", lambda e: e.tensor_scalar_mul(gbc[:], gbc[:], 0.5),
           waits=[("LMISC", 32)], inc="V")
    P.emit("gpsimd", lambda e: e.dma_start(
        out=b2rowb[0:1, :], in_=bass.AP(b2.tensor, 0, [[1024, 1], [1, 1024]])),
        inc="LMISCG", inc_by=16)
    LMISC_ALL = P.c("LMISC")

    P.emit("tensor", lambda e: e.transpose(bank(6, 0, 32), varsink[0:32, 0:128],
                                           identf[0:32, 0:32]),
           waits=[("LMISC", 32), ("G", G_INIT)], inc="T")
    P.emit("vector", lambda e: e.tensor_copy(b1sb[:], bank(6, 0, 32)),
           waits=[("T", P.c("T"))], inc="V")
    pfree[6] = [("V", P.c("V"))]

    xT_done, xT_evac = {}, {}
    for pr in range(NS // 2):
        sl = pr % 4
        waits = [("T", xT_done[2 * pr - 7])] if pr >= 4 else []
        P.emit("gpsimd",
               (lambda pr_, sl_: lambda e: e.dma_start(
                   out=xch[:, sl_ * 2048:(sl_ + 1) * 2048].rearrange(
                       "p (a b) -> p a b", a=2),
                   in_=bass.AP(x.tensor, pr_ * 2 * 128 * 1024,
                               [[1024, 128], [128 * 1024, 2], [1, 1024]])))(pr, sl),
               waits=waits, inc=f"LXC{sl}", inc_by=16)
        for half in range(2):
            ti = 2 * pr + half
            bk = 4 + (ti % 2)
            twaits = [(f"LXC{sl}", 16 * (pr // 4 + 1)), ("G", G_INIT)] + pfree[bk]
            def tfn(e, ti=ti, bk=bk, sl=sl, half=half):
                ins = None
                for ci in range(NC):
                    ins = e.transpose(bankb(bk, ci * 128, 128),
                                      xc[sl * 2 + half][:, ci * 128:(ci + 1) * 128],
                                      ident[:])
                return ins
            P.emit("tensor", tfn, waits=twaits, inc="T")
            xT_done[ti] = P.c("T")
            P.emit("vector",
                   (lambda ti_, bk_: lambda e: e.tensor_copy(
                       xfm_all[:, :, ti_ * 128:(ti_ + 1) * 128], bankb3(bk_)))(ti, bk),
                   waits=[("T", xT_done[ti])], inc="V")
            xT_evac[ti] = P.c("V")
            pfree[bk] = [("V", xT_evac[ti])]
    XFM_ALL_V = P.c("V")

    wdram = {0: wq, 1: wk, 2: wv}
    rot = [0]

    def qkv_mm_tilegroup(lhsT_sl, rhs_sl, out_fn, N, extra_waits, evac_eng):
        bk = rot[0] % 4
        rot[0] += 1
        waits = list(extra_waits) + pfree[bk]
        def tfn(e, bk=bk, lhsT_sl=lhsT_sl, rhs_sl=rhs_sl, N=N):
            ins = None
            for ci in range(NC):
                ins = e.matmul(bank(bk, 0, N), lhsT_sl(ci), rhs_sl(ci),
                               start=(ci == 0), stop=(ci == NC - 1))
            return ins
        P.emit("tensor", tfn, waits=waits, inc="T")
        tcnt = P.c("T")
        sem = "Sc" if evac_eng == "scalar" else "V"
        def efn(e, bk=bk, out_fn=out_fn, N=N):
            out = out_fn()
            return (e.copy(out, bank(bk, 0, N)) if evac_eng == "scalar"
                    else e.tensor_copy(out, bank(bk, 0, N)))
        P.emit(evac_eng, efn, waits=[("T", tcnt)], inc=sem)
        pfree[bk] = [(sem, P.c(sem))]

    wT_done = {}
    lwb_cnt = [0, 0, 0, 0]
    wrow_idx_box = [0]
    wt_ready = {}

    def emit_w_lt(k):
        wi, grp = divmod(k, 2)
        bi = k % 2
        for rp in range(2):
            wrow_idx = wrow_idx_box[0]
            wp = wrow_idx // 2
            sl = wp % 4
            dj0 = grp * 4 + rp * 2
            if wp >= 4:
                waits = [("T", wT_done[wrow_idx - 7])]
            else:
                waits = [("T", xT_done[9 + 2 * wp])]
            P.emit("gpsimd",
                   (lambda wi_, dj_, sl_: lambda e: e.dma_start(
                       out=wbw[:, sl_ * 2048:(sl_ + 1) * 2048].rearrange(
                           "p (a b) -> p a b", a=2),
                       in_=bass.AP(wdram[wi_].tensor, dj_ * 128 * 1024,
                                   [[1024, 128], [128 * 1024, 2], [1, 1024]])))(
                       wi, dj0, sl),
                   waits=waits, inc=f"LWB{sl}", inc_by=16)
            lwb_cnt[sl] += 1
            for half in range(2):
                r = rp * 2 + half
                bk = 4 + (wrow_idx_box[0] % 4)
                twaits = [(f"LWB{sl}", 16 * lwb_cnt[sl]), ("G", G_INIT)] + pfree[bk]
                if k >= 2:
                    twaits.append(("T", wt_mm_done[k - 2]))
                def tfn(e, sl=sl, bk=bk, half=half):
                    base = sl * 2048 + half * 1024
                    ins = None
                    for ci in range(NC):
                        ins = e.transpose(
                            bankb(bk, ci * 128, 128),
                            wbw[:, base + ci * 128: base + (ci + 1) * 128],
                            ident[:])
                    return ins
                P.emit("tensor", tfn, waits=twaits, inc="T")
                wT_done[wrow_idx_box[0]] = P.c("T")
                P.emit("vector",
                       (lambda r_, bk_, bi_: lambda e: e.tensor_copy(
                           wtg_all2[bi_][:, :, r_ * 128:(r_ + 1) * 128],
                           bankb3(bk_)))(r, bk, bi),
                       waits=[("T", wT_done[wrow_idx_box[0]])], inc="V")
                pfree[bk] = [("V", P.c("V"))]
                wrow_idx_box[0] += 1
        wt_ready[k] = P.c("V")

    wt_mm_done = {}
    emit_w_lt(0)
    for k in range(6):
        wi, grp = divmod(k, 2)
        bi = k % 2
        if wi < 2:
            outsl = fsl if wi == 0 else gsl
            tiles = [(r, st) for r in range(4) for st in range(4)]
            for idx, (r, st) in enumerate(tiles):
                if idx == 8 and k + 1 < 6:
                    emit_w_lt(k + 1)
                dj = grp * 4 + r
                qkv_mm_tilegroup(
                    (lambda r_, bi_: lambda ci: WTg2(bi_, ci, r_ * 128, 128))(r, bi),
                    (lambda st_: lambda ci: xfm(ci, st_ * 512, 512))(st),
                    (lambda dj_, st_, o_=outsl: lambda: o_(dj_, st_ * 512, 512))(dj, st),
                    512, [("V", max(wt_ready[k], XFM_ALL_V))], "scalar")
        else:
            for si in range(NS):
                if si == 8 and k + 1 < 6:
                    emit_w_lt(k + 1)
                qkv_mm_tilegroup(
                    (lambda si_, bi_: lambda ci: xfm(ci, si_ * 128, 128))(si, bi),
                    (lambda bi_: lambda ci: WTg2(bi_, ci, 0, 512))(bi),
                    (lambda si_, grp_: lambda: hsl(si_, grp_ * 512, 512))(si, grp),
                    512, [("V", max(wt_ready[k], XFM_ALL_V))], "vector")
        wt_mm_done[k] = P.c("T")

    A_END_T = P.c("T")
    A_END_V = P.c("V")
    A_END_S = P.c("Sc")

    for i in range(8):
        P.emit("gpsimd",
               (lambda i_: lambda e: e.dma_start(
                   out=w1bf[i_ * 128:(i_ + 1) * 128, :],
                   in_=w1[i_ * 128:(i_ + 1) * 128, :]))(i),
               inc="LCW", inc_by=16)
    for i in range(16):
        P.emit("gpsimd",
               (lambda i_: lambda e: e.dma_start(
                   out=w2bf[i_ * 256:(i_ + 1) * 256, :],
                   in_=w2[i_ * 256:(i_ + 1) * 256, :]))(i),
               inc="LCW", inc_by=16)
    LCW_ALL = 384

    P.emit("scalar", lambda e: e.dma_start(
        out=rA_f[:, 4 * 1024:16 * 1024].rearrange("p (a b) -> p a b", a=12),
        in_=bass.AP(x.tensor, 4 * 128 * 1024, [[1024, 128], [128 * 1024, 12], [1, 1024]])),
        waits=[("T", A_END_T)], inc="LXB", inc_by=16)

    sc_done, exp_done, trT_done = {}, {}, {}
    ebsm_evac, ebsm_evac_h1, av_done, oevac_done = {}, {}, {}, {}
    nm_done = {}

    def emit_score_tile(ti, st, extra_waits):
        par = ti % 2
        bk = par * 4 + st
        def tfn(e, ti=ti, st=st, bk=bk):
            ins = None
            for d in range(NC):
                ins = e.matmul(bank(bk, 0, 512), gsl(d, ti * 128, 128),
                               fsl(d, st * 512, 512), start=(d == 0), stop=(d == NC - 1))
            return ins
        P.emit("tensor", tfn, waits=extra_waits, inc="T")
        sc_done[(ti, st)] = P.c("T")
        P.emit("vector",
               (lambda par_, st_, bk_: lambda e: e.reduce_max(
                   NM(par_, st_), bank(bk_, 0, 512), axis=AX, negate=True))(par, st, bk),
               waits=[("T", sc_done[(ti, st)])], inc="V")

    def emit_negmax_combine(ti):
        par = ti % 2
        P.emit("vector",
               (lambda par_: lambda e: e.tensor_reduce(
                   NEGMAX(par_), NMALL(par_), axis=AX, op=OP.min))(par), inc="V")
        nm_done[ti] = P.c("V")

    def emit_softmax(ti):
        par = ti % 2
        g0 = par * 4
        waits = [("V", nm_done[ti])]
        if ti >= 2:
            waits.append(ebsm_evac[ti - 2])
        for st in range(4):
            P.emit("scalar",
                   (lambda st_, g0_, par_: lambda e: e.activation(
                       out=ebT[par_][:, st_ * 512:(st_ + 1) * 512],
                       in_=bank(g0_ + st_, 0, 512), func=AF.Exp,
                       bias=NEGMAX(par_), scale=1.0,
                       accum_out=ZP(par_, st_)))(st, g0, par),
                   waits=(waits if st == 0 else []), inc="Sc")
        exp_done[ti] = P.c("Sc")

    lte_cnt = [0, 0]
    tail_trT = {}

    def emit_transposes(ti):
        par = ti % 2
        eb = ebsm2[par]
        if 2 <= ti < NS - 2:
            waits = [("Sc", exp_done[ti])]
            if ti >= 2:
                waits.append(("T", av_done.get(ti - 2, 0)))
                waits.append((f"LTE{par}", lte_cnt[par]))
            P.emit("sync",
                   (lambda par_, eb_=eb: lambda e: e.dma_start_transpose(
                       eb_[:].rearrange("p (k c) -> p k c", k=16),
                       ebT[par_][:].rearrange("p (k c) -> p k c", k=16)))(par),
                   waits=waits, inc=f"LTE{par}", inc_by=16)
            lte_cnt[par] += 16
            ebsm_evac[ti] = (f"LTE{par}", lte_cnt[par])
        else:
            g0 = par * 4
            def tfn(e, par=par, g0=g0):
                ins = None
                for k in range(NS):
                    ins = e.transpose(
                        ps_bf[:, g0 * 1024 + k * 128: g0 * 1024 + (k + 1) * 128],
                        ebT[par][:, k * 128:(k + 1) * 128], ident[:])
                return ins
            P.emit("tensor", tfn,
                   waits=[("Sc", exp_done[ti]), (f"LTE{par}", lte_cnt[par])],
                   inc="T")
            trT = P.c("T")
            tail_trT[ti] = trT
            P.emit("vector",
                   (lambda g0_, eb_=eb: lambda e: e.tensor_copy(
                       eb_[:, 0:2048], ps_bf[:, g0_ * 1024: g0_ * 1024 + 2048]))(g0),
                   waits=[("T", trT), ("T", av_done.get(ti - 2, 0))], inc="V")
            ebsm_evac[ti] = ("V", P.c("V"))
        P.emit("vector",
               (lambda par_: lambda e: e.tensor_reduce(
                   ZCOL, ZPALL(par_), axis=AX, op=OP.add))(par),
               waits=[("Sc", exp_done[ti]), ("LMISC", LMISC_ALL)], inc="V")
        P.emit("vector", lambda e: e.reciprocal(ZICOL, ZCOL), inc="V")
        P.emit("vector",
               (lambda par_: lambda e: e.tensor_mul(ZG2[par_], ZICOL, gbc[:]))(par),
               inc="V")

    def emit_av(ti):
        g0 = (ti % 2) * 4
        eb = ebsm2[ti % 2]
        for ct in range(2):
            bk = g0 + 2 + ct
            def tfn_a(e, ct=ct, bk=bk, eb=eb):
                ins = None
                for k in range(NS):
                    ins = e.matmul(bank(bk, 0, 512), eb[:, k * 128:(k + 1) * 128],
                                   hsl(k, ct * 512, 512),
                                   start=(k == 0), stop=(k == NS - 1))
                return ins
            P.emit("tensor", tfn_a, waits=[ebsm_evac[ti]], inc="T")
        av_done[ti] = P.c("T")

    oevac_h1 = {}

    def emit_oevac(ti):
        g0 = (ti % 2) * 4
        xw = ("LXA", 16) if ti < 4 else ("LXB", 16)
        for ct in range(2):
            P.emit("vector",
                   (lambda ti_, ct_, bk_: lambda e: e.scalar_tensor_tensor(
                       out=resid2(ti_, ct_), in0=bank(bk_, 0, 512), scalar=ZG2[ti_ % 2],
                       in1=resid2(ti_, ct_), op0=OP.mult, op1=OP.add))(ti, ct, g0 + 2 + ct),
                   waits=[("T", av_done[ti]), xw], inc="V")
            if ct == 0:
                oevac_h1[ti] = P.c("V")
        oevac_done[ti] = P.c("V")

    R1COL = lambda ti: stats[:, 32 + ti:33 + ti]
    M1COL = lambda ti: stats[:, 48 + ti:49 + ti]
    BNS = stats[:, 21:22]
    BVS2 = [stats[:, 23:24], stats[:, 27:28]]
    BLL2 = [stats[:, 24:25], stats[:, 28:29]]
    K1TMP = [stats[:, 25:26], stats[:, 26:27]]

    bstats_sc = {}

    def emit_ln1_bstats(ti):
        P.emit("vector",
               (lambda t_: lambda e: e.reduce_sum(BNS, resid(t_), axis=AX,
                                                  negate=True))(ti), inc="V")
        P.emit("vector",
               (lambda t_: lambda e: e.tensor_scalar_mul(
                   M1COL(t_), BNS, 1.0 / 1024))(ti), inc="V")
        vwaits = []
        if ti >= 2:
            vwaits.append(("Sc", bstats_sc[ti - 2]))
        P.emit("vector",
               (lambda t_: lambda e: e.scalar_tensor_tensor(
                   out=varsink[:], in0=resid(t_), scalar=M1COL(t_), in1=resid(t_),
                   op0=OP.add, op1=OP.mult, accum_out=BVS2[t_ % 2]))(ti),
               waits=vwaits, inc="V")
        vv = P.c("V")
        P.emit("scalar",
               (lambda t_: lambda e: e.activation(
                   out=BLL2[t_ % 2], in_=BVS2[t_ % 2], func=AF.Ln,
                   scale=1.0 / 1024, bias=EPS))(ti),
               waits=[("V", vv)], inc="Sc")
        P.emit("scalar",
               (lambda t_: lambda e: e.activation(out=R1COL(t_), in_=BLL2[t_ % 2],
                                                  func=AF.Exp, scale=-0.5))(ti),
               inc="Sc")
        bstats_sc[ti] = P.c("Sc")

    LNS = lambda b: stats[:, b:b + 1]
    LNM = lambda b: stats[:, b + 1:b + 2]
    LNV4 = lambda b: stats[:, b + 2:b + 3]
    LNL = lambda b: stats[:, b + 3:b + 4]
    LNR = lambda b: stats[:, b + 4:b + 5]
    LNK = lambda b: stats[:, b + 5:b + 6]
    vcast_n = {}

    def emit_lnb_v1(chains):
        for b, cch in chains:
            P.emit("vector",
                   (lambda c_, b_: lambda e: e.reduce_sum(
                       LNS(b_), resid(c_), axis=AX, negate=True))(cch, b), inc="V")
        for b, cch in chains:
            P.emit("vector",
                   (lambda b_: lambda e: e.tensor_scalar_mul(
                       LNM(b_), LNS(b_), 1.0 / 1024))(b), inc="V")
        for b, cch in chains:
            P.emit("vector",
                   (lambda c_, b_: lambda e: e.scalar_tensor_tensor(
                       out=varsink[:], in0=resid(c_), scalar=LNM(b_), in1=resid(c_),
                       op0=OP.add, op1=OP.mult, accum_out=LNV4(b_)))(cch, b), inc="V")
        return P.c("V")

    def emit_lnb_sc(chains, vwait):
        for b, cch in chains:
            P.emit("scalar",
                   (lambda b_: lambda e: e.activation(
                       out=LNL(b_), in_=LNV4(b_), func=AF.Ln,
                       scale=1.0 / 1024, bias=EPS))(b),
                   waits=[("V", vwait)], inc="Sc")
        for b, cch in chains:
            P.emit("scalar",
                   (lambda b_: lambda e: e.activation(
                       out=LNR(b_), in_=LNL(b_), func=AF.Exp, scale=-0.5))(b),
                   inc="Sc")
        return P.c("Sc")

    def emit_lnb_v2(chains, scwait, cast_e8=None, store=()):
        for b, cch in chains:
            P.emit("vector",
                   (lambda b_: lambda e: e.tensor_mul(LNK(b_), LNM(b_), LNR(b_)))(b),
                   waits=[("Sc", scwait)], inc="V")
        if cast_e8 is not None:
            slot = (cast_e8 % 2) * 2
            for sj, (b, cch) in enumerate(chains[-2:]):
                P.emit("vector",
                       (lambda c_, b_, sl_=slot + sj: lambda e: e.tensor_scalar(
                           out=xc[sl_][:], in0=resid(c_), scalar1=LNR(b_),
                           scalar2=LNK(b_), op0=OP.mult, op1=OP.add))(cch, b),
                       inc="V")
            vcast_n[cast_e8] = P.c("V")
        for b, cch in chains:
            P.emit("vector",
                   (lambda c_, b_: lambda e: e.tensor_scalar(
                       out=resid(c_), in0=resid(c_), scalar1=LNR(b_), scalar2=LNK(b_),
                       op0=OP.mult, op1=OP.add))(cch, b), inc="V")
        for b, cch in store:
            vfin = P.c("V")
            P.emit("sync",
                   (lambda c_: lambda e: e.dma_start(
                       out=y[c_ * 128:(c_ + 1) * 128, :], in_=resid(c_)))(cch),
                   waits=[("V", vfin)], inc="STY", inc_by=16)

    vcast_n_early = {}

    def emit_ln1_final_c0():
        scw = bstats_sc[1]
        P.emit("vector", lambda e: e.engine_nop(), waits=[("T", A_END_T)], inc="V")
        for sj, cch in enumerate((0, 1)):
            P.emit("vector",
                   (lambda c_, sj_: lambda e: e.tensor_mul(
                       K1TMP[sj_], M1COL(c_), R1COL(c_)))(cch, sj),
                   waits=[("Sc", scw)], inc="V")
        for sj, cch in enumerate((0, 1)):
            P.emit("vector",
                   (lambda c_, sl_, sj_: lambda e: e.tensor_scalar(
                       out=xc[sl_][:], in0=resid(c_), scalar1=R1COL(c_),
                       scalar2=K1TMP[sj_], op0=OP.mult, op1=OP.add))(cch, sj, sj),
                   inc="V")
        vcast_n_early[0] = P.c("V")
        for sj, cch in enumerate((0, 1)):
            P.emit("vector",
                   (lambda c_, sj_: lambda e: e.tensor_scalar(
                       out=resid(c_), in0=resid(c_), scalar1=R1COL(c_),
                       scalar2=K1TMP[sj_], op0=OP.mult, op1=OP.add))(cch, sj),
                   inc="V")

    for ti in (0, 1):
        for st in range(4):
            bk = (ti % 2) * 4 + st
            emit_score_tile(ti, st, [("V", A_END_V), ("Sc", A_END_S)] + pfree[bk])
        emit_negmax_combine(ti)

    for tj in range(NS):
        emit_softmax(tj)
        emit_transposes(tj)
        if tj >= 1:
            emit_ln1_bstats(tj - 1)
        emit_av(tj)
        emit_oevac(tj)
        if tj + 2 < NS:
            for st in (0, 1):
                ws = [("Sc", exp_done[tj])]
                if tj < 2:
                    ws.append(ebsm_evac[tj])
                emit_score_tile(tj + 2, st, ws)
            emit_score_tile(tj + 2, 2, [("V", oevac_h1[tj])])
            emit_score_tile(tj + 2, 3, [("V", oevac_done[tj])])
            emit_negmax_combine(tj + 2)
        if tj == NS - 1:
            emit_ln1_bstats(tj)
        if tj == 4:
            emit_ln1_final_c0()
        if tj == 6:
            P.emit("gpsimd", lambda e: e.dma_start(
                out=xch[:, 4096:8192].rearrange("p (ei c) -> p ei c", ei=8),
                in_=bass.AP(w1bf.tensor, 512,
                            [[4096, 128], [128 * 4096, 8], [1, 512]])),
                waits=[("T", P.c("T")), ("LCW", LCW_ALL)], inc="LW1S1", inc_by=16)
        if tj == 13:
            t_after_s15 = P.c("T")
            w1_all = rFG[:, 0:32768].rearrange("p (ei h) -> p ei h", ei=8)
            w2r3 = lambda i: rH[:, i * 4096:(i + 1) * 4096].rearrange(
                "p (a b) -> p a b", a=4)
            def w1dma(k):
                P.emit("gpsimd",
                       (lambda k_: lambda e: e.dma_start(
                           out=w1_all[:, :, k_ * 512:(k_ + 1) * 512],
                           in_=bass.AP(w1bf.tensor, k_ * 512,
                                       [[4096, 128], [128 * 4096, 8], [1, 512]])))(k),
                       waits=[("T", t_after_s15), ("LCW", LCW_ALL)],
                       inc=f"LW1S{k}", inc_by=16)
            def w2rdma(i):
                P.emit("gpsimd",
                       (lambda i_: lambda e: e.dma_start(
                           out=w2r3(i_),
                           in_=bass.AP(w2bf.tensor, i_ * 4 * 128 * 1024,
                                       [[1024, 128], [128 * 1024, 4], [1, 1024]])))(i),
                       waits=[("T", P.c("T")), ("LCW", LCW_ALL)],
                       inc=f"LW2R{i}", inc_by=16)
            w1dma(0)
    B_END_T = P.c("T")
    B_END_V = P.c("V")
    w1dma(2)
    w2rdma(0)
    w2rdma(1)
    w1dma(3)
    w2rdma(2)
    w2rdma(3)
    w1dma(4)
    w1dma(5)
    w1dma(6)
    w1dma(7)

    xcT_done, x2fm_evac = {}, {}
    ffn1_done, gelu_done, ffn2_done, r2evac_done = {}, {}, {}, {}
    r2evac_h1 = {}
    lw2s_cnt = [0, 0, 0, 0]
    w2s_seq = {}
    FB = (2, 3, 0)
    hid3 = [rE[:, 2048:2304], rE[:, 2304:2560], rE[:, 2560:2816]]

    ltc_cnt = [0, 0]
    x2fm_ready = {}

    def emit_c_head_t(e8):
        slot = (e8 % 2) * 2
        par = e8 % 2
        for sj in range(2):
            waits = [("V", vcast_n[e8])]
            if sj == 0:
                if e8 == 0:
                    waits.append(("T", tail_trT[14]))
                elif e8 == 1:
                    waits.append(("T", B_END_T))
                else:
                    waits.append(("T", ffn1_done[(e8 - 2) * NH + NH - 1]))
                    waits.append((f"LTC{par}", ltc_cnt[par]))
            P.emit("sync",
                   (lambda sj_, sl_, par_: lambda e: e.dma_start_transpose(
                       x2f_all2[par_][:, :, sj_ * 128:(sj_ + 1) * 128],
                       xc[sl_ + sj_][:].rearrange("p (k c) -> p k c", k=8)))(
                       sj, slot, par),
                   waits=waits, inc=f"LTC{par}", inc_by=16)
            ltc_cnt[par] += 16
        x2fm_ready[e8] = (f"LTC{par}", ltc_cnt[par])

    def w2s_prefetch(g, gate_g):
        e8, hc = divmod(g, NH)
        sl = hc % 4
        if gate_g is None:
            waits = [("T", B_END_T)]
        else:
            waits = [("T", ffn2_done[gate_g])]
        waits.append(("LCW", LCW_ALL))
        lw2s_cnt[sl] += 1
        w2s_seq[g] = lw2s_cnt[sl]
        P.emit("gpsimd",
               (lambda hc_, sl_: lambda e: e.dma_start(
                   out=w2s[sl_], in_=w2bf[hc_ * 128:(hc_ + 1) * 128, :]))(hc, sl),
               waits=waits, inc=f"LW2S{sl}", inc_by=16)

    for k in range(4):
        w2s_prefetch(W2_RES + k, None)

    def emit_ffn1(g):
        e8, hc = divmod(g, NH)
        fbk = FB[g % 3]
        waits = [x2fm_ready[e8], (f"LW1S{hc // 4}", 16)]
        if g - 3 >= 0:
            waits.append(("Sc", gelu_done[g - 3]))
        else:
            waits.append(("V", oevac_done[14]))
            if g == 2:
                waits.append(ebsm_evac[14])
        def tfn(e, hc=hc, fbk=fbk, e8=e8):
            ins = None
            xb_ = x2fm2[e8 % 2]
            for ci in range(NC):
                if 4 <= hc < 8:
                    lhsT = xch[:, 4096 + ci * 512 + (hc - 4) * 128:
                               4096 + ci * 512 + (hc - 3) * 128]
                else:
                    lhsT = w1sl(ci, hc * 128, 128)
                ins = e.matmul(bank(fbk, 0, 256), lhsT,
                               xb_[:, ci * 256:(ci + 1) * 256],
                               start=(ci == 0), stop=(ci == NC - 1))
            return ins
        P.emit("tensor", tfn, waits=waits, inc="T")
        ffn1_done[g] = P.c("T")

    def emit_gelu(g):
        e8, hc = divmod(g, NH)
        fbk = FB[g % 3]
        waits = [("T", ffn1_done[g]), ("LMISC", LMISC_ALL)]
        if g - 3 >= 0:
            waits.append(("T", ffn2_done[g - 3]))
        P.emit("scalar",
               (lambda hc_, fbk_, g_: lambda e: e.activation(
                   out=hid3[g_ % 3][:, 0:256], in_=bank(fbk_, 0, 256),
                   func=AF.Gelu, bias=b1sb[:, hc_:hc_ + 1], scale=1.0))(hc, fbk, g),
               waits=waits, inc="Sc")
        gelu_done[g] = P.c("Sc")

    def emit_ffn2(g):
        e8, hc = divmod(g, NH)
        waits = [("Sc", gelu_done[g])]
        if hc < W2_RES and e8 == 0:
            waits.append((f"LW2R{hc // 4}", 16))
        if hc >= W2_RES:
            waits.append((f"LW2S{hc % 4}", 16 * w2s_seq[g]))
        def tfn2(e, hc=hc, g=g, sjs=(0, 1)):
            ins = None
            for sj in sjs:
                for et in range(2):
                    rhs = (w2res(hc, et * 512, 512) if hc < W2_RES
                           else w2s[hc % 4][:, et * 512:(et + 1) * 512])
                    ins = e.matmul(bank(4 + sj * 2 + et, 0, 512),
                                   hid3[g % 3][:, sj * 128:(sj + 1) * 128], rhs,
                                   start=(hc == 0), stop=False)
            return ins
        if hc == 0:
            if e8 > 0:
                w0, w1_ = [("V", r2evac_h1[e8 - 1])], [("V", r2evac_done[e8 - 1])]
            else:
                w0 = w1_ = [("V", B_END_V)]
            P.emit("tensor", functools.partial(tfn2, sjs=(0,)),
                   waits=waits + w0, inc="T")
            P.emit("tensor", functools.partial(tfn2, sjs=(1,)),
                   waits=w1_, inc="T")
        else:
            P.emit("tensor", tfn2, waits=waits, inc="T")
        ffn2_done[g] = P.c("T")
        if hc >= 16:
            nxt = g + 4 if hc <= 27 else (g - hc) + NH + 16 + (hc % 4)
            if nxt < 8 * NH and nxt % NH >= 16:
                w2s_prefetch(nxt, g)

    def emit_fold_r2(e8):
        c0, c1 = 2 * e8, 2 * e8 + 1
        def foldfn(e, sj):
            ins = None
            for et in range(2):
                ins = e.matmul(bank(4 + sj * 2 + et, 0, 512), ones1[0:1, :],
                               b2rowb[0:1, et * 512:(et + 1) * 512],
                               start=False, stop=True)
            return ins
        for sj, cch in enumerate((c0, c1)):
            P.emit("tensor", (lambda sj_: lambda e: foldfn(e, sj_))(sj),
                   waits=([("LMISCG", 16)] if sj == 0 else []), inc="T")
            fold_t = P.c("T")
            P.emit("vector",
                   (lambda c_, sj_: lambda e: e.tensor_tensor(
                       out=resid(c_), in0=ps[:, (4 + sj_ * 2) * 512:
                                              (6 + sj_ * 2) * 512],
                       in1=resid(c_), op=OP.add))(cch, sj),
                   waits=[("T", fold_t)], inc="V")
            if sj == 0:
                r2evac_h1[e8] = P.c("V")
        r2evac_done[e8] = P.c("V")

    def emit_ln1_final_c(e8):
        slot = (e8 % 2) * 2
        scw = bstats_sc[2 * e8 + 1]
        if e8 >= 2:
            P.emit("vector", lambda e: e.engine_nop(),
                   waits=[x2fm_ready[e8 - 2]], inc="V")
        for sj, cch in enumerate((2 * e8, 2 * e8 + 1)):
            P.emit("vector",
                   (lambda c_, sj_: lambda e: e.tensor_mul(
                       K1TMP[sj_], M1COL(c_), R1COL(c_)))(cch, sj),
                   waits=[("Sc", scw)], inc="V")
        for sj, cch in enumerate((2 * e8, 2 * e8 + 1)):
            P.emit("vector",
                   (lambda c_, sl_, sj_: lambda e: e.tensor_scalar(
                       out=xc[sl_][:], in0=resid(c_), scalar1=R1COL(c_),
                       scalar2=K1TMP[sj_], op0=OP.mult, op1=OP.add))(
                       cch, slot + sj, sj),
                   inc="V")
        vcast_n[e8] = P.c("V")
        for sj, cch in enumerate((2 * e8, 2 * e8 + 1)):
            P.emit("vector",
                   (lambda c_, sj_: lambda e: e.tensor_scalar(
                       out=resid(c_), in0=resid(c_), scalar1=R1COL(c_),
                       scalar2=K1TMP[sj_], op0=OP.mult, op1=OP.add))(cch, sj),
                   inc="V")

    vcast_n[0] = vcast_n_early[0]
    emit_c_head_t(0)
    lnch, lnb_vw, lnb_sw = None, None, None
    for g in range(8 * NH):
        e8, hc = divmod(g, NH)
        if g >= 3:
            gg = g - 3
            emit_ffn2(gg)
            if gg % NH == NH - 1:
                emit_fold_r2(gg // NH)
        emit_ffn1(g)
        emit_gelu(g)
        if hc == 2 and e8 > 0:
            lnch = [(0, 2 * (e8 - 1)), (6, 2 * (e8 - 1) + 1)]
            lnb_vw = emit_lnb_v1(lnch)
        if hc == 6 and e8 + 1 < 8:
            emit_ln1_final_c(e8 + 1)
        if hc == 8 and e8 > 0:
            lnb_sw = emit_lnb_sc(lnch, lnb_vw)
        if hc == 10 and e8 > 0:
            emit_lnb_v2(lnch, lnb_sw, store=lnch)
        if hc == 12 and e8 + 1 < 8:
            emit_c_head_t(e8 + 1)
    emit_ffn2(8 * NH - 3)
    emit_ffn2(8 * NH - 2)
    emit_ffn2(8 * NH - 1)
    emit_fold_r2(7)

    ch7 = [(0, 14), (6, 15)]
    vw7 = emit_lnb_v1(ch7)
    sw7 = emit_lnb_sc(ch7, vw7)
    emit_lnb_v2(ch7, sw7, store=ch7)
    P.emit("sync", lambda e: e.wait_ge(sems["STY"], 16 * NS))

    block = ctx.enter_context(nc.Block())

    def emit_stream(eng_name):
        def run(eng):
            last = {}
            for fn, waits, inc, inc_by in P.ops[eng_name]:
                for s, v in waits:
                    if last.get(s, 0) < v:
                        eng.wait_ge(sems[s], v)
                        last[s] = v
                ins = fn(eng)
                if inc is not None:
                    if ins is None:
                        ins = eng.nop()
                    ins.then_inc(sems[inc], inc_by)
        return run

    block.tensor(emit_stream("tensor"))
    block.vector(emit_stream("vector"))
    block.scalar(emit_stream("scalar"))
    block.gpsimd(emit_stream("gpsimd"))
    block.sync(emit_stream("sync"))

    ctx.close()
    return nc


def make_in_map(inputs, core):
    return {
        "x": np.ascontiguousarray(inputs["L0_attn_x"][core]),
        "wq": np.ascontiguousarray(inputs["L0_attn_Wq"]),
        "wk": np.ascontiguousarray(inputs["L0_attn_Wk"]),
        "wv": np.ascontiguousarray(inputs["L0_attn_Wv"]),
        "gamma": np.ascontiguousarray(inputs["L0_attn_gamma"]).reshape(1, 1),
        "w1": np.ascontiguousarray(inputs["L0_ffn_w1"]),
        "b1": np.ascontiguousarray(inputs["L0_ffn_b1"]),
        "w2": np.ascontiguousarray(inputs["L0_ffn_w2"]),
        "b2": np.ascontiguousarray(inputs["L0_ffn_b2"]),
    }


def make_in_map(inputs, core):
    return {
        "x": np.ascontiguousarray(inputs["L0_attn_x"][core]),
        "wq": np.ascontiguousarray(inputs["L0_attn_Wq"]),
        "wk": np.ascontiguousarray(inputs["L0_attn_Wk"]),
        "wv": np.ascontiguousarray(inputs["L0_attn_Wv"]),
        "gamma": np.ascontiguousarray(inputs["L0_attn_gamma"]).reshape(1, 1),
        "w1": np.ascontiguousarray(inputs["L0_ffn_w1"]),
        "b1": np.ascontiguousarray(inputs["L0_ffn_b1"]),
        "w2": np.ascontiguousarray(inputs["L0_ffn_w2"]),
        "b2": np.ascontiguousarray(inputs["L0_ffn_b2"]),
    }


def make_in_map(inputs, core):
    return {
        "x": np.ascontiguousarray(inputs["L0_attn_x"][core]),
        "wq": np.ascontiguousarray(inputs["L0_attn_Wq"]),
        "wk": np.ascontiguousarray(inputs["L0_attn_Wk"]),
        "wv": np.ascontiguousarray(inputs["L0_attn_Wv"]),
        "gamma": np.ascontiguousarray(inputs["L0_attn_gamma"]).reshape(1, 1),
        "w1": np.ascontiguousarray(inputs["L0_ffn_w1"]),
        "b1": np.ascontiguousarray(inputs["L0_ffn_b1"]),
        "w2": np.ascontiguousarray(inputs["L0_ffn_w2"]),
        "b2": np.ascontiguousarray(inputs["L0_ffn_b2"]),
    }


def make_in_map(inputs, core):
    return {
        "x": np.ascontiguousarray(inputs["L0_attn_x"][core]),
        "wq": np.ascontiguousarray(inputs["L0_attn_Wq"]),
        "wk": np.ascontiguousarray(inputs["L0_attn_Wk"]),
        "wv": np.ascontiguousarray(inputs["L0_attn_Wv"]),
        "gamma": np.ascontiguousarray(inputs["L0_attn_gamma"]).reshape(1, 1),
        "w1": np.ascontiguousarray(inputs["L0_ffn_w1"]),
        "b1": np.ascontiguousarray(inputs["L0_ffn_b1"]),
        "w2": np.ascontiguousarray(inputs["L0_ffn_w2"]),
        "b2": np.ascontiguousarray(inputs["L0_ffn_b2"]),
    }


def make_in_map(inputs, core):
    return {
        "x": np.ascontiguousarray(inputs["L0_attn_x"][core]),
        "wq": np.ascontiguousarray(inputs["L0_attn_Wq"]),
        "wk": np.ascontiguousarray(inputs["L0_attn_Wk"]),
        "wv": np.ascontiguousarray(inputs["L0_attn_Wv"]),
        "gamma": np.ascontiguousarray(inputs["L0_attn_gamma"]).reshape(1, 1),
        "w1": np.ascontiguousarray(inputs["L0_ffn_w1"]),
        "b1": np.ascontiguousarray(inputs["L0_ffn_b1"]),
        "w2": np.ascontiguousarray(inputs["L0_ffn_w2"]),
        "b2": np.ascontiguousarray(inputs["L0_ffn_b2"]),
    }


def make_in_map(inputs, core):
    return {
        "x": np.ascontiguousarray(inputs["L0_attn_x"][core]),
        "wq": np.ascontiguousarray(inputs["L0_attn_Wq"]),
        "wk": np.ascontiguousarray(inputs["L0_attn_Wk"]),
        "wv": np.ascontiguousarray(inputs["L0_attn_Wv"]),
        "gamma": np.ascontiguousarray(inputs["L0_attn_gamma"]).reshape(1, 1),
        "w1": np.ascontiguousarray(inputs["L0_ffn_w1"]),
        "b1": np.ascontiguousarray(inputs["L0_ffn_b1"]),
        "w2": np.ascontiguousarray(inputs["L0_ffn_w2"]),
        "b2": np.ascontiguousarray(inputs["L0_ffn_b2"]),
    }


def make_in_map(inputs, core):
    return {
        "x": np.ascontiguousarray(inputs["L0_attn_x"][core]),
        "wq": np.ascontiguousarray(inputs["L0_attn_Wq"]),
        "wk": np.ascontiguousarray(inputs["L0_attn_Wk"]),
        "wv": np.ascontiguousarray(inputs["L0_attn_Wv"]),
        "gamma": np.ascontiguousarray(inputs["L0_attn_gamma"]).reshape(1, 1),
        "w1": np.ascontiguousarray(inputs["L0_ffn_w1"]),
        "b1": np.ascontiguousarray(inputs["L0_ffn_b1"]),
        "w2": np.ascontiguousarray(inputs["L0_ffn_w2"]),
        "b2": np.ascontiguousarray(inputs["L0_ffn_b2"]),
    }


def make_in_map(inputs, core):
    return {
        "x": np.ascontiguousarray(inputs["L0_attn_x"][core]),
        "wq": np.ascontiguousarray(inputs["L0_attn_Wq"]),
        "wk": np.ascontiguousarray(inputs["L0_attn_Wk"]),
        "wv": np.ascontiguousarray(inputs["L0_attn_Wv"]),
        "gamma": np.ascontiguousarray(inputs["L0_attn_gamma"]).reshape(1, 1),
        "w1": np.ascontiguousarray(inputs["L0_ffn_w1"]),
        "b1": np.ascontiguousarray(inputs["L0_ffn_b1"]),
        "w2": np.ascontiguousarray(inputs["L0_ffn_w2"]),
        "b2": np.ascontiguousarray(inputs["L0_ffn_b2"]),
    }



def make_in_map(inputs, core):
    return {
        "x": np.ascontiguousarray(inputs["L0_attn_x"][core]),
        "wq": np.ascontiguousarray(inputs["L0_attn_Wq"]),
        "wk": np.ascontiguousarray(inputs["L0_attn_Wk"]),
        "wv": np.ascontiguousarray(inputs["L0_attn_Wv"]),
        "gamma": np.ascontiguousarray(inputs["L0_attn_gamma"]).reshape(1, 1),
        "w1": np.ascontiguousarray(inputs["L0_ffn_w1"]),
        "b1": np.ascontiguousarray(inputs["L0_ffn_b1"]),
        "w2": np.ascontiguousarray(inputs["L0_ffn_w2"]),
        "b2": np.ascontiguousarray(inputs["L0_ffn_b2"]),
    }



def make_in_map(inputs, core):
    return {
        "x": np.ascontiguousarray(inputs["L0_attn_x"][core]),
        "wq": np.ascontiguousarray(inputs["L0_attn_Wq"]),
        "wk": np.ascontiguousarray(inputs["L0_attn_Wk"]),
        "wv": np.ascontiguousarray(inputs["L0_attn_Wv"]),
        "gamma": np.ascontiguousarray(inputs["L0_attn_gamma"]).reshape(1, 1),
        "w1": np.ascontiguousarray(inputs["L0_ffn_w1"]),
        "b1": np.ascontiguousarray(inputs["L0_ffn_b1"]),
        "w2": np.ascontiguousarray(inputs["L0_ffn_w2"]),
        "b2": np.ascontiguousarray(inputs["L0_ffn_b2"]),
    }



def make_in_map(inputs, core):
    return {
        "x": np.ascontiguousarray(inputs["L0_attn_x"][core]),
        "wq": np.ascontiguousarray(inputs["L0_attn_Wq"]),
        "wk": np.ascontiguousarray(inputs["L0_attn_Wk"]),
        "wv": np.ascontiguousarray(inputs["L0_attn_Wv"]),
        "gamma": np.ascontiguousarray(inputs["L0_attn_gamma"]).reshape(1, 1),
        "w1": np.ascontiguousarray(inputs["L0_ffn_w1"]),
        "b1": np.ascontiguousarray(inputs["L0_ffn_b1"]),
        "w2": np.ascontiguousarray(inputs["L0_ffn_w2"]),
        "b2": np.ascontiguousarray(inputs["L0_ffn_b2"]),
    }



def make_in_map(inputs, core):
    return {
        "x": np.ascontiguousarray(inputs["L0_attn_x"][core]),
        "wq": np.ascontiguousarray(inputs["L0_attn_Wq"]),
        "wk": np.ascontiguousarray(inputs["L0_attn_Wk"]),
        "wv": np.ascontiguousarray(inputs["L0_attn_Wv"]),
        "gamma": np.ascontiguousarray(inputs["L0_attn_gamma"]).reshape(1, 1),
        "w1": np.ascontiguousarray(inputs["L0_ffn_w1"]),
        "b1": np.ascontiguousarray(inputs["L0_ffn_b1"]),
        "w2": np.ascontiguousarray(inputs["L0_ffn_w2"]),
        "b2": np.ascontiguousarray(inputs["L0_ffn_b2"]),
    }



def make_in_map(inputs, core):
    return {
        "x": np.ascontiguousarray(inputs["L0_attn_x"][core]),
        "wq": np.ascontiguousarray(inputs["L0_attn_Wq"]),
        "wk": np.ascontiguousarray(inputs["L0_attn_Wk"]),
        "wv": np.ascontiguousarray(inputs["L0_attn_Wv"]),
        "gamma": np.ascontiguousarray(inputs["L0_attn_gamma"]).reshape(1, 1),
        "w1": np.ascontiguousarray(inputs["L0_ffn_w1"]),
        "b1": np.ascontiguousarray(inputs["L0_ffn_b1"]),
        "w2": np.ascontiguousarray(inputs["L0_ffn_w2"]),
        "b2": np.ascontiguousarray(inputs["L0_ffn_b2"]),
    }



def make_in_map(inputs, core):
    return {
        "x": np.ascontiguousarray(inputs["L0_attn_x"][core]),
        "wq": np.ascontiguousarray(inputs["L0_attn_Wq"]),
        "wk": np.ascontiguousarray(inputs["L0_attn_Wk"]),
        "wv": np.ascontiguousarray(inputs["L0_attn_Wv"]),
        "gamma": np.ascontiguousarray(inputs["L0_attn_gamma"]).reshape(1, 1),
        "w1": np.ascontiguousarray(inputs["L0_ffn_w1"]),
        "b1": np.ascontiguousarray(inputs["L0_ffn_b1"]),
        "w2": np.ascontiguousarray(inputs["L0_ffn_w2"]),
        "b2": np.ascontiguousarray(inputs["L0_ffn_b2"]),
    }



def make_in_map(inputs, core):
    return {
        "x": np.ascontiguousarray(inputs["L0_attn_x"][core]),
        "wq": np.ascontiguousarray(inputs["L0_attn_Wq"]),
        "wk": np.ascontiguousarray(inputs["L0_attn_Wk"]),
        "wv": np.ascontiguousarray(inputs["L0_attn_Wv"]),
        "gamma": np.ascontiguousarray(inputs["L0_attn_gamma"]).reshape(1, 1),
        "w1": np.ascontiguousarray(inputs["L0_ffn_w1"]),
        "b1": np.ascontiguousarray(inputs["L0_ffn_b1"]),
        "w2": np.ascontiguousarray(inputs["L0_ffn_w2"]),
        "b2": np.ascontiguousarray(inputs["L0_ffn_b2"]),
    }



def make_in_map(inputs, core):
    return {
        "x": np.ascontiguousarray(inputs["L0_attn_x"][core]),
        "wq": np.ascontiguousarray(inputs["L0_attn_Wq"]),
        "wk": np.ascontiguousarray(inputs["L0_attn_Wk"]),
        "wv": np.ascontiguousarray(inputs["L0_attn_Wv"]),
        "gamma": np.ascontiguousarray(inputs["L0_attn_gamma"]).reshape(1, 1),
        "w1": np.ascontiguousarray(inputs["L0_ffn_w1"]),
        "b1": np.ascontiguousarray(inputs["L0_ffn_b1"]),
        "w2": np.ascontiguousarray(inputs["L0_ffn_w2"]),
        "b2": np.ascontiguousarray(inputs["L0_ffn_b2"]),
    }



def make_in_map(inputs, core):
    return {
        "x": np.ascontiguousarray(inputs["L0_attn_x"][core]),
        "wq": np.ascontiguousarray(inputs["L0_attn_Wq"]),
        "wk": np.ascontiguousarray(inputs["L0_attn_Wk"]),
        "wv": np.ascontiguousarray(inputs["L0_attn_Wv"]),
        "gamma": np.ascontiguousarray(inputs["L0_attn_gamma"]).reshape(1, 1),
        "w1": np.ascontiguousarray(inputs["L0_ffn_w1"]),
        "b1": np.ascontiguousarray(inputs["L0_ffn_b1"]),
        "w2": np.ascontiguousarray(inputs["L0_ffn_w2"]),
        "b2": np.ascontiguousarray(inputs["L0_ffn_b2"]),
    }



def make_in_map(inputs, core):
    return {
        "x": np.ascontiguousarray(inputs["L0_attn_x"][core]),
        "wq": np.ascontiguousarray(inputs["L0_attn_Wq"]),
        "wk": np.ascontiguousarray(inputs["L0_attn_Wk"]),
        "wv": np.ascontiguousarray(inputs["L0_attn_Wv"]),
        "gamma": np.ascontiguousarray(inputs["L0_attn_gamma"]).reshape(1, 1),
        "w1": np.ascontiguousarray(inputs["L0_ffn_w1"]),
        "b1": np.ascontiguousarray(inputs["L0_ffn_b1"]),
        "w2": np.ascontiguousarray(inputs["L0_ffn_w2"]),
        "b2": np.ascontiguousarray(inputs["L0_ffn_b2"]),
    }



def make_in_map(inputs, core):
    return {
        "x": np.ascontiguousarray(inputs["L0_attn_x"][core]),
        "wq": np.ascontiguousarray(inputs["L0_attn_Wq"]),
        "wk": np.ascontiguousarray(inputs["L0_attn_Wk"]),
        "wv": np.ascontiguousarray(inputs["L0_attn_Wv"]),
        "gamma": np.ascontiguousarray(inputs["L0_attn_gamma"]).reshape(1, 1),
        "w1": np.ascontiguousarray(inputs["L0_ffn_w1"]),
        "b1": np.ascontiguousarray(inputs["L0_ffn_b1"]),
        "w2": np.ascontiguousarray(inputs["L0_ffn_w2"]),
        "b2": np.ascontiguousarray(inputs["L0_ffn_b2"]),
    }



def make_in_map(inputs, core):
    return {
        "x": np.ascontiguousarray(inputs["L0_attn_x"][core]),
        "wq": np.ascontiguousarray(inputs["L0_attn_Wq"]),
        "wk": np.ascontiguousarray(inputs["L0_attn_Wk"]),
        "wv": np.ascontiguousarray(inputs["L0_attn_Wv"]),
        "gamma": np.ascontiguousarray(inputs["L0_attn_gamma"]).reshape(1, 1),
        "w1": np.ascontiguousarray(inputs["L0_ffn_w1"]),
        "b1": np.ascontiguousarray(inputs["L0_ffn_b1"]),
        "w2": np.ascontiguousarray(inputs["L0_ffn_w2"]),
        "b2": np.ascontiguousarray(inputs["L0_ffn_b2"]),
    }



def make_in_map(inputs, core):
    return {
        "x": np.ascontiguousarray(inputs["L0_attn_x"][core]),
        "wq": np.ascontiguousarray(inputs["L0_attn_Wq"]),
        "wk": np.ascontiguousarray(inputs["L0_attn_Wk"]),
        "wv": np.ascontiguousarray(inputs["L0_attn_Wv"]),
        "gamma": np.ascontiguousarray(inputs["L0_attn_gamma"]).reshape(1, 1),
        "w1": np.ascontiguousarray(inputs["L0_ffn_w1"]),
        "b1": np.ascontiguousarray(inputs["L0_ffn_b1"]),
        "w2": np.ascontiguousarray(inputs["L0_ffn_w2"]),
        "b2": np.ascontiguousarray(inputs["L0_ffn_b2"]),
    }



def make_in_map(inputs, core):
    return {
        "x": np.ascontiguousarray(inputs["L0_attn_x"][core]),
        "wq": np.ascontiguousarray(inputs["L0_attn_Wq"]),
        "wk": np.ascontiguousarray(inputs["L0_attn_Wk"]),
        "wv": np.ascontiguousarray(inputs["L0_attn_Wv"]),
        "gamma": np.ascontiguousarray(inputs["L0_attn_gamma"]).reshape(1, 1),
        "w1": np.ascontiguousarray(inputs["L0_ffn_w1"]),
        "b1": np.ascontiguousarray(inputs["L0_ffn_b1"]),
        "w2": np.ascontiguousarray(inputs["L0_ffn_w2"]),
        "b2": np.ascontiguousarray(inputs["L0_ffn_b2"]),
    }



def make_in_map(inputs, core):
    return {
        "x": np.ascontiguousarray(inputs["L0_attn_x"][core]),
        "wq": np.ascontiguousarray(inputs["L0_attn_Wq"]),
        "wk": np.ascontiguousarray(inputs["L0_attn_Wk"]),
        "wv": np.ascontiguousarray(inputs["L0_attn_Wv"]),
        "gamma": np.ascontiguousarray(inputs["L0_attn_gamma"]).reshape(1, 1),
        "w1": np.ascontiguousarray(inputs["L0_ffn_w1"]),
        "b1": np.ascontiguousarray(inputs["L0_ffn_b1"]),
        "w2": np.ascontiguousarray(inputs["L0_ffn_w2"]),
        "b2": np.ascontiguousarray(inputs["L0_ffn_b2"]),
    }



def make_in_map(inputs, core):
    return {
        "x": np.ascontiguousarray(inputs["L0_attn_x"][core]),
        "wq": np.ascontiguousarray(inputs["L0_attn_Wq"]),
        "wk": np.ascontiguousarray(inputs["L0_attn_Wk"]),
        "wv": np.ascontiguousarray(inputs["L0_attn_Wv"]),
        "gamma": np.ascontiguousarray(inputs["L0_attn_gamma"]).reshape(1, 1),
        "w1": np.ascontiguousarray(inputs["L0_ffn_w1"]),
        "b1": np.ascontiguousarray(inputs["L0_ffn_b1"]),
        "w2": np.ascontiguousarray(inputs["L0_ffn_w2"]),
        "b2": np.ascontiguousarray(inputs["L0_ffn_b2"]),
    }



def make_in_map(inputs, core):
    return {
        "x": np.ascontiguousarray(inputs["L0_attn_x"][core]),
        "wq": np.ascontiguousarray(inputs["L0_attn_Wq"]),
        "wk": np.ascontiguousarray(inputs["L0_attn_Wk"]),
        "wv": np.ascontiguousarray(inputs["L0_attn_Wv"]),
        "gamma": np.ascontiguousarray(inputs["L0_attn_gamma"]).reshape(1, 1),
        "w1": np.ascontiguousarray(inputs["L0_ffn_w1"]),
        "b1": np.ascontiguousarray(inputs["L0_ffn_b1"]),
        "w2": np.ascontiguousarray(inputs["L0_ffn_w2"]),
        "b2": np.ascontiguousarray(inputs["L0_ffn_b2"]),
    }



def make_in_map(inputs, core):
    return {
        "x": np.ascontiguousarray(inputs["L0_attn_x"][core]),
        "wq": np.ascontiguousarray(inputs["L0_attn_Wq"]),
        "wk": np.ascontiguousarray(inputs["L0_attn_Wk"]),
        "wv": np.ascontiguousarray(inputs["L0_attn_Wv"]),
        "gamma": np.ascontiguousarray(inputs["L0_attn_gamma"]).reshape(1, 1),
        "w1": np.ascontiguousarray(inputs["L0_ffn_w1"]),
        "b1": np.ascontiguousarray(inputs["L0_ffn_b1"]),
        "w2": np.ascontiguousarray(inputs["L0_ffn_w2"]),
        "b2": np.ascontiguousarray(inputs["L0_ffn_b2"]),
    }



def make_in_map(inputs, core):
    return {
        "x": np.ascontiguousarray(inputs["L0_attn_x"][core]),
        "wq": np.ascontiguousarray(inputs["L0_attn_Wq"]),
        "wk": np.ascontiguousarray(inputs["L0_attn_Wk"]),
        "wv": np.ascontiguousarray(inputs["L0_attn_Wv"]),
        "gamma": np.ascontiguousarray(inputs["L0_attn_gamma"]).reshape(1, 1),
        "w1": np.ascontiguousarray(inputs["L0_ffn_w1"]),
        "b1": np.ascontiguousarray(inputs["L0_ffn_b1"]),
        "w2": np.ascontiguousarray(inputs["L0_ffn_w2"]),
        "b2": np.ascontiguousarray(inputs["L0_ffn_b2"]),
    }



def make_in_map(inputs, core):
    return {
        "x": np.ascontiguousarray(inputs["L0_attn_x"][core]),
        "wq": np.ascontiguousarray(inputs["L0_attn_Wq"]),
        "wk": np.ascontiguousarray(inputs["L0_attn_Wk"]),
        "wv": np.ascontiguousarray(inputs["L0_attn_Wv"]),
        "gamma": np.ascontiguousarray(inputs["L0_attn_gamma"]).reshape(1, 1),
        "w1": np.ascontiguousarray(inputs["L0_ffn_w1"]),
        "b1": np.ascontiguousarray(inputs["L0_ffn_b1"]),
        "w2": np.ascontiguousarray(inputs["L0_ffn_w2"]),
        "b2": np.ascontiguousarray(inputs["L0_ffn_b2"]),
    }



def make_in_map(inputs, core):
    return {
        "x": np.ascontiguousarray(inputs["L0_attn_x"][core]),
        "wq": np.ascontiguousarray(inputs["L0_attn_Wq"]),
        "wk": np.ascontiguousarray(inputs["L0_attn_Wk"]),
        "wv": np.ascontiguousarray(inputs["L0_attn_Wv"]),
        "gamma": np.ascontiguousarray(inputs["L0_attn_gamma"]).reshape(1, 1),
        "w1": np.ascontiguousarray(inputs["L0_ffn_w1"]),
        "b1": np.ascontiguousarray(inputs["L0_ffn_b1"]),
        "w2": np.ascontiguousarray(inputs["L0_ffn_w2"]),
        "b2": np.ascontiguousarray(inputs["L0_ffn_b2"]),
    }



def make_in_map(inputs, core):
    return {
        "x": np.ascontiguousarray(inputs["L0_attn_x"][core]),
        "wq": np.ascontiguousarray(inputs["L0_attn_Wq"]),
        "wk": np.ascontiguousarray(inputs["L0_attn_Wk"]),
        "wv": np.ascontiguousarray(inputs["L0_attn_Wv"]),
        "gamma": np.ascontiguousarray(inputs["L0_attn_gamma"]).reshape(1, 1),
        "w1": np.ascontiguousarray(inputs["L0_ffn_w1"]),
        "b1": np.ascontiguousarray(inputs["L0_ffn_b1"]),
        "w2": np.ascontiguousarray(inputs["L0_ffn_w2"]),
        "b2": np.ascontiguousarray(inputs["L0_ffn_b2"]),
    }




def make_in_map(inputs, core):
    f32 = np.float32
    return {
        "x": np.ascontiguousarray(np.asarray(inputs["L0_attn_x"], dtype=f32)[core]),
        "wq": np.ascontiguousarray(np.asarray(inputs["L0_attn_Wq"], dtype=f32)),
        "wk": np.ascontiguousarray(np.asarray(inputs["L0_attn_Wk"], dtype=f32)),
        "wv": np.ascontiguousarray(np.asarray(inputs["L0_attn_Wv"], dtype=f32)),
        "gamma": np.ascontiguousarray(np.asarray(inputs["L0_attn_gamma"], dtype=f32)).reshape(1, 1),
        "w1": np.ascontiguousarray(np.asarray(inputs["L0_ffn_w1"], dtype=f32)),
        "b1": np.ascontiguousarray(np.asarray(inputs["L0_ffn_b1"], dtype=f32)),
        "w2": np.ascontiguousarray(np.asarray(inputs["L0_ffn_w2"], dtype=f32)),
        "b2": np.ascontiguousarray(np.asarray(inputs["L0_ffn_b2"], dtype=f32)),
    }


def kernel(**inputs):
    from concourse.bass_utils import run_bass_kernel_spmd
    nc = build_nc()
    in_maps = [make_in_map(inputs, core) for core in range(8)]
    res = run_bass_kernel_spmd(nc, in_maps, core_ids=list(range(8)))
    return np.stack([np.asarray(res.results[i]["y"], dtype=np.float32)
                     for i in range(8)], axis=0)


# version 37
# speedup vs baseline: 1.1426x; 1.0046x over previous
import contextlib
import functools
import numpy as np
import concourse.bass as bass
import concourse.mybir as mybir

F32 = mybir.dt.float32
BF16 = mybir.dt.bfloat16
U8 = mybir.dt.uint8
AX = mybir.AxisListType.X
OP = mybir.AluOpType
AF = mybir.ActivationFunctionType

S, C, HD = 2048, 1024, 4096
NS, NC, NH = 16, 8, 32
W2_RES = 16
EPS = 1e-5

ENGS = ("tensor", "vector", "scalar", "gpsimd", "sync")
STRICT = True
STRICT_ENGS = {"vector", "scalar"}
ENG_SEM = {"tensor": "T", "vector": "V", "scalar": "Sc", "gpsimd": "G"}


class Plan:
    def __init__(self):
        self.ops = {e: [] for e in ENGS}
        self.cnt = {}
        self.strict_now = set()

    def emit(self, eng, fn, waits=(), inc=None, inc_by=1):
        waits = [(s, v) for (s, v) in waits if v > 0]
        if eng in ENG_SEM:
            es = ENG_SEM[eng]
            if (eng == "gpsimd" or (STRICT and eng in STRICT_ENGS)
                    or eng in self.strict_now) and self.c(es) > 0:
                waits.append((es, self.c(es)))
            if inc is None:
                inc = es
        self.ops[eng].append((fn, waits, inc, inc_by))
        if inc is not None:
            self.cnt[inc] = self.cnt.get(inc, 0) + inc_by

    def c(self, sem):
        return self.cnt.get(sem, 0)


def build_nc():
    nc = bass.Bass("TRN2", target_bir_lowering=False)
    x = nc.dram_tensor("x", [S, C], F32, kind="ExternalInput").ap()
    wq = nc.dram_tensor("wq", [C, C], F32, kind="ExternalInput").ap()
    wk = nc.dram_tensor("wk", [C, C], F32, kind="ExternalInput").ap()
    wv = nc.dram_tensor("wv", [C, C], F32, kind="ExternalInput").ap()
    gamma = nc.dram_tensor("gamma", [1, 1], F32, kind="ExternalInput").ap()
    w1 = nc.dram_tensor("w1", [C, HD], F32, kind="ExternalInput").ap()
    b1 = nc.dram_tensor("b1", [HD], F32, kind="ExternalInput").ap()
    w2 = nc.dram_tensor("w2", [HD, C], F32, kind="ExternalInput").ap()
    b2 = nc.dram_tensor("b2", [C], F32, kind="ExternalInput").ap()
    y = nc.dram_tensor("y", [S, C], F32, kind="ExternalOutput").ap()
    w1bf = nc.dram_tensor("w1bf", [C, HD], BF16, kind="Internal").ap()
    w2bf = nc.dram_tensor("w2bf", [HD, C], BF16, kind="Internal").ap()

    ctx = contextlib.ExitStack()
    sb = lambda n, s, d: ctx.enter_context(nc.sbuf_tensor(n, s, d))
    regA = sb("regA", [128, 65536], U8)
    regFG = sb("regFG", [128, 65536], U8)
    regH = sb("regH", [128, 32768], U8)
    regE = sb("regE", [128, 20480], U8)
    rA_f = regA.bitcast(F32)
    rA_b = regA.bitcast(BF16)
    rFG = regFG.bitcast(BF16)
    rH = regH.bitcast(BF16)
    rE = regE.bitcast(BF16)

    resid = lambda ti: rA_f[:, ti * 1024:(ti + 1) * 1024]
    resid2 = lambda ti, et: rA_f[:, ti * 1024 + et * 512: ti * 1024 + et * 512 + 512]

    _wtg_base = (12288, 8192)
    wtg_all2 = [rA_b[:, b:b + 4096].rearrange("p (ci d) -> p ci d", ci=8)
                for b in _wtg_base]
    WTg2 = lambda bi, ci, off, n: rA_b[:, _wtg_base[bi] + ci * 512 + off:
                                       _wtg_base[bi] + ci * 512 + off + n]
    xfm_all = rA_b[:, 16384:32768].rearrange("p (ci s) -> p ci s", ci=8)
    xfm = lambda ci, s0, n: rA_b[:, 16384 + ci * 2048 + s0: 16384 + ci * 2048 + s0 + n]
    fsl = lambda d, s0, n: rFG[:, d * 2048 + s0: d * 2048 + s0 + n]
    gsl = lambda d, s0, n: rFG[:, 16384 + d * 2048 + s0: 16384 + d * 2048 + s0 + n]
    w1sl = lambda ci, h0, n: rFG[:, ci * 4096 + h0: ci * 4096 + h0 + n]
    hsl = lambda si, c0, n: rH[:, si * 1024 + c0: si * 1024 + c0 + n]
    w2res = lambda hc, e0, n: rH[:, hc * 1024 + e0: hc * 1024 + e0 + n]
    ebT = [rE[:, 0:2048], rE[:, 2048:4096]]
    ebsm2 = [rE[:, 4096:6144], rE[:, 6144:8192]]
    x2fm2 = [rE[:, 0:2048], rE[:, 6144:8192]]
    x2f_all2 = [rE[:, 0:2048].rearrange("p (ci s) -> p ci s", ci=8),
                rE[:, 6144:8192].rearrange("p (ci s) -> p ci s", ci=8)]
    hid = [rE[:, 2048:2304], rE[:, 2304:2560]]
    w2s = [rE[:, 3072:4096], rE[:, 4096:5120], rE[:, 8192:9216], rE[:, 9216:10240]]

    ident = sb("ident", [128, 128], BF16)
    identf = sb("identf", [128, 128], F32)
    ones1 = sb("ones1", [1, 128], BF16)
    b2rowb = sb("b2rowb", [1, 1024], BF16)
    b1sb = sb("b1sb", [128, 32], F32)
    gbc = sb("gbc", [128, 1], F32)
    cdb = sb("cdb", [128, 4], F32)
    stats = sb("stats", [128, 64], F32)
    xch = sb("xch", [128, 8192], BF16)
    xc = [xch[:, k * 1024:(k + 1) * 1024] for k in range(8)]
    wbw = xch
    varsink = sb("varsink", [128, 1024], F32)

    CONSTS = [EPS, -0.5, 1.0 / 1024]
    for i, cv in enumerate(CONSTS):
        nc.const_aps.register(F32, cv, cdb[:, i:i + 1])

    NM = lambda par, st: stats[:, par * 4 + st: par * 4 + st + 1]
    NMALL = lambda par: stats[:, par * 4: par * 4 + 4]
    NEGMAX = lambda par: stats[:, 8 + par: 9 + par]
    ZP = lambda par, st: stats[:, 10 + par * 4 + st: 11 + par * 4 + st]
    ZPALL = lambda par: stats[:, 10 + par * 4: 14 + par * 4]
    ZCOL = stats[:, 18:19]
    ZICOL = stats[:, 19:20]
    ZG2 = [stats[:, 20:21], stats[:, 29:30]]
    NSUM = stats[:, 24:25]
    NEGM = stats[:, 25:26]
    VSUM = stats[:, 26:27]
    LNV = stats[:, 27:28]
    RSTD = stats[:, 28:29]
    KCOL = stats[:, 29:30]

    ps = ctx.enter_context(nc.psum_tensor("ps", [128, 4096], F32))
    ps_bf = ps.bitcast(BF16)
    bank = lambda b, off, n: ps[:, b * 512 + off: b * 512 + off + n]
    bankb = lambda b, off, n: ps_bf[:, b * 1024 + off: b * 1024 + off + n]
    bankb3 = lambda b: ps_bf[:, b * 1024:(b + 1) * 1024].rearrange(
        "p (ci s) -> p ci s", ci=8)

    sems = {}
    for name in ("T", "V", "Sc", "G", "LXC0", "LXC1", "LXC2", "LXC3",
                 "LWB0", "LWB1", "LWB2", "LWB3",
                 "LTE0", "LTE1", "LTC0", "LTC1",
                 "LW2S0", "LW2S1", "LW2S2", "LW2S3", "LXA", "LXB",
                 "LW1S0", "LW1S1", "LW1S2", "LW1S3",
                 "LW1S4", "LW1S5", "LW1S6", "LW1S7",
                 "LW2R0", "LW2R1", "LW2R2", "LW2R3", "LMISC", "LMISCG", "STY",
                 "LCW", "LXB2",
                 "LW2R0B", "LW2R1B", "LW2R2B", "LW2R3B"):
        sems[name] = ctx.enter_context(nc.semaphore(name))

    P = Plan()
    pfree = {b: [] for b in range(8)}

    def v_init(v):
        v.memset(ident[:], 1.0)
        v.memset(identf[:], 1.0)
        v.memset(ones1[:], 1.0)
        ins = None
        for i, cv in enumerate(CONSTS):
            ins = v.memset(cdb[:, i:i + 1], cv)
        return ins
    P.emit("vector", v_init, inc="V")
    V_INIT = P.c("V")

    P.emit("gpsimd", lambda g: g.affine_select(
        out=ident[:], in_=ident[:], pattern=[[-1, 128]],
        compare_op=OP.is_equal, fill=0.0, base=0, channel_multiplier=1),
        waits=[("V", V_INIT)], inc="G")
    P.emit("gpsimd", lambda g: g.affine_select(
        out=identf[:], in_=identf[:], pattern=[[-1, 128]],
        compare_op=OP.is_equal, fill=0.0, base=0, channel_multiplier=1),
        inc="G")
    G_INIT = P.c("G")

    P.emit("sync", lambda e: e.dma_start(
        out=rA_f[:, 0:4 * 1024].rearrange("p (a b) -> p a b", a=4),
        in_=bass.AP(x.tensor, 0, [[1024, 128], [128 * 1024, 4], [1, 1024]])),
        inc="LXA", inc_by=16)
    P.emit("sync", lambda e: e.dma_start(
        out=varsink[0:32, 0:128], in_=bass.AP(b1.tensor, 0, [[128, 32], [1, 128]])),
        inc="LMISC", inc_by=16)
    P.emit("sync", lambda e: e.dma_start(
        out=gbc[:], in_=bass.AP(gamma.tensor, 0, [[0, 128], [1, 1]])),
        inc="LMISC", inc_by=16)
    P.emit("vector", lambda e: e.tensor_scalar_mul(gbc[:], gbc[:], 0.5),
           waits=[("LMISC", 32)], inc="V")
    P.emit("gpsimd", lambda e: e.dma_start(
        out=b2rowb[0:1, :], in_=bass.AP(b2.tensor, 0, [[1024, 1], [1, 1024]])),
        inc="LMISCG", inc_by=16)
    LMISC_ALL = P.c("LMISC")

    P.emit("tensor", lambda e: e.transpose(bank(6, 0, 32), varsink[0:32, 0:128],
                                           identf[0:32, 0:32]),
           waits=[("LMISC", 32), ("G", G_INIT)], inc="T")
    P.emit("vector", lambda e: e.tensor_copy(b1sb[:], bank(6, 0, 32)),
           waits=[("T", P.c("T"))], inc="V")
    pfree[6] = [("V", P.c("V"))]

    xT_done, xT_evac = {}, {}
    for pr in range(NS // 2):
        sl = pr % 4
        waits = [("T", xT_done[2 * pr - 7])] if pr >= 4 else []
        P.emit("gpsimd",
               (lambda pr_, sl_: lambda e: e.dma_start(
                   out=xch[:, sl_ * 2048:(sl_ + 1) * 2048].rearrange(
                       "p (a b) -> p a b", a=2),
                   in_=bass.AP(x.tensor, pr_ * 2 * 128 * 1024,
                               [[1024, 128], [128 * 1024, 2], [1, 1024]])))(pr, sl),
               waits=waits, inc=f"LXC{sl}", inc_by=16)
        for half in range(2):
            ti = 2 * pr + half
            bk = 4 + (ti % 2)
            twaits = [(f"LXC{sl}", 16 * (pr // 4 + 1)), ("G", G_INIT)] + pfree[bk]
            def tfn(e, ti=ti, bk=bk, sl=sl, half=half):
                ins = None
                for ci in range(NC):
                    ins = e.transpose(bankb(bk, ci * 128, 128),
                                      xc[sl * 2 + half][:, ci * 128:(ci + 1) * 128],
                                      ident[:])
                return ins
            P.emit("tensor", tfn, waits=twaits, inc="T")
            xT_done[ti] = P.c("T")
            P.emit("vector",
                   (lambda ti_, bk_: lambda e: e.tensor_copy(
                       xfm_all[:, :, ti_ * 128:(ti_ + 1) * 128], bankb3(bk_)))(ti, bk),
                   waits=[("T", xT_done[ti])], inc="V")
            xT_evac[ti] = P.c("V")
            pfree[bk] = [("V", xT_evac[ti])]
    XFM_ALL_V = P.c("V")

    wdram = {0: wq, 1: wk, 2: wv}
    rot = [0]

    def qkv_mm_tilegroup(lhsT_sl, rhs_sl, out_fn, N, extra_waits, evac_eng):
        bk = rot[0] % 4
        rot[0] += 1
        waits = list(extra_waits) + pfree[bk]
        def tfn(e, bk=bk, lhsT_sl=lhsT_sl, rhs_sl=rhs_sl, N=N):
            ins = None
            for ci in range(NC):
                ins = e.matmul(bank(bk, 0, N), lhsT_sl(ci), rhs_sl(ci),
                               start=(ci == 0), stop=(ci == NC - 1))
            return ins
        P.emit("tensor", tfn, waits=waits, inc="T")
        tcnt = P.c("T")
        sem = "Sc" if evac_eng == "scalar" else "V"
        def efn(e, bk=bk, out_fn=out_fn, N=N):
            out = out_fn()
            return (e.copy(out, bank(bk, 0, N)) if evac_eng == "scalar"
                    else e.tensor_copy(out, bank(bk, 0, N)))
        P.emit(evac_eng, efn, waits=[("T", tcnt)], inc=sem)
        pfree[bk] = [(sem, P.c(sem))]

    wT_done = {}
    lwb_cnt = [0, 0, 0, 0]
    wrow_idx_box = [0]
    wt_ready = {}

    def emit_w_lt(k):
        wi, grp = divmod(k, 2)
        bi = k % 2
        for rp in range(2):
            wrow_idx = wrow_idx_box[0]
            wp = wrow_idx // 2
            sl = wp % 4
            dj0 = grp * 4 + rp * 2
            if wp >= 4:
                waits = [("T", wT_done[wrow_idx - 7])]
            else:
                waits = [("T", xT_done[9 + 2 * wp])]
            P.emit("gpsimd",
                   (lambda wi_, dj_, sl_: lambda e: e.dma_start(
                       out=wbw[:, sl_ * 2048:(sl_ + 1) * 2048].rearrange(
                           "p (a b) -> p a b", a=2),
                       in_=bass.AP(wdram[wi_].tensor, dj_ * 128 * 1024,
                                   [[1024, 128], [128 * 1024, 2], [1, 1024]])))(
                       wi, dj0, sl),
                   waits=waits, inc=f"LWB{sl}", inc_by=16)
            lwb_cnt[sl] += 1
            for half in range(2):
                r = rp * 2 + half
                bk = 4 + (wrow_idx_box[0] % 4)
                twaits = [(f"LWB{sl}", 16 * lwb_cnt[sl]), ("G", G_INIT)] + pfree[bk]
                if k >= 2:
                    twaits.append(("T", wt_mm_done[k - 2]))
                def tfn(e, sl=sl, bk=bk, half=half):
                    base = sl * 2048 + half * 1024
                    ins = None
                    for ci in range(NC):
                        ins = e.transpose(
                            bankb(bk, ci * 128, 128),
                            wbw[:, base + ci * 128: base + (ci + 1) * 128],
                            ident[:])
                    return ins
                P.emit("tensor", tfn, waits=twaits, inc="T")
                wT_done[wrow_idx_box[0]] = P.c("T")
                P.emit("vector",
                       (lambda r_, bk_, bi_: lambda e: e.tensor_copy(
                           wtg_all2[bi_][:, :, r_ * 128:(r_ + 1) * 128],
                           bankb3(bk_)))(r, bk, bi),
                       waits=[("T", wT_done[wrow_idx_box[0]])], inc="V")
                pfree[bk] = [("V", P.c("V"))]
                wrow_idx_box[0] += 1
        wt_ready[k] = P.c("V")

    wt_mm_done = {}
    emit_w_lt(0)
    for k in range(6):
        wi, grp = divmod(k, 2)
        bi = k % 2
        if wi < 2:
            outsl = fsl if wi == 0 else gsl
            tiles = [(r, st) for r in range(4) for st in range(4)]
            for idx, (r, st) in enumerate(tiles):
                if idx == 8 and k + 1 < 6:
                    emit_w_lt(k + 1)
                dj = grp * 4 + r
                qkv_mm_tilegroup(
                    (lambda r_, bi_: lambda ci: WTg2(bi_, ci, r_ * 128, 128))(r, bi),
                    (lambda st_: lambda ci: xfm(ci, st_ * 512, 512))(st),
                    (lambda dj_, st_, o_=outsl: lambda: o_(dj_, st_ * 512, 512))(dj, st),
                    512, [("V", max(wt_ready[k], XFM_ALL_V))], "scalar")
        else:
            for si in range(NS):
                if si == 8 and k + 1 < 6:
                    emit_w_lt(k + 1)
                qkv_mm_tilegroup(
                    (lambda si_, bi_: lambda ci: xfm(ci, si_ * 128, 128))(si, bi),
                    (lambda bi_: lambda ci: WTg2(bi_, ci, 0, 512))(bi),
                    (lambda si_, grp_: lambda: hsl(si_, grp_ * 512, 512))(si, grp),
                    512, [("V", max(wt_ready[k], XFM_ALL_V))], "vector")
        wt_mm_done[k] = P.c("T")

    A_END_T = P.c("T")
    A_END_V = P.c("V")
    A_END_S = P.c("Sc")

    for i in range(8):
        P.emit("gpsimd",
               (lambda i_: lambda e: e.dma_start(
                   out=w1bf[i_ * 128:(i_ + 1) * 128, :],
                   in_=w1[i_ * 128:(i_ + 1) * 128, :]))(i),
               inc="LCW", inc_by=16)
    for i in range(16):
        P.emit("gpsimd",
               (lambda i_: lambda e: e.dma_start(
                   out=w2bf[i_ * 256:(i_ + 1) * 256, :],
                   in_=w2[i_ * 256:(i_ + 1) * 256, :]))(i),
               inc="LCW", inc_by=16)
    LCW_ALL = 384

    P.emit("scalar", lambda e: e.dma_start(
        out=rA_f[:, 4 * 1024:16 * 1024].rearrange("p (a b) -> p a b", a=12),
        in_=bass.AP(x.tensor, 4 * 128 * 1024, [[1024, 128], [128 * 1024, 12], [1, 1024]])),
        waits=[("T", A_END_T)], inc="LXB", inc_by=16)

    sc_done, exp_done, trT_done = {}, {}, {}
    ebsm_evac, ebsm_evac_h1, av_done, oevac_done = {}, {}, {}, {}
    nm_done = {}

    def emit_score_tile(ti, st, extra_waits):
        par = ti % 2
        bk = par * 4 + st
        def tfn(e, ti=ti, st=st, bk=bk):
            ins = None
            for d in range(NC):
                ins = e.matmul(bank(bk, 0, 512), gsl(d, ti * 128, 128),
                               fsl(d, st * 512, 512), start=(d == 0), stop=(d == NC - 1))
            return ins
        P.emit("tensor", tfn, waits=extra_waits, inc="T")
        sc_done[(ti, st)] = P.c("T")
        P.emit("vector",
               (lambda par_, st_, bk_: lambda e: e.reduce_max(
                   NM(par_, st_), bank(bk_, 0, 512), axis=AX, negate=True))(par, st, bk),
               waits=[("T", sc_done[(ti, st)])], inc="V")

    def emit_negmax_combine(ti):
        par = ti % 2
        P.emit("vector",
               (lambda par_: lambda e: e.tensor_reduce(
                   NEGMAX(par_), NMALL(par_), axis=AX, op=OP.min))(par), inc="V")
        nm_done[ti] = P.c("V")

    def emit_softmax(ti):
        par = ti % 2
        g0 = par * 4
        waits = [("V", nm_done[ti])]
        if ti >= 2:
            waits.append(ebsm_evac[ti - 2])
        for st in range(4):
            P.emit("scalar",
                   (lambda st_, g0_, par_: lambda e: e.activation(
                       out=ebT[par_][:, st_ * 512:(st_ + 1) * 512],
                       in_=bank(g0_ + st_, 0, 512), func=AF.Exp,
                       bias=NEGMAX(par_), scale=1.0,
                       accum_out=ZP(par_, st_)))(st, g0, par),
                   waits=(waits if st == 0 else []), inc="Sc")
        exp_done[ti] = P.c("Sc")

    lte_cnt = [0, 0]
    tail_trT = {}

    def emit_transposes(ti):
        par = ti % 2
        eb = ebsm2[par]
        if 2 <= ti < NS - 2:
            waits = [("Sc", exp_done[ti])]
            if ti >= 2:
                waits.append(("T", av_done.get(ti - 2, 0)))
                waits.append((f"LTE{par}", lte_cnt[par]))
            P.emit("sync",
                   (lambda par_, eb_=eb: lambda e: e.dma_start_transpose(
                       eb_[:].rearrange("p (k c) -> p k c", k=16),
                       ebT[par_][:].rearrange("p (k c) -> p k c", k=16)))(par),
                   waits=waits, inc=f"LTE{par}", inc_by=16)
            lte_cnt[par] += 16
            ebsm_evac[ti] = (f"LTE{par}", lte_cnt[par])
        else:
            g0 = par * 4
            def tfn(e, par=par, g0=g0):
                ins = None
                for k in range(NS):
                    ins = e.transpose(
                        ps_bf[:, g0 * 1024 + k * 128: g0 * 1024 + (k + 1) * 128],
                        ebT[par][:, k * 128:(k + 1) * 128], ident[:])
                return ins
            P.emit("tensor", tfn,
                   waits=[("Sc", exp_done[ti]), (f"LTE{par}", lte_cnt[par])],
                   inc="T")
            trT = P.c("T")
            tail_trT[ti] = trT
            P.emit("vector",
                   (lambda g0_, eb_=eb: lambda e: e.tensor_copy(
                       eb_[:, 0:2048], ps_bf[:, g0_ * 1024: g0_ * 1024 + 2048]))(g0),
                   waits=[("T", trT), ("T", av_done.get(ti - 2, 0))], inc="V")
            ebsm_evac[ti] = ("V", P.c("V"))
        P.emit("vector",
               (lambda par_: lambda e: e.tensor_reduce(
                   ZCOL, ZPALL(par_), axis=AX, op=OP.add))(par),
               waits=[("Sc", exp_done[ti]), ("LMISC", LMISC_ALL)], inc="V")
        P.emit("vector", lambda e: e.reciprocal(ZICOL, ZCOL), inc="V")
        P.emit("vector",
               (lambda par_: lambda e: e.tensor_mul(ZG2[par_], ZICOL, gbc[:]))(par),
               inc="V")

    def emit_av(ti):
        g0 = (ti % 2) * 4
        eb = ebsm2[ti % 2]
        for ct in range(2):
            bk = g0 + 2 + ct
            def tfn_a(e, ct=ct, bk=bk, eb=eb):
                ins = None
                for k in range(NS):
                    ins = e.matmul(bank(bk, 0, 512), eb[:, k * 128:(k + 1) * 128],
                                   hsl(k, ct * 512, 512),
                                   start=(k == 0), stop=(k == NS - 1))
                return ins
            P.emit("tensor", tfn_a, waits=[ebsm_evac[ti]], inc="T")
        av_done[ti] = P.c("T")

    oevac_h1 = {}

    def emit_oevac(ti):
        g0 = (ti % 2) * 4
        xw = ("LXA", 16) if ti < 4 else ("LXB", 16)
        for ct in range(2):
            P.emit("vector",
                   (lambda ti_, ct_, bk_: lambda e: e.scalar_tensor_tensor(
                       out=resid2(ti_, ct_), in0=bank(bk_, 0, 512), scalar=ZG2[ti_ % 2],
                       in1=resid2(ti_, ct_), op0=OP.mult, op1=OP.add))(ti, ct, g0 + 2 + ct),
                   waits=[("T", av_done[ti]), xw], inc="V")
            if ct == 0:
                oevac_h1[ti] = P.c("V")
        oevac_done[ti] = P.c("V")

    R1COL = lambda ti: stats[:, 32 + ti:33 + ti]
    M1COL = lambda ti: stats[:, 48 + ti:49 + ti]
    BNS = stats[:, 21:22]
    BVS2 = [stats[:, 23:24], stats[:, 27:28]]
    BLL2 = [stats[:, 24:25], stats[:, 28:29]]
    K1TMP = [stats[:, 25:26], stats[:, 26:27]]

    bstats_sc = {}

    def emit_ln1_bstats(ti):
        P.emit("vector",
               (lambda t_: lambda e: e.reduce_sum(BNS, resid(t_), axis=AX,
                                                  negate=True))(ti), inc="V")
        P.emit("vector",
               (lambda t_: lambda e: e.tensor_scalar_mul(
                   M1COL(t_), BNS, 1.0 / 1024))(ti), inc="V")
        vwaits = []
        if ti >= 2:
            vwaits.append(("Sc", bstats_sc[ti - 2]))
        P.emit("vector",
               (lambda t_: lambda e: e.scalar_tensor_tensor(
                   out=varsink[:], in0=resid(t_), scalar=M1COL(t_), in1=resid(t_),
                   op0=OP.add, op1=OP.mult, accum_out=BVS2[t_ % 2]))(ti),
               waits=vwaits, inc="V")
        vv = P.c("V")
        P.emit("scalar",
               (lambda t_: lambda e: e.activation(
                   out=BLL2[t_ % 2], in_=BVS2[t_ % 2], func=AF.Ln,
                   scale=1.0 / 1024, bias=EPS))(ti),
               waits=[("V", vv)], inc="Sc")
        P.emit("scalar",
               (lambda t_: lambda e: e.activation(out=R1COL(t_), in_=BLL2[t_ % 2],
                                                  func=AF.Exp, scale=-0.5))(ti),
               inc="Sc")
        bstats_sc[ti] = P.c("Sc")

    LNS = lambda b: stats[:, b:b + 1]
    LNM = lambda b: stats[:, b + 1:b + 2]
    LNV4 = lambda b: stats[:, b + 2:b + 3]
    LNL = lambda b: stats[:, b + 3:b + 4]
    LNR = lambda b: stats[:, b + 4:b + 5]
    LNK = lambda b: stats[:, b + 5:b + 6]
    vcast_n = {}

    def emit_lnb_v1(chains):
        for b, cch in chains:
            P.emit("vector",
                   (lambda c_, b_: lambda e: e.reduce_sum(
                       LNS(b_), resid(c_), axis=AX, negate=True))(cch, b), inc="V")
        for b, cch in chains:
            P.emit("vector",
                   (lambda b_: lambda e: e.tensor_scalar_mul(
                       LNM(b_), LNS(b_), 1.0 / 1024))(b), inc="V")
        for b, cch in chains:
            P.emit("vector",
                   (lambda c_, b_: lambda e: e.scalar_tensor_tensor(
                       out=varsink[:], in0=resid(c_), scalar=LNM(b_), in1=resid(c_),
                       op0=OP.add, op1=OP.mult, accum_out=LNV4(b_)))(cch, b), inc="V")
        return P.c("V")

    def emit_lnb_sc(chains, vwait):
        for b, cch in chains:
            P.emit("scalar",
                   (lambda b_: lambda e: e.activation(
                       out=LNL(b_), in_=LNV4(b_), func=AF.Ln,
                       scale=1.0 / 1024, bias=EPS))(b),
                   waits=[("V", vwait)], inc="Sc")
        for b, cch in chains:
            P.emit("scalar",
                   (lambda b_: lambda e: e.activation(
                       out=LNR(b_), in_=LNL(b_), func=AF.Exp, scale=-0.5))(b),
                   inc="Sc")
        return P.c("Sc")

    def emit_lnb_v2(chains, scwait, cast_e8=None, store=()):
        for b, cch in chains:
            P.emit("vector",
                   (lambda b_: lambda e: e.tensor_mul(LNK(b_), LNM(b_), LNR(b_)))(b),
                   waits=[("Sc", scwait)], inc="V")
        if cast_e8 is not None:
            slot = (cast_e8 % 2) * 2
            for sj, (b, cch) in enumerate(chains[-2:]):
                P.emit("vector",
                       (lambda c_, b_, sl_=slot + sj: lambda e: e.tensor_scalar(
                           out=xc[sl_][:], in0=resid(c_), scalar1=LNR(b_),
                           scalar2=LNK(b_), op0=OP.mult, op1=OP.add))(cch, b),
                       inc="V")
            vcast_n[cast_e8] = P.c("V")
        for b, cch in chains:
            P.emit("vector",
                   (lambda c_, b_: lambda e: e.tensor_scalar(
                       out=resid(c_), in0=resid(c_), scalar1=LNR(b_), scalar2=LNK(b_),
                       op0=OP.mult, op1=OP.add))(cch, b), inc="V")
        for b, cch in store:
            vfin = P.c("V")
            P.emit("sync",
                   (lambda c_: lambda e: e.dma_start(
                       out=y[c_ * 128:(c_ + 1) * 128, :], in_=resid(c_)))(cch),
                   waits=[("V", vfin)], inc="STY", inc_by=16)

    vcast_n_early = {}

    def emit_ln1_final_c0():
        scw = bstats_sc[1]
        P.emit("vector", lambda e: e.engine_nop(), waits=[("T", A_END_T)], inc="V")
        for sj, cch in enumerate((0, 1)):
            P.emit("vector",
                   (lambda c_, sj_: lambda e: e.tensor_mul(
                       K1TMP[sj_], M1COL(c_), R1COL(c_)))(cch, sj),
                   waits=[("Sc", scw)], inc="V")
        for sj, cch in enumerate((0, 1)):
            P.emit("vector",
                   (lambda c_, sl_, sj_: lambda e: e.tensor_scalar(
                       out=xc[sl_][:], in0=resid(c_), scalar1=R1COL(c_),
                       scalar2=K1TMP[sj_], op0=OP.mult, op1=OP.add))(cch, sj, sj),
                   inc="V")
        vcast_n_early[0] = P.c("V")
        for sj, cch in enumerate((0, 1)):
            P.emit("vector",
                   (lambda c_, sj_: lambda e: e.tensor_scalar(
                       out=resid(c_), in0=resid(c_), scalar1=R1COL(c_),
                       scalar2=K1TMP[sj_], op0=OP.mult, op1=OP.add))(cch, sj),
                   inc="V")

    for ti in (0, 1):
        for st in range(4):
            bk = (ti % 2) * 4 + st
            emit_score_tile(ti, st, [("V", A_END_V), ("Sc", A_END_S)] + pfree[bk])
        emit_negmax_combine(ti)

    for tj in range(NS):
        emit_softmax(tj)
        emit_transposes(tj)
        if tj >= 1:
            emit_ln1_bstats(tj - 1)
        emit_av(tj)
        emit_oevac(tj)
        if tj + 2 < NS:
            for st in (0, 1):
                ws = [("Sc", exp_done[tj])]
                if tj < 2:
                    ws.append(ebsm_evac[tj])
                emit_score_tile(tj + 2, st, ws)
            emit_score_tile(tj + 2, 2, [("V", oevac_h1[tj])])
            emit_score_tile(tj + 2, 3, [("V", oevac_done[tj])])
            emit_negmax_combine(tj + 2)
        if tj == NS - 1:
            emit_ln1_bstats(tj)
        if tj == 4:
            emit_ln1_final_c0()
        if tj == 6:
            P.emit("gpsimd", lambda e: e.dma_start(
                out=xch[:, 4096:8192].rearrange("p (ei c) -> p ei c", ei=8),
                in_=bass.AP(w1bf.tensor, 512,
                            [[4096, 128], [128 * 4096, 8], [1, 512]])),
                waits=[("T", P.c("T")), ("LCW", LCW_ALL)], inc="LW1S1", inc_by=16)
        if tj == 13:
            t_after_s15 = P.c("T")
            w1_all = rFG[:, 0:32768].rearrange("p (ei h) -> p ei h", ei=8)
            w2r3 = lambda i: rH[:, i * 4096:(i + 1) * 4096].rearrange(
                "p (a b) -> p a b", a=4)
            def w1dma(k):
                P.emit("gpsimd",
                       (lambda k_: lambda e: e.dma_start(
                           out=w1_all[:, :, k_ * 512:(k_ + 1) * 512],
                           in_=bass.AP(w1bf.tensor, k_ * 512,
                                       [[4096, 128], [128 * 4096, 8], [1, 512]])))(k),
                       waits=[("T", t_after_s15), ("LCW", LCW_ALL)],
                       inc=f"LW1S{k}", inc_by=16)
            def w2rdma(i, half):
                P.emit("gpsimd",
                       (lambda i_, h_: lambda e: e.dma_start(
                           out=rH[:, (i_ * 4 + h_ * 2) * 1024:
                                  (i_ * 4 + h_ * 2 + 2) * 1024].rearrange(
                               "p (a b) -> p a b", a=2),
                           in_=bass.AP(w2bf.tensor, (i_ * 4 + h_ * 2) * 128 * 1024,
                                       [[1024, 128], [128 * 1024, 2], [1, 1024]])))(
                           i, half),
                       waits=[("T", P.c("T")), ("LCW", LCW_ALL)],
                       inc=f"LW2R{i}" + ("B" if half else ""), inc_by=16)
            w1dma(0)
    B_END_T = P.c("T")
    B_END_V = P.c("V")
    w2rdma(0, 0)
    w1dma(2)
    w2rdma(0, 1)
    w2rdma(1, 0)
    w2rdma(1, 1)
    w1dma(3)
    w2rdma(2, 0)
    w2rdma(2, 1)
    w2rdma(3, 0)
    w2rdma(3, 1)
    w1dma(4)
    w1dma(5)
    w1dma(6)
    w1dma(7)

    xcT_done, x2fm_evac = {}, {}
    ffn1_done, gelu_done, ffn2_done, r2evac_done = {}, {}, {}, {}
    r2evac_h1 = {}
    lw2s_cnt = [0, 0, 0, 0]
    w2s_seq = {}
    FB = (2, 3, 0)
    hid3 = [rE[:, 2048:2304], rE[:, 2304:2560], rE[:, 2560:2816]]

    ltc_cnt = [0, 0]
    x2fm_ready = {}

    def emit_c_head_t(e8):
        slot = (e8 % 2) * 2
        par = e8 % 2
        for sj in range(2):
            waits = [("V", vcast_n[e8])]
            if sj == 0:
                if e8 == 0:
                    waits.append(("T", tail_trT[14]))
                elif e8 == 1:
                    waits.append(("T", B_END_T))
                else:
                    waits.append(("T", ffn1_done[(e8 - 2) * NH + NH - 1]))
                    waits.append((f"LTC{par}", ltc_cnt[par]))
            P.emit("sync",
                   (lambda sj_, sl_, par_: lambda e: e.dma_start_transpose(
                       x2f_all2[par_][:, :, sj_ * 128:(sj_ + 1) * 128],
                       xc[sl_ + sj_][:].rearrange("p (k c) -> p k c", k=8)))(
                       sj, slot, par),
                   waits=waits, inc=f"LTC{par}", inc_by=16)
            ltc_cnt[par] += 16
        x2fm_ready[e8] = (f"LTC{par}", ltc_cnt[par])

    def w2s_prefetch(g, gate_g):
        e8, hc = divmod(g, NH)
        sl = hc % 4
        if gate_g is None:
            waits = [("T", B_END_T)]
        else:
            waits = [("T", ffn2_done[gate_g])]
        waits.append(("LCW", LCW_ALL))
        lw2s_cnt[sl] += 1
        w2s_seq[g] = lw2s_cnt[sl]
        P.emit("gpsimd",
               (lambda hc_, sl_: lambda e: e.dma_start(
                   out=w2s[sl_], in_=w2bf[hc_ * 128:(hc_ + 1) * 128, :]))(hc, sl),
               waits=waits, inc=f"LW2S{sl}", inc_by=16)

    for k in range(4):
        w2s_prefetch(W2_RES + k, None)

    def emit_ffn1(g):
        e8, hc = divmod(g, NH)
        fbk = FB[g % 3]
        waits = [x2fm_ready[e8], (f"LW1S{hc // 4}", 16)]
        if g - 3 >= 0:
            waits.append(("Sc", gelu_done[g - 3]))
        else:
            waits.append(("V", oevac_done[14]))
            if g == 2:
                waits.append(ebsm_evac[14])
        def tfn(e, hc=hc, fbk=fbk, e8=e8):
            ins = None
            xb_ = x2fm2[e8 % 2]
            for ci in range(NC):
                if 4 <= hc < 8:
                    lhsT = xch[:, 4096 + ci * 512 + (hc - 4) * 128:
                               4096 + ci * 512 + (hc - 3) * 128]
                else:
                    lhsT = w1sl(ci, hc * 128, 128)
                ins = e.matmul(bank(fbk, 0, 256), lhsT,
                               xb_[:, ci * 256:(ci + 1) * 256],
                               start=(ci == 0), stop=(ci == NC - 1))
            return ins
        P.emit("tensor", tfn, waits=waits, inc="T")
        ffn1_done[g] = P.c("T")

    def emit_gelu(g):
        e8, hc = divmod(g, NH)
        fbk = FB[g % 3]
        waits = [("T", ffn1_done[g]), ("LMISC", LMISC_ALL)]
        if g - 3 >= 0:
            waits.append(("T", ffn2_done[g - 3]))
        P.emit("scalar",
               (lambda hc_, fbk_, g_: lambda e: e.activation(
                   out=hid3[g_ % 3][:, 0:256], in_=bank(fbk_, 0, 256),
                   func=AF.Gelu, bias=b1sb[:, hc_:hc_ + 1], scale=1.0))(hc, fbk, g),
               waits=waits, inc="Sc")
        gelu_done[g] = P.c("Sc")

    def emit_ffn2(g):
        e8, hc = divmod(g, NH)
        waits = [("Sc", gelu_done[g])]
        if hc < W2_RES and e8 == 0:
            waits.append((f"LW2R{hc // 4}" + ("B" if hc % 4 >= 2 else ""), 16))
        if hc >= W2_RES:
            waits.append((f"LW2S{hc % 4}", 16 * w2s_seq[g]))
        def tfn2(e, hc=hc, g=g, sjs=(0, 1)):
            ins = None
            for sj in sjs:
                for et in range(2):
                    rhs = (w2res(hc, et * 512, 512) if hc < W2_RES
                           else w2s[hc % 4][:, et * 512:(et + 1) * 512])
                    ins = e.matmul(bank(4 + sj * 2 + et, 0, 512),
                                   hid3[g % 3][:, sj * 128:(sj + 1) * 128], rhs,
                                   start=(hc == 0), stop=False)
            return ins
        if hc == 0:
            if e8 > 0:
                w0, w1_ = [("V", r2evac_h1[e8 - 1])], [("V", r2evac_done[e8 - 1])]
            else:
                w0 = [ebsm_evac[15]]
                w1_ = [("V", oevac_done[15])]
            P.emit("tensor", functools.partial(tfn2, sjs=(0,)),
                   waits=waits + w0, inc="T")
            P.emit("tensor", functools.partial(tfn2, sjs=(1,)),
                   waits=w1_, inc="T")
        else:
            P.emit("tensor", tfn2, waits=waits, inc="T")
        ffn2_done[g] = P.c("T")
        if hc >= 16:
            nxt = g + 4 if hc <= 27 else (g - hc) + NH + 16 + (hc % 4)
            if nxt < 8 * NH and nxt % NH >= 16:
                w2s_prefetch(nxt, g)

    def emit_fold_r2(e8):
        c0, c1 = 2 * e8, 2 * e8 + 1
        def foldfn(e, sj):
            ins = None
            for et in range(2):
                ins = e.matmul(bank(4 + sj * 2 + et, 0, 512), ones1[0:1, :],
                               b2rowb[0:1, et * 512:(et + 1) * 512],
                               start=False, stop=True)
            return ins
        for sj, cch in enumerate((c0, c1)):
            P.emit("tensor", (lambda sj_: lambda e: foldfn(e, sj_))(sj),
                   waits=([("LMISCG", 16)] if sj == 0 else []), inc="T")
            fold_t = P.c("T")
            P.emit("vector",
                   (lambda c_, sj_: lambda e: e.tensor_tensor(
                       out=resid(c_), in0=ps[:, (4 + sj_ * 2) * 512:
                                              (6 + sj_ * 2) * 512],
                       in1=resid(c_), op=OP.add))(cch, sj),
                   waits=[("T", fold_t)], inc="V")
            if sj == 0:
                r2evac_h1[e8] = P.c("V")
        r2evac_done[e8] = P.c("V")

    def emit_ln1_final_c(e8):
        slot = (e8 % 2) * 2
        scw = bstats_sc[2 * e8 + 1]
        if e8 >= 2:
            P.emit("vector", lambda e: e.engine_nop(),
                   waits=[x2fm_ready[e8 - 2]], inc="V")
        for sj, cch in enumerate((2 * e8, 2 * e8 + 1)):
            P.emit("vector",
                   (lambda c_, sj_: lambda e: e.tensor_mul(
                       K1TMP[sj_], M1COL(c_), R1COL(c_)))(cch, sj),
                   waits=[("Sc", scw)], inc="V")
        for sj, cch in enumerate((2 * e8, 2 * e8 + 1)):
            P.emit("vector",
                   (lambda c_, sl_, sj_: lambda e: e.tensor_scalar(
                       out=xc[sl_][:], in0=resid(c_), scalar1=R1COL(c_),
                       scalar2=K1TMP[sj_], op0=OP.mult, op1=OP.add))(
                       cch, slot + sj, sj),
                   inc="V")
        vcast_n[e8] = P.c("V")
        for sj, cch in enumerate((2 * e8, 2 * e8 + 1)):
            P.emit("vector",
                   (lambda c_, sj_: lambda e: e.tensor_scalar(
                       out=resid(c_), in0=resid(c_), scalar1=R1COL(c_),
                       scalar2=K1TMP[sj_], op0=OP.mult, op1=OP.add))(cch, sj),
                   inc="V")

    vcast_n[0] = vcast_n_early[0]
    emit_c_head_t(0)
    lnch, lnb_vw, lnb_sw = None, None, None
    for g in range(8 * NH):
        e8, hc = divmod(g, NH)
        if g >= 3:
            gg = g - 3
            emit_ffn2(gg)
            if gg % NH == NH - 1:
                emit_fold_r2(gg // NH)
        emit_ffn1(g)
        emit_gelu(g)
        if hc == 2 and e8 > 0:
            lnch = [(0, 2 * (e8 - 1)), (6, 2 * (e8 - 1) + 1)]
            lnb_vw = emit_lnb_v1(lnch)
        if hc == 6 and e8 + 1 < 8:
            emit_ln1_final_c(e8 + 1)
        if hc == 8 and e8 > 0:
            lnb_sw = emit_lnb_sc(lnch, lnb_vw)
        if hc == 10 and e8 > 0:
            emit_lnb_v2(lnch, lnb_sw, store=lnch)
        if hc == 12 and e8 + 1 < 8:
            emit_c_head_t(e8 + 1)
    emit_ffn2(8 * NH - 3)
    emit_ffn2(8 * NH - 2)
    emit_ffn2(8 * NH - 1)
    emit_fold_r2(7)

    ch7 = [(0, 14), (6, 15)]
    vw7 = emit_lnb_v1(ch7)
    sw7 = emit_lnb_sc(ch7, vw7)
    emit_lnb_v2(ch7, sw7, store=ch7)
    P.emit("sync", lambda e: e.wait_ge(sems["STY"], 16 * NS))

    block = ctx.enter_context(nc.Block())

    def emit_stream(eng_name):
        def run(eng):
            last = {}
            for fn, waits, inc, inc_by in P.ops[eng_name]:
                for s, v in waits:
                    if last.get(s, 0) < v:
                        eng.wait_ge(sems[s], v)
                        last[s] = v
                ins = fn(eng)
                if inc is not None:
                    if ins is None:
                        ins = eng.nop()
                    ins.then_inc(sems[inc], inc_by)
        return run

    block.tensor(emit_stream("tensor"))
    block.vector(emit_stream("vector"))
    block.scalar(emit_stream("scalar"))
    block.gpsimd(emit_stream("gpsimd"))
    block.sync(emit_stream("sync"))

    ctx.close()
    return nc


def make_in_map(inputs, core):
    return {
        "x": np.ascontiguousarray(inputs["L0_attn_x"][core]),
        "wq": np.ascontiguousarray(inputs["L0_attn_Wq"]),
        "wk": np.ascontiguousarray(inputs["L0_attn_Wk"]),
        "wv": np.ascontiguousarray(inputs["L0_attn_Wv"]),
        "gamma": np.ascontiguousarray(inputs["L0_attn_gamma"]).reshape(1, 1),
        "w1": np.ascontiguousarray(inputs["L0_ffn_w1"]),
        "b1": np.ascontiguousarray(inputs["L0_ffn_b1"]),
        "w2": np.ascontiguousarray(inputs["L0_ffn_w2"]),
        "b2": np.ascontiguousarray(inputs["L0_ffn_b2"]),
    }


def make_in_map(inputs, core):
    return {
        "x": np.ascontiguousarray(inputs["L0_attn_x"][core]),
        "wq": np.ascontiguousarray(inputs["L0_attn_Wq"]),
        "wk": np.ascontiguousarray(inputs["L0_attn_Wk"]),
        "wv": np.ascontiguousarray(inputs["L0_attn_Wv"]),
        "gamma": np.ascontiguousarray(inputs["L0_attn_gamma"]).reshape(1, 1),
        "w1": np.ascontiguousarray(inputs["L0_ffn_w1"]),
        "b1": np.ascontiguousarray(inputs["L0_ffn_b1"]),
        "w2": np.ascontiguousarray(inputs["L0_ffn_w2"]),
        "b2": np.ascontiguousarray(inputs["L0_ffn_b2"]),
    }


def make_in_map(inputs, core):
    return {
        "x": np.ascontiguousarray(inputs["L0_attn_x"][core]),
        "wq": np.ascontiguousarray(inputs["L0_attn_Wq"]),
        "wk": np.ascontiguousarray(inputs["L0_attn_Wk"]),
        "wv": np.ascontiguousarray(inputs["L0_attn_Wv"]),
        "gamma": np.ascontiguousarray(inputs["L0_attn_gamma"]).reshape(1, 1),
        "w1": np.ascontiguousarray(inputs["L0_ffn_w1"]),
        "b1": np.ascontiguousarray(inputs["L0_ffn_b1"]),
        "w2": np.ascontiguousarray(inputs["L0_ffn_w2"]),
        "b2": np.ascontiguousarray(inputs["L0_ffn_b2"]),
    }


def make_in_map(inputs, core):
    return {
        "x": np.ascontiguousarray(inputs["L0_attn_x"][core]),
        "wq": np.ascontiguousarray(inputs["L0_attn_Wq"]),
        "wk": np.ascontiguousarray(inputs["L0_attn_Wk"]),
        "wv": np.ascontiguousarray(inputs["L0_attn_Wv"]),
        "gamma": np.ascontiguousarray(inputs["L0_attn_gamma"]).reshape(1, 1),
        "w1": np.ascontiguousarray(inputs["L0_ffn_w1"]),
        "b1": np.ascontiguousarray(inputs["L0_ffn_b1"]),
        "w2": np.ascontiguousarray(inputs["L0_ffn_w2"]),
        "b2": np.ascontiguousarray(inputs["L0_ffn_b2"]),
    }


def make_in_map(inputs, core):
    return {
        "x": np.ascontiguousarray(inputs["L0_attn_x"][core]),
        "wq": np.ascontiguousarray(inputs["L0_attn_Wq"]),
        "wk": np.ascontiguousarray(inputs["L0_attn_Wk"]),
        "wv": np.ascontiguousarray(inputs["L0_attn_Wv"]),
        "gamma": np.ascontiguousarray(inputs["L0_attn_gamma"]).reshape(1, 1),
        "w1": np.ascontiguousarray(inputs["L0_ffn_w1"]),
        "b1": np.ascontiguousarray(inputs["L0_ffn_b1"]),
        "w2": np.ascontiguousarray(inputs["L0_ffn_w2"]),
        "b2": np.ascontiguousarray(inputs["L0_ffn_b2"]),
    }


def make_in_map(inputs, core):
    return {
        "x": np.ascontiguousarray(inputs["L0_attn_x"][core]),
        "wq": np.ascontiguousarray(inputs["L0_attn_Wq"]),
        "wk": np.ascontiguousarray(inputs["L0_attn_Wk"]),
        "wv": np.ascontiguousarray(inputs["L0_attn_Wv"]),
        "gamma": np.ascontiguousarray(inputs["L0_attn_gamma"]).reshape(1, 1),
        "w1": np.ascontiguousarray(inputs["L0_ffn_w1"]),
        "b1": np.ascontiguousarray(inputs["L0_ffn_b1"]),
        "w2": np.ascontiguousarray(inputs["L0_ffn_w2"]),
        "b2": np.ascontiguousarray(inputs["L0_ffn_b2"]),
    }


def make_in_map(inputs, core):
    return {
        "x": np.ascontiguousarray(inputs["L0_attn_x"][core]),
        "wq": np.ascontiguousarray(inputs["L0_attn_Wq"]),
        "wk": np.ascontiguousarray(inputs["L0_attn_Wk"]),
        "wv": np.ascontiguousarray(inputs["L0_attn_Wv"]),
        "gamma": np.ascontiguousarray(inputs["L0_attn_gamma"]).reshape(1, 1),
        "w1": np.ascontiguousarray(inputs["L0_ffn_w1"]),
        "b1": np.ascontiguousarray(inputs["L0_ffn_b1"]),
        "w2": np.ascontiguousarray(inputs["L0_ffn_w2"]),
        "b2": np.ascontiguousarray(inputs["L0_ffn_b2"]),
    }


def make_in_map(inputs, core):
    return {
        "x": np.ascontiguousarray(inputs["L0_attn_x"][core]),
        "wq": np.ascontiguousarray(inputs["L0_attn_Wq"]),
        "wk": np.ascontiguousarray(inputs["L0_attn_Wk"]),
        "wv": np.ascontiguousarray(inputs["L0_attn_Wv"]),
        "gamma": np.ascontiguousarray(inputs["L0_attn_gamma"]).reshape(1, 1),
        "w1": np.ascontiguousarray(inputs["L0_ffn_w1"]),
        "b1": np.ascontiguousarray(inputs["L0_ffn_b1"]),
        "w2": np.ascontiguousarray(inputs["L0_ffn_w2"]),
        "b2": np.ascontiguousarray(inputs["L0_ffn_b2"]),
    }


def make_in_map(inputs, core):
    return {
        "x": np.ascontiguousarray(inputs["L0_attn_x"][core]),
        "wq": np.ascontiguousarray(inputs["L0_attn_Wq"]),
        "wk": np.ascontiguousarray(inputs["L0_attn_Wk"]),
        "wv": np.ascontiguousarray(inputs["L0_attn_Wv"]),
        "gamma": np.ascontiguousarray(inputs["L0_attn_gamma"]).reshape(1, 1),
        "w1": np.ascontiguousarray(inputs["L0_ffn_w1"]),
        "b1": np.ascontiguousarray(inputs["L0_ffn_b1"]),
        "w2": np.ascontiguousarray(inputs["L0_ffn_w2"]),
        "b2": np.ascontiguousarray(inputs["L0_ffn_b2"]),
    }



def make_in_map(inputs, core):
    return {
        "x": np.ascontiguousarray(inputs["L0_attn_x"][core]),
        "wq": np.ascontiguousarray(inputs["L0_attn_Wq"]),
        "wk": np.ascontiguousarray(inputs["L0_attn_Wk"]),
        "wv": np.ascontiguousarray(inputs["L0_attn_Wv"]),
        "gamma": np.ascontiguousarray(inputs["L0_attn_gamma"]).reshape(1, 1),
        "w1": np.ascontiguousarray(inputs["L0_ffn_w1"]),
        "b1": np.ascontiguousarray(inputs["L0_ffn_b1"]),
        "w2": np.ascontiguousarray(inputs["L0_ffn_w2"]),
        "b2": np.ascontiguousarray(inputs["L0_ffn_b2"]),
    }



def make_in_map(inputs, core):
    return {
        "x": np.ascontiguousarray(inputs["L0_attn_x"][core]),
        "wq": np.ascontiguousarray(inputs["L0_attn_Wq"]),
        "wk": np.ascontiguousarray(inputs["L0_attn_Wk"]),
        "wv": np.ascontiguousarray(inputs["L0_attn_Wv"]),
        "gamma": np.ascontiguousarray(inputs["L0_attn_gamma"]).reshape(1, 1),
        "w1": np.ascontiguousarray(inputs["L0_ffn_w1"]),
        "b1": np.ascontiguousarray(inputs["L0_ffn_b1"]),
        "w2": np.ascontiguousarray(inputs["L0_ffn_w2"]),
        "b2": np.ascontiguousarray(inputs["L0_ffn_b2"]),
    }



def make_in_map(inputs, core):
    return {
        "x": np.ascontiguousarray(inputs["L0_attn_x"][core]),
        "wq": np.ascontiguousarray(inputs["L0_attn_Wq"]),
        "wk": np.ascontiguousarray(inputs["L0_attn_Wk"]),
        "wv": np.ascontiguousarray(inputs["L0_attn_Wv"]),
        "gamma": np.ascontiguousarray(inputs["L0_attn_gamma"]).reshape(1, 1),
        "w1": np.ascontiguousarray(inputs["L0_ffn_w1"]),
        "b1": np.ascontiguousarray(inputs["L0_ffn_b1"]),
        "w2": np.ascontiguousarray(inputs["L0_ffn_w2"]),
        "b2": np.ascontiguousarray(inputs["L0_ffn_b2"]),
    }



def make_in_map(inputs, core):
    return {
        "x": np.ascontiguousarray(inputs["L0_attn_x"][core]),
        "wq": np.ascontiguousarray(inputs["L0_attn_Wq"]),
        "wk": np.ascontiguousarray(inputs["L0_attn_Wk"]),
        "wv": np.ascontiguousarray(inputs["L0_attn_Wv"]),
        "gamma": np.ascontiguousarray(inputs["L0_attn_gamma"]).reshape(1, 1),
        "w1": np.ascontiguousarray(inputs["L0_ffn_w1"]),
        "b1": np.ascontiguousarray(inputs["L0_ffn_b1"]),
        "w2": np.ascontiguousarray(inputs["L0_ffn_w2"]),
        "b2": np.ascontiguousarray(inputs["L0_ffn_b2"]),
    }



def make_in_map(inputs, core):
    return {
        "x": np.ascontiguousarray(inputs["L0_attn_x"][core]),
        "wq": np.ascontiguousarray(inputs["L0_attn_Wq"]),
        "wk": np.ascontiguousarray(inputs["L0_attn_Wk"]),
        "wv": np.ascontiguousarray(inputs["L0_attn_Wv"]),
        "gamma": np.ascontiguousarray(inputs["L0_attn_gamma"]).reshape(1, 1),
        "w1": np.ascontiguousarray(inputs["L0_ffn_w1"]),
        "b1": np.ascontiguousarray(inputs["L0_ffn_b1"]),
        "w2": np.ascontiguousarray(inputs["L0_ffn_w2"]),
        "b2": np.ascontiguousarray(inputs["L0_ffn_b2"]),
    }



def make_in_map(inputs, core):
    return {
        "x": np.ascontiguousarray(inputs["L0_attn_x"][core]),
        "wq": np.ascontiguousarray(inputs["L0_attn_Wq"]),
        "wk": np.ascontiguousarray(inputs["L0_attn_Wk"]),
        "wv": np.ascontiguousarray(inputs["L0_attn_Wv"]),
        "gamma": np.ascontiguousarray(inputs["L0_attn_gamma"]).reshape(1, 1),
        "w1": np.ascontiguousarray(inputs["L0_ffn_w1"]),
        "b1": np.ascontiguousarray(inputs["L0_ffn_b1"]),
        "w2": np.ascontiguousarray(inputs["L0_ffn_w2"]),
        "b2": np.ascontiguousarray(inputs["L0_ffn_b2"]),
    }



def make_in_map(inputs, core):
    return {
        "x": np.ascontiguousarray(inputs["L0_attn_x"][core]),
        "wq": np.ascontiguousarray(inputs["L0_attn_Wq"]),
        "wk": np.ascontiguousarray(inputs["L0_attn_Wk"]),
        "wv": np.ascontiguousarray(inputs["L0_attn_Wv"]),
        "gamma": np.ascontiguousarray(inputs["L0_attn_gamma"]).reshape(1, 1),
        "w1": np.ascontiguousarray(inputs["L0_ffn_w1"]),
        "b1": np.ascontiguousarray(inputs["L0_ffn_b1"]),
        "w2": np.ascontiguousarray(inputs["L0_ffn_w2"]),
        "b2": np.ascontiguousarray(inputs["L0_ffn_b2"]),
    }



def make_in_map(inputs, core):
    return {
        "x": np.ascontiguousarray(inputs["L0_attn_x"][core]),
        "wq": np.ascontiguousarray(inputs["L0_attn_Wq"]),
        "wk": np.ascontiguousarray(inputs["L0_attn_Wk"]),
        "wv": np.ascontiguousarray(inputs["L0_attn_Wv"]),
        "gamma": np.ascontiguousarray(inputs["L0_attn_gamma"]).reshape(1, 1),
        "w1": np.ascontiguousarray(inputs["L0_ffn_w1"]),
        "b1": np.ascontiguousarray(inputs["L0_ffn_b1"]),
        "w2": np.ascontiguousarray(inputs["L0_ffn_w2"]),
        "b2": np.ascontiguousarray(inputs["L0_ffn_b2"]),
    }



def make_in_map(inputs, core):
    return {
        "x": np.ascontiguousarray(inputs["L0_attn_x"][core]),
        "wq": np.ascontiguousarray(inputs["L0_attn_Wq"]),
        "wk": np.ascontiguousarray(inputs["L0_attn_Wk"]),
        "wv": np.ascontiguousarray(inputs["L0_attn_Wv"]),
        "gamma": np.ascontiguousarray(inputs["L0_attn_gamma"]).reshape(1, 1),
        "w1": np.ascontiguousarray(inputs["L0_ffn_w1"]),
        "b1": np.ascontiguousarray(inputs["L0_ffn_b1"]),
        "w2": np.ascontiguousarray(inputs["L0_ffn_w2"]),
        "b2": np.ascontiguousarray(inputs["L0_ffn_b2"]),
    }



def make_in_map(inputs, core):
    return {
        "x": np.ascontiguousarray(inputs["L0_attn_x"][core]),
        "wq": np.ascontiguousarray(inputs["L0_attn_Wq"]),
        "wk": np.ascontiguousarray(inputs["L0_attn_Wk"]),
        "wv": np.ascontiguousarray(inputs["L0_attn_Wv"]),
        "gamma": np.ascontiguousarray(inputs["L0_attn_gamma"]).reshape(1, 1),
        "w1": np.ascontiguousarray(inputs["L0_ffn_w1"]),
        "b1": np.ascontiguousarray(inputs["L0_ffn_b1"]),
        "w2": np.ascontiguousarray(inputs["L0_ffn_w2"]),
        "b2": np.ascontiguousarray(inputs["L0_ffn_b2"]),
    }



def make_in_map(inputs, core):
    return {
        "x": np.ascontiguousarray(inputs["L0_attn_x"][core]),
        "wq": np.ascontiguousarray(inputs["L0_attn_Wq"]),
        "wk": np.ascontiguousarray(inputs["L0_attn_Wk"]),
        "wv": np.ascontiguousarray(inputs["L0_attn_Wv"]),
        "gamma": np.ascontiguousarray(inputs["L0_attn_gamma"]).reshape(1, 1),
        "w1": np.ascontiguousarray(inputs["L0_ffn_w1"]),
        "b1": np.ascontiguousarray(inputs["L0_ffn_b1"]),
        "w2": np.ascontiguousarray(inputs["L0_ffn_w2"]),
        "b2": np.ascontiguousarray(inputs["L0_ffn_b2"]),
    }



def make_in_map(inputs, core):
    return {
        "x": np.ascontiguousarray(inputs["L0_attn_x"][core]),
        "wq": np.ascontiguousarray(inputs["L0_attn_Wq"]),
        "wk": np.ascontiguousarray(inputs["L0_attn_Wk"]),
        "wv": np.ascontiguousarray(inputs["L0_attn_Wv"]),
        "gamma": np.ascontiguousarray(inputs["L0_attn_gamma"]).reshape(1, 1),
        "w1": np.ascontiguousarray(inputs["L0_ffn_w1"]),
        "b1": np.ascontiguousarray(inputs["L0_ffn_b1"]),
        "w2": np.ascontiguousarray(inputs["L0_ffn_w2"]),
        "b2": np.ascontiguousarray(inputs["L0_ffn_b2"]),
    }



def make_in_map(inputs, core):
    return {
        "x": np.ascontiguousarray(inputs["L0_attn_x"][core]),
        "wq": np.ascontiguousarray(inputs["L0_attn_Wq"]),
        "wk": np.ascontiguousarray(inputs["L0_attn_Wk"]),
        "wv": np.ascontiguousarray(inputs["L0_attn_Wv"]),
        "gamma": np.ascontiguousarray(inputs["L0_attn_gamma"]).reshape(1, 1),
        "w1": np.ascontiguousarray(inputs["L0_ffn_w1"]),
        "b1": np.ascontiguousarray(inputs["L0_ffn_b1"]),
        "w2": np.ascontiguousarray(inputs["L0_ffn_w2"]),
        "b2": np.ascontiguousarray(inputs["L0_ffn_b2"]),
    }



def make_in_map(inputs, core):
    return {
        "x": np.ascontiguousarray(inputs["L0_attn_x"][core]),
        "wq": np.ascontiguousarray(inputs["L0_attn_Wq"]),
        "wk": np.ascontiguousarray(inputs["L0_attn_Wk"]),
        "wv": np.ascontiguousarray(inputs["L0_attn_Wv"]),
        "gamma": np.ascontiguousarray(inputs["L0_attn_gamma"]).reshape(1, 1),
        "w1": np.ascontiguousarray(inputs["L0_ffn_w1"]),
        "b1": np.ascontiguousarray(inputs["L0_ffn_b1"]),
        "w2": np.ascontiguousarray(inputs["L0_ffn_w2"]),
        "b2": np.ascontiguousarray(inputs["L0_ffn_b2"]),
    }



def make_in_map(inputs, core):
    return {
        "x": np.ascontiguousarray(inputs["L0_attn_x"][core]),
        "wq": np.ascontiguousarray(inputs["L0_attn_Wq"]),
        "wk": np.ascontiguousarray(inputs["L0_attn_Wk"]),
        "wv": np.ascontiguousarray(inputs["L0_attn_Wv"]),
        "gamma": np.ascontiguousarray(inputs["L0_attn_gamma"]).reshape(1, 1),
        "w1": np.ascontiguousarray(inputs["L0_ffn_w1"]),
        "b1": np.ascontiguousarray(inputs["L0_ffn_b1"]),
        "w2": np.ascontiguousarray(inputs["L0_ffn_w2"]),
        "b2": np.ascontiguousarray(inputs["L0_ffn_b2"]),
    }



def make_in_map(inputs, core):
    return {
        "x": np.ascontiguousarray(inputs["L0_attn_x"][core]),
        "wq": np.ascontiguousarray(inputs["L0_attn_Wq"]),
        "wk": np.ascontiguousarray(inputs["L0_attn_Wk"]),
        "wv": np.ascontiguousarray(inputs["L0_attn_Wv"]),
        "gamma": np.ascontiguousarray(inputs["L0_attn_gamma"]).reshape(1, 1),
        "w1": np.ascontiguousarray(inputs["L0_ffn_w1"]),
        "b1": np.ascontiguousarray(inputs["L0_ffn_b1"]),
        "w2": np.ascontiguousarray(inputs["L0_ffn_w2"]),
        "b2": np.ascontiguousarray(inputs["L0_ffn_b2"]),
    }



def make_in_map(inputs, core):
    return {
        "x": np.ascontiguousarray(inputs["L0_attn_x"][core]),
        "wq": np.ascontiguousarray(inputs["L0_attn_Wq"]),
        "wk": np.ascontiguousarray(inputs["L0_attn_Wk"]),
        "wv": np.ascontiguousarray(inputs["L0_attn_Wv"]),
        "gamma": np.ascontiguousarray(inputs["L0_attn_gamma"]).reshape(1, 1),
        "w1": np.ascontiguousarray(inputs["L0_ffn_w1"]),
        "b1": np.ascontiguousarray(inputs["L0_ffn_b1"]),
        "w2": np.ascontiguousarray(inputs["L0_ffn_w2"]),
        "b2": np.ascontiguousarray(inputs["L0_ffn_b2"]),
    }



def make_in_map(inputs, core):
    return {
        "x": np.ascontiguousarray(inputs["L0_attn_x"][core]),
        "wq": np.ascontiguousarray(inputs["L0_attn_Wq"]),
        "wk": np.ascontiguousarray(inputs["L0_attn_Wk"]),
        "wv": np.ascontiguousarray(inputs["L0_attn_Wv"]),
        "gamma": np.ascontiguousarray(inputs["L0_attn_gamma"]).reshape(1, 1),
        "w1": np.ascontiguousarray(inputs["L0_ffn_w1"]),
        "b1": np.ascontiguousarray(inputs["L0_ffn_b1"]),
        "w2": np.ascontiguousarray(inputs["L0_ffn_w2"]),
        "b2": np.ascontiguousarray(inputs["L0_ffn_b2"]),
    }



def make_in_map(inputs, core):
    return {
        "x": np.ascontiguousarray(inputs["L0_attn_x"][core]),
        "wq": np.ascontiguousarray(inputs["L0_attn_Wq"]),
        "wk": np.ascontiguousarray(inputs["L0_attn_Wk"]),
        "wv": np.ascontiguousarray(inputs["L0_attn_Wv"]),
        "gamma": np.ascontiguousarray(inputs["L0_attn_gamma"]).reshape(1, 1),
        "w1": np.ascontiguousarray(inputs["L0_ffn_w1"]),
        "b1": np.ascontiguousarray(inputs["L0_ffn_b1"]),
        "w2": np.ascontiguousarray(inputs["L0_ffn_w2"]),
        "b2": np.ascontiguousarray(inputs["L0_ffn_b2"]),
    }



def make_in_map(inputs, core):
    return {
        "x": np.ascontiguousarray(inputs["L0_attn_x"][core]),
        "wq": np.ascontiguousarray(inputs["L0_attn_Wq"]),
        "wk": np.ascontiguousarray(inputs["L0_attn_Wk"]),
        "wv": np.ascontiguousarray(inputs["L0_attn_Wv"]),
        "gamma": np.ascontiguousarray(inputs["L0_attn_gamma"]).reshape(1, 1),
        "w1": np.ascontiguousarray(inputs["L0_ffn_w1"]),
        "b1": np.ascontiguousarray(inputs["L0_ffn_b1"]),
        "w2": np.ascontiguousarray(inputs["L0_ffn_w2"]),
        "b2": np.ascontiguousarray(inputs["L0_ffn_b2"]),
    }



def make_in_map(inputs, core):
    return {
        "x": np.ascontiguousarray(inputs["L0_attn_x"][core]),
        "wq": np.ascontiguousarray(inputs["L0_attn_Wq"]),
        "wk": np.ascontiguousarray(inputs["L0_attn_Wk"]),
        "wv": np.ascontiguousarray(inputs["L0_attn_Wv"]),
        "gamma": np.ascontiguousarray(inputs["L0_attn_gamma"]).reshape(1, 1),
        "w1": np.ascontiguousarray(inputs["L0_ffn_w1"]),
        "b1": np.ascontiguousarray(inputs["L0_ffn_b1"]),
        "w2": np.ascontiguousarray(inputs["L0_ffn_w2"]),
        "b2": np.ascontiguousarray(inputs["L0_ffn_b2"]),
    }



def make_in_map(inputs, core):
    return {
        "x": np.ascontiguousarray(inputs["L0_attn_x"][core]),
        "wq": np.ascontiguousarray(inputs["L0_attn_Wq"]),
        "wk": np.ascontiguousarray(inputs["L0_attn_Wk"]),
        "wv": np.ascontiguousarray(inputs["L0_attn_Wv"]),
        "gamma": np.ascontiguousarray(inputs["L0_attn_gamma"]).reshape(1, 1),
        "w1": np.ascontiguousarray(inputs["L0_ffn_w1"]),
        "b1": np.ascontiguousarray(inputs["L0_ffn_b1"]),
        "w2": np.ascontiguousarray(inputs["L0_ffn_w2"]),
        "b2": np.ascontiguousarray(inputs["L0_ffn_b2"]),
    }




def make_in_map(inputs, core):
    f32 = np.float32
    return {
        "x": np.ascontiguousarray(np.asarray(inputs["L0_attn_x"], dtype=f32)[core]),
        "wq": np.ascontiguousarray(np.asarray(inputs["L0_attn_Wq"], dtype=f32)),
        "wk": np.ascontiguousarray(np.asarray(inputs["L0_attn_Wk"], dtype=f32)),
        "wv": np.ascontiguousarray(np.asarray(inputs["L0_attn_Wv"], dtype=f32)),
        "gamma": np.ascontiguousarray(np.asarray(inputs["L0_attn_gamma"], dtype=f32)).reshape(1, 1),
        "w1": np.ascontiguousarray(np.asarray(inputs["L0_ffn_w1"], dtype=f32)),
        "b1": np.ascontiguousarray(np.asarray(inputs["L0_ffn_b1"], dtype=f32)),
        "w2": np.ascontiguousarray(np.asarray(inputs["L0_ffn_w2"], dtype=f32)),
        "b2": np.ascontiguousarray(np.asarray(inputs["L0_ffn_b2"], dtype=f32)),
    }


def kernel(**inputs):
    from concourse.bass_utils import run_bass_kernel_spmd
    nc = build_nc()
    in_maps = [make_in_map(inputs, core) for core in range(8)]
    res = run_bass_kernel_spmd(nc, in_maps, core_ids=list(range(8)))
    return np.stack([np.asarray(res.results[i]["y"], dtype=np.float32)
                     for i in range(8)], axis=0)
